# version 01
# baseline (speedup 1.0000x reference)
from contextlib import ExitStack

import numpy as np
import ml_dtypes
import concourse.bass as bass
import concourse.mybir as mybir
from concourse.bass_utils import run_bass_kernel_spmd

F32 = mybir.dt.float32
BF16 = mybir.dt.bfloat16
AF = mybir.ActivationFunctionType
ALU = mybir.AluOpType
AX = mybir.AxisListType

NCORES = 8
D = 1024
T = 2048
TG = 512
NTG = T // TG
NCH = D // 128
SEQ = 4096
MAGIC = 12582912.0

PAIRS = [[0, 1], [2, 3], [4, 5], [6, 7]]
ALL8 = [[0, 1, 2, 3, 4, 5, 6, 7]]
BF_NP = ml_dtypes.bfloat16



class S:
    def __init__(self, h):
        self.h = h
        self.n = 0

    def tok(self):
        return (self, self.n)


class Prog:
    def __init__(self, name="prog"):
        self.nc = bass.Bass("TRN2", target_bir_lowering=False)
        self.stack = ExitStack()
        self.stack.__enter__()
        self.host = {}
        self.out_names = []
        self._uid = 0
        self._maxwait = {}
        self.debug = False
        self.marks = {}

    def mark(self, name, tok):
        self.marks[name] = tok

    def need(self, *names):
        return [self.marks[n] for n in names if n in self.marks]

    def uid(self):
        self._uid += 1
        return self._uid

    def sem(self, name="s"):
        return S(self.stack.enter_context(self.nc.semaphore(f"{name}_{self.uid()}")))

    def gsem(self, name):
        if not hasattr(self, "_gsems"):
            self._gsems = {}
        if name not in self._gsems:
            self._gsems[name] = self.sem(name)
        return self._gsems[name]

    def in_(self, name, arrs, dtype=None):
        if not isinstance(arrs, (list, tuple)):
            arrs = [arrs] * NCORES
        arrs = [np.ascontiguousarray(a) for a in arrs]
        a0 = arrs[0]
        if dtype is None:
            dtype = {np.dtype(np.float32): F32, np.dtype(BF_NP): BF16,
                     np.dtype(np.int32): mybir.dt.int32}[a0.dtype]
        assert name not in self.host
        self.host[name] = arrs
        return self.nc.dram_tensor(name, list(a0.shape), dtype, kind="ExternalInput").ap()

    def out_(self, name, shape, dtype=F32):
        self.out_names.append(name)
        return self.nc.dram_tensor(name, list(shape), dtype, kind="ExternalOutput").ap()

    def dram(self, name, shape, dtype):
        return self.nc.dram_tensor(name, list(shape), dtype, kind="Internal")

    def sb(self, name, shape, dtype):
        return self.stack.enter_context(self.nc.sbuf_tensor(name, list(shape), dtype))

    def ps(self, name, shape, dtype=F32):
        return self.stack.enter_context(self.nc.psum_tensor(name, list(shape), dtype))

    def run(self, trace=False, sim=False, sim_cores=1):
        in_maps = [{k: v[c] for k, v in self.host.items()} for c in range(NCORES)]
        if sim:
            import concourse.bass_interp as bass_interp
            smc = bass_interp.MultiCoreSim(self.nc, sim_cores)
            for i in range(sim_cores):
                for k, v in in_maps[i].items():
                    smc.cores[i].tensor(k)[:] = v
            smc.simulate()

            class R:
                pass
            r = R()
            r.results = [{k: np.asarray(smc.cores[i].mem_tensor(k))
                          for k in self.out_names} for i in range(sim_cores)]
            return r
        return run_bass_kernel_spmd(self.nc, in_maps, core_ids=list(range(NCORES)),
                                    trace=trace)

    def close(self):
        self.stack.__exit__(None, None, None)


class Ph:
    ENGS = ("tensor", "vector", "scalar", "gpsimd", "sync")

    def __init__(self, P, name):
        self.P = P
        self.name = f"{name}_{P.uid()}"
        self.ops = {e: [] for e in self.ENGS}

    def __enter__(self):
        return self

    def __exit__(self, et, ev, tb):
        if et is not None:
            return False
        with self.P.nc.Block(self.name) as block:
            for e in self.ENGS:
                fns = self.ops[e]
                if fns:
                    getattr(block, e)(lambda q, fns=fns: [f(q) for f in fns])
        return False

    def op(self, eng, builder, waits=(), inc=None, drain=False, needs=()):
        P = self.P
        wl = []
        for w in list(waits) + P.need(*needs):
            if w is None:
                continue
            s, n = (w, w.n) if isinstance(w, S) else w
            if n <= 0:
                continue
            key = (eng, id(s))
            if P._maxwait.get(key, -1) >= n:
                continue
            P._maxwait[key] = n
            wl.append((s.h, n))
        tok = None
        incpair = None
        if inc is not None:
            s, k = (inc, 1) if isinstance(inc, S) else inc
            s.n += k
            tok = (s, s.n)
            incpair = (s.h, k)

        def fn(q, wl=wl, incpair=incpair, drain=drain, builder=builder):
            for h, n in wl:
                q.wait_ge(h, n)
            if drain:
                q.drain()
            inst = builder(q)
            if incpair is not None:
                if inst is None:
                    q.nop().then_inc(incpair[0], incpair[1])
                else:
                    inst.then_inc(incpair[0], incpair[1])

        self.ops[eng].append(fn)
        return tok

    def settle(self, eng, sem):
        es = self.P.gsem("e_" + eng)
        return self.op(eng, lambda q: None, waits=[sem], inc=es)

    def dma(self, eng, out, in_, sem, waits=(), needs=(), **kw):
        return self.op(eng, lambda q: q.dma_start(out=out, in_=in_, **kw),
                       waits=waits, inc=(sem, 16), needs=needs)

    def dmat(self, eng, out, in_, sem, waits=(), needs=()):
        return self.op(eng, lambda q: q.dma_start_transpose(out, in_),
                       waits=waits, inc=(sem, 16), needs=needs)


class Ctx:
    pass


def make_ctx(P):
    C = Ctx()
    C.P = P
    C.X = P.sb("X", [128, NCH * T], F32)
    C.Xb = P.sb("Xb", [128, NCH * T], BF16)
    C.ones_col = P.sb("c_ones_col", [128, 1], F32)
    C.ones_col_bf = P.sb("c_ones_col_bf", [128, 1], BF16)
    C.ones_row = P.sb("c_ones_row", [1, TG], F32)
    C.ones_row_bf = P.sb("c_ones_row_bf", [1, TG], BF16)
    C.ident = P.sb("c_ident", [128, 128], F32)
    C.ident_bf = P.sb("c_ident_bf", [128, 128], BF16)
    ld = P.gsem("dma_sync")
    ldg = P.gsem("dma_gp")
    a1 = P.in_("c_in_ones_col", np.ones((128, 1), np.float32))
    a2 = P.in_("c_in_ones_row", np.ones((1, TG), np.float32))
    a3 = P.in_("c_in_ident", np.eye(128, dtype=np.float32))
    with Ph(P, "consts") as ph:
        ph.dma("sync", C.ones_col[:, :], a1, ld)
        ph.dma("sync", C.ones_row[:, :], a2, ld)
        ph.dma("sync", C.ident[:, :], a3, ld)
        ph.dma("gpsimd", C.ones_col_bf[:, :], a1, ldg)
        ph.dma("gpsimd", C.ones_row_bf[:, :], a2, ldg)
        ph.dma("gpsimd", C.ident_bf[:, :], a3, ldg)
        P.mark("consts", ph.settle("sync", ld))
        P.mark("constsg", ph.settle("gpsimd", ldg))
    return C



def emit_ln(P, C, tag):
    nc = P.nc
    with ExitStack() as st:
        E = st.enter_context
        r_mu = E(nc.sbuf_tensor(f"{tag}_rmu", [1, T], F32))
        r_rs = E(nc.sbuf_tensor(f"{tag}_rrs", [1, T], F32))
        r_a = E(nc.sbuf_tensor(f"{tag}_ra", [1, T], F32))
        r_b = E(nc.sbuf_tensor(f"{tag}_rb", [1, T], F32))
        s_rowv = P.gsem("lnrowv")
        s_rows = P.gsem("lnrows")

        with ExitStack() as st2:
            E2 = st2.enter_context
            xsq = E2(nc.sbuf_tensor(f"{tag}_xsq", [128, 2 * T], F32))
            p_sum = E2(nc.psum_tensor(f"{tag}_psum", [1, T], F32))
            p_sq = E2(nc.psum_tensor(f"{tag}_psq", [1, T], F32))
            s_sq = P.gsem("lnsq")
            s_mm = P.gsem("lnmm")
            with Ph(P, tag + "_stats") as ph:
                mm_toks = [None, None]
                for ch in range(NCH):
                    sl = ch % 2
                    tsq = ph.op("scalar", lambda q, ch=ch, sl=sl: q.activation(
                        xsq[:, sl * T:(sl + 1) * T], C.X[:, ch * T:(ch + 1) * T],
                        AF.Square), waits=[mm_toks[sl]], inc=s_sq, needs=("X",))
                    for g in range(NTG):
                        a, b = g * TG, (g + 1) * TG
                        ph.op("tensor", lambda q, ch=ch, a=a, b=b: q.matmul(
                            p_sum[:, a:b], C.ones_col[:, :],
                            C.X[:, ch * T + a: ch * T + b],
                            start=(ch == 0), stop=(ch == NCH - 1)),
                            needs=("X", "consts"))
                        mm_toks[sl] = ph.op("tensor", lambda q, sl=sl, a=a, b=b, ch=ch: q.matmul(
                            p_sq[:, a:b], C.ones_col[:, :], xsq[:, sl * T + a: sl * T + b],
                            start=(ch == 0), stop=(ch == NCH - 1)),
                            waits=[tsq],
                            inc=s_mm if g == NTG - 1 else None)
                t1 = ph.op("vector", lambda q: q.tensor_scalar(
                    r_mu[:, :], p_sum[0:1, :], 1.0 / D, None, ALU.mult),
                    waits=[s_mm.tok()], inc=s_rowv)
                t2 = ph.op("vector", lambda q: q.tensor_scalar(
                    r_a[:, :], p_sq[0:1, :], 1.0 / D, 1e-5, ALU.mult, ALU.add),
                    inc=s_rowv)
                t3 = ph.op("scalar", lambda q: q.activation(
                    r_rs[:, :], r_mu[:, :], AF.Square), waits=[t1], inc=s_rows)
                t4 = ph.op("vector", lambda q: q.tensor_sub(
                    r_a[:, :], r_a[:, :], r_rs[:, :]), waits=[t2, t3], drain=True,
                    inc=s_rowv)
                t5 = ph.op("scalar", lambda q: q.activation(
                    r_b[:, :], r_a[:, :], AF.Ln), waits=[t4], drain=True,
                    inc=s_rows)
                t6 = ph.op("scalar", lambda q: q.activation(
                    r_rs[:, :], r_b[:, :], AF.Exp, scale=-0.5), drain=True, inc=s_rows)

        if P.debug:
            dl = P.gsem("lndbg")
            o_mu = P.out_("dbg_" + tag + "_mu", [1, T], F32)
            o_rs = P.out_("dbg_" + tag + "_rs", [1, T], F32)
            with Ph(P, tag + "_dbg") as ph:
                ph.dma("sync", o_mu, r_mu[:, :], dl, waits=[s_rowv, s_rows])
                ph.dma("sync", o_rs, r_rs[:, :], dl)
                ph.op("sync", lambda q: None, waits=[dl])

        with ExitStack() as st3:
            E3 = st3.enter_context
            p_b = [E3(nc.psum_tensor(f"{tag}_pb{i}", [128, TG], F32)) for i in range(4)]
            s_bc = P.gsem("lnbc")
            s_ap = P.gsem("lnap")
            with Ph(P, tag + "_apply") as ph:
                ev_toks = {}
                for g in range(NTG):
                    a, b = g * TG, (g + 1) * TG
                    pm, pr = p_b[2 * (g % 2)], p_b[2 * (g % 2) + 1]
                    w = [s_rowv.tok(), s_rows.tok()] + ([ev_toks[g - 2]] if g >= 2 else [])
                    ph.op("tensor", lambda q, pm=pm, a=a, b=b: q.matmul(
                        pm[:, :], C.ones_row[:, 0:128], r_mu[:, a:b],
                        start=True, stop=True), waits=w, needs=("consts",))
                    tb = ph.op("tensor", lambda q, pr=pr, a=a, b=b: q.matmul(
                        pr[:, :], C.ones_row[:, 0:128], r_rs[:, a:b],
                        start=True, stop=True), inc=s_bc)
                    last = None
                    for ch in range(NCH):
                        o = ch * T
                        ph.op("vector", lambda q, o=o, a=a, b=b, pm=pm: q.tensor_sub(
                            C.X[:, o + a:o + b], C.X[:, o + a:o + b], pm[:, :]),
                            waits=[tb])
                        ph.op("vector", lambda q, o=o, a=a, b=b, pr=pr: q.tensor_mul(
                            C.X[:, o + a:o + b], C.X[:, o + a:o + b], pr[:, :]),
                            drain=True)
                        last = ph.op("vector", lambda q, o=o, a=a, b=b: q.tensor_copy(
                            C.Xb[:, o + a:o + b], C.X[:, o + a:o + b]), drain=True,
                            inc=s_ap, needs=("Xb",))
                    ev_toks[g] = last
                P.mark("X", last)
                P.mark("Xb", last)



def emit_u_l1ffn(P, C, W):
    nc = P.nc
    wT = np.ascontiguousarray(W["L1_ffn_weight"].astype(np.float32).T)
    bias_col = np.ascontiguousarray(
        W["L1_ffn_bias"].astype(np.float32).reshape(NCH, 128).T)
    a_w = P.in_("u3_wT", wT)
    a_b = P.in_("u3_bias", bias_col)
    g_in = P.dram("u3_gin", [1, 1], F32)
    g_out = P.dram("u3_gout", [1, 1], F32)

    with ExitStack() as st:
        E = st.enter_context
        Wf = E(nc.sbuf_tensor("u3_W", [128, NCH * D], F32))
        QW = E(nc.sbuf_tensor("u3_QW", [128, NCH * D], BF16))
        QX = E(nc.sbuf_tensor("u3_QX", [128, NCH * T], BF16))
        scr = E(nc.sbuf_tensor("u3_scr", [128, D], F32))
        scr2 = E(nc.sbuf_tensor("u3_scr2", [128, T], F32))
        cols = E(nc.sbuf_tensor("u3_cols", [128, 8], F32))
        ccols = E(nc.sbuf_tensor("u3_ccols", [128, 4], F32))
        bias_c = E(nc.sbuf_tensor("u3_biasc", [128, 8], F32))
        regs = E(nc.sbuf_tensor("u3_regs", [1, 16], F32))
        row128 = E(nc.sbuf_tensor("u3_row128", [1, 128], F32))
        p_s = E(nc.psum_tensor("u3_ps", [1, 128], F32))
        p_c = E(nc.psum_tensor("u3_pc", [128, 4], F32))
        p_o = [E(nc.psum_tensor(f"u3_po{i}", [128, TG], F32)) for i in range(2)]
        ld = P.gsem("dma_sync")
        vs = P.gsem("ev")
        ss = P.gsem("es")
        ts = P.gsem("et")
        cc = P.gsem("cc")

        with Ph(P, "u3_a") as ph:
            ph.dma("sync", Wf[:, :].rearrange("p (c d) -> p c d", c=NCH),
                   a_w.rearrange("(c p) d -> p c d", p=128), ld)
            ph.dma("sync", bias_c[:, :], a_b, ld)
            tw = ph.settle("sync", ld)
            for ch in range(NCH):
                ph.op("vector", lambda q, ch=ch: q.tensor_reduce(
                    cols[:, ch:ch + 1], Wf[:, ch * D:(ch + 1) * D], AX.X, ALU.add,
                    apply_absolute_value=True), waits=[tw])
            tsum = ph.op("vector", lambda q: q.tensor_reduce(
                scr[:, 0:1], cols[:, :], AX.X, ALU.add), drain=True, inc=vs)
            tmm = ph.op("tensor", lambda q: q.matmul(
                p_s[:, 0:1], scr[:, 0:1], C.ones_col[:, :], start=True, stop=True),
                waits=[tsum], inc=ts, needs=("consts",))
            tws = ph.op("vector", lambda q: q.tensor_scalar(
                regs[:, 0:1], p_s[0:1, 0:1], 1.0 / (D * D), None, ALU.mult),
                waits=[tmm], inc=vs)
            tth = ph.op("vector", lambda q: q.tensor_scalar(
                regs[:, 1:2], regs[:, 0:1], 0.5, None, ALU.mult), drain=True, inc=vs)
            tb1 = ph.op("tensor", lambda q: q.matmul(
                p_c[:, 0:1], C.ones_row[:, 0:128], regs[:, 1:2], start=True, stop=True),
                waits=[tth], inc=ts)
            tb2 = ph.op("vector", lambda q: q.tensor_copy(ccols[:, 0:1], p_c[:, 0:1]),
                        waits=[tb1], inc=vs)
            prev = None
            for ch in range(NCH):
                o = ch * D
                ta = ph.op("scalar", lambda q, o=o: q.activation(
                    scr[:, :], Wf[:, o:o + D], AF.Abs), waits=[prev, tsum, tmm],
                    drain=True, inc=ss)
                tg = ph.op("vector", lambda q: q.tensor_scalar(
                    scr[:, :], scr[:, :], ccols[:, 0:1], None, ALU.is_gt),
                    waits=[ta, tb2], drain=True, inc=vs)
                tsg = ph.op("scalar", lambda q, o=o: q.sign(
                    QW[:, o:o + D], Wf[:, o:o + D]), waits=[tg], inc=ss)
                prev = ph.op("vector", lambda q, o=o: q.tensor_mul(
                    QW[:, o:o + D], QW[:, o:o + D], scr[:, :]), waits=[tsg],
                    drain=True, inc=vs)
            for ch in range(NCH):
                ph.op("vector", lambda q, ch=ch: q.tensor_reduce(
                    cols[:, ch:ch + 1], C.X[:, ch * T:(ch + 1) * T], AX.X, ALU.max,
                    apply_absolute_value=True), waits=[prev], drain=(ch == 0),
                    needs=("X",))
            tmax = ph.op("vector", lambda q: q.tensor_reduce(
                scr[:, 0:1], cols[:, :], AX.X, ALU.max), drain=True, inc=vs)
            ttr = ph.op("tensor", lambda q: q.matmul(
                p_s[:, :], scr[:, 0:1], C.ident[:, :], start=True, stop=True),
                waits=[tmax], inc=ts)
            trow = ph.op("vector", lambda q: q.tensor_copy(row128[:, :], p_s[0:1, :]),
                         waits=[ttr], inc=vs)
            tlm = ph.op("vector", lambda q: q.reduce_max(
                regs[:, 2:3], row128[:, :], AX.X), drain=True, inc=vs)
            ph.dma("sync", g_in.ap(), regs[:, 2:3], ld, waits=[tlm])
            tst = ph.settle("sync", ld)
            tcc = ph.op("gpsimd", lambda q: q.collective_compute(
                "AllReduce", ALU.max, replica_groups=ALL8,
                ins=[g_in.ap().opt()], outs=[g_out.ap().opt()]), waits=[tst],
                inc=cc)
            ph.dma("sync", regs[:, 3:4], g_out.ap(), ld, waits=[tcc])
            P.mark("u3gmax", ph.settle("sync", ld))

        with Ph(P, "u3_b") as ph:
            t1 = ph.op("vector", lambda q: q.reciprocal(regs[:, 4:5], regs[:, 3:4]),
                       inc=vs, needs=("u3gmax",), drain=True)
            t2 = ph.op("vector", lambda q: q.tensor_scalar(
                regs[:, 4:5], regs[:, 4:5], 127.0, None, ALU.mult), drain=True, inc=vs)
            t3 = ph.op("vector", lambda q: q.tensor_mul(
                regs[:, 5:6], regs[:, 0:1], regs[:, 3:4]), drain=True, inc=vs)
            t4 = ph.op("vector", lambda q: q.tensor_scalar(
                regs[:, 5:6], regs[:, 5:6], 1.0 / 127.0, None, ALU.mult), drain=True,
                inc=vs)
            tb1 = ph.op("tensor", lambda q: q.matmul(
                p_c[:, 1:3], C.ones_row[:, 0:128], regs[:, 4:6], start=True, stop=True),
                waits=[t4], inc=ts)
            tb2 = ph.op("vector", lambda q: q.tensor_copy(ccols[:, 1:3], p_c[:, 1:3]),
                        waits=[tb1], drain=True, inc=vs)
            for ch in range(NCH):
                o = ch * T
                ph.op("vector", lambda q, o=o: q.tensor_scalar(
                    scr2[:, :], C.X[:, o:o + T], ccols[:, 1:2], None, ALU.mult),
                    waits=[tb2], drain=True, needs=("X",))
                ph.op("vector", lambda q: q.tensor_scalar(
                    scr2[:, :], scr2[:, :], MAGIC, None, ALU.add), drain=True)
                ph.op("vector", lambda q: q.tensor_scalar(
                    scr2[:, :], scr2[:, :], MAGIC, None, ALU.subtract), drain=True)
                ph.op("vector", lambda q: q.tensor_scalar(
                    scr2[:, :], scr2[:, :], 127.0, None, ALU.min), drain=True)
                ph.op("vector", lambda q, o=o: q.tensor_scalar(
                    QX[:, o:o + T], scr2[:, :], -128.0, None, ALU.max), drain=True,
                    inc=vs)

        with Ph(P, "u3_c") as ph:
            ev = {}
            i = 0
            for m in range(NCH):
                for g in range(NTG):
                    po = p_o[i % 2]
                    a, b = g * TG, (g + 1) * TG
                    w = [ev.get(i - 2)]
                    for k in range(NCH):
                        tmm = ph.op("tensor", lambda q, po=po, k=k, m=m, a=a, b=b: q.matmul(
                            po[:, :], QW[:, k * D + m * 128: k * D + (m + 1) * 128],
                            QX[:, k * T + a: k * T + b],
                            start=(k == 0), stop=(k == NCH - 1)),
                            waits=w if k == 0 else (), inc=ts if k == NCH - 1 else None)
                    t1 = ph.op("vector", lambda q, po=po, m=m, a=a, b=b: q.scalar_tensor_tensor(
                        C.X[:, m * T + a: m * T + b], po[:, :], ccols[:, 2:3],
                        C.X[:, m * T + a: m * T + b], ALU.mult, ALU.add),
                        waits=[tmm], drain=True)
                    ev[i] = ph.op("vector", lambda q, m=m, a=a, b=b: q.tensor_scalar(
                        C.X[:, m * T + a: m * T + b], C.X[:, m * T + a: m * T + b],
                        bias_c[:, m:m + 1], None, ALU.add), drain=True, inc=vs)
                    i += 1
            P.mark("X", ev[i - 1])

        if P.debug:
            dsem = P.gsem("dbg")
            o_qx = P.out_("dbg_qx", [128, NCH * T], BF16)
            o_qw = P.out_("dbg_qw", [128, NCH * D], BF16)
            o_regs = P.out_("dbg_regs", [1, 16], F32)
            o_xpre = P.out_("dbg_xpre", [128, NCH * T], F32)
            with Ph(P, "u3_dbg") as ph:
                ph.dma("sync", o_qx, QX[:, :], ld)
                ph.dma("sync", o_qw, QW[:, :], ld)
                ph.dma("sync", o_regs, regs[:, :], ld)
                ph.dma("sync", o_xpre, C.X[:, :], ld)
                ph.op("sync", lambda q: None, waits=[ld])

    emit_ln(P, C, "u3ln")



def rne_clip(ph, dst, src, lo, hi, scr, mid_ops=()):
    ph.op("vector", lambda q: q.tensor_scalar(scr[0], src[0], MAGIC, None, ALU.add),
          waits=src[1], drain=True)
    ph.op("vector", lambda q: q.tensor_scalar(scr[0], scr[0], MAGIC, None, ALU.subtract),
          drain=True)
    ph.op("vector", lambda q: q.tensor_scalar(scr[0], scr[0], float(hi), None, ALU.min),
          drain=True)
    return ph.op("vector", lambda q: q.tensor_scalar(dst[0], scr[0], float(lo), None,
                                                     ALU.max), drain=True, inc=dst[1])


def emit_u_l0ffn(P, C, W):
    nc = P.nc
    TT2 = 256
    NT2 = T // TT2
    HCH = 32
    PADW = T + 2

    w1T = np.ascontiguousarray(W["L0_ffn_w1"].astype(np.float32).T)
    w2T = np.ascontiguousarray(W["L0_ffn_w2"].astype(np.float32).T)
    cw = W["L0_ffn_conv_w"].astype(np.float32)[:, 0, :]
    cwcol = np.ascontiguousarray(cw.reshape(HCH, 128, 3).transpose(1, 0, 2).reshape(128, HCH * 3))
    cbcol = np.ascontiguousarray(W["L0_ffn_conv_b"].astype(np.float32).reshape(HCH, 128).T)
    maskL = [np.full((128, 1), float(c % 2), np.float32) for c in range(NCORES)]
    maskR = [np.full((128, 1), float(1 - c % 2), np.float32) for c in range(NCORES)]

    a_w1 = P.in_("u1_w1T", w1T)
    a_w2 = P.in_("u1_w2T", w2T)
    a_cw = P.in_("u1_cw", cwcol)
    a_cb = P.in_("u1_cb", cbcol)
    a_mL = P.in_("u1_maskL", maskL)
    a_mR = P.in_("u1_maskR", maskR)

    qw1 = P.dram("u1_qw1", [128, 8 * 4096], BF16)
    qw2 = P.dram("u1_qw2", [128, 32 * 1024], BF16)
    ex_in = P.dram("u1_exin", [128, 16], F32)
    ex_out = P.dram("u1_exout", [256, 16], F32)

    ld = P.gsem("dma_sync")
    ldg = P.gsem("dma_gp")
    std = P.gsem("dma_st")
    ccs = P.gsem("cc")
    ev = P.gsem("ev")
    es_ = P.gsem("es")
    et = P.gsem("et")

    ustack = ExitStack()
    ustack.__enter__()
    UE = ustack.enter_context
    regs = UE(nc.sbuf_tensor("u1_regs", [1, 16], F32))
    ccol = UE(nc.sbuf_tensor("u1_ccol", [128, 4], F32))
    cwb = UE(nc.sbuf_tensor("u1_cwb", [128, HCH * 3], F32))
    cbb = UE(nc.sbuf_tensor("u1_cbb", [128, HCH], F32))
    mLc = UE(nc.sbuf_tensor("u1_mL", [128, 1], F32))
    mRc = UE(nc.sbuf_tensor("u1_mR", [128, 1], F32))
    QX1 = UE(nc.sbuf_tensor("u1_QX1", [128, NCH * PADW], BF16))
    drow1 = UE(nc.sbuf_tensor("u1_drow1", [1, PADW], F32))

    with ExitStack() as st:
        E = st.enter_context
        wpc = E(nc.sbuf_tensor("u1_wpc", [128, 2 * 4096], F32))
        qpc = E(nc.sbuf_tensor("u1_qpc", [128, 4096], BF16))
        cols = E(nc.sbuf_tensor("u1_cols", [128, 48], F32))
        c1 = E(nc.sbuf_tensor("u1_c1", [128, 1], F32))
        p_s = E(nc.psum_tensor("u1_psA", [1, 128], F32))
        p_c = E(nc.psum_tensor("u1_pcA", [128, 4], F32))

        with Ph(P, "u1_wstat") as ph:
            ph.dma("sync", cwb[:, :], a_cw, ld)
            ph.dma("sync", cbb[:, :], a_cb, ld)
            ph.dma("sync", mLc[:, :], a_mL, ld)
            ph.dma("sync", mRc[:, :], a_mR, ld)
            tks = {}
            for i in range(8):
                sl = i % 2
                ph.dma("sync", wpc[:, sl * 4096:(sl + 1) * 4096],
                       a_w1[i * 128:(i + 1) * 128, :], ld, waits=[tks.get(i - 2)])
                tl = ph.settle("sync", ld)
                tks[i] = ph.op("vector", lambda q, i=i, sl=sl: q.tensor_reduce(
                    cols[:, i:i + 1], wpc[:, sl * 4096:(sl + 1) * 4096], AX.X, ALU.add,
                    apply_absolute_value=True), waits=[tl], inc=ev)
            for i in range(32):
                sl = i % 2
                ph.dma("sync", wpc[:, sl * 4096:sl * 4096 + 1024],
                       a_w2[i * 128:(i + 1) * 128, :], ld, waits=[tks.get(8 + i - 2)])
                tl = ph.settle("sync", ld)
                tks[8 + i] = ph.op("vector", lambda q, i=i, sl=sl: q.tensor_reduce(
                    cols[:, 8 + i:9 + i], wpc[:, sl * 4096:sl * 4096 + 1024], AX.X,
                    ALU.add, apply_absolute_value=True), waits=[tl], inc=ev)
            t1 = ph.op("vector", lambda q: q.tensor_reduce(
                c1[:, 0:1], cols[:, 0:8], AX.X, ALU.add), drain=True, inc=ev)
            tm1 = ph.op("tensor", lambda q: q.matmul(
                p_s[:, 0:1], c1[:, 0:1], C.ones_col[:, :], start=True, stop=True),
                waits=[t1], inc=et, needs=("consts",))
            t2 = ph.op("vector", lambda q: q.tensor_reduce(
                c1[:, 0:1], cols[:, 8:40], AX.X, ALU.add), waits=[tm1], drain=True,
                inc=ev)
            tm2 = ph.op("tensor", lambda q: q.matmul(
                p_s[:, 1:2], c1[:, 0:1], C.ones_col[:, :], start=True, stop=True),
                waits=[t2], inc=et)
            t3 = ph.op("vector", lambda q: q.tensor_scalar(
                regs[:, 0:1], p_s[0:1, 0:1], 1.0 / (1024 * 4096), 1e-5,
                ALU.mult, ALU.max), waits=[tm2], inc=ev)
            t4 = ph.op("vector", lambda q: q.tensor_scalar(
                regs[:, 2:3], p_s[0:1, 1:2], 1.0 / (1024 * 4096), 1e-5,
                ALU.mult, ALU.max), drain=True, inc=ev)
            t5 = ph.op("vector", lambda q: q.reciprocal(regs[:, 1:2], regs[:, 0:1]),
                       drain=True, inc=ev)
            t6 = ph.op("vector", lambda q: q.reciprocal(regs[:, 3:4], regs[:, 2:3]),
                       drain=True, inc=ev)
            tb = ph.op("tensor", lambda q: q.matmul(
                p_c[:, 0:1], C.ones_row[:, 0:128], regs[:, 1:2],
                start=True, stop=True), waits=[t6], inc=et)
            tb2 = ph.op("tensor", lambda q: q.matmul(
                p_c[:, 2:3], C.ones_row[:, 0:128], regs[:, 3:4],
                start=True, stop=True), inc=et)
            tcp = ph.op("vector", lambda q: q.tensor_copy(ccol[:, 0:1], p_c[:, 0:1]),
                        waits=[tb, tb2], drain=True, inc=ev)
            ph.op("vector", lambda q: q.tensor_copy(ccol[:, 1:2], p_c[:, 2:3]),
                  drain=True, inc=ev)

        with Ph(P, "u1_wq") as ph:
            prev = None
            for wi, (src, npcs, width, qdst) in enumerate(
                    [(a_w1, 8, 4096, qw1), (a_w2, 32, 1024, qw2)]):
                scol = 2 * wi
                for i in range(npcs):
                    sl = i % 2
                    ph.dma("sync", wpc[:, sl * 4096:sl * 4096 + width],
                           src[i * 128:(i + 1) * 128, :], ld, waits=[prev])
                    tl = ph.settle("sync", ld)
                    ph.op("vector", lambda q, sl=sl, width=width, scol=scol: q.tensor_scalar(
                        wpc[:, sl * 4096:sl * 4096 + width],
                        wpc[:, sl * 4096:sl * 4096 + width],
                        ccol[:, scol // 2:scol // 2 + 1], None, ALU.mult),
                        waits=[tl], drain=True)
                    tq = rne_clip(
                        ph, (qpc[:, 0:width], ev),
                        (wpc[:, sl * 4096:sl * 4096 + width], ()),
                        -1.0, 1.0, (wpc[:, sl * 4096:sl * 4096 + width], None))
                    prev = ph.dma("sync", qdst.ap()[:, i * width:(i + 1) * width],
                                  qpc[:, 0:width], std, waits=[tq])
            ph.settle("sync", std)

    with ExitStack() as st:
        E = st.enter_context
        exs = E(nc.sbuf_tensor("u1_exs", [128, 16], F32))
        exA = E(nc.sbuf_tensor("u1_exA", [128, 16], F32))
        exB = E(nc.sbuf_tensor("u1_exB", [128, 16], F32))
        hl = E(nc.sbuf_tensor("u1_hl", [128, 16], F32))
        M = E(nc.sbuf_tensor("u1_M", [128, T], F32))
        Mn = E(nc.sbuf_tensor("u1_Mn", [128, T], F32))
        scc = E(nc.sbuf_tensor("u1_scc", [128, 16], F32))
        sc2 = E(nc.sbuf_tensor("u1_sc2", [2, 16], F32))
        srow = E(nc.sbuf_tensor("u1_srow", [1, PADW], F32))
        s2row = E(nc.sbuf_tensor("u1_s2row", [1, PADW], F32))
        scr = E(nc.sbuf_tensor("u1_scrx", [128, TG], F32))
        p_t = [E(nc.psum_tensor(f"u1_pt{i}", [128, 128], F32)) for i in range(2)]
        p_r = E(nc.psum_tensor("u1_pr", [1, 128], F32))
        p_b = [E(nc.psum_tensor(f"u1_pb{i}", [128, TG], F32)) for i in range(2)]

        with Ph(P, "u1_x", ) as ph:
            xv = C.X[:, :].rearrange("p (c t) -> p c t", c=NCH)
            ph.dma("sync", exs[:, 0:8].rearrange("p (c u) -> p c u", u=1),
                   xv[:, :, 0:1], ld, needs=("X",), allow_slow_non_contiguous=True)
            ph.dma("sync", exs[:, 8:16].rearrange("p (c u) -> p c u", u=1),
                   xv[:, :, T - 1:T], ld, allow_slow_non_contiguous=True)
            tse = ph.settle("sync", ld)
            ph.dma("sync", ex_in.ap(), exs[:, :], ld, waits=[tse])
            tse2 = ph.settle("sync", ld)
            tcc = ph.op("gpsimd", lambda q: q.collective_compute(
                "AllGather", ALU.bypass, replica_groups=PAIRS,
                ins=[ex_in.ap().opt()], outs=[ex_out.ap().opt()]), waits=[tse2],
                inc=ccs)
            ph.dma("sync", exA[:, :], ex_out.ap()[0:128, :], ld, waits=[tcc])
            ph.dma("sync", exB[:, :], ex_out.ap()[128:256, :], ld)
            tse3 = ph.settle("sync", ld)
            th1 = ph.op("vector", lambda q: q.tensor_scalar(
                hl[:, 0:8], exA[:, 8:16], mLc[:, 0:1], None, ALU.mult),
                waits=[tse3], inc=ev)
            th2 = ph.op("vector", lambda q: q.tensor_scalar(
                hl[:, 8:16], exB[:, 0:8], mRc[:, 0:1], None, ALU.mult),
                drain=True, inc=ev)
            tm = ph.op("vector", lambda q: q.tensor_max(
                M[:, :], C.X[:, 0:T], C.X[:, T:2 * T]), inc=ev)
            tn = ph.op("vector", lambda q: q.tensor_tensor(
                Mn[:, :], C.X[:, 0:T], C.X[:, T:2 * T], ALU.min), drain=True, inc=ev)
            for ch in range(2, NCH):
                tm = ph.op("vector", lambda q, ch=ch: q.tensor_max(
                    M[:, :], M[:, :], C.X[:, ch * T:(ch + 1) * T]), drain=True, inc=ev)
                tn = ph.op("vector", lambda q, ch=ch: q.tensor_tensor(
                    Mn[:, :], Mn[:, :], C.X[:, ch * T:(ch + 1) * T], ALU.min),
                    drain=True, inc=ev)
            tm = ph.op("vector", lambda q: q.tensor_scalar(
                Mn[:, :], Mn[:, :], -1.0, None, ALU.mult), drain=True, inc=ev)
            tm = ph.op("vector", lambda q: q.tensor_max(
                M[:, :], M[:, :], Mn[:, :]), drain=True, inc=ev)
        with Ph(P, "u1_x2") as ph:
            ta = ph.op("vector", lambda q: q.tensor_reduce(
                scc[:, 0:1], hl[:, 0:8], AX.X, ALU.max, apply_absolute_value=True),
                inc=ev)
            tb_ = ph.op("vector", lambda q: q.tensor_reduce(
                scc[:, 1:2], hl[:, 8:16], AX.X, ALU.max, apply_absolute_value=True),
                drain=True, inc=ev)
            tt = ph.op("tensor", lambda q: q.transpose(
                p_t[0][0:2, :], scc[:, 0:2], C.ident[:, :]), waits=[tb_], inc=et)
            tr = ph.op("vector", lambda q: q.tensor_reduce(
                sc2[0:2, 0:1], p_t[0][0:2, :], AX.X, ALU.max), waits=[tt], drain=True,
                inc=ev)
            tt2 = ph.op("tensor", lambda q: q.matmul(
                p_r[:, 0:2], sc2[0:2, 0:1], C.ident[0:2, 0:2], start=True, stop=True),
                waits=[tr], inc=et)
            tcpa = ph.op("vector", lambda q: q.tensor_copy(
                srow[:, 0:1], p_r[0:1, 0:1]), waits=[tt2], drain=True, inc=ev)
            tcp = ph.op("vector", lambda q: q.tensor_copy(
                srow[:, PADW - 1:PADW], p_r[0:1, 1:2]), drain=True, inc=ev)
            trows = []
            for j in range(16):
                pt = p_t[j % 2]
                tt = ph.op("tensor", lambda q, j=j, pt=pt: q.transpose(
                    pt[:, :], M[:, j * 128:(j + 1) * 128], C.ident[:, :]),
                    waits=[trows[j - 2][1]] if j >= 2 else [tcp], inc=et)
                tr = ph.op("vector", lambda q, j=j, pt=pt: q.tensor_reduce(
                    scc[:, j:j + 1], pt[:, :], AX.X, ALU.max), waits=[tt], drain=True,
                    inc=ev)
                trows.append((tt, tr))
            tasm = []
            for j in range(16):
                tt2 = ph.op("tensor", lambda q, j=j: q.matmul(
                    p_r[:, :], scc[:, j:j + 1], C.ident[:, :], start=True, stop=True),
                    waits=[trows[j][1]] + ([tasm[j - 1]] if j else []), inc=et)
                tasm.append(ph.op("vector", lambda q, j=j: q.tensor_copy(
                    srow[:, 1 + j * 128:1 + (j + 1) * 128], p_r[0:1, :]),
                    waits=[tt2], drain=True, inc=ev))
            tc1 = ph.op("vector", lambda q: q.tensor_scalar(
                srow[:, :], srow[:, :], 1e-5, None, ALU.max), drain=True, inc=ev)
            td1 = ph.op("vector", lambda q: q.tensor_scalar(
                drow1[:, :], srow[:, :], 1.0 / 127.0, None, ALU.mult), drain=True,
                inc=ev)
            td2 = ph.op("vector", lambda q: q.tensor_scalar(
                drow1[:, :], drow1[:, :], regs[0:1, 0:1], None, ALU.mult), drain=True,
                inc=ev)
            tr1 = ph.op("vector", lambda q: q.reciprocal(s2row[:, :], srow[:, :]),
                        drain=True, inc=ev)
            tr2 = ph.op("vector", lambda q: q.tensor_scalar(
                s2row[:, :], s2row[:, :], 127.0, None, ALU.mult), drain=True, inc=ev)

        with Ph(P, "u1_xq") as ph:
            lastmu = {}
            for g in range(NTG):
                a, b = g * TG, (g + 1) * TG
                pb = p_b[g % 2]
                tbq = ph.op("tensor", lambda q, pb=pb, a=a, b=b: q.matmul(
                    pb[:, :], C.ones_row[:, 0:128], s2row[:, 1 + a:1 + b],
                    start=True, stop=True), waits=[lastmu.get(g - 2)], inc=et)
                for ch in range(NCH):
                    tmu = ph.op("vector", lambda q, ch=ch, a=a, b=b, pb=pb: q.tensor_mul(
                        scr[:, :], C.X[:, ch * T + a:ch * T + b], pb[:, :]),
                        waits=[tbq], drain=True,
                        inc=ev if ch == NCH - 1 else None)
                    rne_clip(ph, (QX1[:, ch * PADW + 1 + a:ch * PADW + 1 + b], ev),
                             (scr[:, :], ()), -128.0, 127.0, (scr[:, :], None))
                lastmu[g] = tmu
            tq1 = ph.op("tensor", lambda q: q.matmul(
                p_b[0][:, 0:1], C.ones_row[:, 0:128], s2row[:, 0:1],
                start=True, stop=True), waits=[lastmu.get(2)], inc=et)
            tq2 = ph.op("tensor", lambda q: q.matmul(
                p_b[0][:, 1:2], C.ones_row[:, 0:128], s2row[:, PADW - 1:PADW],
                start=True, stop=True), inc=et)
            thl = ph.op("vector", lambda q: q.tensor_scalar(
                scc[:, 0:8], hl[:, 0:8], p_b[0][:, 0:1], None, ALU.mult),
                waits=[tq2], drain=True)
            thr = ph.op("vector", lambda q: q.tensor_scalar(
                scc[:, 8:16], hl[:, 8:16], p_b[0][:, 1:2], None, ALU.mult), drain=True)
            rne_clip(ph, (QX1[:, :].rearrange("p (c t) -> p c t", c=NCH)[:, :, 0:1], ev),
                     (scc[:, 0:8].rearrange("p (c u) -> p c u", u=1), ()),
                     -128.0, 127.0, (scc[:, 0:8].rearrange("p (c u) -> p c u", u=1), None))
            rne_clip(ph, (QX1[:, :].rearrange("p (c t) -> p c t", c=NCH)[:, :, PADW - 1:PADW], ev),
                     (scc[:, 8:16].rearrange("p (c u) -> p c u", u=1), ()),
                     -128.0, 127.0, (scc[:, 8:16].rearrange("p (c u) -> p c u", u=1), None))

    with ExitStack() as st:
        E = st.enter_context
        hbuf = E(nc.sbuf_tensor("u1_hbuf", [128, HCH * (TT2 + 2)], F32))
        QH = E(nc.sbuf_tensor("u1_QH", [128, HCH * TT2], BF16))
        wt1 = E(nc.sbuf_tensor("u1_wt1", [128, 2 * 1024], BF16))
        wt2 = E(nc.sbuf_tensor("u1_wt2", [128, 32 * 128], BF16))
        hM = E(nc.sbuf_tensor("u1_hM", [128, TT2], F32))
        hMn = E(nc.sbuf_tensor("u1_hMn", [128, TT2], F32))
        hscc = E(nc.sbuf_tensor("u1_hscc", [128, 2], F32))
        hrow = E(nc.sbuf_tensor("u1_hrow", [1, TT2], F32))
        hrow2 = E(nc.sbuf_tensor("u1_hrow2", [1, TT2], F32))
        cscr = E(nc.sbuf_tensor("u1_cscr", [128, TT2], F32))
        c1s = E(nc.sbuf_tensor("u1_c1s", [128, TT2 + 2], F32))
        d2s = E(nc.sbuf_tensor("u1_d2s", [128, TT2], F32))
        p_h = [E(nc.psum_tensor(f"u1_ph{i}", [128, TT2 + 2], F32)) for i in range(2)]
        p_t = [E(nc.psum_tensor(f"u1_pt2{i}", [128, 128], F32)) for i in range(2)]
        p_r = E(nc.psum_tensor("u1_pr2", [1, 128], F32))
        p_o = [E(nc.psum_tensor(f"u1_po{i}", [128, TT2], F32)) for i in range(2)]

        for ti in range(NT2):
            t0 = ti * TT2
            with Ph(P, f"u1_t{ti}") as ph:
                tb = ph.op("tensor", lambda q: q.matmul(
                    p_h[0][:, 0:TT2 + 2], C.ones_row[:, 0:128],
                    drow1[:, t0:t0 + TT2 + 2], start=True, stop=True), inc=et)
                tc = ph.op("vector", lambda q: q.tensor_copy(c1s[:, :], p_h[0][:, :]),
                           waits=[tb], drain=True, inc=ev)
                wload = {}
                hdone = {}
                mmdone = {}
                for hc in range(HCH):
                    sl = hc % 2
                    wload[hc] = ph.dma("sync", wt1[:, sl * 1024:(sl + 1) * 1024],
                                       qw1.ap().rearrange("p (k h) -> p k h", k=8)
                                       [:, :, hc * 128:(hc + 1) * 128], ld,
                                       waits=[mmdone.get(hc - 2)])
                    tl = ph.settle("sync", ld)
                    po = p_h[hc % 2]
                    w0 = [tc, hdone.get(hc - 2)]
                    for k in range(NCH):
                        mm = ph.op("tensor", lambda q, po=po, sl=sl, k=k: q.matmul(
                            po[:, :], wt1[:, sl * 1024 + k * 128:sl * 1024 + (k + 1) * 128],
                            QX1[:, k * PADW + t0:k * PADW + t0 + TT2 + 2],
                            start=(k == 0), stop=(k == NCH - 1)),
                            waits=(w0 + [tl]) if k == 0 else (),
                            inc=et if k == NCH - 1 else None)
                    mmdone[hc] = mm
                    ho = hc * (TT2 + 2)
                    te1 = ph.op("vector", lambda q, po=po, ho=ho: q.tensor_mul(
                        hbuf[:, ho:ho + TT2 + 2], po[:, :], c1s[:, :]), waits=[mm],
                        drain=True)
                    te2 = ph.op("vector", lambda q, ho=ho, hc=hc: q.tensor_scalar(
                        cscr[:, :], hbuf[:, ho:ho + TT2], cwb[:, 3 * hc:3 * hc + 1],
                        None, ALU.mult), drain=True)
                    te3 = ph.op("vector", lambda q, ho=ho, hc=hc: q.scalar_tensor_tensor(
                        cscr[:, :], hbuf[:, ho + 1:ho + 1 + TT2],
                        cwb[:, 3 * hc + 1:3 * hc + 2], cscr[:, :], ALU.mult, ALU.add),
                        drain=True)
                    te4 = ph.op("vector", lambda q, ho=ho, hc=hc: q.scalar_tensor_tensor(
                        cscr[:, :], hbuf[:, ho + 2:ho + 2 + TT2],
                        cwb[:, 3 * hc + 2:3 * hc + 3], cscr[:, :], ALU.mult, ALU.add),
                        drain=True)
                    te5 = ph.op("vector", lambda q, hc=hc: q.tensor_scalar(
                        cscr[:, :], cscr[:, :], cbb[:, hc:hc + 1], None, ALU.add),
                        drain=True, inc=ev)
                    te6 = ph.op("scalar", lambda q, ho=ho: q.activation(
                        hbuf[:, ho:ho + TT2], cscr[:, :], AF.Silu), waits=[te5],
                        inc=es_)
                    if hc == 0:
                        ph.op("vector", lambda q, ho=ho: q.tensor_copy(
                            hM[:, :], hbuf[:, ho:ho + TT2]), waits=[te6], drain=True)
                        hdone[hc] = ph.op("vector", lambda q, ho=ho: q.tensor_copy(
                            hMn[:, :], hbuf[:, ho:ho + TT2]), drain=True, inc=ev)
                    else:
                        ph.op("vector", lambda q, ho=ho: q.tensor_max(
                            hM[:, :], hM[:, :], hbuf[:, ho:ho + TT2]),
                            waits=[te6], drain=True)
                        hdone[hc] = ph.op("vector", lambda q, ho=ho: q.tensor_tensor(
                            hMn[:, :], hMn[:, :], hbuf[:, ho:ho + TT2], ALU.min),
                            waits=[te6], drain=True, inc=ev)
                thn = ph.op("vector", lambda q: q.tensor_scalar(
                    hMn[:, :], hMn[:, :], -1.0, None, ALU.mult),
                    waits=[hdone[HCH - 1]], drain=True)
                thc = ph.op("vector", lambda q: q.tensor_max(
                    hM[:, :], hM[:, :], hMn[:, :]), drain=True, inc=ev)
                tts = []
                for j in range(2):
                    tt = ph.op("tensor", lambda q, j=j: q.transpose(
                        p_t[j][:, :], hM[:, j * 128:(j + 1) * 128], C.ident[:, :]),
                        waits=[thc], inc=et)
                    tr = ph.op("vector", lambda q, j=j: q.tensor_reduce(
                        hscc[:, j:j + 1], p_t[j][:, :], AX.X, ALU.max), waits=[tt],
                        drain=True, inc=ev)
                    tts.append(tr)
                tasm = []
                for j in range(2):
                    tt2 = ph.op("tensor", lambda q, j=j: q.matmul(
                        p_r[:, :], hscc[:, j:j + 1], C.ident[:, :], start=True,
                        stop=True), waits=[tts[j]] + ([tasm[-1]] if tasm else []),
                        inc=et)
                    tasm.append(ph.op("vector", lambda q, j=j: q.tensor_copy(
                        hrow[:, j * 128:(j + 1) * 128], p_r[0:1, :]), waits=[tt2],
                        drain=True, inc=ev))
                tc1 = ph.op("vector", lambda q: q.tensor_scalar(
                    hrow[:, :], hrow[:, :], 1e-5, None, ALU.max), drain=True, inc=ev)
                td1 = ph.op("vector", lambda q: q.tensor_scalar(
                    hrow2[:, :], hrow[:, :], 1.0 / 127.0, None, ALU.mult), drain=True)
                td2 = ph.op("vector", lambda q: q.tensor_scalar(
                    hrow2[:, :], hrow2[:, :], regs[0:1, 2:3], None, ALU.mult),
                    drain=True, inc=ev)
                tr1 = ph.op("vector", lambda q: q.reciprocal(hrow[:, :], hrow[:, :]),
                            drain=True)
                tr2 = ph.op("vector", lambda q: q.tensor_scalar(
                    hrow[:, :], hrow[:, :], 127.0, None, ALU.mult), drain=True, inc=ev)
                tb1 = ph.op("tensor", lambda q: q.matmul(
                    p_o[0][:, :], C.ones_row[:, 0:128], hrow[:, :], start=True,
                    stop=True), waits=[tr2], inc=et)
                tb2 = ph.op("tensor", lambda q: q.matmul(
                    p_o[1][:, :], C.ones_row[:, 0:128], hrow2[:, :], start=True,
                    stop=True), waits=[td2], inc=et)
                tcs = tb1
                tcd = ph.op("vector", lambda q: q.tensor_copy(d2s[:, :], p_o[1][:, :]),
                            waits=[tb2], drain=True, inc=ev)
                tqh = None
                for hc in range(HCH):
                    ho = hc * (TT2 + 2)
                    ph.op("vector", lambda q, ho=ho: q.tensor_mul(
                        cscr[:, :], hbuf[:, ho:ho + TT2], p_o[0][:, :]),
                        waits=[tcs], drain=True)
                    tqh = rne_clip(ph, (QH[:, hc * TT2:(hc + 1) * TT2], ev),
                                   (cscr[:, :], ()), -128.0, 127.0, (cscr[:, :], None))
                wl2 = {}
                ev2 = {}
                for m in range(NCH):
                    wl2[m] = ph.dma("sync", wt2[:, :],
                                    qw2.ap().rearrange("p (k d) -> p k d", k=32)
                                    [:, :, m * 128:(m + 1) * 128], ld,
                                    waits=[ev2.get(m - 1)])
                    tl2 = ph.settle("sync", ld)
                    po = p_o[m % 2]
                    for k in range(HCH):
                        mm = ph.op("tensor", lambda q, po=po, k=k: q.matmul(
                            po[:, :], wt2[:, k * 128:(k + 1) * 128],
                            QH[:, k * TT2:(k + 1) * TT2],
                            start=(k == 0), stop=(k == HCH - 1)),
                            waits=[tl2, tcs, tcd, tqh, ev2.get(m - 2)] if k == 0 else (),
                            inc=et if k == HCH - 1 else None)
                    tx1 = ph.op("vector", lambda q, po=po: q.tensor_mul(
                        cscr[:, :], po[:, :], d2s[:, :]), waits=[mm, tcd], drain=True)
                    ev2[m] = ph.op("vector", lambda q, m=m: q.tensor_add(
                        C.X[:, m * T + t0:m * T + t0 + TT2],
                        C.X[:, m * T + t0:m * T + t0 + TT2], cscr[:, :]),
                        drain=True, inc=ev)
                P.mark("X", ev2[NCH - 1])

    ustack.__exit__(None, None, None)
    emit_ln(P, C, "u1ln")



def emit_u_l1attn(P, C, W):
    nc = P.nc
    H, DH = 16, 64
    NCG = 3072 // TG

    WqkvT = np.ascontiguousarray(W["L1_attn_W_qkv"].astype(np.float32).T)
    brow = np.ascontiguousarray(W["L1_attn_b_qkv"].astype(np.float32).reshape(1, 3072))
    freqs = (1.0 / 10000.0 ** (np.arange(0, DH, 2, dtype=np.float64) / DH))
    tabs = {k: [] for k in ("c", "s", "cq", "sq")}
    for c in range(NCORES):
        tglob = np.arange(c % 2 * T, c % 2 * T + T, dtype=np.float64)
        ang = tglob[:, None] * freqs[None, :]
        cosv = np.cos(ang).astype(np.float32)
        sinv = np.sin(ang).astype(np.float32)
        tabs["c"].append(np.ascontiguousarray(np.tile(cosv, (1, H))))
        tabs["s"].append(np.ascontiguousarray(np.tile(sinv, (1, H))))
        tabs["cq"].append(np.ascontiguousarray(np.tile(cosv / 8.0, (1, H))))
        tabs["sq"].append(np.ascontiguousarray(np.tile(sinv / 8.0, (1, H))))
    a_w = P.in_("u2_WqkvT", WqkvT)
    a_b = P.in_("u2_brow", brow)
    a_c = P.in_("u2_cos", tabs["c"])
    a_s = P.in_("u2_sin", tabs["s"])
    a_cq = P.in_("u2_cosq", tabs["cq"])
    a_sq = P.in_("u2_sinq", tabs["sq"])

    q_dram = P.dram("u2_q", [T, 1024], BF16)
    ex_in = P.dram("u2_exin", [T, 1024 + H * 65], BF16)
    ex_out = P.dram("u2_exout", [SEQ, 1024 + H * 65], BF16)
    gm_in = P.dram("u2_gmin", [1, 1], F32)
    gm_out = P.dram("u2_gmout", [1, 1], F32)

    ld = P.gsem("dma_sync")
    ldg = P.gsem("dma_gp")
    std = P.gsem("dma_st")
    ccs = P.gsem("cc")
    ev = P.gsem("ev")
    et = P.gsem("et")
    es_ = P.gsem("es")

    ustack = ExitStack()
    ustack.__enter__()
    UE = ustack.enter_context
    regs = UE(nc.sbuf_tensor("u2_regs", [1, 8], F32))
    rowB = UE(nc.sbuf_tensor("u2_rowB", [1, T], F32))

    with ExitStack() as st:
        E = st.enter_context
        Wq = E(nc.sbuf_tensor("u2_W", [128, NCH * 3072], BF16))
        biasr = E(nc.sbuf_tensor("u2_bias", [1, 3072], BF16))
        cosb = E(nc.sbuf_tensor("u2_cosb", [128, 1024], F32))
        sinb = E(nc.sbuf_tensor("u2_sinb", [128, 1024], F32))
        raw = E(nc.sbuf_tensor("u2_raw", [128, 3072], F32))
        scr2 = E(nc.sbuf_tensor("u2_scr2", [128, 1024], F32))
        rot = E(nc.sbuf_tensor("u2_rot", [128, 1024], BF16))
        vt = E(nc.sbuf_tensor("u2_vt", [128, H * 65], BF16))
        acol = E(nc.sbuf_tensor("u2_acol", [128, 4], F32))
        nrm = E(nc.sbuf_tensor("u2_nrm", [128, 16], F32))
        arow = E(nc.sbuf_tensor("u2_arow", [1, 128], F32))
        p_q = [E(nc.psum_tensor(f"u2_pq{i}", [128, TG], F32)) for i in range(2)]
        p_s = E(nc.psum_tensor("u2_psm", [1, 128], F32))

        with Ph(P, "u2_load") as ph:
            ph.dma("gpsimd", Wq[:, :].rearrange("p (c d) -> p c d", c=NCH),
                   a_w.rearrange("(c p) d -> p c d", p=128), ldg)
            ph.dma("gpsimd", biasr[:, :], a_b, ldg)
            tlw = ph.settle("gpsimd", ldg)

        for tt in range(T // 128):
            t0 = tt * 128
            with Ph(P, f"u2_p1_{tt}") as ph:
                ph.dma("sync", cosb[:, 0:512], a_c[t0:t0 + 128, :], ld)
                ph.dma("sync", sinb[:, 0:512], a_s[t0:t0 + 128, :], ld)
                ph.dma("sync", cosb[:, 512:1024], a_cq[t0:t0 + 128, :], ld)
                ph.dma("sync", sinb[:, 512:1024], a_sq[t0:t0 + 128, :], ld)
                tlc = ph.settle("sync", ld)
                evq = {}
                for cg in range(NCG):
                    po = p_q[cg % 2]
                    ph.op("tensor", lambda q, po=po, cg=cg: q.matmul(
                        po[:, :], C.ones_row_bf[:, 0:128],
                        biasr[:, cg * TG:(cg + 1) * TG], start=True, stop=False),
                        waits=[tlw, P.marks.get("Xb"), evq.get(cg - 2)],
                        needs=("constsg",))
                    for k in range(NCH):
                        mm = ph.op("tensor", lambda q, po=po, k=k, cg=cg, t0=t0: q.matmul(
                            po[:, :], C.Xb[:, k * T + t0:k * T + t0 + 128],
                            Wq[:, k * 3072 + cg * TG:k * 3072 + (cg + 1) * TG],
                            start=False, stop=(k == NCH - 1)),
                            inc=et if k == NCH - 1 else None)
                    evq[cg] = ph.op("vector", lambda q, po=po, cg=cg: q.tensor_copy(
                        raw[:, cg * TG:(cg + 1) * TG], po[:, :]), waits=[mm],
                        drain=True, inc=ev)

                def rope(src_off, cos_off, w0):
                    e0 = raw[:, src_off + 0:src_off + 1024:2]
                    e1 = raw[:, src_off + 1:src_off + 1024:2]
                    co = cosb[:, cos_off:cos_off + 512]
                    si = sinb[:, cos_off:cos_off + 512]
                    d0 = rot[:, 0:1024:2]
                    d1 = rot[:, 1:1024:2]
                    ph.op("vector", lambda q: q.tensor_mul(scr2[:, 0:512], e0, co),
                          waits=w0, drain=True)
                    ph.op("vector", lambda q: q.tensor_mul(scr2[:, 512:1024], e1, si),
                          drain=True)
                    ph.op("vector", lambda q: q.tensor_sub(
                        d0, scr2[:, 0:512], scr2[:, 512:1024]), drain=True)
                    ph.op("vector", lambda q: q.tensor_mul(scr2[:, 0:512], e0, si),
                          drain=True)
                    ph.op("vector", lambda q: q.tensor_mul(scr2[:, 512:1024], e1, co),
                          drain=True)
                    return ph.op("vector", lambda q: q.tensor_add(
                        d1, scr2[:, 0:512], scr2[:, 512:1024]), drain=True, inc=ev)

                tq = rope(0, 512, [evq[NCG - 1], tlc])
                tdq = ph.dma("sync", q_dram.ap()[t0:t0 + 128, :], rot[:, :], std,
                             waits=[tq])
                tsq = ph.op("scalar", lambda q: q.activation(
                    scr2[:, :], rot[:, :], AF.Square), waits=[tq, tdq], inc=es_)
                tnr = ph.op("vector", lambda q: q.tensor_reduce(
                    nrm[:, :].rearrange("p (h u) -> p h u", u=1),
                    scr2[:, :].rearrange("p (h d) -> p h d", d=64),
                    AX.X, ALU.add), waits=[tsq], drain=True)
                ph.op("vector", lambda q: q.tensor_reduce(
                    acol[:, 0:1], nrm[:, :], AX.X, ALU.max), drain=True)
                if tt == 0:
                    ph.op("vector", lambda q: q.tensor_copy(acol[:, 2:3], acol[:, 0:1]),
                          drain=True, inc=ev)
                else:
                    ph.op("vector", lambda q: q.tensor_max(
                        acol[:, 2:3], acol[:, 2:3], acol[:, 0:1]), drain=True, inc=ev)
                tk = rope(1024, 0, [tdq])
                tdk = ph.dma("sync", ex_in.ap()[t0:t0 + 128, 0:1024], rot[:, :], std,
                             waits=[tk])
                tsk = ph.op("scalar", lambda q: q.activation(
                    scr2[:, :], rot[:, :], AF.Square), waits=[tk, tdk], inc=es_)
                ph.op("vector", lambda q: q.tensor_reduce(
                    nrm[:, :].rearrange("p (h u) -> p h u", u=1),
                    scr2[:, :].rearrange("p (h d) -> p h d", d=64),
                    AX.X, ALU.add), waits=[tsk], drain=True)
                ph.op("vector", lambda q: q.tensor_reduce(
                    acol[:, 0:1], nrm[:, :], AX.X, ALU.max), drain=True)
                if tt == 0:
                    ph.op("vector", lambda q: q.tensor_copy(acol[:, 3:4], acol[:, 0:1]),
                          drain=True, inc=ev)
                else:
                    ph.op("vector", lambda q: q.tensor_max(
                        acol[:, 3:4], acol[:, 3:4], acol[:, 0:1]), drain=True, inc=ev)
                ph.op("vector", lambda q: q.tensor_copy(
                    vt[:, :].rearrange("p (h j) -> p h j", j=65)[:, :, 0:64],
                    raw[:, 2048:3072].rearrange("p (h j) -> p h j", j=64)),
                    waits=[evq[NCG - 1], tdk], drain=True)
                tv2 = ph.op("vector", lambda q: q.memset(
                    vt[:, :].rearrange("p (h j) -> p h j", j=65)[:, :, 64:65], 1.0),
                    drain=True, inc=ev)
                ph.dma("sync", ex_in.ap()[t0:t0 + 128, 1024:], vt[:, :], std,
                       waits=[tv2])

        with Ph(P, "u2_ex") as ph:
            tst = ph.settle("sync", std)
            tt1 = ph.op("tensor", lambda q: q.matmul(
                p_s[:, :], acol[:, 3:4], C.ident[:, :], start=True, stop=True),
                waits=[ev.tok()], inc=et, needs=("consts",))
            tr1 = ph.op("vector", lambda q: q.tensor_copy(arow[:, :], p_s[0:1, :]),
                        waits=[tt1], drain=True, inc=ev)
            tr2 = ph.op("vector", lambda q: q.reduce_max(
                regs[:, 1:2], arow[:, :], AX.X), drain=True, inc=ev)
            ph.dma("sync", gm_in.ap(), regs[:, 1:2], ld, waits=[tr2])
            tst2 = ph.settle("sync", ld)
            tcc = ph.op("gpsimd", lambda q: q.collective_compute(
                "AllReduce", ALU.max, replica_groups=PAIRS,
                ins=[gm_in.ap().opt()], outs=[gm_out.ap().opt()]), waits=[tst2, tst],
                inc=ccs)
            tcc2 = ph.op("gpsimd", lambda q: q.collective_compute(
                "AllGather", ALU.bypass, replica_groups=PAIRS,
                ins=[ex_in.ap().opt()], outs=[ex_out.ap().opt()]), inc=ccs)
            ph.dma("sync", regs[:, 2:3], gm_out.ap(), ld, waits=[tcc])
            tst3 = ph.settle("sync", ld)
            tt2 = ph.op("tensor", lambda q: q.matmul(
                p_s[:, :], acol[:, 2:3], C.ident[:, :], start=True, stop=True),
                waits=[tr1], inc=et)
            tr3 = ph.op("vector", lambda q: q.tensor_copy(arow[:, :], p_s[0:1, :]),
                        waits=[tt2], drain=True)
            tr4 = ph.op("vector", lambda q: q.reduce_max(
                regs[:, 0:1], arow[:, :], AX.X), drain=True, inc=ev)
            tpr = ph.op("vector", lambda q: q.tensor_mul(
                regs[:, 3:4], regs[:, 0:1], regs[:, 2:3]), waits=[tr4, tst3],
                drain=True, inc=ev)
            tsqr = ph.op("scalar", lambda q: q.activation(
                regs[:, 4:5], regs[:, 3:4], AF.Sqrt), waits=[tpr], inc=es_)
            ph.op("vector", lambda q: q.tensor_scalar(
                regs[:, 4:5], regs[:, 4:5], -1.0, None, ALU.mult), waits=[tsqr],
                drain=True, inc=ev)
            ph.op("vector", lambda q: q.memset(rowB[:, :], 1.0), drain=True)
            tm2 = ph.op("vector", lambda q: q.tensor_scalar(
                rowB[:, :], rowB[:, :], regs[0:1, 4:5], None, ALU.mult), drain=True,
                inc=ev)
            P.mark("u2rowB", tm2)
            P.mark("u2cc", (ccs, ccs.n))

    with ExitStack() as st:
        E = st.enter_context
        Kp = E(nc.sbuf_tensor("u2_Kp", [128, SEQ], BF16))
        Qp = E(nc.sbuf_tensor("u2_Qp", [128, T], BF16))
        Kh = [E(nc.sbuf_tensor(f"u2_Kh{i}", [65, SEQ], BF16)) for i in range(2)]
        Qh = [E(nc.sbuf_tensor(f"u2_Qh{i}", [65, T], BF16)) for i in range(2)]
        Vh = [E(nc.sbuf_tensor(f"u2_Vh{i}", [128, 65], BF16)) for i in range(4)]
        PT = E(nc.sbuf_tensor("u2_PT", [128, T], BF16))
        oscr = E(nc.sbuf_tensor("u2_oscr", [128, TG], F32))
        rden = E(nc.sbuf_tensor("u2_rden", [1, TG], F32))
        p_sc = E(nc.psum_tensor("u2_psc", [128, T], F32))
        p_o = [E(nc.psum_tensor(f"u2_po{i}", [65, TG], F32)) for i in range(4)]

        NJ = SEQ // 128
        prev_evac = None
        prev_copies = None
        pair_tok = None
        for h in range(H):
            h2, par = h // 2, h % 2
            sl = h % 2
            with Ph(P, f"u2_h{h}") as ph:
                if par == 0:
                    ph.dmat("sync", Kp[:, :],
                            ex_out.ap()[:, h2 * 128:(h2 + 1) * 128], ld,
                            waits=[prev_copies, P.marks.get("u2cc")])
                    ph.dmat("sync", Qp[:, :],
                            q_dram.ap()[:, h2 * 128:(h2 + 1) * 128], ld)
                    pair_tok = ph.settle("sync", ld)
                base = 64 * par
                tk1 = ph.op("vector", lambda q, sl=sl, base=base: q.tensor_copy(
                    Kh[sl][0:64, :], Kp[base:base + 64, :]),
                    waits=[pair_tok, prev_evac], drain=True)
                ph.op("vector", lambda q, sl=sl: q.memset(Kh[sl][64:65, :], 1.0),
                      drain=True)
                ph.op("vector", lambda q, sl=sl, base=base: q.tensor_copy(
                    Qh[sl][0:64, :], Qp[base:base + 64, :]), drain=True)
                tq2 = ph.op("vector", lambda q, sl=sl: q.tensor_copy(
                    Qh[sl][64:65, :], rowB[:, :]), drain=True, inc=ev,
                    needs=("u2rowB",))
                prev_copies = tq2
                texp = None
                tavd = None
                vload = {}
                avdone = {}
                for j in range(NJ):
                    vload[j] = ph.dma("sync", Vh[j % 4][:, :],
                                      ex_out.ap()[j * 128:(j + 1) * 128,
                                                  1024 + h * 65:1024 + (h + 1) * 65],
                                      ld, waits=[avdone.get(j - 4, prev_evac)])
                    tvl = ph.settle("sync", ld)
                    for ig in range(4):
                        mm = ph.op("tensor", lambda q, sl=sl, j=j, ig=ig: q.matmul(
                            p_sc[:, ig * TG:(ig + 1) * TG],
                            Kh[sl][:, j * 128:(j + 1) * 128],
                            Qh[sl][:, ig * TG:(ig + 1) * TG],
                            start=True, stop=True),
                            waits=[tq2, texp] if ig == 0 else (),
                            inc=et if ig == 3 else None)
                    texp = ph.op("scalar", lambda q: q.activation(
                        PT[:, :], p_sc[:, :], AF.Exp), waits=[mm, tavd], inc=es_)
                    for ig in range(4):
                        av = ph.op("tensor", lambda q, j=j, ig=ig: q.matmul(
                            p_o[ig][:, :], Vh[j % 4][:, :], PT[:, ig * TG:(ig + 1) * TG],
                            start=(j == 0), stop=(j == NJ - 1)),
                            waits=[texp, tvl] if ig == 0 else (),
                            inc=et if ig == 3 else None)
                    tavd = av
                    avdone[j] = av
                ch = h // 2
                tbc = None
                for ig in range(4):
                    trd = ph.op("vector", lambda q, ig=ig: q.reciprocal(
                        rden[:, :], p_o[ig][64:65, :]), waits=[tavd, tbc], drain=True,
                        inc=ev)
                    tbc = ph.op("tensor", lambda q, ig=ig: q.matmul(
                        p_sc[:, 0:TG], C.ones_row[:, 0:128], rden[:, :],
                        start=True, stop=True), waits=[trd, texp], inc=et)
                    tcp = ph.op("scalar", lambda q, ig=ig, base=base: q.activation(
                        oscr[base:base + 64, :], p_o[ig][0:64, :], AF.Copy),
                        waits=[tavd, prev_evac], inc=es_)
                    tml = ph.op("vector", lambda q, base=base: q.tensor_mul(
                        oscr[base:base + 64, :], oscr[base:base + 64, :],
                        p_sc[base:base + 64, 0:TG]),
                        waits=[tbc, tcp], drain=True)
                    prev_evac = ph.op("vector", lambda q, ig=ig, ch=ch, base=base: q.tensor_add(
                        C.X[base:base + 64, ch * T + ig * TG:ch * T + (ig + 1) * TG],
                        C.X[base:base + 64, ch * T + ig * TG:ch * T + (ig + 1) * TG],
                        oscr[base:base + 64, :]), drain=True, inc=ev)
        P.mark("X", prev_evac)

    ustack.__exit__(None, None, None)
    emit_ln(P, C, "u2ln")



def emit_u_l3ffn(P, C, W):
    nc = P.nc
    ewT = W["L3_ffn_expert_w"].astype(np.float32).transpose(0, 2, 1)
    ewT = np.ascontiguousarray(ewT.reshape(8 * 1024, 1024))
    gwT = np.ascontiguousarray(W["L3_ffn_gate_w"].astype(np.float32).T)
    gbr = np.ascontiguousarray(W["L3_ffn_gate_b"].astype(np.float32).reshape(1, 8))
    ebc = np.ascontiguousarray(
        W["L3_ffn_expert_b"].astype(np.float32).reshape(8, 8, 128).transpose(2, 0, 1).reshape(128, 64))
    a_ew = P.in_("u7_ewT", ewT)
    a_gw = P.in_("u7_gwT", gwT)
    a_gb = P.in_("u7_gb", gbr)
    a_eb = P.in_("u7_eb", ebc)
    pl_in = P.dram("u7_plin", [128, 8], F32)
    pl_out = P.dram("u7_plout", [128, 8], F32)

    ld = P.gsem("dma_sync")
    ldg = P.gsem("dma_gp")
    ccs = P.gsem("cc")
    ev = P.gsem("ev")
    es_ = P.gsem("es")
    et = P.gsem("et")

    with ExitStack() as st:
        E = st.enter_context
        Wt = [E(nc.sbuf_tensor(f"u7_Wt{i}", [128, NCH * 1024], BF16)) for i in range(2)]
        gw = E(nc.sbuf_tensor("u7_gw", [128, NCH * 8], F32))
        gb = E(nc.sbuf_tensor("u7_gbs", [1, 8], F32))
        ebb = E(nc.sbuf_tensor("u7_ebs", [128, 64], F32))
        pool = E(nc.sbuf_tensor("u7_pool", [128, 8], F32))
        lg = E(nc.sbuf_tensor("u7_lg", [1, 8], F32))
        r1 = E(nc.sbuf_tensor("u7_r1", [1, 8], F32))
        r2 = E(nc.sbuf_tensor("u7_r2", [1, 8], F32))
        gates = E(nc.sbuf_tensor("u7_gates", [1, 8], F32))
        gcol = E(nc.sbuf_tensor("u7_gcol", [128, 8], F32))
        scr = E(nc.sbuf_tensor("u7_scr", [128, TG], F32))
        p_g = E(nc.psum_tensor("u7_pg", [1, 8], F32))
        p_b = E(nc.psum_tensor("u7_pb", [128, 8], F32))
        p_o = [E(nc.psum_tensor(f"u7_po{i}", [128, TG], F32)) for i in range(2)]

        with Ph(P, "u7_gate") as ph:
            ph.dma("sync", gw[:, :].rearrange("p (c g) -> p c g", g=8),
                   a_gw.rearrange("(c p) g -> p c g", p=128), ld)
            ph.dma("sync", gb[:, :], a_gb, ld)
            ph.dma("sync", ebb[:, :], a_eb, ld)
            tl = ph.settle("sync", ld)
            for ch in range(NCH):
                ph.op("vector", lambda q, ch=ch: q.tensor_reduce(
                    pool[:, ch:ch + 1], C.X[:, ch * T:(ch + 1) * T], AX.X, ALU.add),
                    needs=("X",))
            tps = ph.op("vector", lambda q: q.tensor_scalar(
                pool[:, :], pool[:, :], 1.0 / SEQ, None, ALU.mult), drain=True, inc=ev)
            ph.dma("sync", pl_in.ap(), pool[:, :], ld, waits=[tps])
            tst = ph.settle("sync", ld)
            tcc = ph.op("gpsimd", lambda q: q.collective_compute(
                "AllReduce", ALU.add, replica_groups=PAIRS,
                ins=[pl_in.ap().opt()], outs=[pl_out.ap().opt()]), waits=[tst],
                inc=ccs)
            ph.dma("sync", pool[:, :], pl_out.ap(), ld, waits=[tcc])
            tst2 = ph.settle("sync", ld)
            for ch in range(NCH):
                tmg = ph.op("tensor", lambda q, ch=ch: q.matmul(
                    p_g[:, :], pool[:, ch:ch + 1], gw[:, ch * 8:(ch + 1) * 8],
                    start=(ch == 0), stop=(ch == NCH - 1)),
                    waits=[tst2, tl], inc=et if ch == NCH - 1 else None)
            t1 = ph.op("vector", lambda q: q.tensor_add(lg[:, :], p_g[0:1, :], gb[:, :]),
                       waits=[tmg], drain=True, inc=ev)
            t2 = ph.op("vector", lambda q: q.reduce_max(r1[:, 0:1], lg[:, :], AX.X),
                       drain=True)
            t3 = ph.op("vector", lambda q: q.tensor_scalar(
                r2[:, :], lg[:, :], r1[:, 0:1], None, ALU.is_equal), drain=True)
            t4 = ph.op("vector", lambda q: q.tensor_reduce(
                r1[:, 1:2], r2[:, :], AX.X, ALU.add), drain=True)
            t5 = ph.op("vector", lambda q: q.scalar_tensor_tensor(
                r2[:, :], r2[:, :], -1e30, lg[:, :], ALU.mult, ALU.add), drain=True)
            t6 = ph.op("vector", lambda q: q.reduce_max(r1[:, 2:3], r2[:, :], AX.X),
                       drain=True)
            t7 = ph.op("vector", lambda q: q.tensor_scalar(
                r1[:, 3:4], r1[:, 1:2], 2.0, None, ALU.is_ge), drain=True)
            t8 = ph.op("vector", lambda q: q.tensor_mul(
                r1[:, 4:5], r1[:, 3:4], r1[:, 0:1]), drain=True)
            t9 = ph.op("vector", lambda q: q.tensor_scalar(
                r1[:, 5:6], r1[:, 3:4], -1.0, 1.0, ALU.mult, ALU.add), drain=True)
            t10 = ph.op("vector", lambda q: q.tensor_mul(
                r1[:, 5:6], r1[:, 5:6], r1[:, 2:3]), drain=True)
            t11 = ph.op("vector", lambda q: q.tensor_add(
                r1[:, 4:5], r1[:, 4:5], r1[:, 5:6]), drain=True)
            t12 = ph.op("vector", lambda q: q.tensor_scalar(
                r2[:, :], lg[:, :], r1[:, 4:5], None, ALU.is_ge), drain=True)
            t13 = ph.op("vector", lambda q: q.tensor_scalar(
                r1[:, 6:7], r1[:, 0:1], -1.0, None, ALU.mult), drain=True, inc=ev)
            t14 = ph.op("scalar", lambda q: q.activation(
                gates[:, :], lg[:, :], AF.Exp, bias=r1[:, 6:7]), waits=[t13], inc=es_)
            t15 = ph.op("vector", lambda q: q.tensor_mul(
                gates[:, :], gates[:, :], r2[:, :]), waits=[t14], drain=True)
            t16 = ph.op("vector", lambda q: q.tensor_reduce(
                r1[:, 7:8], gates[:, :], AX.X, ALU.add), drain=True)
            t17 = ph.op("vector", lambda q: q.reciprocal(r1[:, 6:7], r1[:, 7:8]),
                        drain=True)
            t18 = ph.op("vector", lambda q: q.tensor_scalar(
                gates[:, :], gates[:, :], r1[:, 6:7], None, ALU.mult), drain=True,
                inc=ev)
            tbg = ph.op("tensor", lambda q: q.matmul(
                p_b[:, :], C.ones_row[:, 0:128], gates[:, :], start=True, stop=True),
                waits=[t18], inc=et, needs=("consts",))
            ph.op("vector", lambda q: q.tensor_copy(gcol[:, :], p_b[:, :]),
                  waits=[tbg], drain=True, inc=ev)
            P.mark("u7gates", (ev, ev.n))

        wload = {}
        prev = {}
        for e in range(8):
            with Ph(P, f"u7_e{e}") as ph:
                sl = e % 2
                wload[e] = ph.dma("gpsimd", Wt[sl][:, :].rearrange("p (c d) -> p c d", c=NCH),
                                  a_ew[e * 1024:(e + 1) * 1024, :].rearrange(
                                      "(c p) d -> p c d", p=128), ldg,
                                  waits=[prev.get(e - 2)])
                tlw = ph.settle("gpsimd", ldg)
                i = 0
                evt = {}
                for m in range(NCH):
                    for g in range(NTG):
                        po = p_o[i % 2]
                        a, b = g * TG, (g + 1) * TG
                        for k in range(NCH):
                            mm = ph.op("tensor", lambda q, po=po, sl=sl, k=k, m=m, a=a, b=b: q.matmul(
                                po[:, :], Wt[sl][:, k * 1024 + m * 128:k * 1024 + (m + 1) * 128],
                                C.Xb[:, k * T + a:k * T + b],
                                start=(k == 0), stop=(k == NCH - 1)),
                                waits=[tlw, evt.get(i - 2), P.marks.get("Xb")] if k == 0 else (),
                                inc=et if k == NCH - 1 else None)
                        tge = ph.op("scalar", lambda q, po=po, m=m, e=e: q.activation(
                            scr[:, :], po[:, :], AF.Gelu, bias=ebb[:, e * 8 + m:e * 8 + m + 1]),
                            waits=[mm, evt.get(i - 1)], inc=es_, needs=("u7gates",))
                        evt[i] = ph.op("vector", lambda q, m=m, a=a, b=b, e=e: q.scalar_tensor_tensor(
                            C.X[:, m * T + a:m * T + b], scr[:, :], gcol[:, e:e + 1],
                            C.X[:, m * T + a:m * T + b], ALU.mult, ALU.add),
                            waits=[tge], drain=True, inc=ev)
                        i += 1
                prev[e] = evt[i - 1]
        P.mark("X", prev[7])

    emit_ln(P, C, "u7ln")



def emit_u_l2ffn(P, C, W):
    nc = P.nc
    NE = 4
    FF = 4096
    FCH = FF // 128
    w1h = np.ascontiguousarray(W["L2_ffn_W1"].astype(np.float32).reshape(NE * 1024, FF))
    w2h = np.ascontiguousarray(W["L2_ffn_W2"].astype(np.float32).reshape(NE * FF, 1024))
    b1c = np.ascontiguousarray(
        W["L2_ffn_b1"].astype(np.float32).reshape(NE, FCH, 128).transpose(2, 0, 1).reshape(128, NE * FCH))
    b2c = np.ascontiguousarray(
        W["L2_ffn_b2"].astype(np.float32).reshape(NE, NCH, 128).transpose(2, 0, 1).reshape(128, NE * NCH))
    wu = np.ascontiguousarray(W["L2_ffn_Wu"].astype(np.float32))
    bu = np.ascontiguousarray(W["L2_ffn_bu"].astype(np.float32).reshape(1, 1))
    irows, jrows = [], []
    for c in range(NCORES):
        tg = np.arange(c % 2 * T, c % 2 * T + T)
        ie = ((np.arange(NE)[:, None] - tg[None, :]) % NE + 1).astype(np.float32)
        irows.append(np.ascontiguousarray(1.0 / ie))
        jrows.append(np.ascontiguousarray(ie - 1.0))
    a_w1 = P.in_("u5_w1", w1h)
    a_w2 = P.in_("u5_w2", w2h)
    a_b1 = P.in_("u5_b1", b1c)
    a_b2 = P.in_("u5_b2", b2c)
    a_wu = P.in_("u5_wu", wu)
    a_bu = P.in_("u5_bu", bu)
    a_ir = P.in_("u5_ir", irows)
    a_jr = P.in_("u5_jr", jrows)

    ld = P.gsem("dma_sync")
    ldg = P.gsem("dma_gp")
    ev = P.gsem("ev")
    es_ = P.gsem("es")
    et = P.gsem("et")

    with ExitStack() as st:
        E = st.enter_context
        w1t = [E(nc.sbuf_tensor(f"u5_w1t{i}", [128, NCH * 128], BF16)) for i in range(2)]
        w2t = [E(nc.sbuf_tensor(f"u5_w2t{i}", [128, FCH * 128], BF16)) for i in range(2)]
        b1b = E(nc.sbuf_tensor("u5_b1s", [128, NE * FCH], F32))
        b2b = E(nc.sbuf_tensor("u5_b2s", [128, NE * NCH], F32))
        wub = E(nc.sbuf_tensor("u5_wus", [128, NCH], F32))
        irb = E(nc.sbuf_tensor("u5_irs", [1, TG], F32))
        jrb = E(nc.sbuf_tensor("u5_jrs", [1, TG], F32))
        urow = E(nc.sbuf_tensor("u5_urow", [1, T], F32))
        u4row = E(nc.sbuf_tensor("u5_u4", [1, T], F32))
        crow = E(nc.sbuf_tensor("u5_crow", [1, T], F32))
        ht = E(nc.sbuf_tensor("u5_ht", [128, FCH * TG], BF16))
        cbs = E(nc.sbuf_tensor("u5_cbs", [128, TG], F32))
        scr = E(nc.sbuf_tensor("u5_scr", [128, TG], F32))
        bub = E(nc.sbuf_tensor("u5_bub", [1, 1], F32))
        p_u = E(nc.psum_tensor("u5_pu", [1, T], F32))
        p_h = [E(nc.psum_tensor(f"u5_ph{i}", [128, TG], F32)) for i in range(2)]
        p_c = E(nc.psum_tensor("u5_pc", [128, TG], F32))

        with Ph(P, "u5_u") as ph:
            ph.dma("sync", b1b[:, :], a_b1, ld)
            ph.dma("sync", b2b[:, :], a_b2, ld)
            ph.dma("sync", wub[:, :].rearrange("p (c u) -> p c u", u=1),
                   a_wu.rearrange("(c p) u -> p c u", p=128), ld,
                   allow_slow_non_contiguous=True)

            tl = ph.settle("sync", ld)
            for ch in range(NCH):
                for g in range(NTG):
                    a, b = g * TG, (g + 1) * TG
                    mmu = ph.op("tensor", lambda q, ch=ch, a=a, b=b: q.matmul(
                        p_u[:, a:b], wub[:, ch:ch + 1], C.X[:, ch * T + a:ch * T + b],
                        start=(ch == 0), stop=(ch == NCH - 1)),
                        waits=[tl] if ch == 0 and a == 0 else (),
                        needs=("X",), inc=et if ch == NCH - 1 and g == NTG - 1 else None)
            t1 = ph.op("vector", lambda q: q.tensor_copy(urow[:, :], p_u[0:1, :]),
                       waits=[mmu], drain=True, inc=ev)
            pass
        with Ph(P, "u5_u2") as ph:
            ph.dma("sync", bub[:, :], a_bu, ld)
            tl2 = ph.settle("sync", ld)
            t2 = ph.op("vector", lambda q: q.tensor_scalar(
                urow[:, :], urow[:, :], bub[0:1, 0:1], None, ALU.add),
                waits=[tl2], drain=True, inc=ev)
            t3 = ph.op("scalar", lambda q: q.activation(
                crow[:, :], urow[:, :], AF.Exp, scale=-1.0), waits=[t2], inc=es_,
                drain=True)
            t4 = ph.op("vector", lambda q: q.tensor_scalar(
                crow[:, :], crow[:, :], 1.0, None, ALU.add), waits=[t3], drain=True)
            t5 = ph.op("vector", lambda q: q.reciprocal(urow[:, :], crow[:, :]),
                       drain=True)
            t6 = ph.op("vector", lambda q: q.tensor_scalar(
                u4row[:, :], urow[:, :], 4.0, None, ALU.mult), drain=True, inc=ev)
            P.mark("u5u", t6)

        prevh = {}
        for e in range(NE):
            for g in range(NTG):
                a, b = g * TG, (g + 1) * TG
                with Ph(P, f"u5_e{e}g{g}") as ph:
                    ph.dma("sync", irb[:, :], a_ir[e:e + 1, a:b], ld,
                           waits=[prevh.get((e, g - 1), prevh.get((e - 1, NTG - 1)))])
                    ph.dma("sync", jrb[:, :], a_jr[e:e + 1, a:b], ld)
                    tlr = ph.settle("sync", ld)
                    tc1 = ph.op("vector", lambda q, e=e, a=a, b=b: q.tensor_tensor(
                        crow[:, a:b], u4row[:, a:b], jrb[:, :],
                        ALU.is_gt), needs=("u5u",), waits=[tlr], drain=True)
                    tc2 = ph.op("vector", lambda q, e=e, a=a, b=b: q.tensor_mul(
                        crow[:, a:b], crow[:, a:b], urow[:, a:b]), drain=True)
                    tc3 = ph.op("vector", lambda q, e=e, a=a, b=b: q.tensor_mul(
                        crow[:, a:b], crow[:, a:b], irb[:, :]),
                        drain=True, inc=ev)
                    tbc = ph.op("tensor", lambda q, a=a, b=b: q.matmul(
                        p_c[:, :], C.ones_row[:, 0:128], crow[:, a:b],
                        start=True, stop=True), waits=[tc3], inc=et,
                        needs=("consts",))
                    tcb = ph.op("vector", lambda q: q.tensor_copy(cbs[:, :], p_c[:, :]),
                                waits=[tbc], drain=True, inc=ev)
                    hdone = {}
                    for fc in range(FCH):
                        sl = fc % 2
                        wl = ph.dma("gpsimd", w1t[sl][:, :].rearrange("p (c d) -> p c d", c=NCH),
                                    a_w1[e * 1024:(e + 1) * 1024, fc * 128:(fc + 1) * 128]
                                    .rearrange("(c p) d -> p c d", p=128), ldg,
                                    waits=[hdone.get(fc - 2)])
                        tlw = ph.settle("gpsimd", ldg)
                        po = p_h[fc % 2]
                        for k in range(NCH):
                            mm = ph.op("tensor", lambda q, po=po, sl=sl, k=k, a=a, b=b: q.matmul(
                                po[:, :], w1t[sl][:, k * 128:(k + 1) * 128],
                                C.Xb[:, k * T + a:k * T + b],
                                start=(k == 0), stop=(k == NCH - 1)),
                                waits=[tlw, hdone.get(fc - 2), P.marks.get("Xb")] if k == 0 else (),
                                inc=et if k == NCH - 1 else None)
                        hdone[fc] = ph.op("scalar", lambda q, po=po, fc=fc, e=e: q.activation(
                            ht[:, fc * TG:(fc + 1) * TG], po[:, :], AF.Relu,
                            bias=b1b[:, e * FCH + fc:e * FCH + fc + 1]), waits=[mm],
                            inc=es_)
                    mdone = {}
                    for m in range(NCH):
                        sl = m % 2
                        wl2 = ph.dma("gpsimd", w2t[sl][:, :].rearrange("p (c d) -> p c d", c=FCH),
                                     a_w2[e * FF:(e + 1) * FF, m * 128:(m + 1) * 128]
                                     .rearrange("(c p) d -> p c d", p=128), ldg,
                                     waits=[mdone.get(m - 2)])
                        tlw2 = ph.settle("gpsimd", ldg)
                        po = p_h[m % 2]
                        for k in range(FCH):
                            mm = ph.op("tensor", lambda q, po=po, sl=sl, k=k: q.matmul(
                                po[:, :], w2t[sl][:, k * 128:(k + 1) * 128],
                                ht[:, k * TG:(k + 1) * TG],
                                start=(k == 0), stop=(k == FCH - 1)),
                                waits=[tlw2, hdone[FCH - 1], mdone.get(m - 2)] if k == 0 else (),
                                inc=et if k == FCH - 1 else None)
                        ts1 = ph.op("vector", lambda q, po=po, m=m, e=e: q.scalar_tensor_tensor(
                            scr[:, :], po[:, :], b2b[:, e * NCH + m:e * NCH + m + 1],
                            cbs[:, :], ALU.add, ALU.mult), waits=[mm, tcb], drain=True)
                        mdone[m] = ph.op("vector", lambda q, m=m, a=a, b=b: q.tensor_add(
                            C.X[:, m * T + a:m * T + b], C.X[:, m * T + a:m * T + b],
                            scr[:, :]), drain=True, inc=ev)
                    prevh[(e, g)] = mdone[NCH - 1]
        P.mark("X", prevh[(NE - 1, NTG - 1)])

    emit_ln(P, C, "u5ln")



def emit_u_l3attn(P, C, W):
    nc = P.nc
    H, DH = 16, 64
    NCG = 3072 // TG

    Wqkv = np.ascontiguousarray(W["L3_attn_Wqkv"].astype(np.float32))
    WoT = np.ascontiguousarray(W["L3_attn_Wout"].astype(np.float32))
    bout_c = np.ascontiguousarray(W["L3_attn_bout"].astype(np.float32).reshape(NCH, 128).T)
    i_idx = np.arange(DH // 2, dtype=np.float64)
    theta = 10000.0 ** (-2.0 * (i_idx - 1.0) / DH)
    tabs = {k: [] for k in ("c", "s", "cq", "sq")}
    for c in range(NCORES):
        tglob = np.arange(c % 2 * T, c % 2 * T + T, dtype=np.float64)
        ang = tglob[:, None] * theta[None, :]
        cosv = np.cos(ang).astype(np.float32)
        sinv = np.sin(ang).astype(np.float32)
        tabs["c"].append(np.ascontiguousarray(np.tile(cosv, (1, H))))
        tabs["s"].append(np.ascontiguousarray(np.tile(sinv, (1, H))))
        tabs["cq"].append(np.ascontiguousarray(np.tile(cosv / 8.0, (1, H))))
        tabs["sq"].append(np.ascontiguousarray(np.tile(sinv / 8.0, (1, H))))
    camask = np.zeros((128, 1024), np.float32)
    for p in range(128):
        camask[p, 512 + p:] = 1.0
    a_w = P.in_("u6_Wqkv", Wqkv)
    a_wo = P.in_("u6_WoT", WoT)
    a_bo = P.in_("u6_bout", bout_c)
    a_c = P.in_("u6_cos", tabs["c"])
    a_s = P.in_("u6_sin", tabs["s"])
    a_cq = P.in_("u6_cosq", tabs["cq"])
    a_sq = P.in_("u6_sinq", tabs["sq"])
    a_cm = P.in_("u6_camask", camask)

    q_dram = P.dram("u6_q", [T, 2048], BF16)
    ex_in = P.dram("u6_exin", [T, 2048 + H * 65], BF16)
    ex_out = P.dram("u6_exout", [SEQ, 2048 + H * 65], BF16)
    gm_in = P.dram("u6_gmin", [1, 1], F32)
    gm_out = P.dram("u6_gmout", [1, 1], F32)

    ld = P.gsem("dma_sync")
    ldg = P.gsem("dma_gp")
    std = P.gsem("dma_st")
    ccs = P.gsem("cc")
    ev = P.gsem("ev")
    et = P.gsem("et")
    es_ = P.gsem("es")

    ustack = ExitStack(); ustack.__enter__(); UE = ustack.enter_context
    regs = UE(nc.sbuf_tensor("u6_regs", [1, 8], F32))
    rowB = UE(nc.sbuf_tensor("u6_rowB", [1, T], F32))
    qbase = None
    a_h0 = P.in_("u6_h0", [np.full((128, 1), float(1 - c % 2), np.float32)
                           for c in range(NCORES)])
    a_h1 = P.in_("u6_h1", [np.full((128, 1), float(c % 2), np.float32)
                           for c in range(NCORES)])
    h0c = UE(nc.sbuf_tensor("u6_h0s", [128, 1], F32))
    h1c = UE(nc.sbuf_tensor("u6_h1s", [128, 1], F32))

    camask_ext = np.zeros((128, 2176), np.float32)
    for p in range(128):
        camask_ext[p, 512 + p:] = 1.0
    a_cme = P.in_("u6_cme", camask_ext)
    cmske = UE(nc.sbuf_tensor("u6_cmske", [128, 2176], BF16))

    with ExitStack() as st:
        E = st.enter_context
        Wq = E(nc.sbuf_tensor("u6_W", [128, NCH * 3072], BF16))
        cosb = E(nc.sbuf_tensor("u6_cosb", [128, 1024], F32))
        sinb = E(nc.sbuf_tensor("u6_sinb", [128, 1024], F32))
        raw = E(nc.sbuf_tensor("u6_raw", [128, 3072], F32))
        scr2 = E(nc.sbuf_tensor("u6_scr2", [128, 1024], F32))
        rot = E(nc.sbuf_tensor("u6_rot", [128, 1024], BF16))
        vt = E(nc.sbuf_tensor("u6_vt", [128, H * 65], BF16))
        acol = E(nc.sbuf_tensor("u6_acol", [128, 4], F32))
        arow = E(nc.sbuf_tensor("u6_arow", [1, 128], F32))
        nrm = E(nc.sbuf_tensor("u6_nrm", [128, 16], F32))
        p_q = [E(nc.psum_tensor(f"u6_pq{i}", [128, TG], F32)) for i in range(2)]
        p_s = E(nc.psum_tensor("u6_psm", [1, 128], F32))

        with Ph(P, "u6_load") as ph:
            ph.dma("gpsimd", Wq[:, :].rearrange("p (c d) -> p c d", c=NCH),
                   a_w.rearrange("(c p) d -> p c d", p=128), ldg)
            ph.dma("gpsimd", cmske[:, :], a_cme, ldg)
            ph.dma("sync", h0c[:, :], a_h0, ld)
            ph.dma("sync", h1c[:, :], a_h1, ld)
            tlw = ph.settle("gpsimd", ldg)
            ph.settle("sync", ld)

        for tt in range(T // 128):
            t0 = tt * 128
            with Ph(P, f"u6_p1_{tt}") as ph:
                ph.dma("sync", cosb[:, 0:512], a_c[t0:t0 + 128, :], ld)
                ph.dma("sync", sinb[:, 0:512], a_s[t0:t0 + 128, :], ld)
                ph.dma("sync", cosb[:, 512:1024], a_cq[t0:t0 + 128, :], ld)
                ph.dma("sync", sinb[:, 512:1024], a_sq[t0:t0 + 128, :], ld)
                tlc = ph.settle("sync", ld)
                evq = {}
                for cg in range(NCG):
                    po = p_q[cg % 2]
                    for k in range(NCH):
                        mm = ph.op("tensor", lambda q, po=po, k=k, cg=cg, t0=t0: q.matmul(
                            po[:, :], C.Xb[:, k * T + t0:k * T + t0 + 128],
                            Wq[:, k * 3072 + cg * TG:k * 3072 + (cg + 1) * TG],
                            start=(k == 0), stop=(k == NCH - 1)),
                            waits=[tlw, P.marks.get("Xb"), evq.get(cg - 2)] if k == 0 else (),
                            inc=et if k == NCH - 1 else None)
                    evq[cg] = ph.op("vector", lambda q, po=po, cg=cg: q.tensor_copy(
                        raw[:, cg * TG:(cg + 1) * TG], po[:, :]), waits=[mm],
                        drain=True, inc=ev)

                def rope6(src_off, cos_off, w0):
                    e0 = raw[:, src_off + 0:src_off + 1024:2]
                    e1 = raw[:, src_off + 1:src_off + 1024:2]
                    co = cosb[:, cos_off:cos_off + 512]
                    si = sinb[:, cos_off:cos_off + 512]
                    d0 = rot[:, 0:1024:2]
                    d1 = rot[:, 1:1024:2]
                    ph.op("vector", lambda q: q.tensor_mul(scr2[:, 0:512], e0, co),
                          waits=w0, drain=True)
                    ph.op("vector", lambda q: q.tensor_mul(scr2[:, 512:1024], e1, si),
                          drain=True)
                    ph.op("vector", lambda q: q.tensor_sub(
                        d0, scr2[:, 0:512], scr2[:, 512:1024]), drain=True)
                    ph.op("vector", lambda q: q.tensor_mul(scr2[:, 0:512], e0, si),
                          drain=True)
                    ph.op("vector", lambda q: q.tensor_mul(scr2[:, 512:1024], e1, co),
                          drain=True)
                    return ph.op("vector", lambda q: q.tensor_add(
                        d1, scr2[:, 0:512], scr2[:, 512:1024]), drain=True, inc=ev)

                tq = rope6(0, 512, [evq[NCG - 1], tlc])
                tdq = ph.dma("sync", q_dram.ap()[t0:t0 + 128, 0:1024], rot[:, :], std,
                             waits=[tq])
                tqu = ph.op("vector", lambda q: q.tensor_scalar(
                    rot[:, :], raw[:, 0:1024], 1.0 / 8.0, None, ALU.mult),
                    waits=[tdq], drain=True, inc=ev)
                tdqu = ph.dma("sync", q_dram.ap()[t0:t0 + 128, 1024:2048], rot[:, :],
                              std, waits=[tqu])
                tsq = ph.op("scalar", lambda q: q.activation(
                    scr2[:, :], rot[:, :], AF.Square), waits=[tqu, tdqu], inc=es_)
                ph.op("vector", lambda q: q.tensor_reduce(
                    nrm[:, :].rearrange("p (h u) -> p h u", u=1),
                    scr2[:, :].rearrange("p (h d) -> p h d", d=64),
                    AX.X, ALU.add), waits=[tsq], drain=True)
                ph.op("vector", lambda q: q.tensor_reduce(
                    acol[:, 0:1], nrm[:, :], AX.X, ALU.max), drain=True)
                if tt == 0:
                    ph.op("vector", lambda q: q.tensor_copy(acol[:, 2:3], acol[:, 0:1]),
                          drain=True, inc=ev)
                else:
                    ph.op("vector", lambda q: q.tensor_max(
                        acol[:, 2:3], acol[:, 2:3], acol[:, 0:1]), drain=True, inc=ev)
                tk = rope6(1024, 0, [tdqu])
                tdk = ph.dma("sync", ex_in.ap()[t0:t0 + 128, 0:1024], rot[:, :], std,
                             waits=[tk])
                tku = ph.op("vector", lambda q: q.tensor_copy(
                    rot[:, :], raw[:, 1024:2048]), waits=[tdk], drain=True, inc=ev)
                tdku = ph.dma("sync", ex_in.ap()[t0:t0 + 128, 1024:2048], rot[:, :],
                              std, waits=[tku])
                tsk = ph.op("scalar", lambda q: q.activation(
                    scr2[:, :], rot[:, :], AF.Square), waits=[tku, tdku], inc=es_)
                ph.op("vector", lambda q: q.tensor_reduce(
                    nrm[:, :].rearrange("p (h u) -> p h u", u=1),
                    scr2[:, :].rearrange("p (h d) -> p h d", d=64),
                    AX.X, ALU.add), waits=[tsk], drain=True)
                ph.op("vector", lambda q: q.tensor_reduce(
                    acol[:, 0:1], nrm[:, :], AX.X, ALU.max), drain=True)
                if tt == 0:
                    ph.op("vector", lambda q: q.tensor_copy(acol[:, 3:4], acol[:, 0:1]),
                          drain=True, inc=ev)
                else:
                    ph.op("vector", lambda q: q.tensor_max(
                        acol[:, 3:4], acol[:, 3:4], acol[:, 0:1]), drain=True, inc=ev)
                ph.op("vector", lambda q: q.tensor_copy(
                    vt[:, :].rearrange("p (h j) -> p h j", j=65)[:, :, 0:64],
                    raw[:, 2048:3072].rearrange("p (h j) -> p h j", j=64)),
                    waits=[evq[NCG - 1], tdku], drain=True)
                tv2 = ph.op("vector", lambda q: q.memset(
                    vt[:, :].rearrange("p (h j) -> p h j", j=65)[:, :, 64:65], 1.0),
                    drain=True, inc=ev)
                ph.dma("sync", ex_in.ap()[t0:t0 + 128, 2048:], vt[:, :], std,
                       waits=[tv2])

        with Ph(P, "u6_ex") as ph:
            tst = ph.settle("sync", std)
            tt1 = ph.op("tensor", lambda q: q.matmul(
                p_s[:, :], acol[:, 3:4], C.ident[:, :], start=True, stop=True),
                waits=[ev.tok()], inc=et, needs=("consts",))
            tr1 = ph.op("vector", lambda q: q.tensor_copy(arow[:, :], p_s[0:1, :]),
                        waits=[tt1], drain=True, inc=ev)
            tr2 = ph.op("vector", lambda q: q.reduce_max(
                regs[:, 1:2], arow[:, :], AX.X), drain=True, inc=ev)
            ph.dma("sync", gm_in.ap(), regs[:, 1:2], ld, waits=[tr2])
            tst2 = ph.settle("sync", ld)
            tcc = ph.op("gpsimd", lambda q: q.collective_compute(
                "AllReduce", ALU.max, replica_groups=PAIRS,
                ins=[gm_in.ap().opt()], outs=[gm_out.ap().opt()]), waits=[tst2, tst],
                inc=ccs)
            tcc2 = ph.op("gpsimd", lambda q: q.collective_compute(
                "AllGather", ALU.bypass, replica_groups=PAIRS,
                ins=[ex_in.ap().opt()], outs=[ex_out.ap().opt()]), inc=ccs)
            ph.dma("sync", regs[:, 2:3], gm_out.ap(), ld, waits=[tcc])
            tst3 = ph.settle("sync", ld)
            tt2 = ph.op("tensor", lambda q: q.matmul(
                p_s[:, :], acol[:, 2:3], C.ident[:, :], start=True, stop=True),
                waits=[tr1], inc=et)
            tr3 = ph.op("vector", lambda q: q.tensor_copy(arow[:, :], p_s[0:1, :]),
                        waits=[tt2], drain=True, inc=ev)
            tr4 = ph.op("vector", lambda q: q.reduce_max(
                regs[:, 0:1], arow[:, :], AX.X), drain=True, inc=ev)
            tpr = ph.op("vector", lambda q: q.tensor_mul(
                regs[:, 3:4], regs[:, 0:1], regs[:, 2:3]), waits=[tr4, tst3],
                drain=True, inc=ev)
            tsqr = ph.op("scalar", lambda q: q.activation(
                regs[:, 4:5], regs[:, 3:4], AF.Sqrt), waits=[tpr], inc=es_)
            ph.op("vector", lambda q: q.tensor_scalar(
                regs[:, 4:5], regs[:, 4:5], -1.0, None, ALU.mult), waits=[tsqr],
                drain=True, inc=ev)
            ph.op("vector", lambda q: q.memset(rowB[:, :], 1.0), drain=True)
            tm2 = ph.op("vector", lambda q: q.tensor_scalar(
                rowB[:, :], rowB[:, :], regs[0:1, 4:5], None, ALU.mult), drain=True,
                inc=ev)
            P.mark("u6rowB", tm2)
            P.mark("u6cc", (ccs, ccs.n))

    stack23 = ExitStack(); stack23.__enter__()
    ATT = stack23.enter_context(nc.sbuf_tensor("u6_ATT", [128, NCH * T], BF16))
    with ExitStack() as st:
        E = st.enter_context
        Kp = E(nc.sbuf_tensor("u6_Kp", [128, SEQ], BF16))
        Kup = E(nc.sbuf_tensor("u6_Kup", [128, SEQ], BF16))
        Qp = E(nc.sbuf_tensor("u6_Qp", [128, T], BF16))
        Qup = E(nc.sbuf_tensor("u6_Qup", [128, T], BF16))
        Kh = [E(nc.sbuf_tensor("u6_Kh0", [65, SEQ], BF16))]
        Kuh = [E(nc.sbuf_tensor("u6_Kuh0", [65, SEQ], BF16))]
        Qh = [E(nc.sbuf_tensor("u6_Qh0", [65, T], BF16))]
        Quh = [E(nc.sbuf_tensor("u6_Quh0", [65, T], BF16))]
        Vh = [E(nc.sbuf_tensor(f"u6_Vh{i}", [128, 65], BF16)) for i in range(4)]
        PT = E(nc.sbuf_tensor("u6_PT", [128, TG], BF16))
        mskt = E(nc.sbuf_tensor("u6_mskt", [128, TG], BF16))
        oscr = E(nc.sbuf_tensor("u6_oscr", [128, TG], F32))
        rden = E(nc.sbuf_tensor("u6_rden", [1, TG], F32))
        p_sc = [E(nc.psum_tensor(f"u6_psc{i}", [128, TG], F32)) for i in range(2)]
        p_den = E(nc.psum_tensor("u6_pden", [1, TG], F32))
        p_o = E(nc.psum_tensor("u6_po", [65, TG], F32))
        p_bc = E(nc.psum_tensor("u6_pbc", [128, TG], F32))

        NJ = SEQ // 128
        prev_evac = None
        prev_copies = None
        pair_tok = None
        vguard = {}
        for h in range(H):
            h2, par = h // 2, h % 2
            sl = 0
            with Ph(P, f"u6_h{h}") as ph:
                if par == 0:
                    ph.dmat("sync", Kp[:, :],
                            ex_out.ap()[:, h2 * 128:(h2 + 1) * 128], ld,
                            waits=[prev_copies, P.marks.get("u6cc")])
                    ph.dmat("sync", Kup[:, :],
                            ex_out.ap()[:, 1024 + h2 * 128:1024 + (h2 + 1) * 128], ld)
                    ph.dmat("sync", Qp[:, :],
                            q_dram.ap()[:, h2 * 128:(h2 + 1) * 128], ld)
                    ph.dmat("sync", Qup[:, :],
                            q_dram.ap()[:, 1024 + h2 * 128:1024 + (h2 + 1) * 128], ld)
                    pair_tok = ph.settle("sync", ld)
                base = 64 * par
                for (dst, srcp) in ((Kh, Kp), (Kuh, Kup)):
                    ph.op("vector", lambda q, sl=sl, base=base, dst=dst, srcp=srcp:
                          q.tensor_copy(dst[sl][0:64, :], srcp[base:base + 64, :]),
                          waits=[pair_tok, prev_evac], drain=True)
                    ph.op("vector", lambda q, sl=sl, dst=dst: q.memset(
                        dst[sl][64:65, :], 1.0), drain=True)
                for (dst, srcp) in ((Qh, Qp), (Quh, Qup)):
                    ph.op("vector", lambda q, sl=sl, base=base, dst=dst, srcp=srcp:
                          q.tensor_copy(dst[sl][0:64, :], srcp[base:base + 64, :]),
                          drain=True)
                    ph.op("vector", lambda q, sl=sl, dst=dst: q.tensor_copy(
                        dst[sl][64:65, :], rowB[:, :]), drain=True,
                        needs=("u6rowB",))
                tq2 = ph.op("vector", lambda q: None, drain=True, inc=ev)
                prev_copies = tq2
                loaded = {}
                last_exp = None
                for ig in range(4):
                    tden = None
                    texp = {}
                    for j in range(NJ):
                        psl = p_sc[j % 2]
                        mmj = ph.op("tensor", lambda q, psl=psl, sl=sl, j=j, ig=ig: q.matmul(
                            psl[:, :], Kuh[sl][:, j * 128:(j + 1) * 128],
                            Quh[sl][:, ig * TG:(ig + 1) * TG], start=True, stop=True),
                            waits=[tq2, texp.get(j - 2), last_exp], inc=et)
                        te = ph.op("scalar", lambda q, psl=psl: q.activation(
                            PT[:, :], psl[:, :], AF.Exp), waits=[mmj, tden], inc=es_)
                        texp[j] = te
                        tden = ph.op("tensor", lambda q, j=j: q.matmul(
                            p_den[:, :], C.ones_col_bf[:, :], PT[:, :],
                            start=(j == 0), stop=(j == NJ - 1)),
                            waits=[te] + ([prev_evac] if j == 0 else []), inc=et)
                    js = []
                    offs_l = []
                    for j in range(NJ):
                        offs = []
                        skip_all = True
                        for qb, hc in ((0, h0c), (T, h1c)):
                            D = qb + ig * TG - j * 128
                            if D >= 128:
                                offs.append(1152)
                                skip_all = False
                            elif D <= -TG:
                                offs.append(0)
                            else:
                                offs.append(D + 512)
                                skip_all = False
                        if not skip_all:
                            js.append(j)
                            offs_l.append(offs)
                    nexp = {}
                    tavp = None
                    for idx, j in enumerate(js):
                        if j not in loaded:
                            loaded[j] = ph.dma("sync", Vh[j % 4][:, :],
                                               ex_out.ap()[j * 128:(j + 1) * 128,
                                                           2048 + h * 65:2048 + (h + 1) * 65],
                                               ld, waits=[vguard.get(j % 4)])
                            ph.settle("sync", ld)
                        tvl = (P.gsem("e_sync"), P.gsem("e_sync").n)
                        psl = p_sc[j % 2]
                        mmj = ph.op("tensor", lambda q, psl=psl, sl=sl, j=j, ig=ig: q.matmul(
                            psl[:, :], Kh[sl][:, j * 128:(j + 1) * 128],
                            Qh[sl][:, ig * TG:(ig + 1) * TG], start=True, stop=True),
                            waits=[nexp.get(js[idx - 2] if idx >= 2 else -1),
                                   texp.get(NJ - 1), texp.get(NJ - 2)], inc=et)
                        te = ph.op("scalar", lambda q, psl=psl: q.activation(
                            PT[:, :], psl[:, :], AF.Exp), waits=[mmj, tavp], inc=es_)
                        nexp[j] = te
                        s0a, s0b = offs_l[idx]
                        if s0a == 1152 and s0b == 1152:
                            tmk = te
                        else:
                            ph.op("vector", lambda q, s0a=s0a: q.tensor_scalar(
                                mskt[:, :], cmske[:, s0a:s0a + TG], h0c[:, 0:1],
                                None, ALU.mult), waits=[te], drain=True)
                            ph.op("vector", lambda q, s0b=s0b: q.scalar_tensor_tensor(
                                mskt[:, :], cmske[:, s0b:s0b + TG], h1c[:, 0:1],
                                mskt[:, :], ALU.mult, ALU.add), drain=True)
                            tmk = ph.op("vector", lambda q: q.tensor_mul(
                                PT[:, :], PT[:, :], mskt[:, :]), waits=[te],
                                drain=True, inc=ev)
                        tavp = ph.op("tensor", lambda q, j=j, idx=idx, njs=len(js): q.matmul(
                            p_o[:, :], Vh[j % 4][:, :], PT[:, :],
                            start=(idx == 0), stop=(idx == njs - 1)),
                            waits=[tmk, tvl] + ([prev_evac] if idx == 0 else []),
                            inc=et)
                        vguard[j % 4] = tavp
                    last_exp = nexp[js[-1]]
                    trd = ph.op("vector", lambda q: q.reciprocal(
                        rden[:, :], p_den[0:1, :]), waits=[tden, tavp], drain=True,
                        inc=ev)
                    tbc = ph.op("tensor", lambda q: q.matmul(
                        p_bc[:, :], C.ones_row[:, 0:128], rden[:, :],
                        start=True, stop=True), waits=[trd], inc=et)
                    tcp = ph.op("scalar", lambda q, base=base: q.activation(
                        oscr[base:base + 64, :], p_o[0:64, :], AF.Copy),
                        waits=[tavp, prev_evac], inc=es_)
                    ch = h // 2
                    prev_evac = ph.op("vector", lambda q, ig=ig, ch=ch, base=base: q.tensor_mul(
                        ATT[base:base + 64, ch * T + ig * TG:ch * T + (ig + 1) * TG],
                        oscr[base:base + 64, :], p_bc[base:base + 64, :]),
                        waits=[tbc, tcp], drain=True, inc=ev)
        P.mark("u6att", prev_evac)

    with ExitStack() as st:
        E = st.enter_context
        Wo = E(nc.sbuf_tensor("u6_Wo", [128, NCH * 1024], BF16))
        boc = E(nc.sbuf_tensor("u6_boc", [128, NCH], F32))
        p_o2 = [E(nc.psum_tensor(f"u6_po2{i}", [128, TG], F32)) for i in range(2)]
        with Ph(P, "u6_proj") as ph:
            ph.dma("gpsimd", Wo[:, :].rearrange("p (c d) -> p c d", c=NCH),
                   a_wo.rearrange("(c p) d -> p c d", p=128), ldg)
            tlw = ph.settle("gpsimd", ldg)
            ph.dma("sync", boc[:, :], a_bo, ld)
            tlb = ph.settle("sync", ld)
            i = 0
            evt = {}
            for m in range(NCH):
                for g in range(NTG):
                    po = p_o2[i % 2]
                    a, b = g * TG, (g + 1) * TG
                    for k in range(NCH):
                        mm = ph.op("tensor", lambda q, po=po, k=k, m=m, a=a, b=b: q.matmul(
                            po[:, :], Wo[:, k * 1024 + m * 128:k * 1024 + (m + 1) * 128],
                            ATT[:, k * T + a:k * T + b],
                            start=(k == 0), stop=(k == NCH - 1)),
                            waits=[tlw, evt.get(i - 2)] if k == 0 else (),
                            needs=("u6att",),
                            inc=et if k == NCH - 1 else None)
                    evt[i] = ph.op("vector", lambda q, po=po, m=m, a=a, b=b: q.scalar_tensor_tensor(
                        C.X[:, m * T + a:m * T + b], po[:, :], boc[:, m:m + 1],
                        C.X[:, m * T + a:m * T + b], ALU.add, ALU.add),
                        waits=[mm, tlb], drain=True, inc=ev)
                    i += 1
            P.mark("X", evt[i - 1])

    stack23.__exit__(None, None, None)
    ustack.__exit__(None, None, None)
    emit_ln(P, C, "u6ln")



def emit_u_l2attn(P, C, W):
    nc = P.nc
    H, DH = 16, 64
    SC = 4097
    SROWS = 2049

    bq_c = np.ascontiguousarray(W["L2_attn_bq"].astype(np.float32).reshape(NCH, 128).T)
    bo_c = np.ascontiguousarray(W["L2_attn_bo"].astype(np.float32).reshape(NCH, 128).T)
    bp_c = np.ascontiguousarray(W["L2_attn_bp"].astype(np.float32).reshape(NCH, 128).T)
    bk_r = np.ascontiguousarray(W["L2_attn_bk"].astype(np.float32).reshape(1, 1024))
    bv_r = np.ascontiguousarray(W["L2_attn_bv"].astype(np.float32).reshape(1, 1024))
    u_c = np.ascontiguousarray(W["L2_attn_u"].astype(np.float32).reshape(H * DH).reshape(NCH, 128).T)
    v_c = np.ascontiguousarray(W["L2_attn_v"].astype(np.float32).reshape(H * DH).reshape(NCH, 128).T)
    pos_idx = np.arange(SEQ, dtype=np.float64)
    iidx = np.arange(512, dtype=np.float64)
    inv_freq = 1.0 / 10000.0 ** (2.0 * iidx / 1024.0)
    ang = pos_idx[:, None] * inv_freq[None, :]
    embT = np.ascontiguousarray(
        np.concatenate([np.sin(ang), np.cos(ang)], axis=1).T.astype(np.float32))
    hsel = np.zeros((128, 2), np.float32)
    hsel[0:64, 0] = 1.0
    hsel[64:128, 1] = 1.0

    a_wq = P.in_("u4_Wq", np.ascontiguousarray(W["L2_attn_Wq"].astype(np.float32)))
    a_wk = P.in_("u4_Wk", np.ascontiguousarray(W["L2_attn_Wk"].astype(np.float32)))
    a_wv = P.in_("u4_Wv", np.ascontiguousarray(W["L2_attn_Wv"].astype(np.float32)))
    a_wp = P.in_("u4_Wp", np.ascontiguousarray(W["L2_attn_Wp"].astype(np.float32)))
    a_wo = P.in_("u4_Wo", np.ascontiguousarray(W["L2_attn_Wo"].astype(np.float32)))
    a_bq = P.in_("u4_bq", bq_c)
    a_bo = P.in_("u4_bo", bo_c)
    a_bp = P.in_("u4_bp", bp_c)
    a_bk = P.in_("u4_bk", bk_r)
    a_bv = P.in_("u4_bv", bv_r)
    a_u = P.in_("u4_u", u_c)
    a_v = P.in_("u4_v", v_c)
    a_emb = P.in_("u4_embT", embT)
    a_hsel = P.in_("u4_hsel", hsel)
    a_h0 = P.in_("u4_h0", [np.full((128, 1), float(1 - c % 2), np.float32)
                           for c in range(NCORES)])
    a_h1 = P.in_("u4_h1", [np.full((128, 1), float(c % 2), np.float32)
                           for c in range(NCORES)])

    qu_dram = P.dram("u4_qu", [1024, T], BF16)
    qv_dram = P.dram("u4_qv", [1024, T], BF16)
    pos_dram = P.dram("u4_pos", [1024, SEQ], BF16)
    scr_dram_a = P.dram("u4_scra", [8, SROWS * SC], BF16)
    scr_dram_b = P.dram("u4_scrb", [8, SROWS * SC], BF16)
    ex_in = P.dram("u4_exin", [T, 1024 + H * 65], BF16)
    ex_out = P.dram("u4_exout", [SEQ, 1024 + H * 65], BF16)
    gm_in = P.dram("u4_gmin", [1, 1], F32)
    gm_out = P.dram("u4_gmout", [1, 1], F32)
    xh_in = P.dram("u4_xhin", [128, 8], F32)
    xh_out = P.dram("u4_xhout", [256, 8], F32)

    ld = P.gsem("dma_sync")
    ldg = P.gsem("dma_gp")
    std = P.gsem("dma_st")
    ccs = P.gsem("cc")
    ev = P.gsem("ev")
    et = P.gsem("et")
    es_ = P.gsem("es")

    ustack = ExitStack(); ustack.__enter__(); UE = ustack.enter_context
    regs = UE(nc.sbuf_tensor("u4_regs", [1, 12], F32))
    rowB = UE(nc.sbuf_tensor("u4_rowB", [1, T], F32))
    hselb = UE(nc.sbuf_tensor("u4_hselb", [128, 2], F32))
    nacc = UE(nc.sbuf_tensor("u4_nacc", [128, 6], F32))
    nq2 = UE(nc.sbuf_tensor("u4_nq2", [2, 4], F32))
    arow = UE(nc.sbuf_tensor("u4_arow", [1, 128], F32))
    biasc = UE(nc.sbuf_tensor("u4_biasc", [128, 48], F32))
    xhb = UE(nc.sbuf_tensor("u4_xhb", [128, 8], BF16))
    qvh = UE(nc.sbuf_tensor("u4_qvh", [128, 8], BF16))
    h0c = UE(nc.sbuf_tensor("u4_h0c", [128, 1], F32))
    h1c = UE(nc.sbuf_tensor("u4_h1c", [128, 1], F32))

    with ExitStack() as st0:
        xh = st0.enter_context(nc.sbuf_tensor("u4_xh", [128, 8], F32))
        with Ph(P, "u4_ld") as ph:
            ph.dma("sync", biasc[:, 0:8], a_bq, ld)
            ph.dma("sync", biasc[:, 8:16], a_u, ld)
            ph.dma("sync", biasc[:, 16:24], a_v, ld)
            ph.dma("sync", biasc[:, 24:32], a_bp, ld)
            ph.dma("sync", biasc[:, 32:40], a_bo, ld)
            ph.dma("sync", hselb[:, :], a_hsel, ld)
            ph.dma("sync", h0c[:, :], a_h0, ld)
            ph.dma("sync", h1c[:, :], a_h1, ld)
            tbl = ph.settle("sync", ld)
            P.mark("u4bias", tbl)
            ph.op("vector", lambda q: q.memset(nacc[:, :], 0.0))
            tz = ph.op("vector", lambda q: q.memset(nq2[:, :], 0.0), drain=True, inc=ev)
            P.mark("u4nacc0", tz)
            xv = C.X[:, :].rearrange("p (c t) -> p c t", c=NCH)
            ph.dma("sync", xh[:, 0:8].rearrange("p (c u) -> p c u", u=1),
                   xv[:, :, 0:1], ld, needs=("X",), allow_slow_non_contiguous=True)
            tse = ph.settle("sync", ld)
            ph.dma("sync", xh_in.ap(), xh[:, :], ld, waits=[tse])
            tse2 = ph.settle("sync", ld)
            tcc0 = ph.op("gpsimd", lambda q: q.collective_compute(
                "AllGather", ALU.bypass, replica_groups=PAIRS,
                ins=[xh_in.ap().opt()], outs=[xh_out.ap().opt()]), waits=[tse2],
                inc=ccs)
            ph.dma("sync", xh[:, :], xh_out.ap()[128:256, :], ld, waits=[tcc0])
            tse3 = ph.settle("sync", ld)
            tmh = ph.op("vector", lambda q: q.tensor_scalar(
                xh[:, :], xh[:, :], h0c[:, 0:1], None, ALU.mult), waits=[tse3, tbl],
                drain=True)
            tmc = ph.op("vector", lambda q: q.tensor_copy(xhb[:, :], xh[:, :]),
                        drain=True, inc=ev)
            P.mark("u4xh", tmc)

    with ExitStack() as st:
        E = st.enter_context
        Wb = E(nc.sbuf_tensor("u4_Wb", [128, NCH * 1024], BF16))
        scrA = E(nc.sbuf_tensor("u4_scrA", [128, TG], F32))
        scrB = E(nc.sbuf_tensor("u4_scrB", [128, TG], F32))
        outb = E(nc.sbuf_tensor("u4_outb", [128, TG], BF16))
        p_a = [E(nc.psum_tensor(f"u4_pa{i}", [128, TG], F32)) for i in range(2)]
        p_n = E(nc.psum_tensor("u4_pn", [2, TG], F32))
        p_hh = E(nc.psum_tensor("u4_phh", [128, 1], F32))

        with Ph(P, "u4_qw") as ph:
            ph.dma("gpsimd", Wb[:, :].rearrange("p (c d) -> p c d", c=NCH),
                   a_wq.rearrange("(c p) d -> p c d", p=128), ldg)
            tlw = ph.settle("gpsimd", ldg)
        prev_ev = None
        for m in range(NCH):
            with Ph(P, f"u4_q{m}") as ph:
                for g in range(NTG):
                    a, b = g * TG, (g + 1) * TG
                    po = p_a[g % 2]
                    for k in range(NCH):
                        mm = ph.op("tensor", lambda q, po=po, k=k, m=m, a=a, b=b: q.matmul(
                            po[:, :], Wb[:, k * 1024 + m * 128:k * 1024 + (m + 1) * 128],
                            C.Xb[:, k * T + a:k * T + b],
                            start=(k == 0), stop=(k == NCH - 1)),
                            waits=[tlw, prev_ev, P.marks.get("Xb")] if k == 0 else (),
                            inc=et if k == NCH - 1 else None)
                    t1 = ph.op("vector", lambda q, po=po, m=m: q.tensor_scalar(
                        scrA[:, :], po[:, :], biasc[:, m:m + 1], None, ALU.add),
                        waits=[mm], needs=("u4bias",), drain=True)
                    tss1 = ph.settle("sync", std)
                    t2 = ph.op("vector", lambda q, m=m: q.tensor_scalar(
                        outb[:, :], scrA[:, :], biasc[:, 8 + m:9 + m], 0.125,
                        ALU.add, ALU.mult), waits=[tss1], drain=True, inc=ev)
                    ph.dma("sync", qu_dram.ap()[m * 128:(m + 1) * 128, a:b],
                           outb[:, :], std, waits=[t2])
                    tss2 = ph.settle("sync", std)
                    tsq = ph.op("scalar", lambda q: q.activation(
                        scrB[:, :], outb[:, :], AF.Square), waits=[t2], inc=es_)
                    tnm = ph.op("tensor", lambda q: q.matmul(
                        p_n[:, :], hselb[:, :], scrB[:, :], start=True, stop=True),
                        waits=[tsq], inc=et)
                    tnr = ph.op("vector", lambda q: q.tensor_reduce(
                        nq2[:, 2:3], p_n[0:2, :], AX.X, ALU.max), waits=[tnm],
                        needs=("u4nacc0",), drain=True)
                    tna = ph.op("vector", lambda q: q.tensor_max(
                        nq2[:, 0:1], nq2[:, 0:1], nq2[:, 2:3]), drain=True, inc=ev)
                    t4 = ph.op("vector", lambda q, m=m: q.tensor_scalar(
                        outb[:, :], scrA[:, :], biasc[:, 16 + m:17 + m], 0.125,
                        ALU.add, ALU.mult), waits=[tna, tss2], drain=True, inc=ev)
                    ph.dma("sync", qv_dram.ap()[m * 128:(m + 1) * 128, a:b],
                           outb[:, :], std, waits=[t4])
                    tsq2 = ph.op("scalar", lambda q: q.activation(
                        scrB[:, :], outb[:, :], AF.Square), waits=[t4, tnr], inc=es_)
                    tnm2 = ph.op("tensor", lambda q: q.matmul(
                        p_n[:, :], hselb[:, :], scrB[:, :], start=True, stop=True),
                        waits=[tsq2], inc=et)
                    tnr2 = ph.op("vector", lambda q: q.tensor_reduce(
                        nq2[:, 2:3], p_n[0:2, :], AX.X, ALU.max), waits=[tnm2],
                        drain=True)
                    prev_ev = ph.op("vector", lambda q: q.tensor_max(
                        nq2[:, 1:2], nq2[:, 1:2], nq2[:, 2:3]), drain=True, inc=ev)
        with Ph(P, "u4_qh") as ph:
            thq = None
            for m in range(NCH):
                for k in range(NCH):
                    mmh = ph.op("tensor", lambda q, m=m, k=k: q.matmul(
                        p_hh[:, :], Wb[:, k * 1024 + m * 128:k * 1024 + (m + 1) * 128],
                        xhb[:, k:k + 1],
                        start=(k == 0), stop=(k == NCH - 1)),
                        waits=[P.marks.get("u4xh"), thq] if k == 0 else (),
                        inc=et if k == NCH - 1 else None)
                th1 = ph.op("vector", lambda q, m=m: q.tensor_scalar(
                    scrA[:, 0:1], p_hh[:, :], biasc[:, m:m + 1], None, ALU.add),
                    waits=[mmh], drain=True)
                thq = ph.op("vector", lambda q, m=m: q.tensor_scalar(
                    qvh[:, m:m + 1], scrA[:, 0:1], biasc[:, 16 + m:17 + m], 0.125,
                    ALU.add, ALU.mult), drain=True, inc=ev)
            P.mark("u4qvh", thq)
        with Ph(P, "u4_qsettle") as ph:
            ph.settle("sync", std)

    with ExitStack() as st:
        E = st.enter_context
        Wkb = E(nc.sbuf_tensor("u4_Wkb", [128, NCH * 1024], BF16))
        Wvb = E(nc.sbuf_tensor("u4_Wvb", [128, NCH * 1024], BF16))
        bkb = E(nc.sbuf_tensor("u4_bkb", [1, 1024], BF16))
        bvb = E(nc.sbuf_tensor("u4_bvb", [1, 1024], BF16))
        ktile = E(nc.sbuf_tensor("u4_ktile", [128, 1024], BF16))
        vt = E(nc.sbuf_tensor("u4_vt", [128, H * 65], BF16))
        sq = E(nc.sbuf_tensor("u4_sq", [128, 1024], F32))
        nrm = E(nc.sbuf_tensor("u4_nrm", [128, 16], F32))
        p_a = [E(nc.psum_tensor(f"u4_pk{i}", [128, TG], F32)) for i in range(2)]

        with Ph(P, "u4_kw") as ph:
            ph.dma("gpsimd", Wkb[:, :].rearrange("p (c d) -> p c d", c=NCH),
                   a_wk.rearrange("(c p) d -> p c d", p=128), ldg)
            ph.dma("gpsimd", Wvb[:, :].rearrange("p (c d) -> p c d", c=NCH),
                   a_wv.rearrange("(c p) d -> p c d", p=128), ldg)
            ph.dma("gpsimd", bkb[:, :], a_bk, ldg)
            ph.dma("gpsimd", bvb[:, :], a_bv, ldg)
            tlw2 = ph.settle("gpsimd", ldg)
        for tt in range(T // 128):
            t0 = tt * 128
            with Ph(P, f"u4_kv{tt}") as ph:
                evk = {}
                for cg in range(2):
                    po = p_a[cg % 2]
                    ph.op("tensor", lambda q, po=po, cg=cg: q.matmul(
                        po[:, :], C.ones_row_bf[:, 0:128], bkb[:, cg * TG:(cg + 1) * TG],
                        start=True, stop=False),
                        waits=[tlw2, evk.get(cg - 2)], needs=("constsg",))
                    for k in range(NCH):
                        mm = ph.op("tensor", lambda q, po=po, k=k, cg=cg, t0=t0: q.matmul(
                            po[:, :], C.Xb[:, k * T + t0:k * T + t0 + 128],
                            Wkb[:, k * 1024 + cg * TG:k * 1024 + (cg + 1) * TG],
                            start=False, stop=(k == NCH - 1)),
                            inc=et if k == NCH - 1 else None)
                    evk[cg] = ph.op("vector", lambda q, po=po, cg=cg: q.tensor_copy(
                        ktile[:, cg * TG:(cg + 1) * TG], po[:, :]), waits=[mm],
                        drain=True, inc=ev)
                tdk = ph.dma("sync", ex_in.ap()[t0:t0 + 128, 0:1024], ktile[:, :], std,
                             waits=[evk[1]])
                tks = ph.op("scalar", lambda q: q.activation(
                    sq[:, :], ktile[:, :], AF.Square), waits=[evk[1]], inc=es_)
                ph.op("vector", lambda q: q.tensor_reduce(
                    nrm[:, :].rearrange("p (h u) -> p h u", u=1),
                    sq[:, :].rearrange("p (h d) -> p h d", d=64),
                    AX.X, ALU.add), waits=[tks], drain=True)
                ph.op("vector", lambda q: q.tensor_reduce(
                    nacc[:, 4:5], nrm[:, :], AX.X, ALU.max), drain=True)
                ph.op("vector", lambda q: q.tensor_max(
                    nacc[:, 2:3], nacc[:, 2:3], nacc[:, 4:5]), drain=True, inc=ev)
                for cg in range(2):
                    po = p_a[cg % 2]
                    ph.op("tensor", lambda q, po=po, cg=cg: q.matmul(
                        po[:, :], C.ones_row_bf[:, 0:128], bvb[:, cg * TG:(cg + 1) * TG],
                        start=True, stop=False), waits=[evk[cg]])
                    for k in range(NCH):
                        mm = ph.op("tensor", lambda q, po=po, k=k, cg=cg, t0=t0: q.matmul(
                            po[:, :], C.Xb[:, k * T + t0:k * T + t0 + 128],
                            Wvb[:, k * 1024 + cg * TG:k * 1024 + (cg + 1) * TG],
                            start=False, stop=(k == NCH - 1)),
                            inc=et if k == NCH - 1 else None)
                    evk[cg] = ph.op("vector", lambda q, po=po, cg=cg: q.tensor_copy(
                        ktile[:, cg * TG:(cg + 1) * TG], po[:, :]), waits=[mm, tdk],
                        drain=True, inc=ev)
                tv1 = ph.op("vector", lambda q: q.tensor_copy(
                    vt[:, :].rearrange("p (h j) -> p h j", j=65)[:, :, 0:64],
                    ktile[:, :].rearrange("p (h j) -> p h j", j=64)),
                    waits=[evk[1]], drain=True)
                tv2 = ph.op("vector", lambda q: q.memset(
                    vt[:, :].rearrange("p (h j) -> p h j", j=65)[:, :, 64:65], 1.0),
                    drain=True, inc=ev)
                ph.dma("sync", ex_in.ap()[t0:t0 + 128, 1024:], vt[:, :], std,
                       waits=[tv2])

    with ExitStack() as st:
        E = st.enter_context
        Wpb = E(nc.sbuf_tensor("u4_Wpb2", [128, NCH * 1024], BF16))
        embb = E(nc.sbuf_tensor("u4_embb2", [128, NCH * TG], BF16))
        outp = E(nc.sbuf_tensor("u4_outp2", [128, TG], BF16))
        sqp = E(nc.sbuf_tensor("u4_sqp2", [128, TG], F32))
        p_a = [E(nc.psum_tensor(f"u4_pq2{i}", [128, TG], F32)) for i in range(2)]
        p_n = E(nc.psum_tensor("u4_pn3", [2, TG], F32))

        with Ph(P, "u4_pw2") as ph:
            ph.dma("gpsimd", Wpb[:, :].rearrange("p (c d) -> p c d", c=NCH),
                   a_wp.rearrange("(c p) d -> p c d", p=128), ldg)
            tlw3 = ph.settle("gpsimd", ldg)
        prevp = None
        for mg in range(SEQ // TG):
            with Ph(P, f"u4_pos{mg}") as ph:
                ph.dma("gpsimd", embb[:, :].rearrange("p (c t) -> p c t", c=NCH),
                       a_emb[:, mg * TG:(mg + 1) * TG].rearrange(
                           "(c p) t -> p c t", p=128), ldg, waits=[prevp])
                tle = ph.settle("gpsimd", ldg)
                evp = {}
                for m in range(NCH):
                    po = p_a[m % 2]
                    for k in range(NCH):
                        mm = ph.op("tensor", lambda q, po=po, k=k, m=m: q.matmul(
                            po[:, :], Wpb[:, k * 1024 + m * 128:k * 1024 + (m + 1) * 128],
                            embb[:, k * TG:(k + 1) * TG],
                            start=(k == 0), stop=(k == NCH - 1)),
                            waits=[tlw3, tle, evp.get(m - 2)] if k == 0 else (),
                            inc=et if k == NCH - 1 else None)
                    tssp = ph.settle("sync", std)
                    t1 = ph.op("vector", lambda q, po=po, m=m: q.tensor_scalar(
                        outp[:, :], po[:, :], biasc[:, 24 + m:25 + m], None, ALU.add),
                        waits=[mm, tssp], drain=True, inc=ev)
                    tds = ph.dma("sync", pos_dram.ap()[m * 128:(m + 1) * 128,
                                                       mg * TG:(mg + 1) * TG],
                                 outp[:, :], std, waits=[t1])
                    tps = ph.op("scalar", lambda q: q.activation(
                        sqp[:, :], outp[:, :], AF.Square), waits=[t1], inc=es_)
                    tpm = ph.op("tensor", lambda q: q.matmul(
                        p_n[:, :], hselb[:, :], sqp[:, :], start=True, stop=True),
                        waits=[tps], inc=et)
                    tpr = ph.op("vector", lambda q: q.tensor_reduce(
                        nq2[:, 2:3], p_n[0:2, :], AX.X, ALU.max), waits=[tpm],
                        drain=True)
                    evp[m] = ph.op("vector", lambda q: q.tensor_max(
                        nq2[:, 3:4], nq2[:, 3:4], nq2[:, 2:3]), drain=True, inc=ev)
                prevp = evp[NCH - 1]

    with ExitStack() as st:
        E = st.enter_context
        p_s = E(nc.psum_tensor("u4_psx", [1, 128], F32))
        with Ph(P, "u4_ex") as ph:
            tst = ph.settle("sync", std)
            tt1 = ph.op("tensor", lambda q: q.matmul(
                p_s[0:1, :], nacc[:, 2:3], C.ident[:, :], start=True, stop=True),
                waits=[ev.tok()], inc=et, needs=("consts",))
            tr1 = ph.op("vector", lambda q: q.tensor_copy(arow[:, :], p_s[0:1, :]),
                        waits=[tt1], drain=True, inc=ev)
            tr2 = ph.op("vector", lambda q: q.reduce_max(
                regs[:, 2:3], arow[:, :], AX.X), drain=True, inc=ev)
            ph.dma("sync", gm_in.ap(), regs[:, 2:3], ld, waits=[tr2])
            tst2 = ph.settle("sync", ld)
            tcc = ph.op("gpsimd", lambda q: q.collective_compute(
                "AllReduce", ALU.max, replica_groups=PAIRS,
                ins=[gm_in.ap().opt()], outs=[gm_out.ap().opt()]), waits=[tst2, tst],
                inc=ccs)
            tcc2 = ph.op("gpsimd", lambda q: q.collective_compute(
                "AllGather", ALU.bypass, replica_groups=PAIRS,
                ins=[ex_in.ap().opt()], outs=[ex_out.ap().opt()]), inc=ccs)
            ph.dma("sync", regs[:, 2:3], gm_out.ap(), ld, waits=[tcc])
            tst3 = ph.settle("sync", ld)
            tprev = tr1
            for col in (0, 1, 3):
                tt2 = ph.op("tensor", lambda q, col=col: q.matmul(
                    p_s[0:1, 0:2], nq2[0:2, col:col + 1], C.ident[0:2, 0:2],
                    start=True, stop=True), waits=[tprev], inc=et)
                tprev = ph.op("vector", lambda q: q.tensor_copy(
                    arow[:, 0:2], p_s[0:1, 0:2]), waits=[tt2], drain=True, inc=ev)
                ph.op("vector", lambda q, col=col: q.reduce_max(
                    regs[:, col:col + 1], arow[:, 0:2], AX.X), drain=True, inc=ev)
            tb1 = ph.op("vector", lambda q: q.tensor_mul(
                regs[:, 4:5], regs[:, 0:1], regs[:, 2:3]), waits=[tst3], drain=True)
            tb2 = ph.op("vector", lambda q: q.tensor_mul(
                regs[:, 5:6], regs[:, 1:2], regs[:, 3:4]), drain=True, inc=ev)
            tb3 = ph.op("scalar", lambda q: q.activation(
                regs[:, 6:7], regs[:, 4:5], AF.Sqrt), waits=[tb2], inc=es_)
            tb4 = ph.op("scalar", lambda q: q.activation(
                regs[:, 7:8], regs[:, 5:6], AF.Sqrt), drain=True, inc=es_)
            tb5 = ph.op("vector", lambda q: q.tensor_add(
                regs[:, 8:9], regs[:, 6:7], regs[:, 7:8]), waits=[tb4], drain=True)
            tb6 = ph.op("vector", lambda q: q.tensor_scalar(
                regs[:, 8:9], regs[:, 8:9], -1.0, None, ALU.mult), drain=True, inc=ev)
            ph.op("vector", lambda q: q.memset(rowB[:, :], 1.0), drain=True)
            tm2 = ph.op("vector", lambda q: q.tensor_scalar(
                rowB[:, :], rowB[:, :], regs[0:1, 8:9], None, ALU.mult), drain=True,
                inc=ev)
            P.mark("u4rowB", tm2)
            P.mark("u4cc", (ccs, ccs.n))

    with ExitStack() as st:
        E = st.enter_context
        qvb = E(nc.sbuf_tensor("u4_qvb", [128, T], BF16))
        posb = E(nc.sbuf_tensor("u4_posb", [128, SEQ], BF16))
        rawt = E(nc.sbuf_tensor("u4_rawt", [128, SC + 1], BF16))
        rawh = E(nc.sbuf_tensor("u4_rawh", [1, SC + 1], BF16))
        p_r = [E(nc.psum_tensor(f"u4_pr{i}", [128, TG], F32)) for i in range(2)]
        p_r1 = E(nc.psum_tensor("u4_prh", [1, TG], F32))

        prevh = None
        tlq = None
        for h in range(H):
            hb = 64 * (h % 2)
            with Ph(P, f"u4_raw{h}") as ph:
                if h % 2 == 0:
                    ph.dma("sync", qvb[:, :],
                           qv_dram.ap()[(h // 2) * 128:(h // 2 + 1) * 128, :], ld,
                           waits=[prevh])
                    ph.dma("sync", posb[:, :],
                           pos_dram.ap()[(h // 2) * 128:(h // 2 + 1) * 128, :], ld)
                    tlq = ph.settle("sync", ld)
                tz = ph.op("vector", lambda q: q.memset(rawt[:, 0:1], 0.0), inc=ev) \
                    if h == 0 else None
                tz2 = ph.op("vector", lambda q: q.memset(rawh[:, 0:1], 0.0), inc=ev) \
                    if h == 0 else None
                for qt in range(16):
                    evr = {}
                    for mg in range(NCH):
                        po = p_r[mg % 2]
                        mm = ph.op("tensor", lambda q, po=po, qt=qt, mg=mg, hb=hb: q.matmul(
                            po[:, :], qvb[hb:hb + 64, qt * 128:(qt + 1) * 128],
                            posb[hb:hb + 64, mg * TG:(mg + 1) * TG],
                            start=True, stop=True),
                            waits=[tlq, evr.get(mg - 2), prevh], inc=et)
                        evr[mg] = ph.op("vector", lambda q, po=po, mg=mg: q.tensor_copy(
                            rawt[:, 1 + mg * TG:1 + (mg + 1) * TG], po[:, :]),
                            waits=[mm], drain=True, inc=ev)
                    prevh = ph.dma("sync",
                                   bass.AP(scr_dram_a if h < 8 else scr_dram_b,
                                           (h % 8) * (SROWS * SC) + qt * 128 * SC,
                                           [[SC, 128], [1, SC]]),
                                   rawt[:, 0:SC], std, waits=[evr[NCH - 1]])
                evh = None
                for mg in range(NCH):
                    mmh = ph.op("tensor", lambda q, h=h, hb=hb, mg=mg: q.matmul(
                        p_r1[:, :], qvh[hb:hb + 64, h // 2:h // 2 + 1],
                        posb[hb:hb + 64, mg * TG:(mg + 1) * TG],
                        start=True, stop=True),
                        waits=[P.marks.get("u4qvh"), evh], inc=et)
                    evh = ph.op("vector", lambda q, mg=mg: q.tensor_copy(
                        rawh[:, 1 + mg * TG:1 + (mg + 1) * TG], p_r1[0:1, :]),
                        waits=[mmh], drain=True, inc=ev)
                prevh = ph.dma("sync",
                               bass.AP(scr_dram_a if h < 8 else scr_dram_b,
                                       (h % 8) * (SROWS * SC) + 2048 * SC,
                                       [[SC, 1], [1, SC]]),
                               rawh[:, 0:SC], std, waits=[evh])
        P.mark("u4scr", (std, std.n))

    stack23 = ExitStack(); stack23.__enter__()
    ATT = stack23.enter_context(nc.sbuf_tensor("u4_ATT", [128, NCH * T], BF16))
    with ExitStack() as st:
        E = st.enter_context
        Kp = E(nc.sbuf_tensor("u4_Kp", [128, SEQ], BF16))
        Kh = E(nc.sbuf_tensor("u4_Kh", [65, SEQ], BF16))
        Qh = E(nc.sbuf_tensor("u4_Qh", [65, T], BF16))
        Ih0 = E(nc.sbuf_tensor("u4_Ih0", [128, 128], BF16))
        Ih1 = E(nc.sbuf_tensor("u4_Ih1", [128, 128], BF16))
        pts = [E(nc.sbuf_tensor(f"u4_pts{i}", [128, T], BF16)) for i in range(4)]
        PT = E(nc.sbuf_tensor("u4_PT", [128, T], BF16))
        Vh = [E(nc.sbuf_tensor(f"u4_Vh{i}", [128, 65], BF16)) for i in range(4)]
        oscr = E(nc.sbuf_tensor("u4_oscr", [128, TG], F32))
        rden = E(nc.sbuf_tensor("u4_rden", [1, TG], F32))
        p_sc = E(nc.psum_tensor("u4_psc", [128, T], F32))
        p_o = [E(nc.psum_tensor(f"u4_po{i}", [65, TG], F32)) for i in range(4)]

        with Ph(P, "u4_ih") as ph:
            t1 = ph.op("vector", lambda q: q.tensor_scalar(
                Ih0[:, :], C.ident_bf[:, :], h0c[:, 0:1], None, ALU.mult),
                needs=("constsg", "u4bias"), drain=True)
            t2 = ph.op("vector", lambda q: q.tensor_scalar(
                Ih1[:, :], C.ident_bf[:, :], h1c[:, 0:1], None, ALU.mult),
                drain=True, inc=ev)
            P.mark("u4ih", t2)

        NJ = SEQ // 128
        prev_evac = None
        prev_copies = None
        vguard = {}
        ptguard = {}
        for h in range(H):
            with Ph(P, f"u4_h{h}") as ph:
                if h % 2 == 0:
                    ph.dmat("sync", Kp[:, :],
                            ex_out.ap()[:, (h // 2) * 128:(h // 2 + 1) * 128], ld,
                            waits=[prev_copies, P.marks.get("u4cc")])
                    prev_pair = ph.settle("sync", ld)
                base = 64 * (h % 2)
                tk1 = ph.op("vector", lambda q, base=base: q.tensor_copy(
                    Kh[0:64, :], Kp[base:base + 64, :]),
                    waits=[prev_pair, prev_evac], drain=True)
                ph.op("vector", lambda q: q.memset(Kh[64:65, :], 1.0), drain=True)
                tq0 = ph.dma("sync", Qh[0:64, :].rearrange("p t -> p t"),
                             qu_dram.ap()[h * 64:(h + 1) * 64, :], ld,
                             waits=[prev_evac])
                tq0s = ph.settle("sync", ld)
                tq2 = ph.op("vector", lambda q: q.tensor_copy(
                    Qh[64:65, :], rowB[:, :]), waits=[tq0s], drain=True, inc=ev,
                    needs=("u4rowB",))
                prev_copies = tq2
                texp = {}
                tav = {}
                for j in range(NJ):
                    qtr = j % 4
                    pA = pts[(j % 2) * 2]
                    pB = pts[(j % 2) * 2 + 1]
                    scrh = scr_dram_a if h < 8 else scr_dram_b
                    tpa = ph.dmat("sync", pA[:, :],
                                  bass.AP(scrh, (h % 8) * (SROWS * SC) + 4096 + j * 128,
                                          [[4096, 2048], [1, 128]]), ld,
                                  waits=[ptguard.get((j % 2, 0))])
                    tpb = ph.dmat("sync", pB[:, :],
                                  bass.AP(scrh, (h % 8) * (SROWS * SC) + 2048 + j * 128,
                                          [[4096, 2048], [1, 128]]), ld,
                                  waits=[ptguard.get((j % 2, 1))])
                    tps = ph.settle("sync", ld)
                    vload = ph.dma("sync", Vh[j % 4][:, :],
                                   ex_out.ap()[j * 128:(j + 1) * 128,
                                               1024 + h * 65:1024 + (h + 1) * 65],
                                   ld, waits=[vguard.get(j % 4)])
                    tvl = ph.settle("sync", ld)
                    for ig in range(4):
                        ph.op("tensor", lambda q, j=j, ig=ig: q.matmul(
                            p_sc[:, ig * TG:(ig + 1) * TG],
                            Kh[:, j * 128:(j + 1) * 128],
                            Qh[:, ig * TG:(ig + 1) * TG],
                            start=True, stop=False),
                            waits=[tq2, texp.get(j - 1), P.marks.get("u4scr"),
                                   P.marks.get("u4ih")] if ig == 0 else ())
                        ph.op("tensor", lambda q, pA=pA, ig=ig: q.matmul(
                            p_sc[:, ig * TG:(ig + 1) * TG], Ih0[:, :],
                            pA[:, ig * TG:(ig + 1) * TG], start=False, stop=False),
                            waits=[tps] if ig == 0 else ())
                        mm = ph.op("tensor", lambda q, pB=pB, ig=ig: q.matmul(
                            p_sc[:, ig * TG:(ig + 1) * TG], Ih1[:, :],
                            pB[:, ig * TG:(ig + 1) * TG], start=False, stop=True),
                            inc=et if ig == 3 else None)
                    ptguard[(j % 2, 0)] = mm
                    ptguard[(j % 2, 1)] = mm
                    te = ph.op("scalar", lambda q: q.activation(
                        PT[:, :], p_sc[:, :], AF.Exp),
                        waits=[mm, tav.get(j - 1)], inc=es_)
                    texp[j] = te
                    for ig in range(4):
                        av = ph.op("tensor", lambda q, j=j, ig=ig: q.matmul(
                            p_o[ig][:, :], Vh[j % 4][:, :], PT[:, ig * TG:(ig + 1) * TG],
                            start=(j == 0), stop=(j == NJ - 1)),
                            waits=[te, tvl] if ig == 0 else (),
                            inc=et if ig == 3 else None)
                    tav[j] = av
                    vguard[j % 4] = av
                ch = h // 2
                tbc = None
                for ig in range(4):
                    trd = ph.op("vector", lambda q, ig=ig: q.reciprocal(
                        rden[:, :], p_o[ig][64:65, :]), waits=[tav[NJ - 1], tbc],
                        drain=True, inc=ev)
                    tbc = ph.op("tensor", lambda q, ig=ig: q.matmul(
                        p_sc[:, 0:TG], C.ones_row[:, 0:128], rden[:, :],
                        start=True, stop=True), waits=[trd, texp[NJ - 1]], inc=et)
                    tcp = ph.op("scalar", lambda q, ig=ig, base=base: q.activation(
                        oscr[base:base + 64, :], p_o[ig][0:64, :], AF.Copy),
                        waits=[tav[NJ - 1], prev_evac], inc=es_)
                    prev_evac = ph.op("vector", lambda q, ig=ig, ch=ch, base=base: q.tensor_mul(
                        ATT[base:base + 64, ch * T + ig * TG:ch * T + (ig + 1) * TG],
                        oscr[base:base + 64, :], p_sc[base:base + 64, 0:TG]),
                        waits=[tbc, tcp], drain=True, inc=ev)
        P.mark("u4att", prev_evac)

    with ExitStack() as st:
        E = st.enter_context
        Wo = E(nc.sbuf_tensor("u4_Wob", [128, NCH * 1024], BF16))
        p_o2 = [E(nc.psum_tensor(f"u4_po2{i}", [128, TG], F32)) for i in range(2)]
        with Ph(P, "u4_proj") as ph:
            ph.dma("gpsimd", Wo[:, :].rearrange("p (c d) -> p c d", c=NCH),
                   a_wo.rearrange("(c p) d -> p c d", p=128), ldg)
            tlw = ph.settle("gpsimd", ldg)
            i = 0
            evt = {}
            for m in range(NCH):
                for g in range(NTG):
                    po = p_o2[i % 2]
                    a, b = g * TG, (g + 1) * TG
                    for k in range(NCH):
                        mm = ph.op("tensor", lambda q, po=po, k=k, m=m, a=a, b=b: q.matmul(
                            po[:, :], Wo[:, k * 1024 + m * 128:k * 1024 + (m + 1) * 128],
                            ATT[:, k * T + a:k * T + b],
                            start=(k == 0), stop=(k == NCH - 1)),
                            waits=[tlw, evt.get(i - 2)] if k == 0 else (),
                            needs=("u4att",),
                            inc=et if k == NCH - 1 else None)
                    evt[i] = ph.op("vector", lambda q, po=po, m=m, a=a, b=b: q.scalar_tensor_tensor(
                        C.X[:, m * T + a:m * T + b], po[:, :], biasc[:, 32 + m:33 + m],
                        C.X[:, m * T + a:m * T + b], ALU.add, ALU.add),
                        waits=[mm], drain=True, inc=ev)
                    i += 1
            P.mark("X", evt[i - 1])

    stack23.__exit__(None, None, None)
    ustack.__exit__(None, None, None)
    emit_ln(P, C, "u4ln")




def emit_u_l0attn(P, C, W):
    nc = P.nc
    H = 4
    HALO = 64
    TH = HALO + T
    VW = 32 + T
    NGRP = T // 128

    a_wq = P.in_("u0_Wq", np.asarray(W["L0_attn_W_q"], np.float32))
    a_wk = P.in_("u0_Wk", np.asarray(W["L0_attn_W_k"], np.float32))
    a_wv = P.in_("u0_Wv", np.asarray(W["L0_attn_W_v"], np.float32))
    a_wb = P.in_("u0_Wb", np.asarray(W["L0_attn_W_b"], np.float32))
    a_wf1 = P.in_("u0_Wf1", np.asarray(W["L0_attn_W_fuse1"], np.float32))
    a_wf2 = P.in_("u0_Wf2", np.asarray(W["L0_attn_W_fuse2"], np.float32))
    a_wo = P.in_("u0_Wo", np.asarray(W["L0_attn_W_o"], np.float32))
    cw3 = [np.asarray(W[f"L0_attn_conv_w_{nm}"], np.float32)
           .reshape(NCH, 128, 4).transpose(1, 0, 2).reshape(128, NCH * 4)
           for nm in ("q", "k", "v")]
    a_cw = P.in_("u0_cw", np.concatenate(cw3, axis=1))
    fw_s = np.asarray(W["L0_attn_fir_short_w"], np.float32).reshape(1024, 3)
    fw_l = np.asarray(W["L0_attn_fir_long_w"], np.float32).reshape(1024, 31)
    a_fir = P.in_("u0_fir", np.concatenate(
        [fw_s.reshape(NCH, 128, 3).transpose(1, 0, 2).reshape(128, NCH * 3),
         fw_l.reshape(NCH, 128, 31).transpose(1, 0, 2).reshape(128, NCH * 31)],
        axis=1))
    a_bf1 = P.in_("u0_bf1", np.ascontiguousarray(
        np.asarray(W["L0_attn_b_fuse1"], np.float32).reshape(16, 128).T))
    a_bf2 = P.in_("u0_bf2", np.asarray(W["L0_attn_b_fuse2"], np.float32)
                  .reshape(1, 16))
    a_tau = P.in_("u0_tau", np.asarray(W["L0_attn_others_log_tau"], np.float32)
                  .reshape(1, 4))
    a_nw = P.in_("u0_nw", np.ascontiguousarray(
        np.asarray(W["L0_attn_norm_w"], np.float32).reshape(2, 128).T))
    mneg = np.zeros((32, 32), np.float32)
    mloc = np.zeros((32, 32), np.float32)
    for j in range(32):
        mneg[j, j + 1:] = -1.0
        mloc[j, j:] = 1.0
    a_mneg = P.in_("u0_mneg", mneg)
    mloc = np.tile(mloc, (4, 1))
    sel16 = np.zeros((16, 16 * 128), np.float32)
    for r in range(16):
        sel16[r, r * 128:(r + 1) * 128] = 1.0
    a_sel = P.in_("u0_sel", sel16)
    a_mloc = P.in_("u0_mloc", mloc)
    a_gam = P.in_("u0_gam", [np.full((128, 1), float(c % 2), np.float32)
                             for c in range(NCORES)])
    xf = np.asarray(W["L0_attn_hidden_states"], np.float32)
    xh_shards = []
    for c in range(NCORES):
        bidx, half = c // 2, c % 2
        if half == 0:
            xh_shards.append(np.zeros((D, HALO), np.float32))
        else:
            xh_shards.append(np.ascontiguousarray(xf[bidx, T - HALO:T, :].T))
    a_xh = P.in_("u0_xh", xh_shards)

    qn_dram = P.dram("u0_qn", [1024, T], BF16)
    kn_dram = P.dram("u0_kn", [1024, T], BF16)
    v_dram = P.dram("u0_v", [1024, VW], BF16)
    b_dram = P.dram("u0_bd", [4, T], F32)
    bn_dram = P.dram("u0_bn", [32, 16 * 512], F32)
    ab_dram = P.dram("u0_ab", [128, 2048], F32)
    u_dram = P.dram("u0_ud", [256 * 32, 256], BF16)
    w_dram = P.dram("u0_wd", [256 * 256, 32], BF16)
    fs_dram = P.dram("u0_fsd", [1024, T], BF16)
    fl_dram = P.dram("u0_fld", [1024, T], BF16)
    dl_dram = P.dram("u0_dld", [1024, T], BF16)
    s_in = P.dram("u0_sin", [1024, 256], F32)
    s_out = P.dram("u0_sout", [2048, 256], F32)

    ld = P.gsem("dma_sync")
    ldg = P.gsem("dma_gp")
    std = P.gsem("dma_st")
    ccs = P.gsem("cc")
    ev = P.gsem("ev")
    es_ = P.gsem("es")
    et = P.gsem("et")

    ustack = ExitStack()
    ustack.__enter__()
    UE = ustack.enter_context
    cwb = UE(nc.sbuf_tensor("u0_cwb", [128, 96], F32))
    firw = UE(nc.sbuf_tensor("u0_firw", [128, 272], F32))
    mnegb = UE(nc.sbuf_tensor("u0_mnegb", [32, 32], F32))
    mlocb = UE(nc.sbuf_tensor("u0_mlocb", [128, 32], F32))
    gamc = UE(nc.sbuf_tensor("u0_gamc", [128, 1], F32))
    bf1b = UE(nc.sbuf_tensor("u0_bf1b", [128, 16], F32))
    bf2r = UE(nc.sbuf_tensor("u0_bf2r", [1, 16], F32))
    taur = UE(nc.sbuf_tensor("u0_taur", [1, 4], F32))
    nwb = UE(nc.sbuf_tensor("u0_nwb", [128, 2], F32))
    wf2b = UE(nc.sbuf_tensor("u0_wf2b", [128, 256], BF16))
    beta = UE(nc.sbuf_tensor("u0_beta", [4, T], F32))
    selb = UE(nc.sbuf_tensor("u0_selb", [16, 16 * 128], F32))
    XH = UE(nc.sbuf_tensor("u0_XH", [128, NCH * HALO], BF16))

    with Ph(P, "u0_ld") as ph:
        ph.dma("sync", cwb[:, :], a_cw, ld)
        ph.dma("sync", firw[:, :], a_fir, ld)
        ph.dma("sync", mnegb[:, :], a_mneg, ld)
        ph.dma("sync", mlocb[:, :], a_mloc, ld)
        ph.dma("sync", gamc[:, :], a_gam, ld)
        ph.dma("sync", bf1b[:, :], a_bf1, ld)
        ph.dma("sync", bf2r[:, :], a_bf2, ld)
        ph.dma("sync", taur[:, :], a_tau, ld)
        ph.dma("sync", nwb[:, :], a_nw, ld)
        ph.dma("sync", selb[:, :], a_sel, ld)
        ph.dma("gpsimd", wf2b[:, :].rearrange("p (c j) -> p c j", j=16),
               a_wf2.rearrange("(c p) j -> p c j", p=128), ldg)
        ph.dma("gpsimd", XH[:, :].rearrange("p (c t) -> p c t", c=NCH),
               a_xh.rearrange("(c p) t -> p c t", p=128), ldg)
        P.mark("u0ld", ph.settle("sync", ld))
        P.mark("u0ldg", ph.settle("gpsimd", ldg))

    with ExitStack() as st:
        E = st.enter_context
        Wb = E(nc.sbuf_tensor("u0_Wbuf", [128, NCH * 1024], BF16))
        pj = E(nc.sbuf_tensor("u0_pj", [128, TH], F32))
        cs = E(nc.sbuf_tensor("u0_cs", [128, TH], F32))
        allb = E(nc.sbuf_tensor("u0_allb", [128, 2 * VW], BF16))
        sqs = E(nc.sbuf_tensor("u0_sqs", [128, T], F32))
        stg = E(nc.sbuf_tensor("u0_stg", [128, 2 * T], BF16))
        rsr = E(nc.sbuf_tensor("u0_rsr", [1, T], F32))
        p_a = [E(nc.psum_tensor(f"u0_pa{i}", [128, TG], F32)) for i in range(2)]
        p_s = E(nc.psum_tensor("u0_ps", [1, T], F32))
        p_b = [E(nc.psum_tensor(f"u0_pb{i}", [128, TG], F32)) for i in range(2)]

        pev = {}
        gctr = 0
        tlast_mm = None
        tfree = None
        tstg_free = None
        tsq_war = None
        tsilp = None
        tps_free = None
        groups = [(-HALO, 0)] + [(g * TG, (g + 1) * TG) for g in range(NTG)]
        for pi, (wap, cwoff, dst) in enumerate(
                [(a_wq, 0, qn_dram), (a_wk, 32, kn_dram), (a_wv, 64, v_dram)]):
            with Ph(P, f"u0_pw{pi}") as ph:
                ph.dma("gpsimd", Wb[:, :].rearrange("p (c d) -> p c d", c=NCH),
                       wap.rearrange("(c p) d -> p c d", p=128), ldg,
                       waits=[tlast_mm])
                tlw = ph.settle("gpsimd", ldg)
            for hp in range(4):
                with Ph(P, f"u0_p{pi}_{hp}") as ph:
                    tsil = {}
                    for ci in range(2):
                        m = 2 * hp + ci
                        for (aa, bb) in groups:
                            po = p_a[gctr % 2]
                            wdt = bb - aa
                            for k in range(NCH):
                                if aa < 0:
                                    rhs = XH[:, k * HALO:(k + 1) * HALO]
                                else:
                                    rhs = C.Xb[:, k * T + aa:k * T + bb]
                                w0 = [tlw, pev.get(gctr - 2)] if k == 0 else ()
                                tmm = ph.op("tensor", lambda q, po=po, k=k, m=m,
                                            rhs=rhs, wdt=wdt: q.matmul(
                                    po[:, 0:wdt],
                                    Wb[:, k * 1024 + m * 128:k * 1024 + (m + 1) * 128],
                                    rhs, start=(k == 0), stop=(k == NCH - 1)),
                                    waits=w0,
                                    needs=("Xb", "u0ldg") if k == 0 else (),
                                    inc=et if k == NCH - 1 else None)
                            off = HALO + aa
                            pev[gctr] = ph.op("vector", lambda q, po=po, off=off,
                                              wdt=wdt: q.tensor_copy(
                                pj[:, off:off + wdt], po[:, 0:wdt]),
                                waits=[tmm], drain=True, inc=ev)
                            gctr += 1
                            tlast_mm = tmm
                        c0 = HALO - 32
                        t1 = ph.op("vector", lambda q, m=m, c0=c0: q.tensor_scalar(
                            cs[:, c0:TH], pj[:, c0 - 3:TH - 3],
                            cwb[:, cwoff + m * 4:cwoff + m * 4 + 1], None,
                            ALU.mult), waits=[pev[gctr - 1], tsilp],
                            needs=("u0ld",), drain=True)
                        for j in range(1, 4):
                            t1 = ph.op("vector", lambda q, m=m, j=j, c0=c0:
                                       q.scalar_tensor_tensor(
                                cs[:, c0:TH], pj[:, c0 - 3 + j:TH - 3 + j],
                                cwb[:, cwoff + m * 4 + j:cwoff + m * 4 + j + 1],
                                cs[:, c0:TH], ALU.mult, ALU.add), drain=True,
                                inc=ev if j == 3 else None)
                        tsil[ci] = ph.op("scalar", lambda q, ci=ci, c0=c0:
                                         q.activation(
                            allb[:, ci * VW:ci * VW + VW], cs[:, c0:TH], AF.Silu),
                            waits=[t1, tfree], inc=es_)
                        tsilp = tsil[ci]
                    if pi < 2:
                        tq = None
                        for ci in range(2):
                            tsq = ph.op("scalar", lambda q, ci=ci: q.activation(
                                sqs[:, :], allb[:, ci * VW + 32:ci * VW + 32 + T],
                                AF.Square),
                                waits=[tsil[ci], tq, tsq_war if ci == 0 else None],
                                inc=es_)
                            for g in range(NTG):
                                a, b = g * TG, (g + 1) * TG
                                tq = ph.op("tensor", lambda q, a=a, b=b, ci=ci:
                                           q.matmul(
                                    p_s[:, a:b], C.ones_col[:, :], sqs[:, a:b],
                                    start=(ci == 0), stop=(ci == 1)),
                                    waits=[tsq, tps_free if ci == 0 and g == 0
                                           else None],
                                    needs=("consts",),
                                    inc=et if g == NTG - 1 else None)
                        tsq_war = tq
                        t1 = ph.op("vector", lambda q: q.tensor_scalar(
                            rsr[:, :], p_s[0:1, :], 1.0, 1e-6, ALU.mult, ALU.add),
                            waits=[tq], drain=True, inc=ev)
                        tps_free = t1
                        t2 = ph.op("scalar", lambda q: q.activation(
                            rsr[:, :], rsr[:, :], AF.Ln), waits=[t1], drain=True)
                        t3 = ph.op("scalar", lambda q: q.activation(
                            rsr[:, :], rsr[:, :], AF.Exp, scale=-0.5), drain=True,
                            inc=es_)
                        tlast = None
                        tlg = {}
                        for g in range(NTG):
                            a, b = g * TG, (g + 1) * TG
                            pbg = p_b[g % 2]
                            tbc = ph.op("tensor", lambda q, pbg=pbg, a=a, b=b:
                                        q.matmul(
                                pbg[:, :], C.ones_row[0:1, 0:128], rsr[0:1, a:b],
                                start=True, stop=True),
                                waits=[t3, tstg_free, tlg.get(g - 2)], inc=et)
                            for ci in range(2):
                                tlast = ph.op("vector", lambda q, ci=ci, a=a, b=b,
                                              pbg=pbg: q.tensor_mul(
                                    stg[:, ci * T + a:ci * T + b],
                                    allb[:, ci * VW + 32 + a:ci * VW + 32 + b],
                                    pbg[:, :]), waits=[tbc], drain=True, inc=ev)
                            tlg[g] = tlast
                        tfree = tlast
                        for ci in range(2):
                            m = 2 * hp + ci
                            ph.dma("sync", dst.ap()[m * 128:(m + 1) * 128, :],
                                   stg[:, ci * T:(ci + 1) * T], std,
                                   waits=[tlast])
                        tstg_free = ph.settle("sync", std)
                    else:
                        for ci in range(2):
                            m = 2 * hp + ci
                            ph.dma("sync", dst.ap()[m * 128:(m + 1) * 128, :],
                                   allb[:, ci * VW:(ci + 1) * VW], std,
                                   waits=[tsil[ci]])
                        tfree = ph.settle("sync", std)
                        if hp == 3:
                            P.mark("u0v", tfree)
            if pi == 0:
                P.mark("u0qn", tstg_free)
            elif pi == 1:
                P.mark("u0kn", tstg_free)

        with Ph(P, "u0_beta") as ph:
            ph.dma("gpsimd", Wb[:, 0:NCH * 4].rearrange("p (c d) -> p c d", c=NCH),
                   a_wb.rearrange("(c p) d -> p c d", p=128), ldg,
                   waits=[tlast_mm])
            tlb = ph.settle("gpsimd", ldg)
            tbeta = None
            for g in range(NTG):
                a, b = g * TG, (g + 1) * TG
                po = p_a[g % 2]
                for k in range(NCH):
                    tmm = ph.op("tensor", lambda q, po=po, k=k, a=a, b=b: q.matmul(
                        po[0:4, 0:TG], Wb[:, k * 4:(k + 1) * 4],
                        C.Xb[:, k * T + a:k * T + b],
                        start=(k == 0), stop=(k == NCH - 1)),
                        waits=[tlb, tbeta, pev.get(gctr - 1), pev.get(gctr - 2)]
                        if k == 0 else (),
                        inc=et if k == NCH - 1 else None)
                tbeta = ph.op("scalar", lambda q, po=po, a=a, b=b: q.activation(
                    beta[:, a:b], po[0:4, 0:TG], AF.Sigmoid), waits=[tmm],
                    inc=es_)
            ph.dma("sync", b_dram.ap(), beta[:, :], std, waits=[tbeta])
            P.mark("u0beta", ph.settle("sync", std))

    with ExitStack() as st:
        E = st.enter_context
        vball = E(nc.sbuf_tensor("u0_vball", [128, VW], BF16))
        fscr = E(nc.sbuf_tensor("u0_fscr", [128, T], F32))
        fout = E(nc.sbuf_tensor("u0_fout", [128, 2 * T], BF16))
        tprev = None
        tstore = None
        for m in range(NCH):
            with Ph(P, f"u0_fir{m}") as ph:
                ph.dma("sync", vball[:, :], v_dram.ap()[m * 128:(m + 1) * 128, :],
                       ld, needs=("u0v", "u0ld"), waits=[tprev])
                tvs = ph.settle("sync", ld)
                for fi, (ntap, woff, dstd) in enumerate(
                        [(3, m * 3, fs_dram), (31, 24 + m * 31, fl_dram)]):
                    s0 = 32 - (ntap - 1)
                    t1 = ph.op("vector", lambda q, woff=woff, s0=s0:
                               q.tensor_scalar(
                        fscr[:, :], vball[:, s0:s0 + T],
                        firw[:, woff:woff + 1], None, ALU.mult),
                        waits=[tvs, tstore], drain=True)
                    for j in range(1, ntap):
                        t1 = ph.op("vector", lambda q, woff=woff, j=j, s0=s0:
                                   q.scalar_tensor_tensor(
                            fscr[:, :], vball[:, s0 + j:s0 + j + T],
                            firw[:, woff + j:woff + j + 1], fscr[:, :],
                            ALU.mult, ALU.add), drain=True)
                    tprev = ph.op("vector", lambda q, fi=fi: q.tensor_copy(
                        fout[:, fi * T:(fi + 1) * T], fscr[:, :]), drain=True,
                        inc=ev)
                    ph.dma("sync", dstd.ap()[m * 128:(m + 1) * 128, :],
                           fout[:, fi * T:(fi + 1) * T], std, waits=[tprev])
                tstore = ph.settle("sync", std)
        with Ph(P, "u0_firdone") as ph:
            P.mark("u0fir", tstore)

    with ExitStack() as st:
        E = st.enter_context
        knb = E(nc.sbuf_tensor("u0_knb", [128, 2 * T], BF16))
        bbc = E(nc.sbuf_tensor("u0_bbc", [128, T], F32))
        kbb = E(nc.sbuf_tensor("u0_kbb", [128, 2 * T], BF16))
        abat = E(nc.sbuf_tensor("u0_abat", [32, 512], F32))
        AB = E(nc.sbuf_tensor("u0_AB", [128, 2048], F32))
        rowi = E(nc.sbuf_tensor("u0_rowi", [128, 32], F32))
        p_t = E(nc.psum_tensor("u0_pt", [32, 32], F32))
        p_bb = [E(nc.psum_tensor(f"u0_pbb{i}", [128, TG], F32)) for i in range(2)]

        tknfree = None
        tabset = None
        for hv in range(2):
            for hh in range(2):
                h = 2 * hv + hh
                with Ph(P, f"u0_att{h}") as ph:
                    ph.dma("sync", knb[:, :].rearrange("p (c t) -> p c t", c=2),
                           kn_dram.ap()[h * 256:(h + 1) * 256, :].rearrange(
                               "(c p) t -> p c t", p=128), ld,
                           needs=("u0kn",), waits=[tknfree])
                    tk = ph.settle("sync", ld)
                    tcb = None
                    for g in range(NTG):
                        a, b = g * TG, (g + 1) * TG
                        pbg = p_bb[g % 2]
                        tbc = ph.op("tensor", lambda q, pbg=pbg, h=h, a=a, b=b:
                                    q.matmul(
                            pbg[:, :], selb[0:4, h * 128:(h + 1) * 128],
                            beta[0:4, a:b],
                            start=True, stop=True), needs=("u0beta", "u0ld"),
                            waits=[tcb, tknfree if g < 2 else None], inc=et)
                        tcb = ph.op("vector", lambda q, pbg=pbg, a=a, b=b:
                                    q.tensor_copy(bbc[:, a:b], pbg[:, :]),
                                    waits=[tbc], drain=True, inc=ev)
                    tkb = ph.op("vector", lambda q: q.tensor_mul(
                        kbb[:, 0:T], knb[:, 0:T], bbc[:, :]),
                        waits=[tk, tcb, tknfree], drain=True)
                    tkb = ph.op("vector", lambda q: q.tensor_mul(
                        kbb[:, T:2 * T], knb[:, T:2 * T], bbc[:, :]), drain=True,
                        inc=ev)
                    tmm = None
                    for bat in range(4):
                        cidx = h * 4 + bat
                        evb = {}
                        for ci in range(16):
                            c = bat * 16 + ci
                            for dc in range(2):
                                tmm = ph.op("tensor", lambda q, c=c, dc=dc:
                                            q.matmul(
                                    p_t[:, :],
                                    knb[:, dc * T + c * 32:dc * T + (c + 1) * 32],
                                    kbb[:, dc * T + c * 32:dc * T + (c + 1) * 32],
                                    start=(dc == 0), stop=(dc == 1)),
                                    waits=[tkb, evb.get(ci - 1), tabset]
                                    if dc == 0 else (),
                                    inc=et if dc == 1 else None)
                            evb[ci] = ph.op("vector", lambda q, ci=ci:
                                            q.tensor_mul(
                                abat[:, ci * 32:(ci + 1) * 32], p_t[:, :],
                                mnegb[:, :]), waits=[tmm], drain=True, inc=ev)
                        ph.dma("sync",
                               bn_dram.ap()[:, cidx * 512:(cidx + 1) * 512],
                               abat[:, :], std, waits=[evb[15]])
                        tabset = ph.settle("sync", std)
                        r0 = hh * 64 + bat * 16
                        ph.dma("sync",
                               AB[r0:r0 + 16, hv * 1024:(hv + 1) * 1024],
                               bass.AP(bn_dram, cidx * 512,
                                       [[32, 16], [8192, 32], [1, 32]]),
                               ld, waits=[tabset])
                    tknfree = tmm
        with Ph(P, "u0_absettle") as ph:
            P.mark("u0abin", ph.settle("sync", ld))

        with Ph(P, "u0_rec") as ph:
            trec = None
            for hv in range(2):
                o0 = hv * 1024
                for i in range(1, 32):
                    ph.op("vector", lambda q, o0=o0, i=i: q.tensor_copy(
                        rowi[:, 0:i], AB[:, o0 + i:o0 + i + 32 * i:32]),
                        needs=("u0abin",), drain=True)
                    for m in range(i):
                        ph.op("vector", lambda q, o0=o0, i=i, m=m:
                              q.scalar_tensor_tensor(
                            AB[:, o0 + i:o0 + i + 32 * i:32],
                            AB[:, o0 + m:o0 + m + 32 * i:32],
                            rowi[:, m:m + 1],
                            AB[:, o0 + i:o0 + i + 32 * i:32],
                            ALU.mult, ALU.add), drain=True)
                trec = ph.op("vector", lambda q, o0=o0: q.tensor_scalar(
                    AB[:, o0:o0 + 1024:33], AB[:, o0:o0 + 1024:33],
                    1.0, None, ALU.add), drain=True, inc=ev)
            ph.dma("sync", ab_dram.ap(), AB[:, :], std, waits=[trec])
            P.mark("u0ab", ph.settle("sync", std))

    with ExitStack() as st:
        E = st.enter_context
        vtok = E(nc.sbuf_tensor("u0_vtok", [128, 256], BF16))
        ktok = E(nc.sbuf_tensor("u0_ktok", [128, 256], BF16))
        vbtok = E(nc.sbuf_tensor("u0_vbtok", [128, 256], BF16))
        kbtok = E(nc.sbuf_tensor("u0_kbtok", [128, 256], BF16))
        btok = E(nc.sbuf_tensor("u0_btok", [128, 1], F32))
        TT2z = [E(nc.sbuf_tensor(f"u0_TT2z{i}", [128, 32], BF16))
                for i in range(4)]
        usb = E(nc.sbuf_tensor("u0_usb", [32, 2 * 256], BF16))
        wsb = E(nc.sbuf_tensor("u0_wsb", [128, 2 * 64], BF16))
        p_u = E(nc.psum_tensor("u0_pu", [32, 256], F32))
        p_w = [E(nc.psum_tensor(f"u0_pw{i}", [128, 32], F32)) for i in range(2)]

        tml = {}
        tsl_store = {}
        te2p = None
        with Ph(P, "u0_uwz") as ph:
            tz = None
            for i in range(4):
                tz = ph.op("vector", lambda q, i=i: q.memset(
                    TT2z[i][:, :], 0.0), drain=True, inc=ev if i == 3 else None)
            P.mark("u0ttz", tz)
        for h in range(H):
            for grp in range(NGRP):
                with Ph(P, f"u0_uw{h}_{grp}") as ph:
                    i0 = h * 64 + grp * 4
                    ph.dmat("sync", vtok[:, :],
                            v_dram.ap()[h * 256:(h + 1) * 256,
                                        32 + grp * 128:32 + (grp + 1) * 128],
                            ld, waits=[tml.get(i0 - 1)] + P.need("u0v", "u0ab"))
                    ph.dmat("sync", ktok[:, :],
                            kn_dram.ap()[h * 256:(h + 1) * 256,
                                         grp * 128:(grp + 1) * 128], ld)
                    ph.dma("sync", btok[:, :],
                           bass.AP(b_dram, h * T + grp * 128, [[1, 128], [1, 1]]),
                           ld, needs=("u0beta",))
                    tl = ph.settle("sync", ld)
                    tb2 = ph.op("vector", lambda q: q.tensor_scalar(
                        kbtok[:, :], ktok[:, :], btok[:, 0:1], None, ALU.mult),
                        waits=[tl], drain=True)
                    tb2 = ph.op("vector", lambda q: q.tensor_scalar(
                        vbtok[:, :], vtok[:, :], btok[:, 0:1], None, ALU.mult),
                        drain=True, inc=ev)
                    for ci in range(4):
                        c = grp * 4 + ci
                        inst = h * 64 + c
                        hv, p = inst // 128, inst % 128
                        sl = ci % 2
                        ph.dma("gpsimd",
                               TT2z[ci][ci * 32:(ci + 1) * 32, :],
                               bass.AP(ab_dram, p * 2048 + hv * 1024,
                                       [[32, 32], [1, 32]]), ldg,
                               waits=[tml.get(inst - 4)], needs=("u0ttz",))
                        tlt = ph.settle("gpsimd", ldg)
                        tu = ph.op("tensor", lambda q, ci=ci: q.matmul(
                            p_u[:, :], TT2z[ci][:, :], vbtok[:, :],
                            start=True, stop=True),
                            waits=[tlt, tb2, tsl_store.get(sl), te2p], inc=et)
                        tw = {}
                        for dc in range(2):
                            tw[dc] = ph.op("tensor", lambda q, ci=ci,
                                           dc=dc: q.matmul(
                                p_w[dc][:, :],
                                kbtok[:, dc * 128:(dc + 1) * 128],
                                TT2z[ci][:, :],
                                start=True, stop=True), inc=et)
                        tml[inst] = tw[1]
                        te1 = ph.op("vector", lambda q, sl=sl: q.tensor_copy(
                            usb[:, sl * 256:(sl + 1) * 256], p_u[:, :]),
                            waits=[tu, tw[1]], drain=True)
                        te2 = None
                        for dc in range(2):
                            te2 = ph.op("vector", lambda q, sl=sl, dc=dc:
                                        q.tensor_copy(
                                wsb[:, sl * 64 + dc * 32:sl * 64 + (dc + 1) * 32],
                                p_w[dc][:, :]), drain=True, inc=ev)
                        te2p = te2
                        ph.dma("sync",
                               u_dram.ap()[inst * 32:(inst + 1) * 32, :],
                               usb[:, sl * 256:(sl + 1) * 256], std, waits=[te2])
                        for dc in range(2):
                            ph.dma("sync",
                                   w_dram.ap()[inst * 256 + dc * 128:
                                               inst * 256 + (dc + 1) * 128, :],
                                   wsb[:, sl * 64 + dc * 32:
                                       sl * 64 + (dc + 1) * 32], std)
                        tsl_store[sl] = ph.settle("sync", std)
        with Ph(P, "u0_uwdone") as ph:
            P.mark("u0uw", ph.settle("sync", std))

    with ExitStack() as st:
        E = st.enter_context
        Ssb = E(nc.sbuf_tensor("u0_Ssb", [128, 512], F32))
        Ssbf = E(nc.sbuf_tensor("u0_Ssbf", [128, 512], BF16))
        ktok2 = E(nc.sbuf_tensor("u0_ktok2", [128, 2 * 256], BF16))
        usb2 = E(nc.sbuf_tensor("u0_usb2", [32, 2 * 256], BF16))
        wsb2 = E(nc.sbuf_tensor("u0_wsb2", [128, 2 * 64], BF16))
        uadj0 = E(nc.sbuf_tensor("u0_uadj0", [32, 2 * 256], BF16))
        uadjz = [E(nc.sbuf_tensor(f"u0_uadjz{i}", [128, 256], BF16))
                 for i in range(4)]
        qnf = E(nc.sbuf_tensor("u0_qnf", [128, 2 * 256], BF16))
        knf = E(nc.sbuf_tensor("u0_knf", [128, 2 * 256], BF16))
        locb = E(nc.sbuf_tensor("u0_locb", [128, 32], BF16))
        locb0 = E(nc.sbuf_tensor("u0_locb0", [32, 32], BF16))
        dls = E(nc.sbuf_tensor("u0_dls", [128, 2 * 256], BF16))
        p_ws = E(nc.psum_tensor("u0_pws", [32, 256], F32))
        p_dS = [E(nc.psum_tensor(f"u0_pds{i}", [128, 256], F32)) for i in range(2)]
        p_loc = E(nc.psum_tensor("u0_ploc", [32, 32], F32))
        p_o = [E(nc.psum_tensor(f"u0_po{i}", [128, 32], F32)) for i in range(2)]

        with Ph(P, "u0_scz") as ph:
            for i in range(4):
                ph.op("vector", lambda q, i=i: q.memset(
                    uadjz[i][:, :], 0.0), drain=True)
            tz = ph.op("vector", lambda q: q.memset(locb[:, :], 0.0),
                       drain=True, inc=ev)
            P.mark("u0uaz", tz)

        def scan(phase2):
            tSstore = None
            tmirw = None
            tag = "s2" if phase2 else "s1"
            nn2 = ("u0cc",) if phase2 else ()
            tds = {}
            tuse = {}
            tdl = {}
            tloc = None
            for h in range(H):
                with Ph(P, f"u0_{tag}h{h}") as ph:
                    if phase2:
                        for dc in range(2):
                            ph.dma("sync", Ssb[:, dc * 256:(dc + 1) * 256],
                                   s_out.ap()[h * 256 + dc * 128:
                                              h * 256 + (dc + 1) * 128, :],
                                   ld, needs=("u0cc",), waits=[tSstore])
                        tlS = ph.settle("sync", ld)
                        tS0 = ph.op("vector", lambda q: q.tensor_scalar(
                            Ssb[:, :], Ssb[:, :], gamc[:, 0:1], None, ALU.mult),
                            waits=[tlS], drain=True, inc=ev)
                        tS = ph.op("scalar", lambda q: q.activation(
                            Ssbf[:, :], Ssb[:, :], AF.Copy),
                            waits=[tS0, tmirw], inc=es_)
                    else:
                        tS0 = ph.op("vector", lambda q: q.memset(Ssb[:, :], 0.0),
                                    needs=("u0uw",), waits=[tSstore],
                                    drain=True, inc=ev)
                        tS = ph.op("scalar", lambda q: q.activation(
                            Ssbf[:, :], Ssb[:, :], AF.Copy),
                            waits=[tS0, tmirw], inc=es_)
                    for grp in range(NGRP):
                        gg = h * NGRP + grp
                        slg = grp % 2
                        ph.dmat("sync", ktok2[:, slg * 256:(slg + 1) * 256],
                                kn_dram.ap()[h * 256:(h + 1) * 256,
                                             grp * 128:(grp + 1) * 128], ld,
                                waits=[tds.get((gg - 2, 0)), tds.get((gg - 2, 1))],
                                needs=("u0kn",) + nn2)
                        if phase2:
                            for dc in range(2):
                                ph.dma("sync",
                                       qnf[:, slg * 256 + dc * 128:
                                           slg * 256 + (dc + 1) * 128],
                                       qn_dram.ap()[h * 256 + dc * 128:
                                                    h * 256 + (dc + 1) * 128,
                                                    grp * 128:(grp + 1) * 128],
                                       ld, needs=("u0qn",),
                                       waits=[tdl.get((gg - 2, "st"))])
                                ph.dma("sync",
                                       knf[:, slg * 256 + dc * 128:
                                           slg * 256 + (dc + 1) * 128],
                                       kn_dram.ap()[h * 256 + dc * 128:
                                                    h * 256 + (dc + 1) * 128,
                                                    grp * 128:(grp + 1) * 128],
                                       ld, waits=[tloc])
                        tgl = ph.settle("sync", ld)
                        for ci in range(4):
                            c = grp * 4 + ci
                            cg = h * 64 + c
                            inst = cg
                            sl = ci % 2
                            ph.dma("sync", usb2[:, sl * 256:(sl + 1) * 256],
                                   u_dram.ap()[inst * 32:(inst + 1) * 32, :], ld,
                                   waits=[tuse.get(cg - 2)],
                                   needs=("u0uw",) + nn2)
                            for dc in range(2):
                                ph.dma("sync",
                                       wsb2[:, sl * 64 + dc * 32:
                                            sl * 64 + (dc + 1) * 32],
                                       w_dram.ap()[inst * 256 + dc * 128:
                                                   inst * 256 + (dc + 1) * 128,
                                                   :], ld)
                            tlu = ph.settle("sync", ld)
                            if phase2:
                                tlc = None
                                for dc in range(2):
                                    tlc = ph.op("tensor", lambda q, slg=slg,
                                                dc=dc, ci=ci: q.matmul(
                                        p_loc[:, :],
                                        knf[:, slg * 256 + dc * 128 + ci * 32:
                                            slg * 256 + dc * 128 + (ci + 1) * 32],
                                        qnf[:, slg * 256 + dc * 128 + ci * 32:
                                            slg * 256 + dc * 128 + (ci + 1) * 32],
                                        start=(dc == 0), stop=(dc == 1)),
                                        waits=[tgl, tloc] if dc == 0 else (),
                                        inc=et if dc == 1 else None)
                                tloc = ph.op("vector", lambda q: q.tensor_mul(
                                    locb0[:, :], p_loc[:, :], mlocb[0:32, :]),
                                    waits=[tlc], drain=True, inc=ev)
                                tlocs = ph.op("scalar", lambda q, ci=ci:
                                              q.activation(
                                    locb[ci * 32:(ci + 1) * 32, :],
                                    locb0[:, :], AF.Copy),
                                    waits=[tloc], inc=es_)
                            tws = None
                            for dc in range(2):
                                tws = ph.op("tensor", lambda q, sl=sl, dc=dc:
                                            q.matmul(
                                    p_ws[:, :],
                                    wsb2[:, sl * 64 + dc * 32:
                                         sl * 64 + (dc + 1) * 32],
                                    Ssbf[:, dc * 256:(dc + 1) * 256],
                                    start=(dc == 0), stop=(dc == 1)),
                                    waits=[tlu, tS] if dc == 0 else (),
                                    inc=et if dc == 1 else None)
                            tua = ph.op("vector", lambda q, sl=sl:
                                        q.tensor_tensor(
                                uadj0[:, sl * 256:(sl + 1) * 256],
                                usb2[:, sl * 256:(sl + 1) * 256],
                                p_ws[:, :], ALU.subtract),
                                waits=[tws], drain=True, inc=ev)
                            tshift = ph.op("scalar", lambda q, ci=ci, sl=sl:
                                           q.activation(
                                uadjz[ci][ci * 32:(ci + 1) * 32, :],
                                uadj0[:, sl * 256:(sl + 1) * 256], AF.Copy),
                                waits=[tua], needs=("u0uaz",), inc=es_)
                            tuse[cg] = tua
                            to = {}
                            if phase2:
                                for dj in range(2):
                                    for dc in range(2):
                                        ph.op("tensor", lambda q, dj=dj, dc=dc,
                                              slg=slg, ci=ci: q.matmul(
                                            p_o[dj][:, :],
                                            Ssbf[:, dc * 256 + dj * 128:
                                                 dc * 256 + (dj + 1) * 128],
                                            qnf[:, slg * 256 + dc * 128 + ci * 32:
                                                slg * 256 + dc * 128 +
                                                (ci + 1) * 32],
                                            start=(dc == 0), stop=False),
                                            waits=[tdl.get((cg - 1, dj))]
                                            if dc == 0 else ())
                                    to[dj] = ph.op("tensor", lambda q, dj=dj,
                                                   ci=ci: q.matmul(
                                        p_o[dj][:, :],
                                        uadjz[ci][:, dj * 128:(dj + 1) * 128],
                                        locb[:, :],
                                        start=False, stop=True),
                                        waits=[tshift, tlocs], inc=et)
                            for dc in range(2):
                                tds[(gg, dc)] = ph.op("tensor",
                                                      lambda q, slg=slg, ci=ci,
                                                      sl=sl, dc=dc: q.matmul(
                                    p_dS[dc][:, :],
                                    ktok2[:, slg * 256 + dc * 128:
                                          slg * 256 + (dc + 1) * 128],
                                    uadjz[ci][:, :],
                                    start=True, stop=True),
                                    waits=[tshift] if dc == 0 else (), inc=et)
                            for dc in range(2):
                                tS0 = ph.op("vector", lambda q, dc=dc:
                                            q.tensor_add(
                                    Ssb[:, dc * 256:(dc + 1) * 256],
                                    Ssb[:, dc * 256:(dc + 1) * 256],
                                    p_dS[dc][:, :]),
                                    waits=[tds[(gg, 1)]], drain=True, inc=ev)
                            tmirw = to[1] if phase2 else tws
                            tS = ph.op("scalar", lambda q: q.activation(
                                Ssbf[:, :], Ssb[:, :], AF.Copy),
                                waits=[tS0, tmirw], inc=es_)
                            if phase2:
                                for dj in range(2):
                                    tdl[(cg, dj)] = ph.op(
                                        "vector", lambda q, dj=dj, slg=slg,
                                        ci=ci: q.tensor_copy(
                                            dls[:, slg * 256 + dj * 128 +
                                                ci * 32:
                                                slg * 256 + dj * 128 +
                                                (ci + 1) * 32],
                                            p_o[dj][:, :]),
                                        waits=[to[dj],
                                               tdl.get((gg - 2, "st"))],
                                        drain=True, inc=ev)
                        if phase2:
                            for dj in range(2):
                                ph.dma("sync",
                                       dl_dram.ap()[h * 256 + dj * 128:
                                                    h * 256 + (dj + 1) * 128,
                                                    grp * 128:(grp + 1) * 128],
                                       dls[:, slg * 256 + dj * 128:
                                           slg * 256 + dj * 128 + 128],
                                       std,
                                       waits=[tdl[(cg, dj)]])
                            tdl[(gg, "st")] = ph.settle("sync", std)
                    if not phase2:
                        for dc in range(2):
                            ph.dma("sync",
                                   s_in.ap()[h * 256 + dc * 128:
                                             h * 256 + (dc + 1) * 128, :],
                                   Ssb[:, dc * 256:(dc + 1) * 256], std,
                                   waits=[tS])
                        tSstore = ph.settle("sync", std)
                    else:
                        tSstore = tS
            return tSstore

        t1s = scan(False)
        with Ph(P, "u0_cc") as ph:
            tcc = ph.op("gpsimd", lambda q: q.collective_compute(
                "AllGather", ALU.bypass, replica_groups=PAIRS,
                ins=[s_in.ap().opt()], outs=[s_out.ap().opt()]),
                waits=[t1s], inc=ccs)
            P.mark("u0cc", tcc)
        scan(True)
        with Ph(P, "u0_dldone") as ph:
            P.mark("u0dl", ph.settle("sync", std))

    with ExitStack() as st:
        E = st.enter_context
        fsb = E(nc.sbuf_tensor("u0_fsb", [128, NCH * TG], BF16))
        flb = E(nc.sbuf_tensor("u0_flb", [128, NCH * TG], BF16))
        dlb = E(nc.sbuf_tensor("u0_dlb", [128, NCH * TG], BF16))
        wf1m = E(nc.sbuf_tensor("u0_wf1m", [128, 2 * 32 * 128], BF16))
        hidb = E(nc.sbuf_tensor("u0_hidb", [128, 16 * TG], BF16))
        gl = E(nc.sbuf_tensor("u0_gl", [128, 16], F32))
        itb = E(nc.sbuf_tensor("u0_itb", [128, 16], F32))
        b2b = E(nc.sbuf_tensor("u0_b2b", [128, 16], F32))
        row16 = E(nc.sbuf_tensor("u0_row16", [1, 16], F32))
        pr = E(nc.sbuf_tensor("u0_pr", [128, 4], F32))
        fl_ = E(nc.sbuf_tensor("u0_fl_", [128, 4], F32))
        pv = E(nc.sbuf_tensor("u0_pv", [128, 4], F32))
        m3 = E(nc.sbuf_tensor("u0_m3", [128, 4], F32))
        s3 = E(nc.sbuf_tensor("u0_s3", [128, 4], F32))
        fac = E(nc.sbuf_tensor("u0_fac", [128, 4], F32))
        Grows = E(nc.sbuf_tensor("u0_Grows", [16, T], F32))
        p_h = [E(nc.psum_tensor(f"u0_phh{i}", [128, TG], F32)) for i in range(2)]
        p_f2 = E(nc.psum_tensor("u0_pf2", [128, 16], F32))
        p_tr = E(nc.psum_tensor("u0_ptr", [16, 128], F32))
        p_b2 = E(nc.psum_tensor("u0_pb2", [128, 16], F32))

        with Ph(P, "u0_gprep") as ph:
            t1 = ph.op("scalar", lambda q: q.activation(
                taur[:, :], taur[:, :], AF.Exp, scale=-1.0), needs=("u0ld",),
                drain=True, inc=es_)
            t2 = None
            for j in range(3):
                t2 = ph.op("vector", lambda q, j=j: q.tensor_copy(
                    row16[:, j:16:4], taur[:, :]), waits=[t1], drain=True)
            t3 = ph.op("vector", lambda q: q.tensor_copy(
                row16[:, 3:16:4], C.ones_row[0:1, 0:4]), needs=("consts",),
                drain=True, inc=ev)
            t4 = ph.op("tensor", lambda q: q.matmul(
                p_b2[:, :], C.ones_row[0:1, 0:128], row16[0:1, :],
                start=True, stop=True), waits=[t3], inc=et)
            t5 = ph.op("vector", lambda q: q.tensor_copy(itb[:, :], p_b2[:, :]),
                       waits=[t4], drain=True, inc=ev)
            t6 = ph.op("tensor", lambda q: q.matmul(
                p_b2[:, :], C.ones_row[0:1, 0:128], bf2r[0:1, :],
                start=True, stop=True), waits=[t5], inc=et)
            t7 = ph.op("vector", lambda q: q.tensor_copy(b2b[:, :], p_b2[:, :]),
                       waits=[t6], drain=True, inc=ev)
            P.mark("u0itb", t7)

        tgel = {}
        tf2 = {}
        tgfree = None
        for g in range(NTG):
            a, b = g * TG, (g + 1) * TG
            with Ph(P, f"u0_fu{g}") as ph:
                for (buf, dr, mk) in [(fsb, fs_dram, "u0fir"),
                                      (flb, fl_dram, None),
                                      (dlb, dl_dram, "u0dl")]:
                    ph.dma("sync",
                           buf[:, :].rearrange("p (c t) -> p c t", c=NCH),
                           dr.ap()[:, a:b].rearrange("(c p) t -> p c t", p=128),
                           ld, needs=(mk,) if mk else (),
                           waits=[tgel.get((g - 1, 15))])
                tld = ph.settle("sync", ld)
                for m in range(16):
                    slw = m % 2
                    ph.dma("gpsimd",
                           wf1m[:, slw * 4096:(slw + 1) * 4096].rearrange(
                               "p (k c) -> p k c", k=32),
                           a_wf1[:, m * 128:(m + 1) * 128].rearrange(
                               "(k p) c -> p k c", p=128), ldg,
                           waits=[tgel.get((g, m - 2)),
                                  tgel.get((g - 1, m + 14))])
                    tlw = ph.settle("gpsimd", ldg)
                    ps = p_h[m % 2]
                    for k in range(32):
                        if k < 8:
                            rhs = C.Xb[:, k * T + a:k * T + b]
                        elif k < 16:
                            rhs = fsb[:, (k - 8) * TG:(k - 7) * TG]
                        elif k < 24:
                            rhs = flb[:, (k - 16) * TG:(k - 15) * TG]
                        else:
                            rhs = dlb[:, (k - 24) * TG:(k - 23) * TG]
                        tmm = ph.op("tensor", lambda q, ps=ps, k=k, slw=slw,
                                    rhs=rhs: q.matmul(
                            ps[:, :], wf1m[:, slw * 4096 + k * 128:
                                           slw * 4096 + (k + 1) * 128],
                            rhs, start=(k == 0), stop=(k == 31)),
                            waits=[tlw, tld, tgel.get((g, m - 2))]
                            if k == 0 else (),
                            needs=("Xb",) if k == 0 else (),
                            inc=et if k == 31 else None)
                    tgel[(g, m)] = ph.op("scalar", lambda q, ps=ps, m=m:
                                         q.activation(
                        hidb[:, m * TG:(m + 1) * TG], ps[:, :], AF.Gelu,
                        bias=bf1b[:, m:m + 1]),
                        waits=[tmm, tf2.get((g - 1, 3))],
                        needs=("u0ld",), inc=es_)
                for tt in range(4):
                    for mc in range(16):
                        tmm = ph.op("tensor", lambda q, mc=mc, tt=tt: q.matmul(
                            p_f2[:, :],
                            hidb[:, mc * TG + tt * 128:mc * TG + (tt + 1) * 128],
                            wf2b[:, mc * 16:(mc + 1) * 16],
                            start=(mc == 0), stop=(mc == 15)),
                            waits=[tgel[(g, 15)], tgfree] if mc == 0 else (),
                            needs=("u0ldg",) if mc == 0 else (),
                            inc=et if mc == 15 else None)
                    tv = ph.op("vector", lambda q: q.tensor_add(
                        gl[:, :], p_f2[:, :], b2b[:, :]), waits=[tmm],
                        needs=("u0itb",), drain=True)
                    tv = ph.op("vector", lambda q: q.tensor_mul(
                        gl[:, :], gl[:, :], itb[:, :]), drain=True, inc=ev)
                    tsg = ph.op("scalar", lambda q: q.activation(
                        pr[:, :], gl[:, 3:16:4], AF.Sigmoid), waits=[tv],
                        inc=es_)
                    tv = ph.op("vector", lambda q: q.tensor_scalar(
                        fl_[:, :], pr[:, :], -0.09, 0.1, ALU.mult, ALU.add),
                        waits=[tsg], drain=True)
                    tv = ph.op("vector", lambda q: q.tensor_scalar(
                        fac[:, :], fl_[:, :], -1.0, 1.0, ALU.mult, ALU.add),
                        drain=True)
                    tv = ph.op("vector", lambda q: q.tensor_mul(
                        pv[:, :], fac[:, :], pr[:, :]), drain=True)
                    tv = ph.op("vector", lambda q: q.tensor_max(
                        m3[:, :], gl[:, 0:16:4], gl[:, 1:16:4]), drain=True)
                    tv = ph.op("vector", lambda q: q.tensor_max(
                        m3[:, :], m3[:, :], gl[:, 2:16:4]), drain=True)
                    for j in range(3):
                        tv = ph.op("vector", lambda q, j=j: q.tensor_sub(
                            gl[:, j:16:4], gl[:, j:16:4], m3[:, :]),
                            drain=True, inc=ev if j == 2 else None)
                    tex = None
                    for j in range(3):
                        tex = ph.op("scalar", lambda q, j=j: q.activation(
                            gl[:, j:16:4], gl[:, j:16:4], AF.Exp),
                            waits=[tv] if j == 0 else (), inc=es_)
                    tv = ph.op("vector", lambda q: q.tensor_add(
                        s3[:, :], gl[:, 0:16:4], gl[:, 1:16:4]), waits=[tex],
                        drain=True)
                    tv = ph.op("vector", lambda q: q.tensor_add(
                        s3[:, :], s3[:, :], gl[:, 2:16:4]), drain=True)
                    tv = ph.op("vector", lambda q: q.reciprocal(
                        s3[:, :], s3[:, :]), drain=True)
                    tv = ph.op("vector", lambda q: q.tensor_scalar(
                        fac[:, :], pv[:, :], -1.0, 1.0, ALU.mult, ALU.add),
                        drain=True)
                    tv = ph.op("vector", lambda q: q.tensor_mul(
                        fac[:, :], fac[:, :], s3[:, :]), drain=True)
                    for j in range(3):
                        tv = ph.op("vector", lambda q, j=j: q.tensor_mul(
                            gl[:, j:16:4], gl[:, j:16:4], fac[:, :]),
                            drain=True)
                    tv = ph.op("vector", lambda q: q.tensor_copy(
                        gl[:, 3:16:4], pv[:, :]), drain=True, inc=ev)
                    ttr = ph.op("tensor", lambda q: q.matmul(
                        p_tr[:, :], gl[:, :], C.ident[:, 0:128],
                        start=True, stop=True), waits=[tv], inc=et)
                    tgfree = ph.op("vector", lambda q, g=g, tt=tt: q.tensor_copy(
                        Grows[0:16, g * TG + tt * 128:g * TG + (tt + 1) * 128],
                        p_tr[0:16, :]), waits=[ttr], drain=True, inc=ev)
                    tf2[(g, tt)] = tgfree
        P.mark("u0gates", tgfree)

    with ExitStack() as st:
        E = st.enter_context
        wob = E(nc.sbuf_tensor("u0_wob", [128, NCH * 1024], BF16))
        gb = E(nc.sbuf_tensor("u0_gb", [128, 4 * TG], BF16))
        fst = E(nc.sbuf_tensor("u0_fst", [128, 2 * TG], BF16))
        flt = E(nc.sbuf_tensor("u0_flt", [128, 2 * TG], BF16))
        dlt = E(nc.sbuf_tensor("u0_dlt", [128, 2 * TG], BF16))
        vt = E(nc.sbuf_tensor("u0_vt", [128, 2 * TG], BF16))
        ocs = E(nc.sbuf_tensor("u0_ocs", [128, 2 * TG], F32))
        tmp = E(nc.sbuf_tensor("u0_tmp", [128, TG], F32))
        onb = E(nc.sbuf_tensor("u0_onb", [128, NCH * TG], BF16))
        sq2 = E(nc.sbuf_tensor("u0_sq2", [128, TG], F32))
        r2 = E(nc.sbuf_tensor("u0_r2", [1, TG], F32))
        p_bcg = [E(nc.psum_tensor(f"u0_pbc{i}", [128, TG], F32))
                 for i in range(2)]
        p_r = E(nc.psum_tensor("u0_pr2", [1, TG], F32))
        p_br = E(nc.psum_tensor("u0_pbr", [128, TG], F32))
        p_wo = [E(nc.psum_tensor(f"u0_pwo{i}", [128, TG], F32)) for i in range(2)]

        with Ph(P, "u0_wol") as ph:
            ph.dma("gpsimd", wob[:, :].rearrange("p (k m) -> p k m", k=NCH),
                   a_wo.rearrange("(k p) m -> p k m", p=128), ldg)
            P.mark("u0wo", ph.settle("gpsimd", ldg))

        tpb = {}
        tonb = None
        twolast = {}
        for g in range(NTG):
            a, b = g * TG, (g + 1) * TG
            for h in range(H):
                jg0 = (g * H + h) * 4
                with Ph(P, f"u0_oc{g}_{h}") as ph:
                    for j in range(4):
                        pbg = p_bcg[j % 2]
                        tb = ph.op("tensor", lambda q, pbg=pbg, h=h, j=j, a=a,
                                   b=b: q.matmul(
                            pbg[:, :],
                            selb[0:16, (4 * h + j) * 128:(4 * h + j + 1) * 128],
                            Grows[0:16, a:b],
                            start=True, stop=True),
                            waits=[tpb.get(jg0 + j - 2)],
                            needs=("u0gates", "u0ld") if j == 0 else (),
                            inc=et)
                        tpb[jg0 + j] = ph.op("vector", lambda q, pbg=pbg, j=j:
                                             q.tensor_copy(
                            gb[:, j * TG:(j + 1) * TG], pbg[:, :]),
                            waits=[tb], drain=True, inc=ev)
                    for (buf, dr, c0) in [(fst, fs_dram, a), (flt, fl_dram, a),
                                          (dlt, dl_dram, a),
                                          (vt, v_dram, 32 + a)]:
                        ph.dma("sync",
                               buf[:, :].rearrange("p (c t) -> p c t", c=2),
                               dr.ap()[h * 256:(h + 1) * 256, c0:c0 + TG]
                               .rearrange("(c p) t -> p c t", p=128), ld,
                               waits=[tonb])
                    tldt = ph.settle("sync", ld)
                    tsq = {}
                    tmr = None
                    for dj in range(2):
                        tv = ph.op("vector", lambda q, dj=dj: q.tensor_mul(
                            ocs[:, dj * TG:(dj + 1) * TG],
                            fst[:, dj * TG:(dj + 1) * TG], gb[:, 0:TG]),
                            waits=[tldt, tpb[jg0 + 3]], drain=True)
                        for (buf, j) in [(flt, 1), (dlt, 2), (vt, 3)]:
                            tv = ph.op("vector", lambda q, buf=buf, j=j, dj=dj:
                                       q.tensor_mul(
                                tmp[:, :], buf[:, dj * TG:(dj + 1) * TG],
                                gb[:, j * TG:(j + 1) * TG]), drain=True)
                            tv = ph.op("vector", lambda q, dj=dj: q.tensor_add(
                                ocs[:, dj * TG:(dj + 1) * TG],
                                ocs[:, dj * TG:(dj + 1) * TG], tmp[:, :]),
                                drain=True,
                                inc=ev if j == 3 else None)
                        tsq[dj] = ph.op("scalar", lambda q, dj=dj: q.activation(
                            sq2[:, :], ocs[:, dj * TG:(dj + 1) * TG],
                            AF.Square), waits=[tv, tmr], inc=es_)
                        tmr = ph.op("tensor", lambda q, dj=dj: q.matmul(
                            p_r[:, :], C.ones_col[:, :], sq2[:, :],
                            start=(dj == 0), stop=(dj == 1)),
                            waits=[tsq[dj]], needs=("consts",), inc=et)
                    tv = ph.op("vector", lambda q: q.tensor_scalar(
                        r2[:, :], p_r[0:1, :], 1.0 / 256, 1e-5, ALU.mult,
                        ALU.add), waits=[tmr], drain=True, inc=ev)
                    tv = ph.op("scalar", lambda q: q.activation(
                        r2[:, :], r2[:, :], AF.Ln), waits=[tv], drain=True)
                    tv = ph.op("scalar", lambda q: q.activation(
                        r2[:, :], r2[:, :], AF.Exp, scale=-0.5), drain=True,
                        inc=es_)
                    tbr = ph.op("tensor", lambda q: q.matmul(
                        p_br[:, :], C.ones_row[0:1, 0:128], r2[0:1, :],
                        start=True, stop=True), waits=[tv], inc=et)
                    for dj in range(2):
                        tv = ph.op("vector", lambda q, dj=dj: q.tensor_mul(
                            ocs[:, dj * TG:(dj + 1) * TG],
                            ocs[:, dj * TG:(dj + 1) * TG], p_br[:, :]),
                            waits=[tbr, twolast.get(g - 1)], drain=True)
                        tonb = ph.op("vector", lambda q, h=h, dj=dj:
                                     q.tensor_scalar(
                            onb[:, (h * 2 + dj) * TG:(h * 2 + dj + 1) * TG],
                            ocs[:, dj * TG:(dj + 1) * TG], nwb[:, dj:dj + 1],
                            None, ALU.mult), drain=True, inc=ev)
            with Ph(P, f"u0_wo{g}") as ph:
                for m in range(NCH):
                    pw = p_wo[m % 2]
                    for k in range(NCH):
                        tmm = ph.op("tensor", lambda q, pw=pw, k=k, m=m:
                                    q.matmul(
                            pw[:, :], wob[:, k * 1024 + m * 128:
                                          k * 1024 + (m + 1) * 128],
                            onb[:, k * TG:(k + 1) * TG],
                            start=(k == 0), stop=(k == NCH - 1)),
                            waits=[tonb, twolast.get((g, m - 2))]
                            if k == 0 else (),
                            needs=("u0wo",) if k == 0 else (),
                            inc=et if k == NCH - 1 else None)
                    twolast[(g, m)] = ph.op("vector", lambda q, pw=pw, m=m,
                                            a=a, b=b: q.tensor_add(
                        C.X[:, m * T + a:m * T + b],
                        C.X[:, m * T + a:m * T + b],
                        pw[:, :]), waits=[tmm], drain=True, inc=ev,
                        needs=("X",))
                twolast[g] = twolast[(g, NCH - 1)]
        P.mark("X", twolast[NTG - 1])

    ustack.__exit__(None, None, None)
    emit_ln(P, C, "u0ln")


def emit_unit(P, C, idx, W):
    fn = {0: emit_u_l0attn, 1: emit_u_l0ffn, 2: emit_u_l1attn, 3: emit_u_l1ffn, 4: emit_u_l2attn,
          5: emit_u_l2ffn, 6: emit_u_l3attn, 7: emit_u_l3ffn}.get(idx)
    if fn is None:
        raise NotImplementedError(f"unit {idx}")
    fn(P, C, W)


def shard_stream(x_full):
    outs = []
    for c in range(NCORES):
        bidx, half = c // 2, c % 2
        sl = x_full[bidx, half * T:(half + 1) * T, :]
        outs.append(np.ascontiguousarray(sl.T.astype(np.float32)))
    return outs


def unshard_stream(shards):
    x = np.zeros((4, SEQ, D), np.float32)
    for c in range(NCORES):
        bidx, half = c // 2, c % 2
        x[bidx, half * T:(half + 1) * T, :] = shards[c].T
    return x


def build_unit_prog(units, W, x_shards):
    P = Prog()
    C = make_ctx(P)
    a_x = P.in_("xin", x_shards)
    o_x = P.out_("xout", [D, T], F32)
    sload = P.gsem("dma_sync")
    sloadg = P.gsem("dma_gp")
    with Ph(P, "loadx") as ph:
        src = a_x.rearrange("(c p) t -> p c t", p=128)
        ph.dma("sync", C.X[:, :].rearrange("p (c t) -> p c t", c=NCH), src, sload)
        ph.dma("gpsimd", C.Xb[:, :].rearrange("p (c t) -> p c t", c=NCH), src, sloadg)
        P.mark("X", ph.settle("sync", sload))
        P.mark("Xb", ph.settle("gpsimd", sloadg))
    for u in units:
        emit_unit(P, C, u, W)
    sst = P.gsem("dma_st")
    with Ph(P, "storex") as ph:
        dst = o_x.rearrange("(c p) t -> p c t", p=128)
        ph.dma("sync", dst, C.X[:, :].rearrange("p (c t) -> p c t", c=NCH), sst,
               needs=("X",))
        ph.op("sync", lambda q: None, waits=[sst])
    return P


def run_units(units, W, x_full):
    x_shards = shard_stream(x_full)
    P = build_unit_prog(units, W, x_shards)
    res = P.run()
    out = unshard_stream([res.results[c]["xout"] for c in range(NCORES)])
    P.close()
    return out


def kernel(**inputs):
    x = np.asarray(inputs["L0_attn_hidden_states"], dtype=np.float32)
    for group in ([0], [1], [2], [3], [4], [5], [6], [7]):
        x = run_units(group, inputs, x)
    return x


# version 02
# speedup vs baseline: 1.0110x; 1.0110x over previous
from contextlib import ExitStack

import numpy as np
import ml_dtypes
import concourse.bass as bass
import concourse.mybir as mybir
from concourse.bass_utils import run_bass_kernel_spmd

F32 = mybir.dt.float32
BF16 = mybir.dt.bfloat16
AF = mybir.ActivationFunctionType
ALU = mybir.AluOpType
AX = mybir.AxisListType

NCORES = 8
D = 1024
T = 2048
TG = 512
NTG = T // TG
NCH = D // 128
SEQ = 4096
MAGIC = 12582912.0

PAIRS = [[0, 1], [2, 3], [4, 5], [6, 7]]
ALL8 = [[0, 1, 2, 3, 4, 5, 6, 7]]
BF_NP = ml_dtypes.bfloat16



class S:
    def __init__(self, h):
        self.h = h
        self.n = 0

    def tok(self):
        return (self, self.n)


class Prog:
    def __init__(self, name="prog"):
        self.nc = bass.Bass("TRN2", target_bir_lowering=False)
        self.stack = ExitStack()
        self.stack.__enter__()
        self.host = {}
        self.out_names = []
        self._uid = 0
        self._maxwait = {}
        self.debug = False
        self.marks = {}

    def mark(self, name, tok):
        self.marks[name] = tok

    def need(self, *names):
        return [self.marks[n] for n in names if n in self.marks]

    def uid(self):
        self._uid += 1
        return self._uid

    def sem(self, name="s"):
        return S(self.stack.enter_context(self.nc.semaphore(f"{name}_{self.uid()}")))

    def gsem(self, name):
        if not hasattr(self, "_gsems"):
            self._gsems = {}
        if name not in self._gsems:
            self._gsems[name] = self.sem(name)
        return self._gsems[name]

    def in_(self, name, arrs, dtype=None):
        if not isinstance(arrs, (list, tuple)):
            arrs = [arrs] * NCORES
        arrs = [np.ascontiguousarray(a) for a in arrs]
        a0 = arrs[0]
        if dtype is None:
            dtype = {np.dtype(np.float32): F32, np.dtype(BF_NP): BF16,
                     np.dtype(np.int32): mybir.dt.int32}[a0.dtype]
        assert name not in self.host
        self.host[name] = arrs
        return self.nc.dram_tensor(name, list(a0.shape), dtype, kind="ExternalInput").ap()

    def out_(self, name, shape, dtype=F32):
        self.out_names.append(name)
        return self.nc.dram_tensor(name, list(shape), dtype, kind="ExternalOutput").ap()

    def dram(self, name, shape, dtype):
        return self.nc.dram_tensor(name, list(shape), dtype, kind="Internal")

    def sb(self, name, shape, dtype):
        return self.stack.enter_context(self.nc.sbuf_tensor(name, list(shape), dtype))

    def ps(self, name, shape, dtype=F32):
        return self.stack.enter_context(self.nc.psum_tensor(name, list(shape), dtype))

    def run(self, trace=False, sim=False, sim_cores=1):
        in_maps = [{k: v[c] for k, v in self.host.items()} for c in range(NCORES)]
        if sim:
            import concourse.bass_interp as bass_interp
            smc = bass_interp.MultiCoreSim(self.nc, sim_cores)
            for i in range(sim_cores):
                for k, v in in_maps[i].items():
                    smc.cores[i].tensor(k)[:] = v
            smc.simulate()

            class R:
                pass
            r = R()
            r.results = [{k: np.asarray(smc.cores[i].mem_tensor(k))
                          for k in self.out_names} for i in range(sim_cores)]
            return r
        return run_bass_kernel_spmd(self.nc, in_maps, core_ids=list(range(NCORES)),
                                    trace=trace)

    def close(self):
        self.stack.__exit__(None, None, None)


class Ph:
    ENGS = ("tensor", "vector", "scalar", "gpsimd", "sync")

    def __init__(self, P, name):
        self.P = P
        self.name = f"{name}_{P.uid()}"
        self.ops = {e: [] for e in self.ENGS}

    def __enter__(self):
        return self

    def __exit__(self, et, ev, tb):
        if et is not None:
            return False
        with self.P.nc.Block(self.name) as block:
            for e in self.ENGS:
                fns = self.ops[e]
                if fns:
                    getattr(block, e)(lambda q, fns=fns: [f(q) for f in fns])
        return False

    def op(self, eng, builder, waits=(), inc=None, drain=False, needs=()):
        P = self.P
        wl = []
        for w in list(waits) + P.need(*needs):
            if w is None:
                continue
            s, n = (w, w.n) if isinstance(w, S) else w
            if n <= 0:
                continue
            key = (eng, id(s))
            if P._maxwait.get(key, -1) >= n:
                continue
            P._maxwait[key] = n
            wl.append((s.h, n))
        tok = None
        incpair = None
        if inc is not None:
            s, k = (inc, 1) if isinstance(inc, S) else inc
            s.n += k
            tok = (s, s.n)
            incpair = (s.h, k)

        def fn(q, wl=wl, incpair=incpair, drain=drain, builder=builder):
            for h, n in wl:
                q.wait_ge(h, n)
            if drain:
                q.drain()
            inst = builder(q)
            if incpair is not None:
                if inst is None:
                    q.nop().then_inc(incpair[0], incpair[1])
                else:
                    inst.then_inc(incpair[0], incpair[1])

        self.ops[eng].append(fn)
        return tok

    def settle(self, eng, sem):
        es = self.P.gsem("e_" + eng)
        return self.op(eng, lambda q: None, waits=[sem], inc=es)

    def dma(self, eng, out, in_, sem, waits=(), needs=(), **kw):
        return self.op(eng, lambda q: q.dma_start(out=out, in_=in_, **kw),
                       waits=waits, inc=(sem, 16), needs=needs)

    def dmat(self, eng, out, in_, sem, waits=(), needs=()):
        return self.op(eng, lambda q: q.dma_start_transpose(out, in_),
                       waits=waits, inc=(sem, 16), needs=needs)


class Ctx:
    pass


def make_ctx(P):
    C = Ctx()
    C.P = P
    C.X = P.sb("X", [128, NCH * T], F32)
    C.Xb = P.sb("Xb", [128, NCH * T], BF16)
    C.ones_col = P.sb("c_ones_col", [128, 1], F32)
    C.ones_col_bf = P.sb("c_ones_col_bf", [128, 1], BF16)
    C.ones_row = P.sb("c_ones_row", [1, TG], F32)
    C.ones_row_bf = P.sb("c_ones_row_bf", [1, TG], BF16)
    C.ident = P.sb("c_ident", [128, 128], F32)
    C.ident_bf = P.sb("c_ident_bf", [128, 128], BF16)
    ld = P.gsem("dma_sync")
    ldg = P.gsem("dma_gp")
    a1 = P.in_("c_in_ones_col", np.ones((128, 1), np.float32))
    a2 = P.in_("c_in_ones_row", np.ones((1, TG), np.float32))
    a3 = P.in_("c_in_ident", np.eye(128, dtype=np.float32))
    with Ph(P, "consts") as ph:
        ph.dma("sync", C.ones_col[:, :], a1, ld)
        ph.dma("sync", C.ones_row[:, :], a2, ld)
        ph.dma("sync", C.ident[:, :], a3, ld)
        ph.dma("gpsimd", C.ones_col_bf[:, :], a1, ldg)
        ph.dma("gpsimd", C.ones_row_bf[:, :], a2, ldg)
        ph.dma("gpsimd", C.ident_bf[:, :], a3, ldg)
        P.mark("consts", ph.settle("sync", ld))
        P.mark("constsg", ph.settle("gpsimd", ldg))
    return C



def emit_ln(P, C, tag):
    nc = P.nc
    with ExitStack() as st:
        E = st.enter_context
        r_mu = E(nc.sbuf_tensor(f"{tag}_rmu", [1, T], F32))
        r_rs = E(nc.sbuf_tensor(f"{tag}_rrs", [1, T], F32))
        r_a = E(nc.sbuf_tensor(f"{tag}_ra", [1, T], F32))
        r_b = E(nc.sbuf_tensor(f"{tag}_rb", [1, T], F32))
        s_rowv = P.gsem("lnrowv")
        s_rows = P.gsem("lnrows")

        with ExitStack() as st2:
            E2 = st2.enter_context
            xsq = E2(nc.sbuf_tensor(f"{tag}_xsq", [128, 2 * T], F32))
            p_sum = E2(nc.psum_tensor(f"{tag}_psum", [1, T], F32))
            p_sq = E2(nc.psum_tensor(f"{tag}_psq", [1, T], F32))
            s_sq = P.gsem("lnsq")
            s_mm = P.gsem("lnmm")
            with Ph(P, tag + "_stats") as ph:
                mm_toks = [None, None]
                for ch in range(NCH):
                    sl = ch % 2
                    tsq = ph.op("scalar", lambda q, ch=ch, sl=sl: q.activation(
                        xsq[:, sl * T:(sl + 1) * T], C.X[:, ch * T:(ch + 1) * T],
                        AF.Square), waits=[mm_toks[sl]], inc=s_sq, needs=("X",))
                    for g in range(NTG):
                        a, b = g * TG, (g + 1) * TG
                        ph.op("tensor", lambda q, ch=ch, a=a, b=b: q.matmul(
                            p_sum[:, a:b], C.ones_col[:, :],
                            C.X[:, ch * T + a: ch * T + b],
                            start=(ch == 0), stop=(ch == NCH - 1)),
                            needs=("X", "consts"))
                        mm_toks[sl] = ph.op("tensor", lambda q, sl=sl, a=a, b=b, ch=ch: q.matmul(
                            p_sq[:, a:b], C.ones_col[:, :], xsq[:, sl * T + a: sl * T + b],
                            start=(ch == 0), stop=(ch == NCH - 1)),
                            waits=[tsq],
                            inc=s_mm if g == NTG - 1 else None)
                t1 = ph.op("vector", lambda q: q.tensor_scalar(
                    r_mu[:, :], p_sum[0:1, :], 1.0 / D, None, ALU.mult),
                    waits=[s_mm.tok()], inc=s_rowv)
                t2 = ph.op("vector", lambda q: q.tensor_scalar(
                    r_a[:, :], p_sq[0:1, :], 1.0 / D, 1e-5, ALU.mult, ALU.add),
                    inc=s_rowv)
                t3 = ph.op("scalar", lambda q: q.activation(
                    r_rs[:, :], r_mu[:, :], AF.Square), waits=[t1], inc=s_rows)
                t4 = ph.op("vector", lambda q: q.tensor_sub(
                    r_a[:, :], r_a[:, :], r_rs[:, :]), waits=[t2, t3], drain=True,
                    inc=s_rowv)
                t5 = ph.op("scalar", lambda q: q.activation(
                    r_b[:, :], r_a[:, :], AF.Ln), waits=[t4], drain=True,
                    inc=s_rows)
                t6 = ph.op("scalar", lambda q: q.activation(
                    r_rs[:, :], r_b[:, :], AF.Exp, scale=-0.5), drain=True, inc=s_rows)

        if P.debug:
            dl = P.gsem("lndbg")
            o_mu = P.out_("dbg_" + tag + "_mu", [1, T], F32)
            o_rs = P.out_("dbg_" + tag + "_rs", [1, T], F32)
            with Ph(P, tag + "_dbg") as ph:
                ph.dma("sync", o_mu, r_mu[:, :], dl, waits=[s_rowv, s_rows])
                ph.dma("sync", o_rs, r_rs[:, :], dl)
                ph.op("sync", lambda q: None, waits=[dl])

        with ExitStack() as st3:
            E3 = st3.enter_context
            p_b = [E3(nc.psum_tensor(f"{tag}_pb{i}", [128, TG], F32)) for i in range(4)]
            s_bc = P.gsem("lnbc")
            s_ap = P.gsem("lnap")
            with Ph(P, tag + "_apply") as ph:
                ev_toks = {}
                for g in range(NTG):
                    a, b = g * TG, (g + 1) * TG
                    pm, pr = p_b[2 * (g % 2)], p_b[2 * (g % 2) + 1]
                    w = [s_rowv.tok(), s_rows.tok()] + ([ev_toks[g - 2]] if g >= 2 else [])
                    ph.op("tensor", lambda q, pm=pm, a=a, b=b: q.matmul(
                        pm[:, :], C.ones_row[:, 0:128], r_mu[:, a:b],
                        start=True, stop=True), waits=w, needs=("consts",))
                    tb = ph.op("tensor", lambda q, pr=pr, a=a, b=b: q.matmul(
                        pr[:, :], C.ones_row[:, 0:128], r_rs[:, a:b],
                        start=True, stop=True), inc=s_bc)
                    last = None
                    for ch in range(NCH):
                        o = ch * T
                        ph.op("vector", lambda q, o=o, a=a, b=b, pm=pm: q.tensor_sub(
                            C.X[:, o + a:o + b], C.X[:, o + a:o + b], pm[:, :]),
                            waits=[tb])
                        ph.op("vector", lambda q, o=o, a=a, b=b, pr=pr: q.tensor_mul(
                            C.X[:, o + a:o + b], C.X[:, o + a:o + b], pr[:, :]),
                            drain=True)
                        last = ph.op("vector", lambda q, o=o, a=a, b=b: q.tensor_copy(
                            C.Xb[:, o + a:o + b], C.X[:, o + a:o + b]), drain=True,
                            inc=s_ap, needs=("Xb",))
                    ev_toks[g] = last
                P.mark("X", last)
                P.mark("Xb", last)



def emit_u_l1ffn(P, C, W):
    nc = P.nc
    wT = np.ascontiguousarray(W["L1_ffn_weight"].astype(np.float32).T)
    bias_col = np.ascontiguousarray(
        W["L1_ffn_bias"].astype(np.float32).reshape(NCH, 128).T)
    a_w = P.in_("u3_wT", wT)
    a_b = P.in_("u3_bias", bias_col)
    g_in = P.dram("u3_gin", [1, 1], F32)
    g_out = P.dram("u3_gout", [1, 1], F32)

    with ExitStack() as st:
        E = st.enter_context
        Wf = E(nc.sbuf_tensor("u3_W", [128, NCH * D], F32))
        QW = E(nc.sbuf_tensor("u3_QW", [128, NCH * D], BF16))
        QX = E(nc.sbuf_tensor("u3_QX", [128, NCH * T], BF16))
        scr = E(nc.sbuf_tensor("u3_scr", [128, D], F32))
        scr2 = E(nc.sbuf_tensor("u3_scr2", [128, T], F32))
        cols = E(nc.sbuf_tensor("u3_cols", [128, 8], F32))
        ccols = E(nc.sbuf_tensor("u3_ccols", [128, 4], F32))
        bias_c = E(nc.sbuf_tensor("u3_biasc", [128, 8], F32))
        regs = E(nc.sbuf_tensor("u3_regs", [1, 16], F32))
        row128 = E(nc.sbuf_tensor("u3_row128", [1, 128], F32))
        p_s = E(nc.psum_tensor("u3_ps", [1, 128], F32))
        p_c = E(nc.psum_tensor("u3_pc", [128, 4], F32))
        p_o = [E(nc.psum_tensor(f"u3_po{i}", [128, TG], F32)) for i in range(2)]
        ld = P.gsem("dma_sync")
        vs = P.gsem("ev")
        ss = P.gsem("es")
        ts = P.gsem("et")
        cc = P.gsem("cc")

        with Ph(P, "u3_a") as ph:
            ph.dma("sync", Wf[:, :].rearrange("p (c d) -> p c d", c=NCH),
                   a_w.rearrange("(c p) d -> p c d", p=128), ld)
            ph.dma("sync", bias_c[:, :], a_b, ld)
            tw = ph.settle("sync", ld)
            for ch in range(NCH):
                ph.op("vector", lambda q, ch=ch: q.tensor_reduce(
                    cols[:, ch:ch + 1], Wf[:, ch * D:(ch + 1) * D], AX.X, ALU.add,
                    apply_absolute_value=True), waits=[tw])
            tsum = ph.op("vector", lambda q: q.tensor_reduce(
                scr[:, 0:1], cols[:, :], AX.X, ALU.add), drain=True, inc=vs)
            tmm = ph.op("tensor", lambda q: q.matmul(
                p_s[:, 0:1], scr[:, 0:1], C.ones_col[:, :], start=True, stop=True),
                waits=[tsum], inc=ts, needs=("consts",))
            tws = ph.op("vector", lambda q: q.tensor_scalar(
                regs[:, 0:1], p_s[0:1, 0:1], 1.0 / (D * D), None, ALU.mult),
                waits=[tmm], inc=vs)
            tth = ph.op("vector", lambda q: q.tensor_scalar(
                regs[:, 1:2], regs[:, 0:1], 0.5, None, ALU.mult), drain=True, inc=vs)
            tb1 = ph.op("tensor", lambda q: q.matmul(
                p_c[:, 0:1], C.ones_row[:, 0:128], regs[:, 1:2], start=True, stop=True),
                waits=[tth], inc=ts)
            tb2 = ph.op("vector", lambda q: q.tensor_copy(ccols[:, 0:1], p_c[:, 0:1]),
                        waits=[tb1], inc=vs)
            prev = None
            for ch in range(NCH):
                o = ch * D
                ta = ph.op("scalar", lambda q, o=o: q.activation(
                    scr[:, :], Wf[:, o:o + D], AF.Abs), waits=[prev, tsum, tmm],
                    drain=True, inc=ss)
                tg = ph.op("vector", lambda q: q.tensor_scalar(
                    scr[:, :], scr[:, :], ccols[:, 0:1], None, ALU.is_gt),
                    waits=[ta, tb2], drain=True, inc=vs)
                tsg = ph.op("scalar", lambda q, o=o: q.sign(
                    QW[:, o:o + D], Wf[:, o:o + D]), waits=[tg], inc=ss)
                prev = ph.op("vector", lambda q, o=o: q.tensor_mul(
                    QW[:, o:o + D], QW[:, o:o + D], scr[:, :]), waits=[tsg],
                    drain=True, inc=vs)
            for ch in range(NCH):
                ph.op("vector", lambda q, ch=ch: q.tensor_reduce(
                    cols[:, ch:ch + 1], C.X[:, ch * T:(ch + 1) * T], AX.X, ALU.max,
                    apply_absolute_value=True), waits=[prev], drain=(ch == 0),
                    needs=("X",))
            tmax = ph.op("vector", lambda q: q.tensor_reduce(
                scr[:, 0:1], cols[:, :], AX.X, ALU.max), drain=True, inc=vs)
            ttr = ph.op("tensor", lambda q: q.matmul(
                p_s[:, :], scr[:, 0:1], C.ident[:, :], start=True, stop=True),
                waits=[tmax], inc=ts)
            trow = ph.op("vector", lambda q: q.tensor_copy(row128[:, :], p_s[0:1, :]),
                         waits=[ttr], inc=vs)
            tlm = ph.op("vector", lambda q: q.reduce_max(
                regs[:, 2:3], row128[:, :], AX.X), drain=True, inc=vs)
            ph.dma("sync", g_in.ap(), regs[:, 2:3], ld, waits=[tlm])
            tst = ph.settle("sync", ld)
            tcc = ph.op("gpsimd", lambda q: q.collective_compute(
                "AllReduce", ALU.max, replica_groups=ALL8,
                ins=[g_in.ap().opt()], outs=[g_out.ap().opt()]), waits=[tst],
                inc=cc)
            ph.dma("sync", regs[:, 3:4], g_out.ap(), ld, waits=[tcc])
            P.mark("u3gmax", ph.settle("sync", ld))

        with Ph(P, "u3_b") as ph:
            t1 = ph.op("vector", lambda q: q.reciprocal(regs[:, 4:5], regs[:, 3:4]),
                       inc=vs, needs=("u3gmax",), drain=True)
            t2 = ph.op("vector", lambda q: q.tensor_scalar(
                regs[:, 4:5], regs[:, 4:5], 127.0, None, ALU.mult), drain=True, inc=vs)
            t3 = ph.op("vector", lambda q: q.tensor_mul(
                regs[:, 5:6], regs[:, 0:1], regs[:, 3:4]), drain=True, inc=vs)
            t4 = ph.op("vector", lambda q: q.tensor_scalar(
                regs[:, 5:6], regs[:, 5:6], 1.0 / 127.0, None, ALU.mult), drain=True,
                inc=vs)
            tb1 = ph.op("tensor", lambda q: q.matmul(
                p_c[:, 1:3], C.ones_row[:, 0:128], regs[:, 4:6], start=True, stop=True),
                waits=[t4], inc=ts)
            tb2 = ph.op("vector", lambda q: q.tensor_copy(ccols[:, 1:3], p_c[:, 1:3]),
                        waits=[tb1], drain=True, inc=vs)
            for ch in range(NCH):
                o = ch * T
                ph.op("vector", lambda q, o=o: q.tensor_scalar(
                    scr2[:, :], C.X[:, o:o + T], ccols[:, 1:2], None, ALU.mult),
                    waits=[tb2], drain=True, needs=("X",))
                ph.op("vector", lambda q: q.tensor_scalar(
                    scr2[:, :], scr2[:, :], MAGIC, None, ALU.add), drain=True)
                ph.op("vector", lambda q: q.tensor_scalar(
                    scr2[:, :], scr2[:, :], MAGIC, None, ALU.subtract), drain=True)
                ph.op("vector", lambda q: q.tensor_scalar(
                    scr2[:, :], scr2[:, :], 127.0, None, ALU.min), drain=True)
                ph.op("vector", lambda q, o=o: q.tensor_scalar(
                    QX[:, o:o + T], scr2[:, :], -128.0, None, ALU.max), drain=True,
                    inc=vs)

        with Ph(P, "u3_c") as ph:
            ev = {}
            i = 0
            for m in range(NCH):
                for g in range(NTG):
                    po = p_o[i % 2]
                    a, b = g * TG, (g + 1) * TG
                    w = [ev.get(i - 2)]
                    for k in range(NCH):
                        tmm = ph.op("tensor", lambda q, po=po, k=k, m=m, a=a, b=b: q.matmul(
                            po[:, :], QW[:, k * D + m * 128: k * D + (m + 1) * 128],
                            QX[:, k * T + a: k * T + b],
                            start=(k == 0), stop=(k == NCH - 1)),
                            waits=w if k == 0 else (), inc=ts if k == NCH - 1 else None)
                    t1 = ph.op("vector", lambda q, po=po, m=m, a=a, b=b: q.scalar_tensor_tensor(
                        C.X[:, m * T + a: m * T + b], po[:, :], ccols[:, 2:3],
                        C.X[:, m * T + a: m * T + b], ALU.mult, ALU.add),
                        waits=[tmm], drain=True)
                    ev[i] = ph.op("vector", lambda q, m=m, a=a, b=b: q.tensor_scalar(
                        C.X[:, m * T + a: m * T + b], C.X[:, m * T + a: m * T + b],
                        bias_c[:, m:m + 1], None, ALU.add), drain=True, inc=vs)
                    i += 1
            P.mark("X", ev[i - 1])

        if P.debug:
            dsem = P.gsem("dbg")
            o_qx = P.out_("dbg_qx", [128, NCH * T], BF16)
            o_qw = P.out_("dbg_qw", [128, NCH * D], BF16)
            o_regs = P.out_("dbg_regs", [1, 16], F32)
            o_xpre = P.out_("dbg_xpre", [128, NCH * T], F32)
            with Ph(P, "u3_dbg") as ph:
                ph.dma("sync", o_qx, QX[:, :], ld)
                ph.dma("sync", o_qw, QW[:, :], ld)
                ph.dma("sync", o_regs, regs[:, :], ld)
                ph.dma("sync", o_xpre, C.X[:, :], ld)
                ph.op("sync", lambda q: None, waits=[ld])

    emit_ln(P, C, "u3ln")



def rne_clip(ph, dst, src, lo, hi, scr, mid_ops=()):
    ph.op("vector", lambda q: q.tensor_scalar(scr[0], src[0], MAGIC, None, ALU.add),
          waits=src[1], drain=True)
    ph.op("vector", lambda q: q.tensor_scalar(scr[0], scr[0], MAGIC, None, ALU.subtract),
          drain=True)
    ph.op("vector", lambda q: q.tensor_scalar(scr[0], scr[0], float(hi), None, ALU.min),
          drain=True)
    return ph.op("vector", lambda q: q.tensor_scalar(dst[0], scr[0], float(lo), None,
                                                     ALU.max), drain=True, inc=dst[1])


def emit_u_l0ffn(P, C, W):
    nc = P.nc
    TT2 = 256
    NT2 = T // TT2
    HCH = 32
    PADW = T + 2

    w1T = np.ascontiguousarray(W["L0_ffn_w1"].astype(np.float32).T)
    w2T = np.ascontiguousarray(W["L0_ffn_w2"].astype(np.float32).T)
    cw = W["L0_ffn_conv_w"].astype(np.float32)[:, 0, :]
    cwcol = np.ascontiguousarray(cw.reshape(HCH, 128, 3).transpose(1, 0, 2).reshape(128, HCH * 3))
    cbcol = np.ascontiguousarray(W["L0_ffn_conv_b"].astype(np.float32).reshape(HCH, 128).T)
    maskL = [np.full((128, 1), float(c % 2), np.float32) for c in range(NCORES)]
    maskR = [np.full((128, 1), float(1 - c % 2), np.float32) for c in range(NCORES)]

    a_w1 = P.in_("u1_w1T", w1T)
    a_w2 = P.in_("u1_w2T", w2T)
    a_cw = P.in_("u1_cw", cwcol)
    a_cb = P.in_("u1_cb", cbcol)
    a_mL = P.in_("u1_maskL", maskL)
    a_mR = P.in_("u1_maskR", maskR)

    qw1 = P.dram("u1_qw1", [128, 8 * 4096], BF16)
    qw2 = P.dram("u1_qw2", [128, 32 * 1024], BF16)
    ex_in = P.dram("u1_exin", [128, 16], F32)
    ex_out = P.dram("u1_exout", [256, 16], F32)

    ld = P.gsem("dma_sync")
    ldg = P.gsem("dma_gp")
    std = P.gsem("dma_st")
    ccs = P.gsem("cc")
    ev = P.gsem("ev")
    es_ = P.gsem("es")
    et = P.gsem("et")

    ustack = ExitStack()
    ustack.__enter__()
    UE = ustack.enter_context
    regs = UE(nc.sbuf_tensor("u1_regs", [1, 16], F32))
    ccol = UE(nc.sbuf_tensor("u1_ccol", [128, 4], F32))
    cwb = UE(nc.sbuf_tensor("u1_cwb", [128, HCH * 3], F32))
    cbb = UE(nc.sbuf_tensor("u1_cbb", [128, HCH], F32))
    mLc = UE(nc.sbuf_tensor("u1_mL", [128, 1], F32))
    mRc = UE(nc.sbuf_tensor("u1_mR", [128, 1], F32))
    QX1 = UE(nc.sbuf_tensor("u1_QX1", [128, NCH * PADW], BF16))
    drow1 = UE(nc.sbuf_tensor("u1_drow1", [1, PADW], F32))

    with ExitStack() as st:
        E = st.enter_context
        wpc = E(nc.sbuf_tensor("u1_wpc", [128, 2 * 4096], F32))
        qpc = E(nc.sbuf_tensor("u1_qpc", [128, 4096], BF16))
        cols = E(nc.sbuf_tensor("u1_cols", [128, 48], F32))
        c1 = E(nc.sbuf_tensor("u1_c1", [128, 1], F32))
        p_s = E(nc.psum_tensor("u1_psA", [1, 128], F32))
        p_c = E(nc.psum_tensor("u1_pcA", [128, 4], F32))

        with Ph(P, "u1_wstat") as ph:
            ph.dma("sync", cwb[:, :], a_cw, ld)
            ph.dma("sync", cbb[:, :], a_cb, ld)
            ph.dma("sync", mLc[:, :], a_mL, ld)
            ph.dma("sync", mRc[:, :], a_mR, ld)
            tks = {}
            for i in range(8):
                sl = i % 2
                ph.dma("sync", wpc[:, sl * 4096:(sl + 1) * 4096],
                       a_w1[i * 128:(i + 1) * 128, :], ld, waits=[tks.get(i - 2)])
                tl = ph.settle("sync", ld)
                tks[i] = ph.op("vector", lambda q, i=i, sl=sl: q.tensor_reduce(
                    cols[:, i:i + 1], wpc[:, sl * 4096:(sl + 1) * 4096], AX.X, ALU.add,
                    apply_absolute_value=True), waits=[tl], inc=ev)
            for i in range(32):
                sl = i % 2
                ph.dma("sync", wpc[:, sl * 4096:sl * 4096 + 1024],
                       a_w2[i * 128:(i + 1) * 128, :], ld, waits=[tks.get(8 + i - 2)])
                tl = ph.settle("sync", ld)
                tks[8 + i] = ph.op("vector", lambda q, i=i, sl=sl: q.tensor_reduce(
                    cols[:, 8 + i:9 + i], wpc[:, sl * 4096:sl * 4096 + 1024], AX.X,
                    ALU.add, apply_absolute_value=True), waits=[tl], inc=ev)
            t1 = ph.op("vector", lambda q: q.tensor_reduce(
                c1[:, 0:1], cols[:, 0:8], AX.X, ALU.add), drain=True, inc=ev)
            tm1 = ph.op("tensor", lambda q: q.matmul(
                p_s[:, 0:1], c1[:, 0:1], C.ones_col[:, :], start=True, stop=True),
                waits=[t1], inc=et, needs=("consts",))
            t2 = ph.op("vector", lambda q: q.tensor_reduce(
                c1[:, 0:1], cols[:, 8:40], AX.X, ALU.add), waits=[tm1], drain=True,
                inc=ev)
            tm2 = ph.op("tensor", lambda q: q.matmul(
                p_s[:, 1:2], c1[:, 0:1], C.ones_col[:, :], start=True, stop=True),
                waits=[t2], inc=et)
            t3 = ph.op("vector", lambda q: q.tensor_scalar(
                regs[:, 0:1], p_s[0:1, 0:1], 1.0 / (1024 * 4096), 1e-5,
                ALU.mult, ALU.max), waits=[tm2], inc=ev)
            t4 = ph.op("vector", lambda q: q.tensor_scalar(
                regs[:, 2:3], p_s[0:1, 1:2], 1.0 / (1024 * 4096), 1e-5,
                ALU.mult, ALU.max), drain=True, inc=ev)
            t5 = ph.op("vector", lambda q: q.reciprocal(regs[:, 1:2], regs[:, 0:1]),
                       drain=True, inc=ev)
            t6 = ph.op("vector", lambda q: q.reciprocal(regs[:, 3:4], regs[:, 2:3]),
                       drain=True, inc=ev)
            tb = ph.op("tensor", lambda q: q.matmul(
                p_c[:, 0:1], C.ones_row[:, 0:128], regs[:, 1:2],
                start=True, stop=True), waits=[t6], inc=et)
            tb2 = ph.op("tensor", lambda q: q.matmul(
                p_c[:, 2:3], C.ones_row[:, 0:128], regs[:, 3:4],
                start=True, stop=True), inc=et)
            tcp = ph.op("vector", lambda q: q.tensor_copy(ccol[:, 0:1], p_c[:, 0:1]),
                        waits=[tb, tb2], drain=True, inc=ev)
            ph.op("vector", lambda q: q.tensor_copy(ccol[:, 1:2], p_c[:, 2:3]),
                  drain=True, inc=ev)

        with Ph(P, "u1_wq") as ph:
            prev = None
            for wi, (src, npcs, width, qdst) in enumerate(
                    [(a_w1, 8, 4096, qw1), (a_w2, 32, 1024, qw2)]):
                scol = 2 * wi
                for i in range(npcs):
                    sl = i % 2
                    ph.dma("sync", wpc[:, sl * 4096:sl * 4096 + width],
                           src[i * 128:(i + 1) * 128, :], ld, waits=[prev])
                    tl = ph.settle("sync", ld)
                    ph.op("vector", lambda q, sl=sl, width=width, scol=scol: q.tensor_scalar(
                        wpc[:, sl * 4096:sl * 4096 + width],
                        wpc[:, sl * 4096:sl * 4096 + width],
                        ccol[:, scol // 2:scol // 2 + 1], None, ALU.mult),
                        waits=[tl], drain=True)
                    tq = rne_clip(
                        ph, (qpc[:, 0:width], ev),
                        (wpc[:, sl * 4096:sl * 4096 + width], ()),
                        -1.0, 1.0, (wpc[:, sl * 4096:sl * 4096 + width], None))
                    prev = ph.dma("sync", qdst.ap()[:, i * width:(i + 1) * width],
                                  qpc[:, 0:width], std, waits=[tq])
            ph.settle("sync", std)

    with ExitStack() as st:
        E = st.enter_context
        exs = E(nc.sbuf_tensor("u1_exs", [128, 16], F32))
        exA = E(nc.sbuf_tensor("u1_exA", [128, 16], F32))
        exB = E(nc.sbuf_tensor("u1_exB", [128, 16], F32))
        hl = E(nc.sbuf_tensor("u1_hl", [128, 16], F32))
        M = E(nc.sbuf_tensor("u1_M", [128, T], F32))
        Mn = E(nc.sbuf_tensor("u1_Mn", [128, T], F32))
        scc = E(nc.sbuf_tensor("u1_scc", [128, 16], F32))
        sc2 = E(nc.sbuf_tensor("u1_sc2", [2, 16], F32))
        srow = E(nc.sbuf_tensor("u1_srow", [1, PADW], F32))
        s2row = E(nc.sbuf_tensor("u1_s2row", [1, PADW], F32))
        scr = E(nc.sbuf_tensor("u1_scrx", [128, TG], F32))
        p_t = [E(nc.psum_tensor(f"u1_pt{i}", [128, 128], F32)) for i in range(2)]
        p_r = E(nc.psum_tensor("u1_pr", [1, 128], F32))
        p_b = [E(nc.psum_tensor(f"u1_pb{i}", [128, TG], F32)) for i in range(2)]

        with Ph(P, "u1_x", ) as ph:
            xv = C.X[:, :].rearrange("p (c t) -> p c t", c=NCH)
            ph.dma("sync", exs[:, 0:8].rearrange("p (c u) -> p c u", u=1),
                   xv[:, :, 0:1], ld, needs=("X",), allow_slow_non_contiguous=True)
            ph.dma("sync", exs[:, 8:16].rearrange("p (c u) -> p c u", u=1),
                   xv[:, :, T - 1:T], ld, allow_slow_non_contiguous=True)
            tse = ph.settle("sync", ld)
            ph.dma("sync", ex_in.ap(), exs[:, :], ld, waits=[tse])
            tse2 = ph.settle("sync", ld)
            tcc = ph.op("gpsimd", lambda q: q.collective_compute(
                "AllGather", ALU.bypass, replica_groups=PAIRS,
                ins=[ex_in.ap().opt()], outs=[ex_out.ap().opt()]), waits=[tse2],
                inc=ccs)
            ph.dma("sync", exA[:, :], ex_out.ap()[0:128, :], ld, waits=[tcc])
            ph.dma("sync", exB[:, :], ex_out.ap()[128:256, :], ld)
            tse3 = ph.settle("sync", ld)
            th1 = ph.op("vector", lambda q: q.tensor_scalar(
                hl[:, 0:8], exA[:, 8:16], mLc[:, 0:1], None, ALU.mult),
                waits=[tse3], inc=ev)
            th2 = ph.op("vector", lambda q: q.tensor_scalar(
                hl[:, 8:16], exB[:, 0:8], mRc[:, 0:1], None, ALU.mult),
                drain=True, inc=ev)
            tm = ph.op("vector", lambda q: q.tensor_max(
                M[:, :], C.X[:, 0:T], C.X[:, T:2 * T]), inc=ev)
            tn = ph.op("vector", lambda q: q.tensor_tensor(
                Mn[:, :], C.X[:, 0:T], C.X[:, T:2 * T], ALU.min), drain=True, inc=ev)
            for ch in range(2, NCH):
                tm = ph.op("vector", lambda q, ch=ch: q.tensor_max(
                    M[:, :], M[:, :], C.X[:, ch * T:(ch + 1) * T]), drain=True, inc=ev)
                tn = ph.op("vector", lambda q, ch=ch: q.tensor_tensor(
                    Mn[:, :], Mn[:, :], C.X[:, ch * T:(ch + 1) * T], ALU.min),
                    drain=True, inc=ev)
            tm = ph.op("vector", lambda q: q.tensor_scalar(
                Mn[:, :], Mn[:, :], -1.0, None, ALU.mult), drain=True, inc=ev)
            tm = ph.op("vector", lambda q: q.tensor_max(
                M[:, :], M[:, :], Mn[:, :]), drain=True, inc=ev)
        with Ph(P, "u1_x2") as ph:
            ta = ph.op("vector", lambda q: q.tensor_reduce(
                scc[:, 0:1], hl[:, 0:8], AX.X, ALU.max, apply_absolute_value=True),
                inc=ev)
            tb_ = ph.op("vector", lambda q: q.tensor_reduce(
                scc[:, 1:2], hl[:, 8:16], AX.X, ALU.max, apply_absolute_value=True),
                drain=True, inc=ev)
            tt = ph.op("tensor", lambda q: q.transpose(
                p_t[0][0:2, :], scc[:, 0:2], C.ident[:, :]), waits=[tb_], inc=et)
            tr = ph.op("vector", lambda q: q.tensor_reduce(
                sc2[0:2, 0:1], p_t[0][0:2, :], AX.X, ALU.max), waits=[tt], drain=True,
                inc=ev)
            tt2 = ph.op("tensor", lambda q: q.matmul(
                p_r[:, 0:2], sc2[0:2, 0:1], C.ident[0:2, 0:2], start=True, stop=True),
                waits=[tr], inc=et)
            tcpa = ph.op("vector", lambda q: q.tensor_copy(
                srow[:, 0:1], p_r[0:1, 0:1]), waits=[tt2], drain=True, inc=ev)
            tcp = ph.op("vector", lambda q: q.tensor_copy(
                srow[:, PADW - 1:PADW], p_r[0:1, 1:2]), drain=True, inc=ev)
            trows = []
            for j in range(16):
                pt = p_t[j % 2]
                tt = ph.op("tensor", lambda q, j=j, pt=pt: q.transpose(
                    pt[:, :], M[:, j * 128:(j + 1) * 128], C.ident[:, :]),
                    waits=[trows[j - 2][1]] if j >= 2 else [tcp], inc=et)
                tr = ph.op("vector", lambda q, j=j, pt=pt: q.tensor_reduce(
                    scc[:, j:j + 1], pt[:, :], AX.X, ALU.max), waits=[tt], drain=True,
                    inc=ev)
                trows.append((tt, tr))
            tasm = []
            for j in range(16):
                tt2 = ph.op("tensor", lambda q, j=j: q.matmul(
                    p_r[:, :], scc[:, j:j + 1], C.ident[:, :], start=True, stop=True),
                    waits=[trows[j][1]] + ([tasm[j - 1]] if j else []), inc=et)
                tasm.append(ph.op("vector", lambda q, j=j: q.tensor_copy(
                    srow[:, 1 + j * 128:1 + (j + 1) * 128], p_r[0:1, :]),
                    waits=[tt2], drain=True, inc=ev))
            tc1 = ph.op("vector", lambda q: q.tensor_scalar(
                srow[:, :], srow[:, :], 1e-5, None, ALU.max), drain=True, inc=ev)
            td1 = ph.op("vector", lambda q: q.tensor_scalar(
                drow1[:, :], srow[:, :], 1.0 / 127.0, None, ALU.mult), drain=True,
                inc=ev)
            td2 = ph.op("vector", lambda q: q.tensor_scalar(
                drow1[:, :], drow1[:, :], regs[0:1, 0:1], None, ALU.mult), drain=True,
                inc=ev)
            tr1 = ph.op("vector", lambda q: q.reciprocal(s2row[:, :], srow[:, :]),
                        drain=True, inc=ev)
            tr2 = ph.op("vector", lambda q: q.tensor_scalar(
                s2row[:, :], s2row[:, :], 127.0, None, ALU.mult), drain=True, inc=ev)

        with Ph(P, "u1_xq") as ph:
            lastmu = {}
            for g in range(NTG):
                a, b = g * TG, (g + 1) * TG
                pb = p_b[g % 2]
                tbq = ph.op("tensor", lambda q, pb=pb, a=a, b=b: q.matmul(
                    pb[:, :], C.ones_row[:, 0:128], s2row[:, 1 + a:1 + b],
                    start=True, stop=True), waits=[lastmu.get(g - 2)], inc=et)
                for ch in range(NCH):
                    tmu = ph.op("vector", lambda q, ch=ch, a=a, b=b, pb=pb: q.tensor_mul(
                        scr[:, :], C.X[:, ch * T + a:ch * T + b], pb[:, :]),
                        waits=[tbq], drain=True,
                        inc=ev if ch == NCH - 1 else None)
                    rne_clip(ph, (QX1[:, ch * PADW + 1 + a:ch * PADW + 1 + b], ev),
                             (scr[:, :], ()), -128.0, 127.0, (scr[:, :], None))
                lastmu[g] = tmu
            tq1 = ph.op("tensor", lambda q: q.matmul(
                p_b[0][:, 0:1], C.ones_row[:, 0:128], s2row[:, 0:1],
                start=True, stop=True), waits=[lastmu.get(2)], inc=et)
            tq2 = ph.op("tensor", lambda q: q.matmul(
                p_b[0][:, 1:2], C.ones_row[:, 0:128], s2row[:, PADW - 1:PADW],
                start=True, stop=True), inc=et)
            thl = ph.op("vector", lambda q: q.tensor_scalar(
                scc[:, 0:8], hl[:, 0:8], p_b[0][:, 0:1], None, ALU.mult),
                waits=[tq2], drain=True)
            thr = ph.op("vector", lambda q: q.tensor_scalar(
                scc[:, 8:16], hl[:, 8:16], p_b[0][:, 1:2], None, ALU.mult), drain=True)
            rne_clip(ph, (QX1[:, :].rearrange("p (c t) -> p c t", c=NCH)[:, :, 0:1], ev),
                     (scc[:, 0:8].rearrange("p (c u) -> p c u", u=1), ()),
                     -128.0, 127.0, (scc[:, 0:8].rearrange("p (c u) -> p c u", u=1), None))
            rne_clip(ph, (QX1[:, :].rearrange("p (c t) -> p c t", c=NCH)[:, :, PADW - 1:PADW], ev),
                     (scc[:, 8:16].rearrange("p (c u) -> p c u", u=1), ()),
                     -128.0, 127.0, (scc[:, 8:16].rearrange("p (c u) -> p c u", u=1), None))

    with ExitStack() as st:
        E = st.enter_context
        hbuf = E(nc.sbuf_tensor("u1_hbuf", [128, HCH * (TT2 + 2)], F32))
        QH = E(nc.sbuf_tensor("u1_QH", [128, HCH * TT2], BF16))
        wt1 = E(nc.sbuf_tensor("u1_wt1", [128, 2 * 1024], BF16))
        wt2 = E(nc.sbuf_tensor("u1_wt2", [128, 32 * 128], BF16))
        hM = E(nc.sbuf_tensor("u1_hM", [128, TT2], F32))
        hMn = E(nc.sbuf_tensor("u1_hMn", [128, TT2], F32))
        hscc = E(nc.sbuf_tensor("u1_hscc", [128, 2], F32))
        hrow = E(nc.sbuf_tensor("u1_hrow", [1, TT2], F32))
        hrow2 = E(nc.sbuf_tensor("u1_hrow2", [1, TT2], F32))
        cscr = E(nc.sbuf_tensor("u1_cscr", [128, TT2], F32))
        c1s = E(nc.sbuf_tensor("u1_c1s", [128, TT2 + 2], F32))
        d2s = E(nc.sbuf_tensor("u1_d2s", [128, TT2], F32))
        p_h = [E(nc.psum_tensor(f"u1_ph{i}", [128, TT2 + 2], F32)) for i in range(2)]
        p_t = [E(nc.psum_tensor(f"u1_pt2{i}", [128, 128], F32)) for i in range(2)]
        p_r = E(nc.psum_tensor("u1_pr2", [1, 128], F32))
        p_o = [E(nc.psum_tensor(f"u1_po{i}", [128, TT2], F32)) for i in range(2)]

        for ti in range(NT2):
            t0 = ti * TT2
            with Ph(P, f"u1_t{ti}") as ph:
                tb = ph.op("tensor", lambda q: q.matmul(
                    p_h[0][:, 0:TT2 + 2], C.ones_row[:, 0:128],
                    drow1[:, t0:t0 + TT2 + 2], start=True, stop=True), inc=et)
                tc = ph.op("vector", lambda q: q.tensor_copy(c1s[:, :], p_h[0][:, :]),
                           waits=[tb], drain=True, inc=ev)
                wload = {}
                hdone = {}
                mmdone = {}
                for hc in range(HCH):
                    sl = hc % 2
                    wload[hc] = ph.dma("sync", wt1[:, sl * 1024:(sl + 1) * 1024],
                                       qw1.ap().rearrange("p (k h) -> p k h", k=8)
                                       [:, :, hc * 128:(hc + 1) * 128], ld,
                                       waits=[mmdone.get(hc - 2)])
                    tl = ph.settle("sync", ld)
                    po = p_h[hc % 2]
                    w0 = [tc, hdone.get(hc - 2)]
                    for k in range(NCH):
                        mm = ph.op("tensor", lambda q, po=po, sl=sl, k=k: q.matmul(
                            po[:, :], wt1[:, sl * 1024 + k * 128:sl * 1024 + (k + 1) * 128],
                            QX1[:, k * PADW + t0:k * PADW + t0 + TT2 + 2],
                            start=(k == 0), stop=(k == NCH - 1)),
                            waits=(w0 + [tl]) if k == 0 else (),
                            inc=et if k == NCH - 1 else None)
                    mmdone[hc] = mm
                    ho = hc * (TT2 + 2)
                    te1 = ph.op("vector", lambda q, po=po, ho=ho: q.tensor_mul(
                        hbuf[:, ho:ho + TT2 + 2], po[:, :], c1s[:, :]), waits=[mm],
                        drain=True)
                    te2 = ph.op("vector", lambda q, ho=ho, hc=hc: q.tensor_scalar(
                        cscr[:, :], hbuf[:, ho:ho + TT2], cwb[:, 3 * hc:3 * hc + 1],
                        None, ALU.mult), drain=True)
                    te3 = ph.op("vector", lambda q, ho=ho, hc=hc: q.scalar_tensor_tensor(
                        cscr[:, :], hbuf[:, ho + 1:ho + 1 + TT2],
                        cwb[:, 3 * hc + 1:3 * hc + 2], cscr[:, :], ALU.mult, ALU.add),
                        drain=True)
                    te4 = ph.op("vector", lambda q, ho=ho, hc=hc: q.scalar_tensor_tensor(
                        cscr[:, :], hbuf[:, ho + 2:ho + 2 + TT2],
                        cwb[:, 3 * hc + 2:3 * hc + 3], cscr[:, :], ALU.mult, ALU.add),
                        drain=True)
                    te5 = ph.op("vector", lambda q, hc=hc: q.tensor_scalar(
                        cscr[:, :], cscr[:, :], cbb[:, hc:hc + 1], None, ALU.add),
                        drain=True, inc=ev)
                    te6 = ph.op("scalar", lambda q, ho=ho: q.activation(
                        hbuf[:, ho:ho + TT2], cscr[:, :], AF.Silu), waits=[te5],
                        inc=es_)
                    if hc == 0:
                        ph.op("vector", lambda q, ho=ho: q.tensor_copy(
                            hM[:, :], hbuf[:, ho:ho + TT2]), waits=[te6], drain=True)
                        hdone[hc] = ph.op("vector", lambda q, ho=ho: q.tensor_copy(
                            hMn[:, :], hbuf[:, ho:ho + TT2]), drain=True, inc=ev)
                    else:
                        ph.op("vector", lambda q, ho=ho: q.tensor_max(
                            hM[:, :], hM[:, :], hbuf[:, ho:ho + TT2]),
                            waits=[te6], drain=True)
                        hdone[hc] = ph.op("vector", lambda q, ho=ho: q.tensor_tensor(
                            hMn[:, :], hMn[:, :], hbuf[:, ho:ho + TT2], ALU.min),
                            waits=[te6], drain=True, inc=ev)
                thn = ph.op("vector", lambda q: q.tensor_scalar(
                    hMn[:, :], hMn[:, :], -1.0, None, ALU.mult),
                    waits=[hdone[HCH - 1]], drain=True)
                thc = ph.op("vector", lambda q: q.tensor_max(
                    hM[:, :], hM[:, :], hMn[:, :]), drain=True, inc=ev)
                tts = []
                for j in range(2):
                    tt = ph.op("tensor", lambda q, j=j: q.transpose(
                        p_t[j][:, :], hM[:, j * 128:(j + 1) * 128], C.ident[:, :]),
                        waits=[thc], inc=et)
                    tr = ph.op("vector", lambda q, j=j: q.tensor_reduce(
                        hscc[:, j:j + 1], p_t[j][:, :], AX.X, ALU.max), waits=[tt],
                        drain=True, inc=ev)
                    tts.append(tr)
                tasm = []
                for j in range(2):
                    tt2 = ph.op("tensor", lambda q, j=j: q.matmul(
                        p_r[:, :], hscc[:, j:j + 1], C.ident[:, :], start=True,
                        stop=True), waits=[tts[j]] + ([tasm[-1]] if tasm else []),
                        inc=et)
                    tasm.append(ph.op("vector", lambda q, j=j: q.tensor_copy(
                        hrow[:, j * 128:(j + 1) * 128], p_r[0:1, :]), waits=[tt2],
                        drain=True, inc=ev))
                tc1 = ph.op("vector", lambda q: q.tensor_scalar(
                    hrow[:, :], hrow[:, :], 1e-5, None, ALU.max), drain=True, inc=ev)
                td1 = ph.op("vector", lambda q: q.tensor_scalar(
                    hrow2[:, :], hrow[:, :], 1.0 / 127.0, None, ALU.mult), drain=True)
                td2 = ph.op("vector", lambda q: q.tensor_scalar(
                    hrow2[:, :], hrow2[:, :], regs[0:1, 2:3], None, ALU.mult),
                    drain=True, inc=ev)
                tr1 = ph.op("vector", lambda q: q.reciprocal(hrow[:, :], hrow[:, :]),
                            drain=True)
                tr2 = ph.op("vector", lambda q: q.tensor_scalar(
                    hrow[:, :], hrow[:, :], 127.0, None, ALU.mult), drain=True, inc=ev)
                tb1 = ph.op("tensor", lambda q: q.matmul(
                    p_o[0][:, :], C.ones_row[:, 0:128], hrow[:, :], start=True,
                    stop=True), waits=[tr2], inc=et)
                tb2 = ph.op("tensor", lambda q: q.matmul(
                    p_o[1][:, :], C.ones_row[:, 0:128], hrow2[:, :], start=True,
                    stop=True), waits=[td2], inc=et)
                tcs = tb1
                tcd = ph.op("vector", lambda q: q.tensor_copy(d2s[:, :], p_o[1][:, :]),
                            waits=[tb2], drain=True, inc=ev)
                tqh = None
                for hc in range(HCH):
                    ho = hc * (TT2 + 2)
                    ph.op("vector", lambda q, ho=ho: q.tensor_mul(
                        cscr[:, :], hbuf[:, ho:ho + TT2], p_o[0][:, :]),
                        waits=[tcs], drain=True)
                    tqh = rne_clip(ph, (QH[:, hc * TT2:(hc + 1) * TT2], ev),
                                   (cscr[:, :], ()), -128.0, 127.0, (cscr[:, :], None))
                wl2 = {}
                ev2 = {}
                for m in range(NCH):
                    wl2[m] = ph.dma("sync", wt2[:, :],
                                    qw2.ap().rearrange("p (k d) -> p k d", k=32)
                                    [:, :, m * 128:(m + 1) * 128], ld,
                                    waits=[ev2.get(m - 1)])
                    tl2 = ph.settle("sync", ld)
                    po = p_o[m % 2]
                    for k in range(HCH):
                        mm = ph.op("tensor", lambda q, po=po, k=k: q.matmul(
                            po[:, :], wt2[:, k * 128:(k + 1) * 128],
                            QH[:, k * TT2:(k + 1) * TT2],
                            start=(k == 0), stop=(k == HCH - 1)),
                            waits=[tl2, tcs, tcd, tqh, ev2.get(m - 2)] if k == 0 else (),
                            inc=et if k == HCH - 1 else None)
                    tx1 = ph.op("vector", lambda q, po=po: q.tensor_mul(
                        cscr[:, :], po[:, :], d2s[:, :]), waits=[mm, tcd], drain=True)
                    ev2[m] = ph.op("vector", lambda q, m=m: q.tensor_add(
                        C.X[:, m * T + t0:m * T + t0 + TT2],
                        C.X[:, m * T + t0:m * T + t0 + TT2], cscr[:, :]),
                        drain=True, inc=ev)
                P.mark("X", ev2[NCH - 1])

    ustack.__exit__(None, None, None)
    emit_ln(P, C, "u1ln")



def emit_u_l1attn(P, C, W):
    nc = P.nc
    H, DH = 16, 64
    NCG = 3072 // TG

    WqkvT = np.ascontiguousarray(W["L1_attn_W_qkv"].astype(np.float32).T)
    brow = np.ascontiguousarray(W["L1_attn_b_qkv"].astype(np.float32).reshape(1, 3072))
    freqs = (1.0 / 10000.0 ** (np.arange(0, DH, 2, dtype=np.float64) / DH))
    tabs = {k: [] for k in ("c", "s", "cq", "sq")}
    for c in range(NCORES):
        tglob = np.arange(c % 2 * T, c % 2 * T + T, dtype=np.float64)
        ang = tglob[:, None] * freqs[None, :]
        cosv = np.cos(ang).astype(np.float32)
        sinv = np.sin(ang).astype(np.float32)
        tabs["c"].append(np.ascontiguousarray(np.tile(cosv, (1, H))))
        tabs["s"].append(np.ascontiguousarray(np.tile(sinv, (1, H))))
        tabs["cq"].append(np.ascontiguousarray(np.tile(cosv / 8.0, (1, H))))
        tabs["sq"].append(np.ascontiguousarray(np.tile(sinv / 8.0, (1, H))))
    a_w = P.in_("u2_WqkvT", WqkvT)
    a_b = P.in_("u2_brow", brow)
    a_c = P.in_("u2_cos", tabs["c"])
    a_s = P.in_("u2_sin", tabs["s"])
    a_cq = P.in_("u2_cosq", tabs["cq"])
    a_sq = P.in_("u2_sinq", tabs["sq"])

    q_dram = P.dram("u2_q", [T, 1024], BF16)
    ex_in = P.dram("u2_exin", [T, 1024 + H * 65], BF16)
    ex_out = P.dram("u2_exout", [SEQ, 1024 + H * 65], BF16)
    gm_in = P.dram("u2_gmin", [1, 1], F32)
    gm_out = P.dram("u2_gmout", [1, 1], F32)

    ld = P.gsem("dma_sync")
    ldg = P.gsem("dma_gp")
    std = P.gsem("dma_st")
    ccs = P.gsem("cc")
    ev = P.gsem("ev")
    et = P.gsem("et")
    es_ = P.gsem("es")

    ustack = ExitStack()
    ustack.__enter__()
    UE = ustack.enter_context
    regs = UE(nc.sbuf_tensor("u2_regs", [1, 8], F32))
    rowB = UE(nc.sbuf_tensor("u2_rowB", [1, T], F32))

    with ExitStack() as st:
        E = st.enter_context
        Wq = E(nc.sbuf_tensor("u2_W", [128, NCH * 3072], BF16))
        biasr = E(nc.sbuf_tensor("u2_bias", [1, 3072], BF16))
        cosb = E(nc.sbuf_tensor("u2_cosb", [128, 1024], F32))
        sinb = E(nc.sbuf_tensor("u2_sinb", [128, 1024], F32))
        raw = E(nc.sbuf_tensor("u2_raw", [128, 3072], F32))
        scr2 = E(nc.sbuf_tensor("u2_scr2", [128, 1024], F32))
        rot = E(nc.sbuf_tensor("u2_rot", [128, 1024], BF16))
        vt = E(nc.sbuf_tensor("u2_vt", [128, H * 65], BF16))
        acol = E(nc.sbuf_tensor("u2_acol", [128, 4], F32))
        nrm = E(nc.sbuf_tensor("u2_nrm", [128, 16], F32))
        arow = E(nc.sbuf_tensor("u2_arow", [1, 128], F32))
        p_q = [E(nc.psum_tensor(f"u2_pq{i}", [128, TG], F32)) for i in range(2)]
        p_s = E(nc.psum_tensor("u2_psm", [1, 128], F32))

        with Ph(P, "u2_load") as ph:
            ph.dma("gpsimd", Wq[:, :].rearrange("p (c d) -> p c d", c=NCH),
                   a_w.rearrange("(c p) d -> p c d", p=128), ldg)
            ph.dma("gpsimd", biasr[:, :], a_b, ldg)
            tlw = ph.settle("gpsimd", ldg)

        for tt in range(T // 128):
            t0 = tt * 128
            with Ph(P, f"u2_p1_{tt}") as ph:
                ph.dma("sync", cosb[:, 0:512], a_c[t0:t0 + 128, :], ld)
                ph.dma("sync", sinb[:, 0:512], a_s[t0:t0 + 128, :], ld)
                ph.dma("sync", cosb[:, 512:1024], a_cq[t0:t0 + 128, :], ld)
                ph.dma("sync", sinb[:, 512:1024], a_sq[t0:t0 + 128, :], ld)
                tlc = ph.settle("sync", ld)
                evq = {}
                for cg in range(NCG):
                    po = p_q[cg % 2]
                    ph.op("tensor", lambda q, po=po, cg=cg: q.matmul(
                        po[:, :], C.ones_row_bf[:, 0:128],
                        biasr[:, cg * TG:(cg + 1) * TG], start=True, stop=False),
                        waits=[tlw, P.marks.get("Xb"), evq.get(cg - 2)],
                        needs=("constsg",))
                    for k in range(NCH):
                        mm = ph.op("tensor", lambda q, po=po, k=k, cg=cg, t0=t0: q.matmul(
                            po[:, :], C.Xb[:, k * T + t0:k * T + t0 + 128],
                            Wq[:, k * 3072 + cg * TG:k * 3072 + (cg + 1) * TG],
                            start=False, stop=(k == NCH - 1)),
                            inc=et if k == NCH - 1 else None)
                    evq[cg] = ph.op("vector", lambda q, po=po, cg=cg: q.tensor_copy(
                        raw[:, cg * TG:(cg + 1) * TG], po[:, :]), waits=[mm],
                        drain=True, inc=ev)

                def rope(src_off, cos_off, w0):
                    e0 = raw[:, src_off + 0:src_off + 1024:2]
                    e1 = raw[:, src_off + 1:src_off + 1024:2]
                    co = cosb[:, cos_off:cos_off + 512]
                    si = sinb[:, cos_off:cos_off + 512]
                    d0 = rot[:, 0:1024:2]
                    d1 = rot[:, 1:1024:2]
                    ph.op("vector", lambda q: q.tensor_mul(scr2[:, 0:512], e0, co),
                          waits=w0, drain=True)
                    ph.op("vector", lambda q: q.tensor_mul(scr2[:, 512:1024], e1, si),
                          drain=True)
                    ph.op("vector", lambda q: q.tensor_sub(
                        d0, scr2[:, 0:512], scr2[:, 512:1024]), drain=True)
                    ph.op("vector", lambda q: q.tensor_mul(scr2[:, 0:512], e0, si),
                          drain=True)
                    ph.op("vector", lambda q: q.tensor_mul(scr2[:, 512:1024], e1, co),
                          drain=True)
                    return ph.op("vector", lambda q: q.tensor_add(
                        d1, scr2[:, 0:512], scr2[:, 512:1024]), drain=True, inc=ev)

                tq = rope(0, 512, [evq[NCG - 1], tlc])
                tdq = ph.dma("sync", q_dram.ap()[t0:t0 + 128, :], rot[:, :], std,
                             waits=[tq])
                tsq = ph.op("scalar", lambda q: q.activation(
                    scr2[:, :], rot[:, :], AF.Square), waits=[tq, tdq], inc=es_)
                tnr = ph.op("vector", lambda q: q.tensor_reduce(
                    nrm[:, :].rearrange("p (h u) -> p h u", u=1),
                    scr2[:, :].rearrange("p (h d) -> p h d", d=64),
                    AX.X, ALU.add), waits=[tsq], drain=True)
                ph.op("vector", lambda q: q.tensor_reduce(
                    acol[:, 0:1], nrm[:, :], AX.X, ALU.max), drain=True)
                if tt == 0:
                    ph.op("vector", lambda q: q.tensor_copy(acol[:, 2:3], acol[:, 0:1]),
                          drain=True, inc=ev)
                else:
                    ph.op("vector", lambda q: q.tensor_max(
                        acol[:, 2:3], acol[:, 2:3], acol[:, 0:1]), drain=True, inc=ev)
                tk = rope(1024, 0, [tdq])
                tdk = ph.dma("sync", ex_in.ap()[t0:t0 + 128, 0:1024], rot[:, :], std,
                             waits=[tk])
                tsk = ph.op("scalar", lambda q: q.activation(
                    scr2[:, :], rot[:, :], AF.Square), waits=[tk, tdk], inc=es_)
                ph.op("vector", lambda q: q.tensor_reduce(
                    nrm[:, :].rearrange("p (h u) -> p h u", u=1),
                    scr2[:, :].rearrange("p (h d) -> p h d", d=64),
                    AX.X, ALU.add), waits=[tsk], drain=True)
                ph.op("vector", lambda q: q.tensor_reduce(
                    acol[:, 0:1], nrm[:, :], AX.X, ALU.max), drain=True)
                if tt == 0:
                    ph.op("vector", lambda q: q.tensor_copy(acol[:, 3:4], acol[:, 0:1]),
                          drain=True, inc=ev)
                else:
                    ph.op("vector", lambda q: q.tensor_max(
                        acol[:, 3:4], acol[:, 3:4], acol[:, 0:1]), drain=True, inc=ev)
                ph.op("vector", lambda q: q.tensor_copy(
                    vt[:, :].rearrange("p (h j) -> p h j", j=65)[:, :, 0:64],
                    raw[:, 2048:3072].rearrange("p (h j) -> p h j", j=64)),
                    waits=[evq[NCG - 1], tdk], drain=True)
                tv2 = ph.op("vector", lambda q: q.memset(
                    vt[:, :].rearrange("p (h j) -> p h j", j=65)[:, :, 64:65], 1.0),
                    drain=True, inc=ev)
                ph.dma("sync", ex_in.ap()[t0:t0 + 128, 1024:], vt[:, :], std,
                       waits=[tv2])

        with Ph(P, "u2_ex") as ph:
            tst = ph.settle("sync", std)
            tt1 = ph.op("tensor", lambda q: q.matmul(
                p_s[:, :], acol[:, 3:4], C.ident[:, :], start=True, stop=True),
                waits=[ev.tok()], inc=et, needs=("consts",))
            tr1 = ph.op("vector", lambda q: q.tensor_copy(arow[:, :], p_s[0:1, :]),
                        waits=[tt1], drain=True, inc=ev)
            tr2 = ph.op("vector", lambda q: q.reduce_max(
                regs[:, 1:2], arow[:, :], AX.X), drain=True, inc=ev)
            ph.dma("sync", gm_in.ap(), regs[:, 1:2], ld, waits=[tr2])
            tst2 = ph.settle("sync", ld)
            tcc = ph.op("gpsimd", lambda q: q.collective_compute(
                "AllReduce", ALU.max, replica_groups=PAIRS,
                ins=[gm_in.ap().opt()], outs=[gm_out.ap().opt()]), waits=[tst2, tst],
                inc=ccs)
            tcc2 = ph.op("gpsimd", lambda q: q.collective_compute(
                "AllGather", ALU.bypass, replica_groups=PAIRS,
                ins=[ex_in.ap().opt()], outs=[ex_out.ap().opt()]), inc=ccs)
            ph.dma("sync", regs[:, 2:3], gm_out.ap(), ld, waits=[tcc])
            tst3 = ph.settle("sync", ld)
            tt2 = ph.op("tensor", lambda q: q.matmul(
                p_s[:, :], acol[:, 2:3], C.ident[:, :], start=True, stop=True),
                waits=[tr1], inc=et)
            tr3 = ph.op("vector", lambda q: q.tensor_copy(arow[:, :], p_s[0:1, :]),
                        waits=[tt2], drain=True)
            tr4 = ph.op("vector", lambda q: q.reduce_max(
                regs[:, 0:1], arow[:, :], AX.X), drain=True, inc=ev)
            tpr = ph.op("vector", lambda q: q.tensor_mul(
                regs[:, 3:4], regs[:, 0:1], regs[:, 2:3]), waits=[tr4, tst3],
                drain=True, inc=ev)
            tsqr = ph.op("scalar", lambda q: q.activation(
                regs[:, 4:5], regs[:, 3:4], AF.Sqrt), waits=[tpr], inc=es_)
            ph.op("vector", lambda q: q.tensor_scalar(
                regs[:, 4:5], regs[:, 4:5], -1.0, None, ALU.mult), waits=[tsqr],
                drain=True, inc=ev)
            ph.op("vector", lambda q: q.memset(rowB[:, :], 1.0), drain=True)
            tm2 = ph.op("vector", lambda q: q.tensor_scalar(
                rowB[:, :], rowB[:, :], regs[0:1, 4:5], None, ALU.mult), drain=True,
                inc=ev)
            P.mark("u2rowB", tm2)
            P.mark("u2cc", (ccs, ccs.n))

    with ExitStack() as st:
        E = st.enter_context
        Kp = E(nc.sbuf_tensor("u2_Kp", [128, SEQ], BF16))
        Qp = E(nc.sbuf_tensor("u2_Qp", [128, T], BF16))
        Kh = [E(nc.sbuf_tensor(f"u2_Kh{i}", [65, SEQ], BF16)) for i in range(2)]
        Qh = [E(nc.sbuf_tensor(f"u2_Qh{i}", [65, T], BF16)) for i in range(2)]
        Vh = [E(nc.sbuf_tensor(f"u2_Vh{i}", [128, 65], BF16)) for i in range(4)]
        PT = E(nc.sbuf_tensor("u2_PT", [128, T], BF16))
        oscr = E(nc.sbuf_tensor("u2_oscr", [128, TG], F32))
        rden = E(nc.sbuf_tensor("u2_rden", [1, TG], F32))
        p_sc = E(nc.psum_tensor("u2_psc", [128, T], F32))
        p_o = [E(nc.psum_tensor(f"u2_po{i}", [65, TG], F32)) for i in range(4)]

        NJ = SEQ // 128
        prev_evac = None
        prev_copies = None
        pair_tok = None
        for h in range(H):
            h2, par = h // 2, h % 2
            sl = h % 2
            with Ph(P, f"u2_h{h}") as ph:
                if par == 0:
                    ph.dmat("sync", Kp[:, :],
                            ex_out.ap()[:, h2 * 128:(h2 + 1) * 128], ld,
                            waits=[prev_copies, P.marks.get("u2cc")])
                    ph.dmat("sync", Qp[:, :],
                            q_dram.ap()[:, h2 * 128:(h2 + 1) * 128], ld)
                    pair_tok = ph.settle("sync", ld)
                base = 64 * par
                tk1 = ph.op("vector", lambda q, sl=sl, base=base: q.tensor_copy(
                    Kh[sl][0:64, :], Kp[base:base + 64, :]),
                    waits=[pair_tok, prev_evac], drain=True)
                ph.op("vector", lambda q, sl=sl: q.memset(Kh[sl][64:65, :], 1.0),
                      drain=True)
                ph.op("vector", lambda q, sl=sl, base=base: q.tensor_copy(
                    Qh[sl][0:64, :], Qp[base:base + 64, :]), drain=True)
                tq2 = ph.op("vector", lambda q, sl=sl: q.tensor_copy(
                    Qh[sl][64:65, :], rowB[:, :]), drain=True, inc=ev,
                    needs=("u2rowB",))
                prev_copies = tq2
                texp = None
                tavd = None
                vload = {}
                avdone = {}
                for j in range(NJ):
                    vload[j] = ph.dma("sync", Vh[j % 4][:, :],
                                      ex_out.ap()[j * 128:(j + 1) * 128,
                                                  1024 + h * 65:1024 + (h + 1) * 65],
                                      ld, waits=[avdone.get(j - 4, prev_evac)])
                    tvl = ph.settle("sync", ld)
                    for ig in range(4):
                        mm = ph.op("tensor", lambda q, sl=sl, j=j, ig=ig: q.matmul(
                            p_sc[:, ig * TG:(ig + 1) * TG],
                            Kh[sl][:, j * 128:(j + 1) * 128],
                            Qh[sl][:, ig * TG:(ig + 1) * TG],
                            start=True, stop=True),
                            waits=[tq2, texp] if ig == 0 else (),
                            inc=et if ig == 3 else None)
                    texp = ph.op("scalar", lambda q: q.activation(
                        PT[:, :], p_sc[:, :], AF.Exp), waits=[mm, tavd], inc=es_)
                    for ig in range(4):
                        av = ph.op("tensor", lambda q, j=j, ig=ig: q.matmul(
                            p_o[ig][:, :], Vh[j % 4][:, :], PT[:, ig * TG:(ig + 1) * TG],
                            start=(j == 0), stop=(j == NJ - 1)),
                            waits=[texp, tvl] if ig == 0 else (),
                            inc=et if ig == 3 else None)
                    tavd = av
                    avdone[j] = av
                ch = h // 2
                tbc = None
                for ig in range(4):
                    trd = ph.op("vector", lambda q, ig=ig: q.reciprocal(
                        rden[:, :], p_o[ig][64:65, :]), waits=[tavd, tbc], drain=True,
                        inc=ev)
                    tbc = ph.op("tensor", lambda q, ig=ig: q.matmul(
                        p_sc[:, 0:TG], C.ones_row[:, 0:128], rden[:, :],
                        start=True, stop=True), waits=[trd, texp], inc=et)
                    tcp = ph.op("scalar", lambda q, ig=ig, base=base: q.activation(
                        oscr[base:base + 64, :], p_o[ig][0:64, :], AF.Copy),
                        waits=[tavd, prev_evac], inc=es_)
                    tml = ph.op("vector", lambda q, base=base: q.tensor_mul(
                        oscr[base:base + 64, :], oscr[base:base + 64, :],
                        p_sc[base:base + 64, 0:TG]),
                        waits=[tbc, tcp], drain=True)
                    prev_evac = ph.op("vector", lambda q, ig=ig, ch=ch, base=base: q.tensor_add(
                        C.X[base:base + 64, ch * T + ig * TG:ch * T + (ig + 1) * TG],
                        C.X[base:base + 64, ch * T + ig * TG:ch * T + (ig + 1) * TG],
                        oscr[base:base + 64, :]), drain=True, inc=ev)
        P.mark("X", prev_evac)

    ustack.__exit__(None, None, None)
    emit_ln(P, C, "u2ln")



def emit_u_l3ffn(P, C, W):
    nc = P.nc
    ewT = W["L3_ffn_expert_w"].astype(np.float32).transpose(0, 2, 1)
    ewT = np.ascontiguousarray(ewT.reshape(8 * 1024, 1024))
    gwT = np.ascontiguousarray(W["L3_ffn_gate_w"].astype(np.float32).T)
    gbr = np.ascontiguousarray(W["L3_ffn_gate_b"].astype(np.float32).reshape(1, 8))
    ebc = np.ascontiguousarray(
        W["L3_ffn_expert_b"].astype(np.float32).reshape(8, 8, 128).transpose(2, 0, 1).reshape(128, 64))
    a_ew = P.in_("u7_ewT", ewT)
    a_gw = P.in_("u7_gwT", gwT)
    a_gb = P.in_("u7_gb", gbr)
    a_eb = P.in_("u7_eb", ebc)
    pl_in = P.dram("u7_plin", [128, 8], F32)
    pl_out = P.dram("u7_plout", [128, 8], F32)

    ld = P.gsem("dma_sync")
    ldg = P.gsem("dma_gp")
    ccs = P.gsem("cc")
    ev = P.gsem("ev")
    es_ = P.gsem("es")
    et = P.gsem("et")

    with ExitStack() as st:
        E = st.enter_context
        Wt = [E(nc.sbuf_tensor(f"u7_Wt{i}", [128, NCH * 1024], BF16)) for i in range(2)]
        gw = E(nc.sbuf_tensor("u7_gw", [128, NCH * 8], F32))
        gb = E(nc.sbuf_tensor("u7_gbs", [1, 8], F32))
        ebb = E(nc.sbuf_tensor("u7_ebs", [128, 64], F32))
        pool = E(nc.sbuf_tensor("u7_pool", [128, 8], F32))
        lg = E(nc.sbuf_tensor("u7_lg", [1, 8], F32))
        r1 = E(nc.sbuf_tensor("u7_r1", [1, 8], F32))
        r2 = E(nc.sbuf_tensor("u7_r2", [1, 8], F32))
        gates = E(nc.sbuf_tensor("u7_gates", [1, 8], F32))
        gcol = E(nc.sbuf_tensor("u7_gcol", [128, 8], F32))
        scr = E(nc.sbuf_tensor("u7_scr", [128, TG], F32))
        p_g = E(nc.psum_tensor("u7_pg", [1, 8], F32))
        p_b = E(nc.psum_tensor("u7_pb", [128, 8], F32))
        p_o = [E(nc.psum_tensor(f"u7_po{i}", [128, TG], F32)) for i in range(2)]

        with Ph(P, "u7_gate") as ph:
            ph.dma("sync", gw[:, :].rearrange("p (c g) -> p c g", g=8),
                   a_gw.rearrange("(c p) g -> p c g", p=128), ld)
            ph.dma("sync", gb[:, :], a_gb, ld)
            ph.dma("sync", ebb[:, :], a_eb, ld)
            tl = ph.settle("sync", ld)
            for ch in range(NCH):
                ph.op("vector", lambda q, ch=ch: q.tensor_reduce(
                    pool[:, ch:ch + 1], C.X[:, ch * T:(ch + 1) * T], AX.X, ALU.add),
                    needs=("X",))
            tps = ph.op("vector", lambda q: q.tensor_scalar(
                pool[:, :], pool[:, :], 1.0 / SEQ, None, ALU.mult), drain=True, inc=ev)
            ph.dma("sync", pl_in.ap(), pool[:, :], ld, waits=[tps])
            tst = ph.settle("sync", ld)
            tcc = ph.op("gpsimd", lambda q: q.collective_compute(
                "AllReduce", ALU.add, replica_groups=PAIRS,
                ins=[pl_in.ap().opt()], outs=[pl_out.ap().opt()]), waits=[tst],
                inc=ccs)
            ph.dma("sync", pool[:, :], pl_out.ap(), ld, waits=[tcc])
            tst2 = ph.settle("sync", ld)
            for ch in range(NCH):
                tmg = ph.op("tensor", lambda q, ch=ch: q.matmul(
                    p_g[:, :], pool[:, ch:ch + 1], gw[:, ch * 8:(ch + 1) * 8],
                    start=(ch == 0), stop=(ch == NCH - 1)),
                    waits=[tst2, tl], inc=et if ch == NCH - 1 else None)
            t1 = ph.op("vector", lambda q: q.tensor_add(lg[:, :], p_g[0:1, :], gb[:, :]),
                       waits=[tmg], drain=True, inc=ev)
            t2 = ph.op("vector", lambda q: q.reduce_max(r1[:, 0:1], lg[:, :], AX.X),
                       drain=True)
            t3 = ph.op("vector", lambda q: q.tensor_scalar(
                r2[:, :], lg[:, :], r1[:, 0:1], None, ALU.is_equal), drain=True)
            t4 = ph.op("vector", lambda q: q.tensor_reduce(
                r1[:, 1:2], r2[:, :], AX.X, ALU.add), drain=True)
            t5 = ph.op("vector", lambda q: q.scalar_tensor_tensor(
                r2[:, :], r2[:, :], -1e30, lg[:, :], ALU.mult, ALU.add), drain=True)
            t6 = ph.op("vector", lambda q: q.reduce_max(r1[:, 2:3], r2[:, :], AX.X),
                       drain=True)
            t7 = ph.op("vector", lambda q: q.tensor_scalar(
                r1[:, 3:4], r1[:, 1:2], 2.0, None, ALU.is_ge), drain=True)
            t8 = ph.op("vector", lambda q: q.tensor_mul(
                r1[:, 4:5], r1[:, 3:4], r1[:, 0:1]), drain=True)
            t9 = ph.op("vector", lambda q: q.tensor_scalar(
                r1[:, 5:6], r1[:, 3:4], -1.0, 1.0, ALU.mult, ALU.add), drain=True)
            t10 = ph.op("vector", lambda q: q.tensor_mul(
                r1[:, 5:6], r1[:, 5:6], r1[:, 2:3]), drain=True)
            t11 = ph.op("vector", lambda q: q.tensor_add(
                r1[:, 4:5], r1[:, 4:5], r1[:, 5:6]), drain=True)
            t12 = ph.op("vector", lambda q: q.tensor_scalar(
                r2[:, :], lg[:, :], r1[:, 4:5], None, ALU.is_ge), drain=True)
            t13 = ph.op("vector", lambda q: q.tensor_scalar(
                r1[:, 6:7], r1[:, 0:1], -1.0, None, ALU.mult), drain=True, inc=ev)
            t14 = ph.op("scalar", lambda q: q.activation(
                gates[:, :], lg[:, :], AF.Exp, bias=r1[:, 6:7]), waits=[t13], inc=es_)
            t15 = ph.op("vector", lambda q: q.tensor_mul(
                gates[:, :], gates[:, :], r2[:, :]), waits=[t14], drain=True)
            t16 = ph.op("vector", lambda q: q.tensor_reduce(
                r1[:, 7:8], gates[:, :], AX.X, ALU.add), drain=True)
            t17 = ph.op("vector", lambda q: q.reciprocal(r1[:, 6:7], r1[:, 7:8]),
                        drain=True)
            t18 = ph.op("vector", lambda q: q.tensor_scalar(
                gates[:, :], gates[:, :], r1[:, 6:7], None, ALU.mult), drain=True,
                inc=ev)
            tbg = ph.op("tensor", lambda q: q.matmul(
                p_b[:, :], C.ones_row[:, 0:128], gates[:, :], start=True, stop=True),
                waits=[t18], inc=et, needs=("consts",))
            ph.op("vector", lambda q: q.tensor_copy(gcol[:, :], p_b[:, :]),
                  waits=[tbg], drain=True, inc=ev)
            P.mark("u7gates", (ev, ev.n))

        wload = {}
        prev = {}
        for e in range(8):
            with Ph(P, f"u7_e{e}") as ph:
                sl = e % 2
                wload[e] = ph.dma("gpsimd", Wt[sl][:, :].rearrange("p (c d) -> p c d", c=NCH),
                                  a_ew[e * 1024:(e + 1) * 1024, :].rearrange(
                                      "(c p) d -> p c d", p=128), ldg,
                                  waits=[prev.get(e - 2)])
                tlw = ph.settle("gpsimd", ldg)
                i = 0
                evt = {}
                for m in range(NCH):
                    for g in range(NTG):
                        po = p_o[i % 2]
                        a, b = g * TG, (g + 1) * TG
                        for k in range(NCH):
                            mm = ph.op("tensor", lambda q, po=po, sl=sl, k=k, m=m, a=a, b=b: q.matmul(
                                po[:, :], Wt[sl][:, k * 1024 + m * 128:k * 1024 + (m + 1) * 128],
                                C.Xb[:, k * T + a:k * T + b],
                                start=(k == 0), stop=(k == NCH - 1)),
                                waits=[tlw, evt.get(i - 2), P.marks.get("Xb")] if k == 0 else (),
                                inc=et if k == NCH - 1 else None)
                        tge = ph.op("scalar", lambda q, po=po, m=m, e=e: q.activation(
                            scr[:, :], po[:, :], AF.Gelu, bias=ebb[:, e * 8 + m:e * 8 + m + 1]),
                            waits=[mm, evt.get(i - 1)], inc=es_, needs=("u7gates",))
                        evt[i] = ph.op("vector", lambda q, m=m, a=a, b=b, e=e: q.scalar_tensor_tensor(
                            C.X[:, m * T + a:m * T + b], scr[:, :], gcol[:, e:e + 1],
                            C.X[:, m * T + a:m * T + b], ALU.mult, ALU.add),
                            waits=[tge], drain=True, inc=ev)
                        i += 1
                prev[e] = evt[i - 1]
        P.mark("X", prev[7])

    emit_ln(P, C, "u7ln")



def emit_u_l2ffn(P, C, W):
    nc = P.nc
    NE = 4
    FF = 4096
    FCH = FF // 128
    w1h = np.ascontiguousarray(W["L2_ffn_W1"].astype(np.float32).reshape(NE * 1024, FF))
    w2h = np.ascontiguousarray(W["L2_ffn_W2"].astype(np.float32).reshape(NE * FF, 1024))
    b1c = np.ascontiguousarray(
        W["L2_ffn_b1"].astype(np.float32).reshape(NE, FCH, 128).transpose(2, 0, 1).reshape(128, NE * FCH))
    b2c = np.ascontiguousarray(
        W["L2_ffn_b2"].astype(np.float32).reshape(NE, NCH, 128).transpose(2, 0, 1).reshape(128, NE * NCH))
    wu = np.ascontiguousarray(W["L2_ffn_Wu"].astype(np.float32))
    bu = np.ascontiguousarray(W["L2_ffn_bu"].astype(np.float32).reshape(1, 1))
    irows, jrows = [], []
    for c in range(NCORES):
        tg = np.arange(c % 2 * T, c % 2 * T + T)
        ie = ((np.arange(NE)[:, None] - tg[None, :]) % NE + 1).astype(np.float32)
        irows.append(np.ascontiguousarray(1.0 / ie))
        jrows.append(np.ascontiguousarray(ie - 1.0))
    a_w1 = P.in_("u5_w1", w1h)
    a_w2 = P.in_("u5_w2", w2h)
    a_b1 = P.in_("u5_b1", b1c)
    a_b2 = P.in_("u5_b2", b2c)
    a_wu = P.in_("u5_wu", wu)
    a_bu = P.in_("u5_bu", bu)
    a_ir = P.in_("u5_ir", irows)
    a_jr = P.in_("u5_jr", jrows)

    ld = P.gsem("dma_sync")
    ldg = P.gsem("dma_gp")
    ev = P.gsem("ev")
    es_ = P.gsem("es")
    et = P.gsem("et")

    with ExitStack() as st:
        E = st.enter_context
        w1t = [E(nc.sbuf_tensor(f"u5_w1t{i}", [128, NCH * 128], BF16)) for i in range(2)]
        w2t = [E(nc.sbuf_tensor(f"u5_w2t{i}", [128, FCH * 128], BF16)) for i in range(2)]
        b1b = E(nc.sbuf_tensor("u5_b1s", [128, NE * FCH], F32))
        b2b = E(nc.sbuf_tensor("u5_b2s", [128, NE * NCH], F32))
        wub = E(nc.sbuf_tensor("u5_wus", [128, NCH], F32))
        irb = E(nc.sbuf_tensor("u5_irs", [1, TG], F32))
        jrb = E(nc.sbuf_tensor("u5_jrs", [1, TG], F32))
        urow = E(nc.sbuf_tensor("u5_urow", [1, T], F32))
        u4row = E(nc.sbuf_tensor("u5_u4", [1, T], F32))
        crow = E(nc.sbuf_tensor("u5_crow", [1, T], F32))
        ht = E(nc.sbuf_tensor("u5_ht", [128, FCH * TG], BF16))
        cbs = E(nc.sbuf_tensor("u5_cbs", [128, TG], F32))
        scr = E(nc.sbuf_tensor("u5_scr", [128, TG], F32))
        bub = E(nc.sbuf_tensor("u5_bub", [1, 1], F32))
        p_u = E(nc.psum_tensor("u5_pu", [1, T], F32))
        p_h = [E(nc.psum_tensor(f"u5_ph{i}", [128, TG], F32)) for i in range(2)]
        p_c = E(nc.psum_tensor("u5_pc", [128, TG], F32))

        with Ph(P, "u5_u") as ph:
            ph.dma("sync", b1b[:, :], a_b1, ld)
            ph.dma("sync", b2b[:, :], a_b2, ld)
            ph.dma("sync", wub[:, :].rearrange("p (c u) -> p c u", u=1),
                   a_wu.rearrange("(c p) u -> p c u", p=128), ld,
                   allow_slow_non_contiguous=True)

            tl = ph.settle("sync", ld)
            for ch in range(NCH):
                for g in range(NTG):
                    a, b = g * TG, (g + 1) * TG
                    mmu = ph.op("tensor", lambda q, ch=ch, a=a, b=b: q.matmul(
                        p_u[:, a:b], wub[:, ch:ch + 1], C.X[:, ch * T + a:ch * T + b],
                        start=(ch == 0), stop=(ch == NCH - 1)),
                        waits=[tl] if ch == 0 and a == 0 else (),
                        needs=("X",), inc=et if ch == NCH - 1 and g == NTG - 1 else None)
            t1 = ph.op("vector", lambda q: q.tensor_copy(urow[:, :], p_u[0:1, :]),
                       waits=[mmu], drain=True, inc=ev)
            pass
        with Ph(P, "u5_u2") as ph:
            ph.dma("sync", bub[:, :], a_bu, ld)
            tl2 = ph.settle("sync", ld)
            t2 = ph.op("vector", lambda q: q.tensor_scalar(
                urow[:, :], urow[:, :], bub[0:1, 0:1], None, ALU.add),
                waits=[tl2], drain=True, inc=ev)
            t3 = ph.op("scalar", lambda q: q.activation(
                crow[:, :], urow[:, :], AF.Exp, scale=-1.0), waits=[t2], inc=es_,
                drain=True)
            t4 = ph.op("vector", lambda q: q.tensor_scalar(
                crow[:, :], crow[:, :], 1.0, None, ALU.add), waits=[t3], drain=True)
            t5 = ph.op("vector", lambda q: q.reciprocal(urow[:, :], crow[:, :]),
                       drain=True)
            t6 = ph.op("vector", lambda q: q.tensor_scalar(
                u4row[:, :], urow[:, :], 4.0, None, ALU.mult), drain=True, inc=ev)
            P.mark("u5u", t6)

        prevh = {}
        for e in range(NE):
            for g in range(NTG):
                a, b = g * TG, (g + 1) * TG
                with Ph(P, f"u5_e{e}g{g}") as ph:
                    ph.dma("sync", irb[:, :], a_ir[e:e + 1, a:b], ld,
                           waits=[prevh.get((e, g - 1), prevh.get((e - 1, NTG - 1)))])
                    ph.dma("sync", jrb[:, :], a_jr[e:e + 1, a:b], ld)
                    tlr = ph.settle("sync", ld)
                    tc1 = ph.op("vector", lambda q, e=e, a=a, b=b: q.tensor_tensor(
                        crow[:, a:b], u4row[:, a:b], jrb[:, :],
                        ALU.is_gt), needs=("u5u",), waits=[tlr], drain=True)
                    tc2 = ph.op("vector", lambda q, e=e, a=a, b=b: q.tensor_mul(
                        crow[:, a:b], crow[:, a:b], urow[:, a:b]), drain=True)
                    tc3 = ph.op("vector", lambda q, e=e, a=a, b=b: q.tensor_mul(
                        crow[:, a:b], crow[:, a:b], irb[:, :]),
                        drain=True, inc=ev)
                    tbc = ph.op("tensor", lambda q, a=a, b=b: q.matmul(
                        p_c[:, :], C.ones_row[:, 0:128], crow[:, a:b],
                        start=True, stop=True), waits=[tc3], inc=et,
                        needs=("consts",))
                    tcb = ph.op("vector", lambda q: q.tensor_copy(cbs[:, :], p_c[:, :]),
                                waits=[tbc], drain=True, inc=ev)
                    hdone = {}
                    for fc in range(FCH):
                        sl = fc % 2
                        wl = ph.dma("gpsimd", w1t[sl][:, :].rearrange("p (c d) -> p c d", c=NCH),
                                    a_w1[e * 1024:(e + 1) * 1024, fc * 128:(fc + 1) * 128]
                                    .rearrange("(c p) d -> p c d", p=128), ldg,
                                    waits=[hdone.get(fc - 2)])
                        tlw = ph.settle("gpsimd", ldg)
                        po = p_h[fc % 2]
                        for k in range(NCH):
                            mm = ph.op("tensor", lambda q, po=po, sl=sl, k=k, a=a, b=b: q.matmul(
                                po[:, :], w1t[sl][:, k * 128:(k + 1) * 128],
                                C.Xb[:, k * T + a:k * T + b],
                                start=(k == 0), stop=(k == NCH - 1)),
                                waits=[tlw, hdone.get(fc - 2), P.marks.get("Xb")] if k == 0 else (),
                                inc=et if k == NCH - 1 else None)
                        hdone[fc] = ph.op("scalar", lambda q, po=po, fc=fc, e=e: q.activation(
                            ht[:, fc * TG:(fc + 1) * TG], po[:, :], AF.Relu,
                            bias=b1b[:, e * FCH + fc:e * FCH + fc + 1]), waits=[mm],
                            inc=es_)
                    mdone = {}
                    for m in range(NCH):
                        sl = m % 2
                        wl2 = ph.dma("gpsimd", w2t[sl][:, :].rearrange("p (c d) -> p c d", c=FCH),
                                     a_w2[e * FF:(e + 1) * FF, m * 128:(m + 1) * 128]
                                     .rearrange("(c p) d -> p c d", p=128), ldg,
                                     waits=[mdone.get(m - 2)])
                        tlw2 = ph.settle("gpsimd", ldg)
                        po = p_h[m % 2]
                        for k in range(FCH):
                            mm = ph.op("tensor", lambda q, po=po, sl=sl, k=k: q.matmul(
                                po[:, :], w2t[sl][:, k * 128:(k + 1) * 128],
                                ht[:, k * TG:(k + 1) * TG],
                                start=(k == 0), stop=(k == FCH - 1)),
                                waits=[tlw2, hdone[FCH - 1], mdone.get(m - 2)] if k == 0 else (),
                                inc=et if k == FCH - 1 else None)
                        ts1 = ph.op("vector", lambda q, po=po, m=m, e=e: q.scalar_tensor_tensor(
                            scr[:, :], po[:, :], b2b[:, e * NCH + m:e * NCH + m + 1],
                            cbs[:, :], ALU.add, ALU.mult), waits=[mm, tcb], drain=True)
                        mdone[m] = ph.op("vector", lambda q, m=m, a=a, b=b: q.tensor_add(
                            C.X[:, m * T + a:m * T + b], C.X[:, m * T + a:m * T + b],
                            scr[:, :]), drain=True, inc=ev)
                    prevh[(e, g)] = mdone[NCH - 1]
        P.mark("X", prevh[(NE - 1, NTG - 1)])

    emit_ln(P, C, "u5ln")



def emit_u_l3attn(P, C, W):
    nc = P.nc
    H, DH = 16, 64
    NCG = 3072 // TG

    Wqkv = np.ascontiguousarray(W["L3_attn_Wqkv"].astype(np.float32))
    WoT = np.ascontiguousarray(W["L3_attn_Wout"].astype(np.float32))
    bout_c = np.ascontiguousarray(W["L3_attn_bout"].astype(np.float32).reshape(NCH, 128).T)
    i_idx = np.arange(DH // 2, dtype=np.float64)
    theta = 10000.0 ** (-2.0 * (i_idx - 1.0) / DH)
    tabs = {k: [] for k in ("c", "s", "cq", "sq")}
    for c in range(NCORES):
        tglob = np.arange(c % 2 * T, c % 2 * T + T, dtype=np.float64)
        ang = tglob[:, None] * theta[None, :]
        cosv = np.cos(ang).astype(np.float32)
        sinv = np.sin(ang).astype(np.float32)
        tabs["c"].append(np.ascontiguousarray(np.tile(cosv, (1, H))))
        tabs["s"].append(np.ascontiguousarray(np.tile(sinv, (1, H))))
        tabs["cq"].append(np.ascontiguousarray(np.tile(cosv / 8.0, (1, H))))
        tabs["sq"].append(np.ascontiguousarray(np.tile(sinv / 8.0, (1, H))))
    camask = np.zeros((128, 1024), np.float32)
    for p in range(128):
        camask[p, 512 + p:] = 1.0
    a_w = P.in_("u6_Wqkv", Wqkv)
    a_wo = P.in_("u6_WoT", WoT)
    a_bo = P.in_("u6_bout", bout_c)
    a_c = P.in_("u6_cos", tabs["c"])
    a_s = P.in_("u6_sin", tabs["s"])
    a_cq = P.in_("u6_cosq", tabs["cq"])
    a_sq = P.in_("u6_sinq", tabs["sq"])
    a_cm = P.in_("u6_camask", camask)

    q_dram = P.dram("u6_q", [T, 2048], BF16)
    ex_in = P.dram("u6_exin", [T, 2048 + H * 65], BF16)
    ex_out = P.dram("u6_exout", [SEQ, 2048 + H * 65], BF16)
    gm_in = P.dram("u6_gmin", [1, 1], F32)
    gm_out = P.dram("u6_gmout", [1, 1], F32)

    ld = P.gsem("dma_sync")
    ldg = P.gsem("dma_gp")
    std = P.gsem("dma_st")
    ccs = P.gsem("cc")
    ev = P.gsem("ev")
    et = P.gsem("et")
    es_ = P.gsem("es")

    ustack = ExitStack(); ustack.__enter__(); UE = ustack.enter_context
    regs = UE(nc.sbuf_tensor("u6_regs", [1, 8], F32))
    rowB = UE(nc.sbuf_tensor("u6_rowB", [1, T], F32))
    qbase = None
    a_h0 = P.in_("u6_h0", [np.full((128, 1), float(1 - c % 2), np.float32)
                           for c in range(NCORES)])
    a_h1 = P.in_("u6_h1", [np.full((128, 1), float(c % 2), np.float32)
                           for c in range(NCORES)])
    h0c = UE(nc.sbuf_tensor("u6_h0s", [128, 1], F32))
    h1c = UE(nc.sbuf_tensor("u6_h1s", [128, 1], F32))

    camask_ext = np.zeros((128, 2176), np.float32)
    for p in range(128):
        camask_ext[p, 512 + p:] = 1.0
    a_cme = P.in_("u6_cme", camask_ext)
    cmske = UE(nc.sbuf_tensor("u6_cmske", [128, 2176], BF16))

    with ExitStack() as st:
        E = st.enter_context
        Wq = E(nc.sbuf_tensor("u6_W", [128, NCH * 3072], BF16))
        cosb = E(nc.sbuf_tensor("u6_cosb", [128, 1024], F32))
        sinb = E(nc.sbuf_tensor("u6_sinb", [128, 1024], F32))
        raw = E(nc.sbuf_tensor("u6_raw", [128, 3072], F32))
        scr2 = E(nc.sbuf_tensor("u6_scr2", [128, 1024], F32))
        rot = E(nc.sbuf_tensor("u6_rot", [128, 1024], BF16))
        vt = E(nc.sbuf_tensor("u6_vt", [128, H * 65], BF16))
        acol = E(nc.sbuf_tensor("u6_acol", [128, 4], F32))
        arow = E(nc.sbuf_tensor("u6_arow", [1, 128], F32))
        nrm = E(nc.sbuf_tensor("u6_nrm", [128, 16], F32))
        p_q = [E(nc.psum_tensor(f"u6_pq{i}", [128, TG], F32)) for i in range(2)]
        p_s = E(nc.psum_tensor("u6_psm", [1, 128], F32))

        with Ph(P, "u6_load") as ph:
            ph.dma("gpsimd", Wq[:, :].rearrange("p (c d) -> p c d", c=NCH),
                   a_w.rearrange("(c p) d -> p c d", p=128), ldg)
            ph.dma("gpsimd", cmske[:, :], a_cme, ldg)
            ph.dma("sync", h0c[:, :], a_h0, ld)
            ph.dma("sync", h1c[:, :], a_h1, ld)
            tlw = ph.settle("gpsimd", ldg)
            ph.settle("sync", ld)

        for tt in range(T // 128):
            t0 = tt * 128
            with Ph(P, f"u6_p1_{tt}") as ph:
                ph.dma("sync", cosb[:, 0:512], a_c[t0:t0 + 128, :], ld)
                ph.dma("sync", sinb[:, 0:512], a_s[t0:t0 + 128, :], ld)
                ph.dma("sync", cosb[:, 512:1024], a_cq[t0:t0 + 128, :], ld)
                ph.dma("sync", sinb[:, 512:1024], a_sq[t0:t0 + 128, :], ld)
                tlc = ph.settle("sync", ld)
                evq = {}
                for cg in range(NCG):
                    po = p_q[cg % 2]
                    for k in range(NCH):
                        mm = ph.op("tensor", lambda q, po=po, k=k, cg=cg, t0=t0: q.matmul(
                            po[:, :], C.Xb[:, k * T + t0:k * T + t0 + 128],
                            Wq[:, k * 3072 + cg * TG:k * 3072 + (cg + 1) * TG],
                            start=(k == 0), stop=(k == NCH - 1)),
                            waits=[tlw, P.marks.get("Xb"), evq.get(cg - 2)] if k == 0 else (),
                            inc=et if k == NCH - 1 else None)
                    evq[cg] = ph.op("vector", lambda q, po=po, cg=cg: q.tensor_copy(
                        raw[:, cg * TG:(cg + 1) * TG], po[:, :]), waits=[mm],
                        drain=True, inc=ev)

                def rope6(src_off, cos_off, w0):
                    e0 = raw[:, src_off + 0:src_off + 1024:2]
                    e1 = raw[:, src_off + 1:src_off + 1024:2]
                    co = cosb[:, cos_off:cos_off + 512]
                    si = sinb[:, cos_off:cos_off + 512]
                    d0 = rot[:, 0:1024:2]
                    d1 = rot[:, 1:1024:2]
                    ph.op("vector", lambda q: q.tensor_mul(scr2[:, 0:512], e0, co),
                          waits=w0, drain=True)
                    ph.op("vector", lambda q: q.tensor_mul(scr2[:, 512:1024], e1, si),
                          drain=True)
                    ph.op("vector", lambda q: q.tensor_sub(
                        d0, scr2[:, 0:512], scr2[:, 512:1024]), drain=True)
                    ph.op("vector", lambda q: q.tensor_mul(scr2[:, 0:512], e0, si),
                          drain=True)
                    ph.op("vector", lambda q: q.tensor_mul(scr2[:, 512:1024], e1, co),
                          drain=True)
                    return ph.op("vector", lambda q: q.tensor_add(
                        d1, scr2[:, 0:512], scr2[:, 512:1024]), drain=True, inc=ev)

                tq = rope6(0, 512, [evq[NCG - 1], tlc])
                tdq = ph.dma("sync", q_dram.ap()[t0:t0 + 128, 0:1024], rot[:, :], std,
                             waits=[tq])
                tqu = ph.op("vector", lambda q: q.tensor_scalar(
                    rot[:, :], raw[:, 0:1024], 1.0 / 8.0, None, ALU.mult),
                    waits=[tdq], drain=True, inc=ev)
                tdqu = ph.dma("sync", q_dram.ap()[t0:t0 + 128, 1024:2048], rot[:, :],
                              std, waits=[tqu])
                tsq = ph.op("scalar", lambda q: q.activation(
                    scr2[:, :], rot[:, :], AF.Square), waits=[tqu, tdqu], inc=es_)
                ph.op("vector", lambda q: q.tensor_reduce(
                    nrm[:, :].rearrange("p (h u) -> p h u", u=1),
                    scr2[:, :].rearrange("p (h d) -> p h d", d=64),
                    AX.X, ALU.add), waits=[tsq], drain=True)
                ph.op("vector", lambda q: q.tensor_reduce(
                    acol[:, 0:1], nrm[:, :], AX.X, ALU.max), drain=True)
                if tt == 0:
                    ph.op("vector", lambda q: q.tensor_copy(acol[:, 2:3], acol[:, 0:1]),
                          drain=True, inc=ev)
                else:
                    ph.op("vector", lambda q: q.tensor_max(
                        acol[:, 2:3], acol[:, 2:3], acol[:, 0:1]), drain=True, inc=ev)
                tk = rope6(1024, 0, [tdqu])
                tdk = ph.dma("sync", ex_in.ap()[t0:t0 + 128, 0:1024], rot[:, :], std,
                             waits=[tk])
                tku = ph.op("vector", lambda q: q.tensor_copy(
                    rot[:, :], raw[:, 1024:2048]), waits=[tdk], drain=True, inc=ev)
                tdku = ph.dma("sync", ex_in.ap()[t0:t0 + 128, 1024:2048], rot[:, :],
                              std, waits=[tku])
                tsk = ph.op("scalar", lambda q: q.activation(
                    scr2[:, :], rot[:, :], AF.Square), waits=[tku, tdku], inc=es_)
                ph.op("vector", lambda q: q.tensor_reduce(
                    nrm[:, :].rearrange("p (h u) -> p h u", u=1),
                    scr2[:, :].rearrange("p (h d) -> p h d", d=64),
                    AX.X, ALU.add), waits=[tsk], drain=True)
                ph.op("vector", lambda q: q.tensor_reduce(
                    acol[:, 0:1], nrm[:, :], AX.X, ALU.max), drain=True)
                if tt == 0:
                    ph.op("vector", lambda q: q.tensor_copy(acol[:, 3:4], acol[:, 0:1]),
                          drain=True, inc=ev)
                else:
                    ph.op("vector", lambda q: q.tensor_max(
                        acol[:, 3:4], acol[:, 3:4], acol[:, 0:1]), drain=True, inc=ev)
                ph.op("vector", lambda q: q.tensor_copy(
                    vt[:, :].rearrange("p (h j) -> p h j", j=65)[:, :, 0:64],
                    raw[:, 2048:3072].rearrange("p (h j) -> p h j", j=64)),
                    waits=[evq[NCG - 1], tdku], drain=True)
                tv2 = ph.op("vector", lambda q: q.memset(
                    vt[:, :].rearrange("p (h j) -> p h j", j=65)[:, :, 64:65], 1.0),
                    drain=True, inc=ev)
                ph.dma("sync", ex_in.ap()[t0:t0 + 128, 2048:], vt[:, :], std,
                       waits=[tv2])

        with Ph(P, "u6_ex") as ph:
            tst = ph.settle("sync", std)
            tt1 = ph.op("tensor", lambda q: q.matmul(
                p_s[:, :], acol[:, 3:4], C.ident[:, :], start=True, stop=True),
                waits=[ev.tok()], inc=et, needs=("consts",))
            tr1 = ph.op("vector", lambda q: q.tensor_copy(arow[:, :], p_s[0:1, :]),
                        waits=[tt1], drain=True, inc=ev)
            tr2 = ph.op("vector", lambda q: q.reduce_max(
                regs[:, 1:2], arow[:, :], AX.X), drain=True, inc=ev)
            ph.dma("sync", gm_in.ap(), regs[:, 1:2], ld, waits=[tr2])
            tst2 = ph.settle("sync", ld)
            tcc = ph.op("gpsimd", lambda q: q.collective_compute(
                "AllReduce", ALU.max, replica_groups=PAIRS,
                ins=[gm_in.ap().opt()], outs=[gm_out.ap().opt()]), waits=[tst2, tst],
                inc=ccs)
            tcc2 = ph.op("gpsimd", lambda q: q.collective_compute(
                "AllGather", ALU.bypass, replica_groups=PAIRS,
                ins=[ex_in.ap().opt()], outs=[ex_out.ap().opt()]), inc=ccs)
            ph.dma("sync", regs[:, 2:3], gm_out.ap(), ld, waits=[tcc])
            tst3 = ph.settle("sync", ld)
            tt2 = ph.op("tensor", lambda q: q.matmul(
                p_s[:, :], acol[:, 2:3], C.ident[:, :], start=True, stop=True),
                waits=[tr1], inc=et)
            tr3 = ph.op("vector", lambda q: q.tensor_copy(arow[:, :], p_s[0:1, :]),
                        waits=[tt2], drain=True, inc=ev)
            tr4 = ph.op("vector", lambda q: q.reduce_max(
                regs[:, 0:1], arow[:, :], AX.X), drain=True, inc=ev)
            tpr = ph.op("vector", lambda q: q.tensor_mul(
                regs[:, 3:4], regs[:, 0:1], regs[:, 2:3]), waits=[tr4, tst3],
                drain=True, inc=ev)
            tsqr = ph.op("scalar", lambda q: q.activation(
                regs[:, 4:5], regs[:, 3:4], AF.Sqrt), waits=[tpr], inc=es_)
            ph.op("vector", lambda q: q.tensor_scalar(
                regs[:, 4:5], regs[:, 4:5], -1.0, None, ALU.mult), waits=[tsqr],
                drain=True, inc=ev)
            ph.op("vector", lambda q: q.memset(rowB[:, :], 1.0), drain=True)
            tm2 = ph.op("vector", lambda q: q.tensor_scalar(
                rowB[:, :], rowB[:, :], regs[0:1, 4:5], None, ALU.mult), drain=True,
                inc=ev)
            P.mark("u6rowB", tm2)
            P.mark("u6cc", (ccs, ccs.n))

    stack23 = ExitStack(); stack23.__enter__()
    ATT = stack23.enter_context(nc.sbuf_tensor("u6_ATT", [128, NCH * T], BF16))
    with ExitStack() as st:
        E = st.enter_context
        Kp = E(nc.sbuf_tensor("u6_Kp", [128, SEQ], BF16))
        Kup = E(nc.sbuf_tensor("u6_Kup", [128, SEQ], BF16))
        Qp = E(nc.sbuf_tensor("u6_Qp", [128, T], BF16))
        Qup = E(nc.sbuf_tensor("u6_Qup", [128, T], BF16))
        Kh = [E(nc.sbuf_tensor("u6_Kh0", [65, SEQ], BF16))]
        Kuh = [E(nc.sbuf_tensor("u6_Kuh0", [65, SEQ], BF16))]
        Qh = [E(nc.sbuf_tensor("u6_Qh0", [65, T], BF16))]
        Quh = [E(nc.sbuf_tensor("u6_Quh0", [65, T], BF16))]
        Vh = [E(nc.sbuf_tensor(f"u6_Vh{i}", [128, 65], BF16)) for i in range(4)]
        PT = E(nc.sbuf_tensor("u6_PT", [128, TG], BF16))
        mskt = E(nc.sbuf_tensor("u6_mskt", [128, TG], BF16))
        oscr = E(nc.sbuf_tensor("u6_oscr", [128, TG], F32))
        rden = E(nc.sbuf_tensor("u6_rden", [1, TG], F32))
        p_sc = [E(nc.psum_tensor(f"u6_psc{i}", [128, TG], F32)) for i in range(2)]
        p_den = E(nc.psum_tensor("u6_pden", [1, TG], F32))
        p_o = E(nc.psum_tensor("u6_po", [65, TG], F32))
        p_bc = E(nc.psum_tensor("u6_pbc", [128, TG], F32))

        NJ = SEQ // 128
        prev_evac = None
        prev_copies = None
        pair_tok = None
        vguard = {}
        for h in range(H):
            h2, par = h // 2, h % 2
            sl = 0
            with Ph(P, f"u6_h{h}") as ph:
                if par == 0:
                    ph.dmat("sync", Kp[:, :],
                            ex_out.ap()[:, h2 * 128:(h2 + 1) * 128], ld,
                            waits=[prev_copies, P.marks.get("u6cc")])
                    ph.dmat("sync", Kup[:, :],
                            ex_out.ap()[:, 1024 + h2 * 128:1024 + (h2 + 1) * 128], ld)
                    ph.dmat("sync", Qp[:, :],
                            q_dram.ap()[:, h2 * 128:(h2 + 1) * 128], ld)
                    ph.dmat("sync", Qup[:, :],
                            q_dram.ap()[:, 1024 + h2 * 128:1024 + (h2 + 1) * 128], ld)
                    pair_tok = ph.settle("sync", ld)
                base = 64 * par
                for (dst, srcp) in ((Kh, Kp), (Kuh, Kup)):
                    ph.op("vector", lambda q, sl=sl, base=base, dst=dst, srcp=srcp:
                          q.tensor_copy(dst[sl][0:64, :], srcp[base:base + 64, :]),
                          waits=[pair_tok, prev_evac], drain=True)
                    ph.op("vector", lambda q, sl=sl, dst=dst: q.memset(
                        dst[sl][64:65, :], 1.0), drain=True)
                for (dst, srcp) in ((Qh, Qp), (Quh, Qup)):
                    ph.op("vector", lambda q, sl=sl, base=base, dst=dst, srcp=srcp:
                          q.tensor_copy(dst[sl][0:64, :], srcp[base:base + 64, :]),
                          drain=True)
                    ph.op("vector", lambda q, sl=sl, dst=dst: q.tensor_copy(
                        dst[sl][64:65, :], rowB[:, :]), drain=True,
                        needs=("u6rowB",))
                tq2 = ph.op("vector", lambda q: None, drain=True, inc=ev)
                prev_copies = tq2
                loaded = {}
                last_exp = None
                for ig in range(4):
                    tden = None
                    texp = {}
                    for j in range(NJ):
                        psl = p_sc[j % 2]
                        mmj = ph.op("tensor", lambda q, psl=psl, sl=sl, j=j, ig=ig: q.matmul(
                            psl[:, :], Kuh[sl][:, j * 128:(j + 1) * 128],
                            Quh[sl][:, ig * TG:(ig + 1) * TG], start=True, stop=True),
                            waits=[tq2, texp.get(j - 2), last_exp], inc=et)
                        te = ph.op("scalar", lambda q, psl=psl: q.activation(
                            PT[:, :], psl[:, :], AF.Exp), waits=[mmj, tden], inc=es_)
                        texp[j] = te
                        tden = ph.op("tensor", lambda q, j=j: q.matmul(
                            p_den[:, :], C.ones_col_bf[:, :], PT[:, :],
                            start=(j == 0), stop=(j == NJ - 1)),
                            waits=[te] + ([prev_evac] if j == 0 else []), inc=et)
                    js = []
                    offs_l = []
                    for j in range(NJ):
                        offs = []
                        skip_all = True
                        for qb, hc in ((0, h0c), (T, h1c)):
                            D = qb + ig * TG - j * 128
                            if D >= 128:
                                offs.append(1152)
                                skip_all = False
                            elif D <= -TG:
                                offs.append(0)
                            else:
                                offs.append(D + 512)
                                skip_all = False
                        if not skip_all:
                            js.append(j)
                            offs_l.append(offs)
                    nexp = {}
                    tavp = None
                    for idx, j in enumerate(js):
                        if j not in loaded:
                            loaded[j] = ph.dma("sync", Vh[j % 4][:, :],
                                               ex_out.ap()[j * 128:(j + 1) * 128,
                                                           2048 + h * 65:2048 + (h + 1) * 65],
                                               ld, waits=[vguard.get(j % 4)])
                            ph.settle("sync", ld)
                        tvl = (P.gsem("e_sync"), P.gsem("e_sync").n)
                        psl = p_sc[j % 2]
                        mmj = ph.op("tensor", lambda q, psl=psl, sl=sl, j=j, ig=ig: q.matmul(
                            psl[:, :], Kh[sl][:, j * 128:(j + 1) * 128],
                            Qh[sl][:, ig * TG:(ig + 1) * TG], start=True, stop=True),
                            waits=[nexp.get(js[idx - 2] if idx >= 2 else -1),
                                   texp.get(NJ - 1), texp.get(NJ - 2)], inc=et)
                        te = ph.op("scalar", lambda q, psl=psl: q.activation(
                            PT[:, :], psl[:, :], AF.Exp), waits=[mmj, tavp], inc=es_)
                        nexp[j] = te
                        s0a, s0b = offs_l[idx]
                        if s0a == 1152 and s0b == 1152:
                            tmk = te
                        else:
                            ph.op("vector", lambda q, s0a=s0a: q.tensor_scalar(
                                mskt[:, :], cmske[:, s0a:s0a + TG], h0c[:, 0:1],
                                None, ALU.mult), waits=[te], drain=True)
                            ph.op("vector", lambda q, s0b=s0b: q.scalar_tensor_tensor(
                                mskt[:, :], cmske[:, s0b:s0b + TG], h1c[:, 0:1],
                                mskt[:, :], ALU.mult, ALU.add), drain=True)
                            tmk = ph.op("vector", lambda q: q.tensor_mul(
                                PT[:, :], PT[:, :], mskt[:, :]), waits=[te],
                                drain=True, inc=ev)
                        tavp = ph.op("tensor", lambda q, j=j, idx=idx, njs=len(js): q.matmul(
                            p_o[:, :], Vh[j % 4][:, :], PT[:, :],
                            start=(idx == 0), stop=(idx == njs - 1)),
                            waits=[tmk, tvl] + ([prev_evac] if idx == 0 else []),
                            inc=et)
                        vguard[j % 4] = tavp
                    last_exp = nexp[js[-1]]
                    trd = ph.op("vector", lambda q: q.reciprocal(
                        rden[:, :], p_den[0:1, :]), waits=[tden, tavp], drain=True,
                        inc=ev)
                    tbc = ph.op("tensor", lambda q: q.matmul(
                        p_bc[:, :], C.ones_row[:, 0:128], rden[:, :],
                        start=True, stop=True), waits=[trd], inc=et)
                    tcp = ph.op("scalar", lambda q, base=base: q.activation(
                        oscr[base:base + 64, :], p_o[0:64, :], AF.Copy),
                        waits=[tavp, prev_evac], inc=es_)
                    ch = h // 2
                    prev_evac = ph.op("vector", lambda q, ig=ig, ch=ch, base=base: q.tensor_mul(
                        ATT[base:base + 64, ch * T + ig * TG:ch * T + (ig + 1) * TG],
                        oscr[base:base + 64, :], p_bc[base:base + 64, :]),
                        waits=[tbc, tcp], drain=True, inc=ev)
        P.mark("u6att", prev_evac)

    with ExitStack() as st:
        E = st.enter_context
        Wo = E(nc.sbuf_tensor("u6_Wo", [128, NCH * 1024], BF16))
        boc = E(nc.sbuf_tensor("u6_boc", [128, NCH], F32))
        p_o2 = [E(nc.psum_tensor(f"u6_po2{i}", [128, TG], F32)) for i in range(2)]
        with Ph(P, "u6_proj") as ph:
            ph.dma("gpsimd", Wo[:, :].rearrange("p (c d) -> p c d", c=NCH),
                   a_wo.rearrange("(c p) d -> p c d", p=128), ldg)
            tlw = ph.settle("gpsimd", ldg)
            ph.dma("sync", boc[:, :], a_bo, ld)
            tlb = ph.settle("sync", ld)
            i = 0
            evt = {}
            for m in range(NCH):
                for g in range(NTG):
                    po = p_o2[i % 2]
                    a, b = g * TG, (g + 1) * TG
                    for k in range(NCH):
                        mm = ph.op("tensor", lambda q, po=po, k=k, m=m, a=a, b=b: q.matmul(
                            po[:, :], Wo[:, k * 1024 + m * 128:k * 1024 + (m + 1) * 128],
                            ATT[:, k * T + a:k * T + b],
                            start=(k == 0), stop=(k == NCH - 1)),
                            waits=[tlw, evt.get(i - 2)] if k == 0 else (),
                            needs=("u6att",),
                            inc=et if k == NCH - 1 else None)
                    evt[i] = ph.op("vector", lambda q, po=po, m=m, a=a, b=b: q.scalar_tensor_tensor(
                        C.X[:, m * T + a:m * T + b], po[:, :], boc[:, m:m + 1],
                        C.X[:, m * T + a:m * T + b], ALU.add, ALU.add),
                        waits=[mm, tlb], drain=True, inc=ev)
                    i += 1
            P.mark("X", evt[i - 1])

    stack23.__exit__(None, None, None)
    ustack.__exit__(None, None, None)
    emit_ln(P, C, "u6ln")



def emit_u_l2attn(P, C, W):
    nc = P.nc
    H, DH = 16, 64
    SC = 4097
    SROWS = 2049

    bq_c = np.ascontiguousarray(W["L2_attn_bq"].astype(np.float32).reshape(NCH, 128).T)
    bo_c = np.ascontiguousarray(W["L2_attn_bo"].astype(np.float32).reshape(NCH, 128).T)
    bp_c = np.ascontiguousarray(W["L2_attn_bp"].astype(np.float32).reshape(NCH, 128).T)
    bk_r = np.ascontiguousarray(W["L2_attn_bk"].astype(np.float32).reshape(1, 1024))
    bv_r = np.ascontiguousarray(W["L2_attn_bv"].astype(np.float32).reshape(1, 1024))
    u_c = np.ascontiguousarray(W["L2_attn_u"].astype(np.float32).reshape(H * DH).reshape(NCH, 128).T)
    v_c = np.ascontiguousarray(W["L2_attn_v"].astype(np.float32).reshape(H * DH).reshape(NCH, 128).T)
    pos_idx = np.arange(SEQ, dtype=np.float64)
    iidx = np.arange(512, dtype=np.float64)
    inv_freq = 1.0 / 10000.0 ** (2.0 * iidx / 1024.0)
    ang = pos_idx[:, None] * inv_freq[None, :]
    embT = np.ascontiguousarray(
        np.concatenate([np.sin(ang), np.cos(ang)], axis=1).T.astype(np.float32))
    hsel = np.zeros((128, 2), np.float32)
    hsel[0:64, 0] = 1.0
    hsel[64:128, 1] = 1.0

    a_wq = P.in_("u4_Wq", np.ascontiguousarray(W["L2_attn_Wq"].astype(np.float32)))
    a_wk = P.in_("u4_Wk", np.ascontiguousarray(W["L2_attn_Wk"].astype(np.float32)))
    a_wv = P.in_("u4_Wv", np.ascontiguousarray(W["L2_attn_Wv"].astype(np.float32)))
    a_wp = P.in_("u4_Wp", np.ascontiguousarray(W["L2_attn_Wp"].astype(np.float32)))
    a_wo = P.in_("u4_Wo", np.ascontiguousarray(W["L2_attn_Wo"].astype(np.float32)))
    a_bq = P.in_("u4_bq", bq_c)
    a_bo = P.in_("u4_bo", bo_c)
    a_bp = P.in_("u4_bp", bp_c)
    a_bk = P.in_("u4_bk", bk_r)
    a_bv = P.in_("u4_bv", bv_r)
    a_u = P.in_("u4_u", u_c)
    a_v = P.in_("u4_v", v_c)
    a_emb = P.in_("u4_embT", embT)
    a_hsel = P.in_("u4_hsel", hsel)
    a_h0 = P.in_("u4_h0", [np.full((128, 1), float(1 - c % 2), np.float32)
                           for c in range(NCORES)])
    a_h1 = P.in_("u4_h1", [np.full((128, 1), float(c % 2), np.float32)
                           for c in range(NCORES)])

    qu_dram = P.dram("u4_qu", [1024, T], BF16)
    qv_dram = P.dram("u4_qv", [1024, T], BF16)
    pos_dram = P.dram("u4_pos", [1024, SEQ], BF16)
    scr_dram_a = P.dram("u4_scra", [8, SROWS * SC], BF16)
    scr_dram_b = P.dram("u4_scrb", [8, SROWS * SC], BF16)
    ex_in = P.dram("u4_exin", [T, 1024 + H * 65], BF16)
    ex_out = P.dram("u4_exout", [SEQ, 1024 + H * 65], BF16)
    gm_in = P.dram("u4_gmin", [1, 1], F32)
    gm_out = P.dram("u4_gmout", [1, 1], F32)
    xh_in = P.dram("u4_xhin", [128, 8], F32)
    xh_out = P.dram("u4_xhout", [256, 8], F32)

    ld = P.gsem("dma_sync")
    ldg = P.gsem("dma_gp")
    std = P.gsem("dma_st")
    ccs = P.gsem("cc")
    ev = P.gsem("ev")
    et = P.gsem("et")
    es_ = P.gsem("es")

    ustack = ExitStack(); ustack.__enter__(); UE = ustack.enter_context
    regs = UE(nc.sbuf_tensor("u4_regs", [1, 12], F32))
    rowB = UE(nc.sbuf_tensor("u4_rowB", [1, T], F32))
    hselb = UE(nc.sbuf_tensor("u4_hselb", [128, 2], F32))
    nacc = UE(nc.sbuf_tensor("u4_nacc", [128, 6], F32))
    nq2 = UE(nc.sbuf_tensor("u4_nq2", [2, 4], F32))
    arow = UE(nc.sbuf_tensor("u4_arow", [1, 128], F32))
    biasc = UE(nc.sbuf_tensor("u4_biasc", [128, 48], F32))
    xhb = UE(nc.sbuf_tensor("u4_xhb", [128, 8], BF16))
    qvh = UE(nc.sbuf_tensor("u4_qvh", [128, 8], BF16))
    h0c = UE(nc.sbuf_tensor("u4_h0c", [128, 1], F32))
    h1c = UE(nc.sbuf_tensor("u4_h1c", [128, 1], F32))

    with ExitStack() as st0:
        xh = st0.enter_context(nc.sbuf_tensor("u4_xh", [128, 8], F32))
        with Ph(P, "u4_ld") as ph:
            ph.dma("sync", biasc[:, 0:8], a_bq, ld)
            ph.dma("sync", biasc[:, 8:16], a_u, ld)
            ph.dma("sync", biasc[:, 16:24], a_v, ld)
            ph.dma("sync", biasc[:, 24:32], a_bp, ld)
            ph.dma("sync", biasc[:, 32:40], a_bo, ld)
            ph.dma("sync", hselb[:, :], a_hsel, ld)
            ph.dma("sync", h0c[:, :], a_h0, ld)
            ph.dma("sync", h1c[:, :], a_h1, ld)
            tbl = ph.settle("sync", ld)
            P.mark("u4bias", tbl)
            ph.op("vector", lambda q: q.memset(nacc[:, :], 0.0))
            tz = ph.op("vector", lambda q: q.memset(nq2[:, :], 0.0), drain=True, inc=ev)
            P.mark("u4nacc0", tz)
            xv = C.X[:, :].rearrange("p (c t) -> p c t", c=NCH)
            ph.dma("sync", xh[:, 0:8].rearrange("p (c u) -> p c u", u=1),
                   xv[:, :, 0:1], ld, needs=("X",), allow_slow_non_contiguous=True)
            tse = ph.settle("sync", ld)
            ph.dma("sync", xh_in.ap(), xh[:, :], ld, waits=[tse])
            tse2 = ph.settle("sync", ld)
            tcc0 = ph.op("gpsimd", lambda q: q.collective_compute(
                "AllGather", ALU.bypass, replica_groups=PAIRS,
                ins=[xh_in.ap().opt()], outs=[xh_out.ap().opt()]), waits=[tse2],
                inc=ccs)
            ph.dma("sync", xh[:, :], xh_out.ap()[128:256, :], ld, waits=[tcc0])
            tse3 = ph.settle("sync", ld)
            tmh = ph.op("vector", lambda q: q.tensor_scalar(
                xh[:, :], xh[:, :], h0c[:, 0:1], None, ALU.mult), waits=[tse3, tbl],
                drain=True)
            tmc = ph.op("vector", lambda q: q.tensor_copy(xhb[:, :], xh[:, :]),
                        drain=True, inc=ev)
            P.mark("u4xh", tmc)

    with ExitStack() as st:
        E = st.enter_context
        Wb = E(nc.sbuf_tensor("u4_Wb", [128, NCH * 1024], BF16))
        scrA = E(nc.sbuf_tensor("u4_scrA", [128, TG], F32))
        scrB = E(nc.sbuf_tensor("u4_scrB", [128, TG], F32))
        outb = E(nc.sbuf_tensor("u4_outb", [128, TG], BF16))
        p_a = [E(nc.psum_tensor(f"u4_pa{i}", [128, TG], F32)) for i in range(2)]
        p_n = E(nc.psum_tensor("u4_pn", [2, TG], F32))
        p_hh = E(nc.psum_tensor("u4_phh", [128, 1], F32))

        with Ph(P, "u4_qw") as ph:
            ph.dma("gpsimd", Wb[:, :].rearrange("p (c d) -> p c d", c=NCH),
                   a_wq.rearrange("(c p) d -> p c d", p=128), ldg)
            tlw = ph.settle("gpsimd", ldg)
        prev_ev = None
        for m in range(NCH):
            with Ph(P, f"u4_q{m}") as ph:
                for g in range(NTG):
                    a, b = g * TG, (g + 1) * TG
                    po = p_a[g % 2]
                    for k in range(NCH):
                        mm = ph.op("tensor", lambda q, po=po, k=k, m=m, a=a, b=b: q.matmul(
                            po[:, :], Wb[:, k * 1024 + m * 128:k * 1024 + (m + 1) * 128],
                            C.Xb[:, k * T + a:k * T + b],
                            start=(k == 0), stop=(k == NCH - 1)),
                            waits=[tlw, prev_ev, P.marks.get("Xb")] if k == 0 else (),
                            inc=et if k == NCH - 1 else None)
                    t1 = ph.op("vector", lambda q, po=po, m=m: q.tensor_scalar(
                        scrA[:, :], po[:, :], biasc[:, m:m + 1], None, ALU.add),
                        waits=[mm], needs=("u4bias",), drain=True)
                    tss1 = ph.settle("sync", std)
                    t2 = ph.op("vector", lambda q, m=m: q.tensor_scalar(
                        outb[:, :], scrA[:, :], biasc[:, 8 + m:9 + m], 0.125,
                        ALU.add, ALU.mult), waits=[tss1], drain=True, inc=ev)
                    ph.dma("sync", qu_dram.ap()[m * 128:(m + 1) * 128, a:b],
                           outb[:, :], std, waits=[t2])
                    tss2 = ph.settle("sync", std)
                    tsq = ph.op("scalar", lambda q: q.activation(
                        scrB[:, :], outb[:, :], AF.Square), waits=[t2], inc=es_)
                    tnm = ph.op("tensor", lambda q: q.matmul(
                        p_n[:, :], hselb[:, :], scrB[:, :], start=True, stop=True),
                        waits=[tsq], inc=et)
                    tnr = ph.op("vector", lambda q: q.tensor_reduce(
                        nq2[:, 2:3], p_n[0:2, :], AX.X, ALU.max), waits=[tnm],
                        needs=("u4nacc0",), drain=True)
                    tna = ph.op("vector", lambda q: q.tensor_max(
                        nq2[:, 0:1], nq2[:, 0:1], nq2[:, 2:3]), drain=True, inc=ev)
                    t4 = ph.op("vector", lambda q, m=m: q.tensor_scalar(
                        outb[:, :], scrA[:, :], biasc[:, 16 + m:17 + m], 0.125,
                        ALU.add, ALU.mult), waits=[tna, tss2], drain=True, inc=ev)
                    ph.dma("sync", qv_dram.ap()[m * 128:(m + 1) * 128, a:b],
                           outb[:, :], std, waits=[t4])
                    tsq2 = ph.op("scalar", lambda q: q.activation(
                        scrB[:, :], outb[:, :], AF.Square), waits=[t4, tnr], inc=es_)
                    tnm2 = ph.op("tensor", lambda q: q.matmul(
                        p_n[:, :], hselb[:, :], scrB[:, :], start=True, stop=True),
                        waits=[tsq2], inc=et)
                    tnr2 = ph.op("vector", lambda q: q.tensor_reduce(
                        nq2[:, 2:3], p_n[0:2, :], AX.X, ALU.max), waits=[tnm2],
                        drain=True)
                    prev_ev = ph.op("vector", lambda q: q.tensor_max(
                        nq2[:, 1:2], nq2[:, 1:2], nq2[:, 2:3]), drain=True, inc=ev)
        with Ph(P, "u4_qh") as ph:
            thq = None
            for m in range(NCH):
                for k in range(NCH):
                    mmh = ph.op("tensor", lambda q, m=m, k=k: q.matmul(
                        p_hh[:, :], Wb[:, k * 1024 + m * 128:k * 1024 + (m + 1) * 128],
                        xhb[:, k:k + 1],
                        start=(k == 0), stop=(k == NCH - 1)),
                        waits=[P.marks.get("u4xh"), thq] if k == 0 else (),
                        inc=et if k == NCH - 1 else None)
                th1 = ph.op("vector", lambda q, m=m: q.tensor_scalar(
                    scrA[:, 0:1], p_hh[:, :], biasc[:, m:m + 1], None, ALU.add),
                    waits=[mmh], drain=True)
                thq = ph.op("vector", lambda q, m=m: q.tensor_scalar(
                    qvh[:, m:m + 1], scrA[:, 0:1], biasc[:, 16 + m:17 + m], 0.125,
                    ALU.add, ALU.mult), drain=True, inc=ev)
            P.mark("u4qvh", thq)
        with Ph(P, "u4_qsettle") as ph:
            ph.settle("sync", std)

    with ExitStack() as st:
        E = st.enter_context
        Wkb = E(nc.sbuf_tensor("u4_Wkb", [128, NCH * 1024], BF16))
        Wvb = E(nc.sbuf_tensor("u4_Wvb", [128, NCH * 1024], BF16))
        bkb = E(nc.sbuf_tensor("u4_bkb", [1, 1024], BF16))
        bvb = E(nc.sbuf_tensor("u4_bvb", [1, 1024], BF16))
        ktile = E(nc.sbuf_tensor("u4_ktile", [128, 1024], BF16))
        vt = E(nc.sbuf_tensor("u4_vt", [128, H * 65], BF16))
        sq = E(nc.sbuf_tensor("u4_sq", [128, 1024], F32))
        nrm = E(nc.sbuf_tensor("u4_nrm", [128, 16], F32))
        p_a = [E(nc.psum_tensor(f"u4_pk{i}", [128, TG], F32)) for i in range(2)]

        with Ph(P, "u4_kw") as ph:
            ph.dma("gpsimd", Wkb[:, :].rearrange("p (c d) -> p c d", c=NCH),
                   a_wk.rearrange("(c p) d -> p c d", p=128), ldg)
            ph.dma("gpsimd", Wvb[:, :].rearrange("p (c d) -> p c d", c=NCH),
                   a_wv.rearrange("(c p) d -> p c d", p=128), ldg)
            ph.dma("gpsimd", bkb[:, :], a_bk, ldg)
            ph.dma("gpsimd", bvb[:, :], a_bv, ldg)
            tlw2 = ph.settle("gpsimd", ldg)
        for tt in range(T // 128):
            t0 = tt * 128
            with Ph(P, f"u4_kv{tt}") as ph:
                evk = {}
                for cg in range(2):
                    po = p_a[cg % 2]
                    ph.op("tensor", lambda q, po=po, cg=cg: q.matmul(
                        po[:, :], C.ones_row_bf[:, 0:128], bkb[:, cg * TG:(cg + 1) * TG],
                        start=True, stop=False),
                        waits=[tlw2, evk.get(cg - 2)], needs=("constsg",))
                    for k in range(NCH):
                        mm = ph.op("tensor", lambda q, po=po, k=k, cg=cg, t0=t0: q.matmul(
                            po[:, :], C.Xb[:, k * T + t0:k * T + t0 + 128],
                            Wkb[:, k * 1024 + cg * TG:k * 1024 + (cg + 1) * TG],
                            start=False, stop=(k == NCH - 1)),
                            inc=et if k == NCH - 1 else None)
                    evk[cg] = ph.op("vector", lambda q, po=po, cg=cg: q.tensor_copy(
                        ktile[:, cg * TG:(cg + 1) * TG], po[:, :]), waits=[mm],
                        drain=True, inc=ev)
                tdk = ph.dma("sync", ex_in.ap()[t0:t0 + 128, 0:1024], ktile[:, :], std,
                             waits=[evk[1]])
                tks = ph.op("scalar", lambda q: q.activation(
                    sq[:, :], ktile[:, :], AF.Square), waits=[evk[1]], inc=es_)
                ph.op("vector", lambda q: q.tensor_reduce(
                    nrm[:, :].rearrange("p (h u) -> p h u", u=1),
                    sq[:, :].rearrange("p (h d) -> p h d", d=64),
                    AX.X, ALU.add), waits=[tks], drain=True)
                ph.op("vector", lambda q: q.tensor_reduce(
                    nacc[:, 4:5], nrm[:, :], AX.X, ALU.max), drain=True)
                ph.op("vector", lambda q: q.tensor_max(
                    nacc[:, 2:3], nacc[:, 2:3], nacc[:, 4:5]), drain=True, inc=ev)
                for cg in range(2):
                    po = p_a[cg % 2]
                    ph.op("tensor", lambda q, po=po, cg=cg: q.matmul(
                        po[:, :], C.ones_row_bf[:, 0:128], bvb[:, cg * TG:(cg + 1) * TG],
                        start=True, stop=False), waits=[evk[cg]])
                    for k in range(NCH):
                        mm = ph.op("tensor", lambda q, po=po, k=k, cg=cg, t0=t0: q.matmul(
                            po[:, :], C.Xb[:, k * T + t0:k * T + t0 + 128],
                            Wvb[:, k * 1024 + cg * TG:k * 1024 + (cg + 1) * TG],
                            start=False, stop=(k == NCH - 1)),
                            inc=et if k == NCH - 1 else None)
                    evk[cg] = ph.op("vector", lambda q, po=po, cg=cg: q.tensor_copy(
                        ktile[:, cg * TG:(cg + 1) * TG], po[:, :]), waits=[mm, tdk],
                        drain=True, inc=ev)
                tv1 = ph.op("vector", lambda q: q.tensor_copy(
                    vt[:, :].rearrange("p (h j) -> p h j", j=65)[:, :, 0:64],
                    ktile[:, :].rearrange("p (h j) -> p h j", j=64)),
                    waits=[evk[1]], drain=True)
                tv2 = ph.op("vector", lambda q: q.memset(
                    vt[:, :].rearrange("p (h j) -> p h j", j=65)[:, :, 64:65], 1.0),
                    drain=True, inc=ev)
                ph.dma("sync", ex_in.ap()[t0:t0 + 128, 1024:], vt[:, :], std,
                       waits=[tv2])

    with ExitStack() as st:
        E = st.enter_context
        Wpb = E(nc.sbuf_tensor("u4_Wpb2", [128, NCH * 1024], BF16))
        embb = E(nc.sbuf_tensor("u4_embb2", [128, NCH * TG], BF16))
        outp = E(nc.sbuf_tensor("u4_outp2", [128, TG], BF16))
        sqp = E(nc.sbuf_tensor("u4_sqp2", [128, TG], F32))
        p_a = [E(nc.psum_tensor(f"u4_pq2{i}", [128, TG], F32)) for i in range(2)]
        p_n = E(nc.psum_tensor("u4_pn3", [2, TG], F32))

        with Ph(P, "u4_pw2") as ph:
            ph.dma("gpsimd", Wpb[:, :].rearrange("p (c d) -> p c d", c=NCH),
                   a_wp.rearrange("(c p) d -> p c d", p=128), ldg)
            tlw3 = ph.settle("gpsimd", ldg)
        prevp = None
        for mg in range(SEQ // TG):
            with Ph(P, f"u4_pos{mg}") as ph:
                ph.dma("gpsimd", embb[:, :].rearrange("p (c t) -> p c t", c=NCH),
                       a_emb[:, mg * TG:(mg + 1) * TG].rearrange(
                           "(c p) t -> p c t", p=128), ldg, waits=[prevp])
                tle = ph.settle("gpsimd", ldg)
                evp = {}
                for m in range(NCH):
                    po = p_a[m % 2]
                    for k in range(NCH):
                        mm = ph.op("tensor", lambda q, po=po, k=k, m=m: q.matmul(
                            po[:, :], Wpb[:, k * 1024 + m * 128:k * 1024 + (m + 1) * 128],
                            embb[:, k * TG:(k + 1) * TG],
                            start=(k == 0), stop=(k == NCH - 1)),
                            waits=[tlw3, tle, evp.get(m - 2)] if k == 0 else (),
                            inc=et if k == NCH - 1 else None)
                    tssp = ph.settle("sync", std)
                    t1 = ph.op("vector", lambda q, po=po, m=m: q.tensor_scalar(
                        outp[:, :], po[:, :], biasc[:, 24 + m:25 + m], None, ALU.add),
                        waits=[mm, tssp], drain=True, inc=ev)
                    tds = ph.dma("sync", pos_dram.ap()[m * 128:(m + 1) * 128,
                                                       mg * TG:(mg + 1) * TG],
                                 outp[:, :], std, waits=[t1])
                    tps = ph.op("scalar", lambda q: q.activation(
                        sqp[:, :], outp[:, :], AF.Square), waits=[t1], inc=es_)
                    tpm = ph.op("tensor", lambda q: q.matmul(
                        p_n[:, :], hselb[:, :], sqp[:, :], start=True, stop=True),
                        waits=[tps], inc=et)
                    tpr = ph.op("vector", lambda q: q.tensor_reduce(
                        nq2[:, 2:3], p_n[0:2, :], AX.X, ALU.max), waits=[tpm],
                        drain=True)
                    evp[m] = ph.op("vector", lambda q: q.tensor_max(
                        nq2[:, 3:4], nq2[:, 3:4], nq2[:, 2:3]), drain=True, inc=ev)
                prevp = evp[NCH - 1]

    with ExitStack() as st:
        E = st.enter_context
        p_s = E(nc.psum_tensor("u4_psx", [1, 128], F32))
        with Ph(P, "u4_ex") as ph:
            tst = ph.settle("sync", std)
            tt1 = ph.op("tensor", lambda q: q.matmul(
                p_s[0:1, :], nacc[:, 2:3], C.ident[:, :], start=True, stop=True),
                waits=[ev.tok()], inc=et, needs=("consts",))
            tr1 = ph.op("vector", lambda q: q.tensor_copy(arow[:, :], p_s[0:1, :]),
                        waits=[tt1], drain=True, inc=ev)
            tr2 = ph.op("vector", lambda q: q.reduce_max(
                regs[:, 2:3], arow[:, :], AX.X), drain=True, inc=ev)
            ph.dma("sync", gm_in.ap(), regs[:, 2:3], ld, waits=[tr2])
            tst2 = ph.settle("sync", ld)
            tcc = ph.op("gpsimd", lambda q: q.collective_compute(
                "AllReduce", ALU.max, replica_groups=PAIRS,
                ins=[gm_in.ap().opt()], outs=[gm_out.ap().opt()]), waits=[tst2, tst],
                inc=ccs)
            tcc2 = ph.op("gpsimd", lambda q: q.collective_compute(
                "AllGather", ALU.bypass, replica_groups=PAIRS,
                ins=[ex_in.ap().opt()], outs=[ex_out.ap().opt()]), inc=ccs)
            ph.dma("sync", regs[:, 2:3], gm_out.ap(), ld, waits=[tcc])
            tst3 = ph.settle("sync", ld)
            tprev = tr1
            for col in (0, 1, 3):
                tt2 = ph.op("tensor", lambda q, col=col: q.matmul(
                    p_s[0:1, 0:2], nq2[0:2, col:col + 1], C.ident[0:2, 0:2],
                    start=True, stop=True), waits=[tprev], inc=et)
                tprev = ph.op("vector", lambda q: q.tensor_copy(
                    arow[:, 0:2], p_s[0:1, 0:2]), waits=[tt2], drain=True, inc=ev)
                ph.op("vector", lambda q, col=col: q.reduce_max(
                    regs[:, col:col + 1], arow[:, 0:2], AX.X), drain=True, inc=ev)
            tb1 = ph.op("vector", lambda q: q.tensor_mul(
                regs[:, 4:5], regs[:, 0:1], regs[:, 2:3]), waits=[tst3], drain=True)
            tb2 = ph.op("vector", lambda q: q.tensor_mul(
                regs[:, 5:6], regs[:, 1:2], regs[:, 3:4]), drain=True, inc=ev)
            tb3 = ph.op("scalar", lambda q: q.activation(
                regs[:, 6:7], regs[:, 4:5], AF.Sqrt), waits=[tb2], inc=es_)
            tb4 = ph.op("scalar", lambda q: q.activation(
                regs[:, 7:8], regs[:, 5:6], AF.Sqrt), drain=True, inc=es_)
            tb5 = ph.op("vector", lambda q: q.tensor_add(
                regs[:, 8:9], regs[:, 6:7], regs[:, 7:8]), waits=[tb4], drain=True)
            tb6 = ph.op("vector", lambda q: q.tensor_scalar(
                regs[:, 8:9], regs[:, 8:9], -1.0, None, ALU.mult), drain=True, inc=ev)
            ph.op("vector", lambda q: q.memset(rowB[:, :], 1.0), drain=True)
            tm2 = ph.op("vector", lambda q: q.tensor_scalar(
                rowB[:, :], rowB[:, :], regs[0:1, 8:9], None, ALU.mult), drain=True,
                inc=ev)
            P.mark("u4rowB", tm2)
            P.mark("u4cc", (ccs, ccs.n))

    with ExitStack() as st:
        E = st.enter_context
        qvb = E(nc.sbuf_tensor("u4_qvb", [128, T], BF16))
        posb = E(nc.sbuf_tensor("u4_posb", [128, SEQ], BF16))
        rawt = E(nc.sbuf_tensor("u4_rawt", [128, SC + 1], BF16))
        rawh = E(nc.sbuf_tensor("u4_rawh", [1, SC + 1], BF16))
        p_r = [E(nc.psum_tensor(f"u4_pr{i}", [128, TG], F32)) for i in range(2)]
        p_r1 = E(nc.psum_tensor("u4_prh", [1, TG], F32))

        prevh = None
        tlq = None
        for h in range(H):
            hb = 64 * (h % 2)
            with Ph(P, f"u4_raw{h}") as ph:
                if h % 2 == 0:
                    ph.dma("sync", qvb[:, :],
                           qv_dram.ap()[(h // 2) * 128:(h // 2 + 1) * 128, :], ld,
                           waits=[prevh])
                    ph.dma("sync", posb[:, :],
                           pos_dram.ap()[(h // 2) * 128:(h // 2 + 1) * 128, :], ld)
                    tlq = ph.settle("sync", ld)
                tz = ph.op("vector", lambda q: q.memset(rawt[:, 0:1], 0.0), inc=ev) \
                    if h == 0 else None
                tz2 = ph.op("vector", lambda q: q.memset(rawh[:, 0:1], 0.0), inc=ev) \
                    if h == 0 else None
                for qt in range(16):
                    evr = {}
                    for mg in range(NCH):
                        po = p_r[mg % 2]
                        mm = ph.op("tensor", lambda q, po=po, qt=qt, mg=mg, hb=hb: q.matmul(
                            po[:, :], qvb[hb:hb + 64, qt * 128:(qt + 1) * 128],
                            posb[hb:hb + 64, mg * TG:(mg + 1) * TG],
                            start=True, stop=True),
                            waits=[tlq, evr.get(mg - 2), prevh], inc=et)
                        evr[mg] = ph.op("vector", lambda q, po=po, mg=mg: q.tensor_copy(
                            rawt[:, 1 + mg * TG:1 + (mg + 1) * TG], po[:, :]),
                            waits=[mm], drain=True, inc=ev)
                    prevh = ph.dma("sync",
                                   bass.AP(scr_dram_a if h < 8 else scr_dram_b,
                                           (h % 8) * (SROWS * SC) + qt * 128 * SC,
                                           [[SC, 128], [1, SC]]),
                                   rawt[:, 0:SC], std, waits=[evr[NCH - 1]])
                evh = None
                for mg in range(NCH):
                    mmh = ph.op("tensor", lambda q, h=h, hb=hb, mg=mg: q.matmul(
                        p_r1[:, :], qvh[hb:hb + 64, h // 2:h // 2 + 1],
                        posb[hb:hb + 64, mg * TG:(mg + 1) * TG],
                        start=True, stop=True),
                        waits=[P.marks.get("u4qvh"), evh], inc=et)
                    evh = ph.op("vector", lambda q, mg=mg: q.tensor_copy(
                        rawh[:, 1 + mg * TG:1 + (mg + 1) * TG], p_r1[0:1, :]),
                        waits=[mmh], drain=True, inc=ev)
                prevh = ph.dma("sync",
                               bass.AP(scr_dram_a if h < 8 else scr_dram_b,
                                       (h % 8) * (SROWS * SC) + 2048 * SC,
                                       [[SC, 1], [1, SC]]),
                               rawh[:, 0:SC], std, waits=[evh])
        P.mark("u4scr", (std, std.n))

    stack23 = ExitStack(); stack23.__enter__()
    ATT = stack23.enter_context(nc.sbuf_tensor("u4_ATT", [128, NCH * T], BF16))
    with ExitStack() as st:
        E = st.enter_context
        Kp = E(nc.sbuf_tensor("u4_Kp", [128, SEQ], BF16))
        Kh = E(nc.sbuf_tensor("u4_Kh", [65, SEQ], BF16))
        Qh = E(nc.sbuf_tensor("u4_Qh", [65, T], BF16))
        Ih0 = E(nc.sbuf_tensor("u4_Ih0", [128, 128], BF16))
        Ih1 = E(nc.sbuf_tensor("u4_Ih1", [128, 128], BF16))
        pts = [E(nc.sbuf_tensor(f"u4_pts{i}", [128, T], BF16)) for i in range(4)]
        PT = E(nc.sbuf_tensor("u4_PT", [128, T], BF16))
        Vh = [E(nc.sbuf_tensor(f"u4_Vh{i}", [128, 65], BF16)) for i in range(4)]
        oscr = E(nc.sbuf_tensor("u4_oscr", [128, TG], F32))
        rden = E(nc.sbuf_tensor("u4_rden", [1, TG], F32))
        p_sc = E(nc.psum_tensor("u4_psc", [128, T], F32))
        p_o = [E(nc.psum_tensor(f"u4_po{i}", [65, TG], F32)) for i in range(4)]

        with Ph(P, "u4_ih") as ph:
            t1 = ph.op("vector", lambda q: q.tensor_scalar(
                Ih0[:, :], C.ident_bf[:, :], h0c[:, 0:1], None, ALU.mult),
                needs=("constsg", "u4bias"), drain=True)
            t2 = ph.op("vector", lambda q: q.tensor_scalar(
                Ih1[:, :], C.ident_bf[:, :], h1c[:, 0:1], None, ALU.mult),
                drain=True, inc=ev)
            P.mark("u4ih", t2)

        NJ = SEQ // 128
        prev_evac = None
        prev_copies = None
        vguard = {}
        ptguard = {}
        for h in range(H):
            with Ph(P, f"u4_h{h}") as ph:
                if h % 2 == 0:
                    ph.dmat("sync", Kp[:, :],
                            ex_out.ap()[:, (h // 2) * 128:(h // 2 + 1) * 128], ld,
                            waits=[prev_copies, P.marks.get("u4cc")])
                    prev_pair = ph.settle("sync", ld)
                base = 64 * (h % 2)
                tk1 = ph.op("vector", lambda q, base=base: q.tensor_copy(
                    Kh[0:64, :], Kp[base:base + 64, :]),
                    waits=[prev_pair, prev_evac], drain=True)
                ph.op("vector", lambda q: q.memset(Kh[64:65, :], 1.0), drain=True)
                tq0 = ph.dma("sync", Qh[0:64, :].rearrange("p t -> p t"),
                             qu_dram.ap()[h * 64:(h + 1) * 64, :], ld,
                             waits=[prev_evac])
                tq0s = ph.settle("sync", ld)
                tq2 = ph.op("vector", lambda q: q.tensor_copy(
                    Qh[64:65, :], rowB[:, :]), waits=[tq0s], drain=True, inc=ev,
                    needs=("u4rowB",))
                prev_copies = tq2
                texp = {}
                tav = {}
                for j in range(NJ):
                    qtr = j % 4
                    pA = pts[(j % 2) * 2]
                    pB = pts[(j % 2) * 2 + 1]
                    scrh = scr_dram_a if h < 8 else scr_dram_b
                    tpa = ph.dmat("sync", pA[:, :],
                                  bass.AP(scrh, (h % 8) * (SROWS * SC) + 4096 + j * 128,
                                          [[4096, 2048], [1, 128]]), ld,
                                  waits=[ptguard.get((j % 2, 0))])
                    tpb = ph.dmat("sync", pB[:, :],
                                  bass.AP(scrh, (h % 8) * (SROWS * SC) + 2048 + j * 128,
                                          [[4096, 2048], [1, 128]]), ld,
                                  waits=[ptguard.get((j % 2, 1))])
                    tps = ph.settle("sync", ld)
                    vload = ph.dma("sync", Vh[j % 4][:, :],
                                   ex_out.ap()[j * 128:(j + 1) * 128,
                                               1024 + h * 65:1024 + (h + 1) * 65],
                                   ld, waits=[vguard.get(j % 4)])
                    tvl = ph.settle("sync", ld)
                    for ig in range(4):
                        ph.op("tensor", lambda q, j=j, ig=ig: q.matmul(
                            p_sc[:, ig * TG:(ig + 1) * TG],
                            Kh[:, j * 128:(j + 1) * 128],
                            Qh[:, ig * TG:(ig + 1) * TG],
                            start=True, stop=False),
                            waits=[tq2, texp.get(j - 1), P.marks.get("u4scr"),
                                   P.marks.get("u4ih")] if ig == 0 else ())
                        ph.op("tensor", lambda q, pA=pA, ig=ig: q.matmul(
                            p_sc[:, ig * TG:(ig + 1) * TG], Ih0[:, :],
                            pA[:, ig * TG:(ig + 1) * TG], start=False, stop=False),
                            waits=[tps] if ig == 0 else ())
                        mm = ph.op("tensor", lambda q, pB=pB, ig=ig: q.matmul(
                            p_sc[:, ig * TG:(ig + 1) * TG], Ih1[:, :],
                            pB[:, ig * TG:(ig + 1) * TG], start=False, stop=True),
                            inc=et if ig == 3 else None)
                    ptguard[(j % 2, 0)] = mm
                    ptguard[(j % 2, 1)] = mm
                    te = ph.op("scalar", lambda q: q.activation(
                        PT[:, :], p_sc[:, :], AF.Exp),
                        waits=[mm, tav.get(j - 1)], inc=es_)
                    texp[j] = te
                    for ig in range(4):
                        av = ph.op("tensor", lambda q, j=j, ig=ig: q.matmul(
                            p_o[ig][:, :], Vh[j % 4][:, :], PT[:, ig * TG:(ig + 1) * TG],
                            start=(j == 0), stop=(j == NJ - 1)),
                            waits=[te, tvl] if ig == 0 else (),
                            inc=et if ig == 3 else None)
                    tav[j] = av
                    vguard[j % 4] = av
                ch = h // 2
                tbc = None
                for ig in range(4):
                    trd = ph.op("vector", lambda q, ig=ig: q.reciprocal(
                        rden[:, :], p_o[ig][64:65, :]), waits=[tav[NJ - 1], tbc],
                        drain=True, inc=ev)
                    tbc = ph.op("tensor", lambda q, ig=ig: q.matmul(
                        p_sc[:, 0:TG], C.ones_row[:, 0:128], rden[:, :],
                        start=True, stop=True), waits=[trd, texp[NJ - 1]], inc=et)
                    tcp = ph.op("scalar", lambda q, ig=ig, base=base: q.activation(
                        oscr[base:base + 64, :], p_o[ig][0:64, :], AF.Copy),
                        waits=[tav[NJ - 1], prev_evac], inc=es_)
                    prev_evac = ph.op("vector", lambda q, ig=ig, ch=ch, base=base: q.tensor_mul(
                        ATT[base:base + 64, ch * T + ig * TG:ch * T + (ig + 1) * TG],
                        oscr[base:base + 64, :], p_sc[base:base + 64, 0:TG]),
                        waits=[tbc, tcp], drain=True, inc=ev)
        P.mark("u4att", prev_evac)

    with ExitStack() as st:
        E = st.enter_context
        Wo = E(nc.sbuf_tensor("u4_Wob", [128, NCH * 1024], BF16))
        p_o2 = [E(nc.psum_tensor(f"u4_po2{i}", [128, TG], F32)) for i in range(2)]
        with Ph(P, "u4_proj") as ph:
            ph.dma("gpsimd", Wo[:, :].rearrange("p (c d) -> p c d", c=NCH),
                   a_wo.rearrange("(c p) d -> p c d", p=128), ldg)
            tlw = ph.settle("gpsimd", ldg)
            i = 0
            evt = {}
            for m in range(NCH):
                for g in range(NTG):
                    po = p_o2[i % 2]
                    a, b = g * TG, (g + 1) * TG
                    for k in range(NCH):
                        mm = ph.op("tensor", lambda q, po=po, k=k, m=m, a=a, b=b: q.matmul(
                            po[:, :], Wo[:, k * 1024 + m * 128:k * 1024 + (m + 1) * 128],
                            ATT[:, k * T + a:k * T + b],
                            start=(k == 0), stop=(k == NCH - 1)),
                            waits=[tlw, evt.get(i - 2)] if k == 0 else (),
                            needs=("u4att",),
                            inc=et if k == NCH - 1 else None)
                    evt[i] = ph.op("vector", lambda q, po=po, m=m, a=a, b=b: q.scalar_tensor_tensor(
                        C.X[:, m * T + a:m * T + b], po[:, :], biasc[:, 32 + m:33 + m],
                        C.X[:, m * T + a:m * T + b], ALU.add, ALU.add),
                        waits=[mm], drain=True, inc=ev)
                    i += 1
            P.mark("X", evt[i - 1])

    stack23.__exit__(None, None, None)
    ustack.__exit__(None, None, None)
    emit_ln(P, C, "u4ln")




def emit_u_l0attn(P, C, W):
    nc = P.nc
    H = 4
    HALO = 64
    TH = HALO + T
    VW = 32 + T
    NGRP = T // 128

    a_wq = P.in_("u0_Wq", np.asarray(W["L0_attn_W_q"], np.float32))
    a_wk = P.in_("u0_Wk", np.asarray(W["L0_attn_W_k"], np.float32))
    a_wv = P.in_("u0_Wv", np.asarray(W["L0_attn_W_v"], np.float32))
    a_wb = P.in_("u0_Wb", np.asarray(W["L0_attn_W_b"], np.float32))
    a_wf1 = P.in_("u0_Wf1", np.asarray(W["L0_attn_W_fuse1"], np.float32))
    a_wf2 = P.in_("u0_Wf2", np.asarray(W["L0_attn_W_fuse2"], np.float32))
    a_wo = P.in_("u0_Wo", np.asarray(W["L0_attn_W_o"], np.float32))
    cw3 = [np.asarray(W[f"L0_attn_conv_w_{nm}"], np.float32)
           .reshape(NCH, 128, 4).transpose(1, 0, 2).reshape(128, NCH * 4)
           for nm in ("q", "k", "v")]
    a_cw = P.in_("u0_cw", np.concatenate(cw3, axis=1))
    fw_s = np.asarray(W["L0_attn_fir_short_w"], np.float32).reshape(1024, 3)
    fw_l = np.asarray(W["L0_attn_fir_long_w"], np.float32).reshape(1024, 31)
    a_fir = P.in_("u0_fir", np.concatenate(
        [fw_s.reshape(NCH, 128, 3).transpose(1, 0, 2).reshape(128, NCH * 3),
         fw_l.reshape(NCH, 128, 31).transpose(1, 0, 2).reshape(128, NCH * 31)],
        axis=1))
    a_bf1 = P.in_("u0_bf1", np.ascontiguousarray(
        np.asarray(W["L0_attn_b_fuse1"], np.float32).reshape(16, 128).T))
    a_bf2 = P.in_("u0_bf2", np.asarray(W["L0_attn_b_fuse2"], np.float32)
                  .reshape(1, 16))
    a_tau = P.in_("u0_tau", np.asarray(W["L0_attn_others_log_tau"], np.float32)
                  .reshape(1, 4))
    a_nw = P.in_("u0_nw", np.ascontiguousarray(
        np.asarray(W["L0_attn_norm_w"], np.float32).reshape(2, 128).T))
    mneg = np.zeros((32, 32), np.float32)
    mloc = np.zeros((32, 32), np.float32)
    for j in range(32):
        mneg[j, j + 1:] = -1.0
        mloc[j, j:] = 1.0
    a_mneg = P.in_("u0_mneg", mneg)
    mloc = np.tile(mloc, (4, 1))
    sel16 = np.zeros((16, 16 * 128), np.float32)
    for r in range(16):
        sel16[r, r * 128:(r + 1) * 128] = 1.0
    a_sel = P.in_("u0_sel", sel16)
    a_mloc = P.in_("u0_mloc", mloc)
    a_gam = P.in_("u0_gam", [np.full((128, 1), float(c % 2), np.float32)
                             for c in range(NCORES)])
    xf = np.asarray(W["L0_attn_hidden_states"], np.float32)
    xh_shards = []
    for c in range(NCORES):
        bidx, half = c // 2, c % 2
        if half == 0:
            xh_shards.append(np.zeros((D, HALO), np.float32))
        else:
            xh_shards.append(np.ascontiguousarray(xf[bidx, T - HALO:T, :].T))
    a_xh = P.in_("u0_xh", xh_shards)

    qn_dram = P.dram("u0_qn", [1024, T], BF16)
    kn_dram = P.dram("u0_kn", [1024, T], BF16)
    v_dram = P.dram("u0_v", [1024, VW], BF16)
    b_dram = P.dram("u0_bd", [4, T], F32)
    bn_dram = P.dram("u0_bn", [32, 16 * 512], F32)
    ab_dram = P.dram("u0_ab", [128, 2048], F32)
    u_dram = P.dram("u0_ud", [256 * 32, 256], BF16)
    w_dram = P.dram("u0_wd", [256 * 256, 32], BF16)
    fs_dram = P.dram("u0_fsd", [1024, T], BF16)
    fl_dram = P.dram("u0_fld", [1024, T], BF16)
    dl_dram = P.dram("u0_dld", [1024, T], BF16)
    s_in = P.dram("u0_sin", [1024, 256], F32)
    s_out = P.dram("u0_sout", [2048, 256], F32)

    ld = P.gsem("dma_sync")
    ldg = P.gsem("dma_gp")
    std = P.gsem("dma_st")
    ccs = P.gsem("cc")
    ev = P.gsem("ev")
    es_ = P.gsem("es")
    et = P.gsem("et")

    ustack = ExitStack()
    ustack.__enter__()
    UE = ustack.enter_context
    cwb = UE(nc.sbuf_tensor("u0_cwb", [128, 96], F32))
    firw = UE(nc.sbuf_tensor("u0_firw", [128, 272], F32))
    mnegb = UE(nc.sbuf_tensor("u0_mnegb", [32, 32], F32))
    mlocb = UE(nc.sbuf_tensor("u0_mlocb", [128, 32], F32))
    gamc = UE(nc.sbuf_tensor("u0_gamc", [128, 1], F32))
    bf1b = UE(nc.sbuf_tensor("u0_bf1b", [128, 16], F32))
    bf2r = UE(nc.sbuf_tensor("u0_bf2r", [1, 16], F32))
    taur = UE(nc.sbuf_tensor("u0_taur", [1, 4], F32))
    nwb = UE(nc.sbuf_tensor("u0_nwb", [128, 2], F32))
    wf2b = UE(nc.sbuf_tensor("u0_wf2b", [128, 256], BF16))
    beta = UE(nc.sbuf_tensor("u0_beta", [4, T], F32))
    selb = UE(nc.sbuf_tensor("u0_selb", [16, 16 * 128], F32))
    XH = UE(nc.sbuf_tensor("u0_XH", [128, NCH * HALO], BF16))

    with Ph(P, "u0_ld") as ph:
        ph.dma("sync", cwb[:, :], a_cw, ld)
        ph.dma("sync", firw[:, :], a_fir, ld)
        ph.dma("sync", mnegb[:, :], a_mneg, ld)
        ph.dma("sync", mlocb[:, :], a_mloc, ld)
        ph.dma("sync", gamc[:, :], a_gam, ld)
        ph.dma("sync", bf1b[:, :], a_bf1, ld)
        ph.dma("sync", bf2r[:, :], a_bf2, ld)
        ph.dma("sync", taur[:, :], a_tau, ld)
        ph.dma("sync", nwb[:, :], a_nw, ld)
        ph.dma("sync", selb[:, :], a_sel, ld)
        ph.dma("gpsimd", wf2b[:, :].rearrange("p (c j) -> p c j", j=16),
               a_wf2.rearrange("(c p) j -> p c j", p=128), ldg)
        ph.dma("gpsimd", XH[:, :].rearrange("p (c t) -> p c t", c=NCH),
               a_xh.rearrange("(c p) t -> p c t", p=128), ldg)
        P.mark("u0ld", ph.settle("sync", ld))
        P.mark("u0ldg", ph.settle("gpsimd", ldg))

    with ExitStack() as st:
        E = st.enter_context
        Wb = E(nc.sbuf_tensor("u0_Wbuf", [128, NCH * 1024], BF16))
        pj = E(nc.sbuf_tensor("u0_pj", [128, TH], F32))
        cs = E(nc.sbuf_tensor("u0_cs", [128, TH], F32))
        allb = E(nc.sbuf_tensor("u0_allb", [128, 2 * VW], BF16))
        sqs = E(nc.sbuf_tensor("u0_sqs", [128, T], F32))
        stg = E(nc.sbuf_tensor("u0_stg", [128, 2 * T], BF16))
        rsr = E(nc.sbuf_tensor("u0_rsr", [1, T], F32))
        p_a = [E(nc.psum_tensor(f"u0_pa{i}", [128, TG], F32)) for i in range(2)]
        p_s = E(nc.psum_tensor("u0_ps", [1, T], F32))
        p_b = [E(nc.psum_tensor(f"u0_pb{i}", [128, TG], F32)) for i in range(2)]

        pev = {}
        gctr = 0
        tlast_mm = None
        tfree = None
        tstg_free = None
        tsq_war = None
        tsilp = None
        tps_free = None
        groups = [(-HALO, 0)] + [(g * TG, (g + 1) * TG) for g in range(NTG)]
        for pi, (wap, cwoff, dst) in enumerate(
                [(a_wq, 0, qn_dram), (a_wk, 32, kn_dram), (a_wv, 64, v_dram)]):
            with Ph(P, f"u0_pw{pi}") as ph:
                ph.dma("gpsimd", Wb[:, :].rearrange("p (c d) -> p c d", c=NCH),
                       wap.rearrange("(c p) d -> p c d", p=128), ldg,
                       waits=[tlast_mm])
                tlw = ph.settle("gpsimd", ldg)
            for hp in range(4):
                with Ph(P, f"u0_p{pi}_{hp}") as ph:
                    tsil = {}
                    for ci in range(2):
                        m = 2 * hp + ci
                        for (aa, bb) in groups:
                            po = p_a[gctr % 2]
                            wdt = bb - aa
                            for k in range(NCH):
                                if aa < 0:
                                    rhs = XH[:, k * HALO:(k + 1) * HALO]
                                else:
                                    rhs = C.Xb[:, k * T + aa:k * T + bb]
                                w0 = [tlw, pev.get(gctr - 2)] if k == 0 else ()
                                tmm = ph.op("tensor", lambda q, po=po, k=k, m=m,
                                            rhs=rhs, wdt=wdt: q.matmul(
                                    po[:, 0:wdt],
                                    Wb[:, k * 1024 + m * 128:k * 1024 + (m + 1) * 128],
                                    rhs, start=(k == 0), stop=(k == NCH - 1)),
                                    waits=w0,
                                    needs=("Xb", "u0ldg") if k == 0 else (),
                                    inc=et if k == NCH - 1 else None)
                            off = HALO + aa
                            pev[gctr] = ph.op("vector", lambda q, po=po, off=off,
                                              wdt=wdt: q.tensor_copy(
                                pj[:, off:off + wdt], po[:, 0:wdt]),
                                waits=[tmm], drain=True, inc=ev)
                            gctr += 1
                            tlast_mm = tmm
                        c0 = HALO - 32
                        t1 = ph.op("vector", lambda q, m=m, c0=c0: q.tensor_scalar(
                            cs[:, c0:TH], pj[:, c0 - 3:TH - 3],
                            cwb[:, cwoff + m * 4:cwoff + m * 4 + 1], None,
                            ALU.mult), waits=[pev[gctr - 1], tsilp],
                            needs=("u0ld",), drain=True)
                        for j in range(1, 4):
                            t1 = ph.op("vector", lambda q, m=m, j=j, c0=c0:
                                       q.scalar_tensor_tensor(
                                cs[:, c0:TH], pj[:, c0 - 3 + j:TH - 3 + j],
                                cwb[:, cwoff + m * 4 + j:cwoff + m * 4 + j + 1],
                                cs[:, c0:TH], ALU.mult, ALU.add), drain=True,
                                inc=ev if j == 3 else None)
                        tsil[ci] = ph.op("scalar", lambda q, ci=ci, c0=c0:
                                         q.activation(
                            allb[:, ci * VW:ci * VW + VW], cs[:, c0:TH], AF.Silu),
                            waits=[t1, tfree], inc=es_)
                        tsilp = tsil[ci]
                    if pi < 2:
                        tq = None
                        for ci in range(2):
                            tsq = ph.op("scalar", lambda q, ci=ci: q.activation(
                                sqs[:, :], allb[:, ci * VW + 32:ci * VW + 32 + T],
                                AF.Square),
                                waits=[tsil[ci], tq, tsq_war if ci == 0 else None],
                                inc=es_)
                            for g in range(NTG):
                                a, b = g * TG, (g + 1) * TG
                                tq = ph.op("tensor", lambda q, a=a, b=b, ci=ci:
                                           q.matmul(
                                    p_s[:, a:b], C.ones_col[:, :], sqs[:, a:b],
                                    start=(ci == 0), stop=(ci == 1)),
                                    waits=[tsq, tps_free if ci == 0 and g == 0
                                           else None],
                                    needs=("consts",),
                                    inc=et if g == NTG - 1 else None)
                        tsq_war = tq
                        t1 = ph.op("vector", lambda q: q.tensor_scalar(
                            rsr[:, :], p_s[0:1, :], 1.0, 1e-6, ALU.mult, ALU.add),
                            waits=[tq], drain=True, inc=ev)
                        tps_free = t1
                        t2 = ph.op("scalar", lambda q: q.activation(
                            rsr[:, :], rsr[:, :], AF.Ln), waits=[t1], drain=True)
                        t3 = ph.op("scalar", lambda q: q.activation(
                            rsr[:, :], rsr[:, :], AF.Exp, scale=-0.5), drain=True,
                            inc=es_)
                        tlast = None
                        tlg = {}
                        for g in range(NTG):
                            a, b = g * TG, (g + 1) * TG
                            pbg = p_b[g % 2]
                            tbc = ph.op("tensor", lambda q, pbg=pbg, a=a, b=b:
                                        q.matmul(
                                pbg[:, :], C.ones_row[0:1, 0:128], rsr[0:1, a:b],
                                start=True, stop=True),
                                waits=[t3, tstg_free, tlg.get(g - 2)], inc=et)
                            for ci in range(2):
                                tlast = ph.op("vector", lambda q, ci=ci, a=a, b=b,
                                              pbg=pbg: q.tensor_mul(
                                    stg[:, ci * T + a:ci * T + b],
                                    allb[:, ci * VW + 32 + a:ci * VW + 32 + b],
                                    pbg[:, :]), waits=[tbc], drain=True, inc=ev)
                            tlg[g] = tlast
                        tfree = tlast
                        for ci in range(2):
                            m = 2 * hp + ci
                            ph.dma("sync", dst.ap()[m * 128:(m + 1) * 128, :],
                                   stg[:, ci * T:(ci + 1) * T], std,
                                   waits=[tlast])
                        tstg_free = ph.settle("sync", std)
                    else:
                        for ci in range(2):
                            m = 2 * hp + ci
                            ph.dma("sync", dst.ap()[m * 128:(m + 1) * 128, :],
                                   allb[:, ci * VW:(ci + 1) * VW], std,
                                   waits=[tsil[ci]])
                        tfree = ph.settle("sync", std)
                        if hp == 3:
                            P.mark("u0v", tfree)
            if pi == 0:
                P.mark("u0qn", tstg_free)
            elif pi == 1:
                P.mark("u0kn", tstg_free)

        with Ph(P, "u0_beta") as ph:
            ph.dma("gpsimd", Wb[:, 0:NCH * 4].rearrange("p (c d) -> p c d", c=NCH),
                   a_wb.rearrange("(c p) d -> p c d", p=128), ldg,
                   waits=[tlast_mm])
            tlb = ph.settle("gpsimd", ldg)
            tbeta = None
            for g in range(NTG):
                a, b = g * TG, (g + 1) * TG
                po = p_a[g % 2]
                for k in range(NCH):
                    tmm = ph.op("tensor", lambda q, po=po, k=k, a=a, b=b: q.matmul(
                        po[0:4, 0:TG], Wb[:, k * 4:(k + 1) * 4],
                        C.Xb[:, k * T + a:k * T + b],
                        start=(k == 0), stop=(k == NCH - 1)),
                        waits=[tlb, tbeta, pev.get(gctr - 1), pev.get(gctr - 2)]
                        if k == 0 else (),
                        inc=et if k == NCH - 1 else None)
                tbeta = ph.op("scalar", lambda q, po=po, a=a, b=b: q.activation(
                    beta[:, a:b], po[0:4, 0:TG], AF.Sigmoid), waits=[tmm],
                    inc=es_)
            ph.dma("sync", b_dram.ap(), beta[:, :], std, waits=[tbeta])
            P.mark("u0beta", ph.settle("sync", std))

    with ExitStack() as st:
        E = st.enter_context
        vball = E(nc.sbuf_tensor("u0_vball", [128, VW], BF16))
        fscr = E(nc.sbuf_tensor("u0_fscr", [128, T], F32))
        fout = E(nc.sbuf_tensor("u0_fout", [128, 2 * T], BF16))
        tprev = None
        tstore = None
        for m in range(NCH):
            with Ph(P, f"u0_fir{m}") as ph:
                ph.dma("sync", vball[:, :], v_dram.ap()[m * 128:(m + 1) * 128, :],
                       ld, needs=("u0v", "u0ld"), waits=[tprev])
                tvs = ph.settle("sync", ld)
                for fi, (ntap, woff, dstd) in enumerate(
                        [(3, m * 3, fs_dram), (31, 24 + m * 31, fl_dram)]):
                    s0 = 32 - (ntap - 1)
                    t1 = ph.op("vector", lambda q, woff=woff, s0=s0:
                               q.tensor_scalar(
                        fscr[:, :], vball[:, s0:s0 + T],
                        firw[:, woff:woff + 1], None, ALU.mult),
                        waits=[tvs, tstore], drain=True)
                    for j in range(1, ntap):
                        t1 = ph.op("vector", lambda q, woff=woff, j=j, s0=s0:
                                   q.scalar_tensor_tensor(
                            fscr[:, :], vball[:, s0 + j:s0 + j + T],
                            firw[:, woff + j:woff + j + 1], fscr[:, :],
                            ALU.mult, ALU.add), drain=True)
                    tprev = ph.op("vector", lambda q, fi=fi: q.tensor_copy(
                        fout[:, fi * T:(fi + 1) * T], fscr[:, :]), drain=True,
                        inc=ev)
                    ph.dma("sync", dstd.ap()[m * 128:(m + 1) * 128, :],
                           fout[:, fi * T:(fi + 1) * T], std, waits=[tprev])
                tstore = ph.settle("sync", std)
        with Ph(P, "u0_firdone") as ph:
            P.mark("u0fir", tstore)

    with ExitStack() as st:
        E = st.enter_context
        knb = E(nc.sbuf_tensor("u0_knb", [128, 2 * T], BF16))
        bbc = E(nc.sbuf_tensor("u0_bbc", [128, T], F32))
        kbb = E(nc.sbuf_tensor("u0_kbb", [128, 2 * T], BF16))
        abat = E(nc.sbuf_tensor("u0_abat", [32, 512], F32))
        AB = E(nc.sbuf_tensor("u0_AB", [128, 2048], F32))
        rowi = E(nc.sbuf_tensor("u0_rowi", [128, 32], F32))
        p_t = E(nc.psum_tensor("u0_pt", [32, 32], F32))
        p_bb = [E(nc.psum_tensor(f"u0_pbb{i}", [128, TG], F32)) for i in range(2)]

        tknfree = None
        tabset = None
        for hv in range(2):
            for hh in range(2):
                h = 2 * hv + hh
                with Ph(P, f"u0_att{h}") as ph:
                    ph.dma("sync", knb[:, :].rearrange("p (c t) -> p c t", c=2),
                           kn_dram.ap()[h * 256:(h + 1) * 256, :].rearrange(
                               "(c p) t -> p c t", p=128), ld,
                           needs=("u0kn",), waits=[tknfree])
                    tk = ph.settle("sync", ld)
                    tcb = None
                    for g in range(NTG):
                        a, b = g * TG, (g + 1) * TG
                        pbg = p_bb[g % 2]
                        tbc = ph.op("tensor", lambda q, pbg=pbg, h=h, a=a, b=b:
                                    q.matmul(
                            pbg[:, :], selb[0:4, h * 128:(h + 1) * 128],
                            beta[0:4, a:b],
                            start=True, stop=True), needs=("u0beta", "u0ld"),
                            waits=[tcb, tknfree if g < 2 else None], inc=et)
                        tcb = ph.op("vector", lambda q, pbg=pbg, a=a, b=b:
                                    q.tensor_copy(bbc[:, a:b], pbg[:, :]),
                                    waits=[tbc], drain=True, inc=ev)
                    tkb = ph.op("vector", lambda q: q.tensor_mul(
                        kbb[:, 0:T], knb[:, 0:T], bbc[:, :]),
                        waits=[tk, tcb, tknfree], drain=True)
                    tkb = ph.op("vector", lambda q: q.tensor_mul(
                        kbb[:, T:2 * T], knb[:, T:2 * T], bbc[:, :]), drain=True,
                        inc=ev)
                    tmm = None
                    for bat in range(4):
                        cidx = h * 4 + bat
                        evb = {}
                        for ci in range(16):
                            c = bat * 16 + ci
                            for dc in range(2):
                                tmm = ph.op("tensor", lambda q, c=c, dc=dc:
                                            q.matmul(
                                    p_t[:, :],
                                    knb[:, dc * T + c * 32:dc * T + (c + 1) * 32],
                                    kbb[:, dc * T + c * 32:dc * T + (c + 1) * 32],
                                    start=(dc == 0), stop=(dc == 1)),
                                    waits=[tkb, evb.get(ci - 1), tabset]
                                    if dc == 0 else (),
                                    inc=et if dc == 1 else None)
                            evb[ci] = ph.op("vector", lambda q, ci=ci:
                                            q.tensor_mul(
                                abat[:, ci * 32:(ci + 1) * 32], p_t[:, :],
                                mnegb[:, :]), waits=[tmm], drain=True, inc=ev)
                        ph.dma("sync",
                               bn_dram.ap()[:, cidx * 512:(cidx + 1) * 512],
                               abat[:, :], std, waits=[evb[15]])
                        tabset = ph.settle("sync", std)
                        r0 = hh * 64 + bat * 16
                        ph.dma("sync",
                               AB[r0:r0 + 16, hv * 1024:(hv + 1) * 1024],
                               bass.AP(bn_dram, cidx * 512,
                                       [[32, 16], [8192, 32], [1, 32]]),
                               ld, waits=[tabset])
                    tknfree = tmm
        with Ph(P, "u0_absettle") as ph:
            P.mark("u0abin", ph.settle("sync", ld))

        with Ph(P, "u0_rec") as ph:
            trec = None
            for hv in range(2):
                o0 = hv * 1024
                for i in range(1, 32):
                    ph.op("vector", lambda q, o0=o0, i=i: q.tensor_copy(
                        rowi[:, 0:i], AB[:, o0 + i:o0 + i + 32 * i:32]),
                        needs=("u0abin",), drain=True)
                    for m in range(i):
                        ph.op("vector", lambda q, o0=o0, i=i, m=m:
                              q.scalar_tensor_tensor(
                            AB[:, o0 + i:o0 + i + 32 * i:32],
                            AB[:, o0 + m:o0 + m + 32 * i:32],
                            rowi[:, m:m + 1],
                            AB[:, o0 + i:o0 + i + 32 * i:32],
                            ALU.mult, ALU.add), drain=True)
                trec = ph.op("vector", lambda q, o0=o0: q.tensor_scalar(
                    AB[:, o0:o0 + 1024:33], AB[:, o0:o0 + 1024:33],
                    1.0, None, ALU.add), drain=True, inc=ev)
            ph.dma("sync", ab_dram.ap(), AB[:, :], std, waits=[trec])
            P.mark("u0ab", ph.settle("sync", std))

    with ExitStack() as st:
        E = st.enter_context
        vtok = E(nc.sbuf_tensor("u0_vtok", [128, 256], BF16))
        ktok = E(nc.sbuf_tensor("u0_ktok", [128, 256], BF16))
        vbtok = E(nc.sbuf_tensor("u0_vbtok", [128, 256], BF16))
        kbtok = E(nc.sbuf_tensor("u0_kbtok", [128, 256], BF16))
        btok = E(nc.sbuf_tensor("u0_btok", [128, 1], F32))
        TT2z = [E(nc.sbuf_tensor(f"u0_TT2z{i}", [128, 32], BF16))
                for i in range(4)]
        usb = E(nc.sbuf_tensor("u0_usb", [32, 2 * 256], BF16))
        wsb = E(nc.sbuf_tensor("u0_wsb", [128, 2 * 64], BF16))
        p_u = E(nc.psum_tensor("u0_pu", [32, 256], F32))
        p_w = [E(nc.psum_tensor(f"u0_pw{i}", [128, 32], F32)) for i in range(2)]

        tml = {}
        tsl_store = {}
        te2p = None
        with Ph(P, "u0_uwz") as ph:
            tz = None
            for i in range(4):
                tz = ph.op("vector", lambda q, i=i: q.memset(
                    TT2z[i][:, :], 0.0), drain=True, inc=ev if i == 3 else None)
            P.mark("u0ttz", tz)
        for h in range(H):
            for grp in range(NGRP):
                with Ph(P, f"u0_uw{h}_{grp}") as ph:
                    i0 = h * 64 + grp * 4
                    ph.dmat("sync", vtok[:, :],
                            v_dram.ap()[h * 256:(h + 1) * 256,
                                        32 + grp * 128:32 + (grp + 1) * 128],
                            ld, waits=[tml.get(i0 - 1)] + P.need("u0v", "u0ab"))
                    ph.dmat("sync", ktok[:, :],
                            kn_dram.ap()[h * 256:(h + 1) * 256,
                                         grp * 128:(grp + 1) * 128], ld)
                    ph.dma("sync", btok[:, :],
                           bass.AP(b_dram, h * T + grp * 128, [[1, 128], [1, 1]]),
                           ld, needs=("u0beta",))
                    tl = ph.settle("sync", ld)
                    tb2 = ph.op("vector", lambda q: q.tensor_scalar(
                        kbtok[:, :], ktok[:, :], btok[:, 0:1], None, ALU.mult),
                        waits=[tl], drain=True)
                    tb2 = ph.op("vector", lambda q: q.tensor_scalar(
                        vbtok[:, :], vtok[:, :], btok[:, 0:1], None, ALU.mult),
                        drain=True, inc=ev)
                    for ci in range(4):
                        c = grp * 4 + ci
                        inst = h * 64 + c
                        hv, p = inst // 128, inst % 128
                        sl = ci % 2
                        ph.dma("gpsimd",
                               TT2z[ci][ci * 32:(ci + 1) * 32, :],
                               bass.AP(ab_dram, p * 2048 + hv * 1024,
                                       [[32, 32], [1, 32]]), ldg,
                               waits=[tml.get(inst - 4)], needs=("u0ttz",))
                        tlt = ph.settle("gpsimd", ldg)
                        tu = ph.op("tensor", lambda q, ci=ci: q.matmul(
                            p_u[:, :], TT2z[ci][:, :], vbtok[:, :],
                            start=True, stop=True),
                            waits=[tlt, tb2, tsl_store.get(sl), te2p], inc=et)
                        tw = {}
                        for dc in range(2):
                            tw[dc] = ph.op("tensor", lambda q, ci=ci,
                                           dc=dc: q.matmul(
                                p_w[dc][:, :],
                                kbtok[:, dc * 128:(dc + 1) * 128],
                                TT2z[ci][:, :],
                                start=True, stop=True), inc=et)
                        tml[inst] = tw[1]
                        te1 = ph.op("vector", lambda q, sl=sl: q.tensor_copy(
                            usb[:, sl * 256:(sl + 1) * 256], p_u[:, :]),
                            waits=[tu, tw[1]], drain=True)
                        te2 = None
                        for dc in range(2):
                            te2 = ph.op("vector", lambda q, sl=sl, dc=dc:
                                        q.tensor_copy(
                                wsb[:, sl * 64 + dc * 32:sl * 64 + (dc + 1) * 32],
                                p_w[dc][:, :]), drain=True, inc=ev)
                        te2p = te2
                        ph.dma("sync",
                               u_dram.ap()[inst * 32:(inst + 1) * 32, :],
                               usb[:, sl * 256:(sl + 1) * 256], std, waits=[te2])
                        for dc in range(2):
                            ph.dma("sync",
                                   w_dram.ap()[inst * 256 + dc * 128:
                                               inst * 256 + (dc + 1) * 128, :],
                                   wsb[:, sl * 64 + dc * 32:
                                       sl * 64 + (dc + 1) * 32], std)
                        tsl_store[sl] = ph.settle("sync", std)
        with Ph(P, "u0_uwdone") as ph:
            P.mark("u0uw", ph.settle("sync", std))

    with ExitStack() as st:
        E = st.enter_context
        Ssb = E(nc.sbuf_tensor("u0_Ssb", [128, 512], F32))
        Ssbf = E(nc.sbuf_tensor("u0_Ssbf", [128, 512], BF16))
        ktok2 = E(nc.sbuf_tensor("u0_ktok2", [128, 2 * 256], BF16))
        usb2 = E(nc.sbuf_tensor("u0_usb2", [32, 2 * 256], BF16))
        wsb2 = E(nc.sbuf_tensor("u0_wsb2", [128, 2 * 64], BF16))
        uadj0 = E(nc.sbuf_tensor("u0_uadj0", [32, 2 * 256], BF16))
        uadjz = [E(nc.sbuf_tensor(f"u0_uadjz{i}", [128, 256], BF16))
                 for i in range(4)]
        qnf = E(nc.sbuf_tensor("u0_qnf", [128, 2 * 256], BF16))
        knf = E(nc.sbuf_tensor("u0_knf", [128, 2 * 256], BF16))
        locb = E(nc.sbuf_tensor("u0_locb", [128, 32], BF16))
        locb0 = E(nc.sbuf_tensor("u0_locb0", [32, 32], BF16))
        dls = E(nc.sbuf_tensor("u0_dls", [128, 2 * 256], BF16))
        p_ws = E(nc.psum_tensor("u0_pws", [32, 256], F32))
        p_dS = [E(nc.psum_tensor(f"u0_pds{i}", [128, 256], F32)) for i in range(2)]
        p_loc = E(nc.psum_tensor("u0_ploc", [32, 32], F32))
        p_o = [E(nc.psum_tensor(f"u0_po{i}", [128, 32], F32)) for i in range(2)]

        with Ph(P, "u0_scz") as ph:
            for i in range(4):
                ph.op("vector", lambda q, i=i: q.memset(
                    uadjz[i][:, :], 0.0), drain=True)
            tz = ph.op("vector", lambda q: q.memset(locb[:, :], 0.0),
                       drain=True, inc=ev)
            P.mark("u0uaz", tz)

        def scan(phase2):
            tSstore = None
            tmirw = None
            tag = "s2" if phase2 else "s1"
            nn2 = ("u0cc",) if phase2 else ()
            tds = {}
            tuse = {}
            tdl = {}
            tloc = None
            for h in range(H):
                with Ph(P, f"u0_{tag}h{h}") as ph:
                    if phase2:
                        for dc in range(2):
                            ph.dma("sync", Ssb[:, dc * 256:(dc + 1) * 256],
                                   s_out.ap()[h * 256 + dc * 128:
                                              h * 256 + (dc + 1) * 128, :],
                                   ld, needs=("u0cc",), waits=[tSstore])
                        tlS = ph.settle("sync", ld)
                        tS0 = ph.op("vector", lambda q: q.tensor_scalar(
                            Ssb[:, :], Ssb[:, :], gamc[:, 0:1], None, ALU.mult),
                            waits=[tlS], drain=True, inc=ev)
                        tS = ph.op("scalar", lambda q: q.activation(
                            Ssbf[:, :], Ssb[:, :], AF.Copy),
                            waits=[tS0, tmirw], inc=es_)
                    else:
                        tS0 = ph.op("vector", lambda q: q.memset(Ssb[:, :], 0.0),
                                    needs=("u0uw",), waits=[tSstore],
                                    drain=True, inc=ev)
                        tS = ph.op("scalar", lambda q: q.activation(
                            Ssbf[:, :], Ssb[:, :], AF.Copy),
                            waits=[tS0, tmirw], inc=es_)
                    for grp in range(NGRP):
                        gg = h * NGRP + grp
                        slg = grp % 2
                        ph.dmat("sync", ktok2[:, slg * 256:(slg + 1) * 256],
                                kn_dram.ap()[h * 256:(h + 1) * 256,
                                             grp * 128:(grp + 1) * 128], ld,
                                waits=[tds.get((gg - 2, 0)), tds.get((gg - 2, 1))],
                                needs=("u0kn",) + nn2)
                        if phase2:
                            for dc in range(2):
                                ph.dma("sync",
                                       qnf[:, slg * 256 + dc * 128:
                                           slg * 256 + (dc + 1) * 128],
                                       qn_dram.ap()[h * 256 + dc * 128:
                                                    h * 256 + (dc + 1) * 128,
                                                    grp * 128:(grp + 1) * 128],
                                       ld, needs=("u0qn",),
                                       waits=[tdl.get((gg - 2, "st"))])
                                ph.dma("sync",
                                       knf[:, slg * 256 + dc * 128:
                                           slg * 256 + (dc + 1) * 128],
                                       kn_dram.ap()[h * 256 + dc * 128:
                                                    h * 256 + (dc + 1) * 128,
                                                    grp * 128:(grp + 1) * 128],
                                       ld, waits=[tloc])
                        tgl = ph.settle("sync", ld)
                        for ci in range(4):
                            c = grp * 4 + ci
                            cg = h * 64 + c
                            inst = cg
                            sl = ci % 2
                            ph.dma("sync", usb2[:, sl * 256:(sl + 1) * 256],
                                   u_dram.ap()[inst * 32:(inst + 1) * 32, :], ld,
                                   waits=[tuse.get(cg - 2)],
                                   needs=("u0uw",) + nn2)
                            for dc in range(2):
                                ph.dma("sync",
                                       wsb2[:, sl * 64 + dc * 32:
                                            sl * 64 + (dc + 1) * 32],
                                       w_dram.ap()[inst * 256 + dc * 128:
                                                   inst * 256 + (dc + 1) * 128,
                                                   :], ld)
                            tlu = ph.settle("sync", ld)
                            if phase2:
                                tlc = None
                                for dc in range(2):
                                    tlc = ph.op("tensor", lambda q, slg=slg,
                                                dc=dc, ci=ci: q.matmul(
                                        p_loc[:, :],
                                        knf[:, slg * 256 + dc * 128 + ci * 32:
                                            slg * 256 + dc * 128 + (ci + 1) * 32],
                                        qnf[:, slg * 256 + dc * 128 + ci * 32:
                                            slg * 256 + dc * 128 + (ci + 1) * 32],
                                        start=(dc == 0), stop=(dc == 1)),
                                        waits=[tgl, tloc] if dc == 0 else (),
                                        inc=et if dc == 1 else None)
                                tloc = ph.op("vector", lambda q: q.tensor_mul(
                                    locb0[:, :], p_loc[:, :], mlocb[0:32, :]),
                                    waits=[tlc], drain=True, inc=ev)
                                tlocs = ph.op("scalar", lambda q, ci=ci:
                                              q.activation(
                                    locb[ci * 32:(ci + 1) * 32, :],
                                    locb0[:, :], AF.Copy),
                                    waits=[tloc], inc=es_)
                            tws = None
                            for dc in range(2):
                                tws = ph.op("tensor", lambda q, sl=sl, dc=dc:
                                            q.matmul(
                                    p_ws[:, :],
                                    wsb2[:, sl * 64 + dc * 32:
                                         sl * 64 + (dc + 1) * 32],
                                    Ssbf[:, dc * 256:(dc + 1) * 256],
                                    start=(dc == 0), stop=(dc == 1)),
                                    waits=[tlu, tS] if dc == 0 else (),
                                    inc=et if dc == 1 else None)
                            tua = ph.op("vector", lambda q, sl=sl:
                                        q.tensor_tensor(
                                uadj0[:, sl * 256:(sl + 1) * 256],
                                usb2[:, sl * 256:(sl + 1) * 256],
                                p_ws[:, :], ALU.subtract),
                                waits=[tws], drain=True, inc=ev)
                            tshift = ph.op("scalar", lambda q, ci=ci, sl=sl:
                                           q.activation(
                                uadjz[ci][ci * 32:(ci + 1) * 32, :],
                                uadj0[:, sl * 256:(sl + 1) * 256], AF.Copy),
                                waits=[tua], needs=("u0uaz",), inc=es_)
                            tuse[cg] = tua
                            to = {}
                            if phase2:
                                for dj in range(2):
                                    for dc in range(2):
                                        ph.op("tensor", lambda q, dj=dj, dc=dc,
                                              slg=slg, ci=ci: q.matmul(
                                            p_o[dj][:, :],
                                            Ssbf[:, dc * 256 + dj * 128:
                                                 dc * 256 + (dj + 1) * 128],
                                            qnf[:, slg * 256 + dc * 128 + ci * 32:
                                                slg * 256 + dc * 128 +
                                                (ci + 1) * 32],
                                            start=(dc == 0), stop=False),
                                            waits=[tdl.get((cg - 1, dj))]
                                            if dc == 0 else ())
                                    to[dj] = ph.op("tensor", lambda q, dj=dj,
                                                   ci=ci: q.matmul(
                                        p_o[dj][:, :],
                                        uadjz[ci][:, dj * 128:(dj + 1) * 128],
                                        locb[:, :],
                                        start=False, stop=True),
                                        waits=[tshift, tlocs], inc=et)
                            for dc in range(2):
                                tds[(gg, dc)] = ph.op("tensor",
                                                      lambda q, slg=slg, ci=ci,
                                                      sl=sl, dc=dc: q.matmul(
                                    p_dS[dc][:, :],
                                    ktok2[:, slg * 256 + dc * 128:
                                          slg * 256 + (dc + 1) * 128],
                                    uadjz[ci][:, :],
                                    start=True, stop=True),
                                    waits=[tshift] if dc == 0 else (), inc=et)
                            for dc in range(2):
                                tS0 = ph.op("vector", lambda q, dc=dc:
                                            q.tensor_add(
                                    Ssb[:, dc * 256:(dc + 1) * 256],
                                    Ssb[:, dc * 256:(dc + 1) * 256],
                                    p_dS[dc][:, :]),
                                    waits=[tds[(gg, 1)]], drain=True, inc=ev)
                            tmirw = to[1] if phase2 else tws
                            tS = ph.op("scalar", lambda q: q.activation(
                                Ssbf[:, :], Ssb[:, :], AF.Copy),
                                waits=[tS0, tmirw], inc=es_)
                            if phase2:
                                for dj in range(2):
                                    tdl[(cg, dj)] = ph.op(
                                        "vector", lambda q, dj=dj, slg=slg,
                                        ci=ci: q.tensor_copy(
                                            dls[:, slg * 256 + dj * 128 +
                                                ci * 32:
                                                slg * 256 + dj * 128 +
                                                (ci + 1) * 32],
                                            p_o[dj][:, :]),
                                        waits=[to[dj],
                                               tdl.get((gg - 2, "st"))],
                                        drain=True, inc=ev)
                        if phase2:
                            for dj in range(2):
                                ph.dma("sync",
                                       dl_dram.ap()[h * 256 + dj * 128:
                                                    h * 256 + (dj + 1) * 128,
                                                    grp * 128:(grp + 1) * 128],
                                       dls[:, slg * 256 + dj * 128:
                                           slg * 256 + dj * 128 + 128],
                                       std,
                                       waits=[tdl[(cg, dj)]])
                            tdl[(gg, "st")] = ph.settle("sync", std)
                    if not phase2:
                        for dc in range(2):
                            ph.dma("sync",
                                   s_in.ap()[h * 256 + dc * 128:
                                             h * 256 + (dc + 1) * 128, :],
                                   Ssb[:, dc * 256:(dc + 1) * 256], std,
                                   waits=[tS])
                        tSstore = ph.settle("sync", std)
                    else:
                        tSstore = tS
            return tSstore

        t1s = scan(False)
        with Ph(P, "u0_cc") as ph:
            tcc = ph.op("gpsimd", lambda q: q.collective_compute(
                "AllGather", ALU.bypass, replica_groups=PAIRS,
                ins=[s_in.ap().opt()], outs=[s_out.ap().opt()]),
                waits=[t1s], inc=ccs)
            P.mark("u0cc", tcc)
        scan(True)
        with Ph(P, "u0_dldone") as ph:
            P.mark("u0dl", ph.settle("sync", std))

    with ExitStack() as st:
        E = st.enter_context
        fsb = E(nc.sbuf_tensor("u0_fsb", [128, NCH * TG], BF16))
        flb = E(nc.sbuf_tensor("u0_flb", [128, NCH * TG], BF16))
        dlb = E(nc.sbuf_tensor("u0_dlb", [128, NCH * TG], BF16))
        wf1m = E(nc.sbuf_tensor("u0_wf1m", [128, 2 * 32 * 128], BF16))
        hidb = E(nc.sbuf_tensor("u0_hidb", [128, 16 * TG], BF16))
        gl = E(nc.sbuf_tensor("u0_gl", [128, 16], F32))
        itb = E(nc.sbuf_tensor("u0_itb", [128, 16], F32))
        b2b = E(nc.sbuf_tensor("u0_b2b", [128, 16], F32))
        row16 = E(nc.sbuf_tensor("u0_row16", [1, 16], F32))
        pr = E(nc.sbuf_tensor("u0_pr", [128, 4], F32))
        fl_ = E(nc.sbuf_tensor("u0_fl_", [128, 4], F32))
        pv = E(nc.sbuf_tensor("u0_pv", [128, 4], F32))
        m3 = E(nc.sbuf_tensor("u0_m3", [128, 4], F32))
        s3 = E(nc.sbuf_tensor("u0_s3", [128, 4], F32))
        fac = E(nc.sbuf_tensor("u0_fac", [128, 4], F32))
        Grows = E(nc.sbuf_tensor("u0_Grows", [16, T], F32))
        p_h = [E(nc.psum_tensor(f"u0_phh{i}", [128, TG], F32)) for i in range(2)]
        p_f2 = E(nc.psum_tensor("u0_pf2", [128, 16], F32))
        p_tr = E(nc.psum_tensor("u0_ptr", [16, 128], F32))
        p_b2 = E(nc.psum_tensor("u0_pb2", [128, 16], F32))

        with Ph(P, "u0_gprep") as ph:
            t1 = ph.op("scalar", lambda q: q.activation(
                taur[:, :], taur[:, :], AF.Exp, scale=-1.0), needs=("u0ld",),
                drain=True, inc=es_)
            t2 = None
            for j in range(3):
                t2 = ph.op("vector", lambda q, j=j: q.tensor_copy(
                    row16[:, j:16:4], taur[:, :]), waits=[t1], drain=True)
            t3 = ph.op("vector", lambda q: q.tensor_copy(
                row16[:, 3:16:4], C.ones_row[0:1, 0:4]), needs=("consts",),
                drain=True, inc=ev)
            t4 = ph.op("tensor", lambda q: q.matmul(
                p_b2[:, :], C.ones_row[0:1, 0:128], row16[0:1, :],
                start=True, stop=True), waits=[t3], inc=et)
            t5 = ph.op("vector", lambda q: q.tensor_copy(itb[:, :], p_b2[:, :]),
                       waits=[t4], drain=True, inc=ev)
            t6 = ph.op("tensor", lambda q: q.matmul(
                p_b2[:, :], C.ones_row[0:1, 0:128], bf2r[0:1, :],
                start=True, stop=True), waits=[t5], inc=et)
            t7 = ph.op("vector", lambda q: q.tensor_copy(b2b[:, :], p_b2[:, :]),
                       waits=[t6], drain=True, inc=ev)
            P.mark("u0itb", t7)

        tgel = {}
        tf2 = {}
        tgfree = None
        for g in range(NTG):
            a, b = g * TG, (g + 1) * TG
            with Ph(P, f"u0_fu{g}") as ph:
                for (buf, dr, mk) in [(fsb, fs_dram, "u0fir"),
                                      (flb, fl_dram, None),
                                      (dlb, dl_dram, "u0dl")]:
                    ph.dma("sync",
                           buf[:, :].rearrange("p (c t) -> p c t", c=NCH),
                           dr.ap()[:, a:b].rearrange("(c p) t -> p c t", p=128),
                           ld, needs=(mk,) if mk else (),
                           waits=[tgel.get((g - 1, 15))])
                tld = ph.settle("sync", ld)
                for m in range(16):
                    slw = m % 2
                    ph.dma("gpsimd",
                           wf1m[:, slw * 4096:(slw + 1) * 4096].rearrange(
                               "p (k c) -> p k c", k=32),
                           a_wf1[:, m * 128:(m + 1) * 128].rearrange(
                               "(k p) c -> p k c", p=128), ldg,
                           waits=[tgel.get((g, m - 2)),
                                  tgel.get((g - 1, m + 14))])
                    tlw = ph.settle("gpsimd", ldg)
                    ps = p_h[m % 2]
                    for k in range(32):
                        if k < 8:
                            rhs = C.Xb[:, k * T + a:k * T + b]
                        elif k < 16:
                            rhs = fsb[:, (k - 8) * TG:(k - 7) * TG]
                        elif k < 24:
                            rhs = flb[:, (k - 16) * TG:(k - 15) * TG]
                        else:
                            rhs = dlb[:, (k - 24) * TG:(k - 23) * TG]
                        tmm = ph.op("tensor", lambda q, ps=ps, k=k, slw=slw,
                                    rhs=rhs: q.matmul(
                            ps[:, :], wf1m[:, slw * 4096 + k * 128:
                                           slw * 4096 + (k + 1) * 128],
                            rhs, start=(k == 0), stop=(k == 31)),
                            waits=[tlw, tld, tgel.get((g, m - 2))]
                            if k == 0 else (),
                            needs=("Xb",) if k == 0 else (),
                            inc=et if k == 31 else None)
                    tgel[(g, m)] = ph.op("scalar", lambda q, ps=ps, m=m:
                                         q.activation(
                        hidb[:, m * TG:(m + 1) * TG], ps[:, :], AF.Gelu,
                        bias=bf1b[:, m:m + 1]),
                        waits=[tmm, tf2.get((g - 1, 3))],
                        needs=("u0ld",), inc=es_)
                for tt in range(4):
                    for mc in range(16):
                        tmm = ph.op("tensor", lambda q, mc=mc, tt=tt: q.matmul(
                            p_f2[:, :],
                            hidb[:, mc * TG + tt * 128:mc * TG + (tt + 1) * 128],
                            wf2b[:, mc * 16:(mc + 1) * 16],
                            start=(mc == 0), stop=(mc == 15)),
                            waits=[tgel[(g, 15)], tgfree] if mc == 0 else (),
                            needs=("u0ldg",) if mc == 0 else (),
                            inc=et if mc == 15 else None)
                    tv = ph.op("vector", lambda q: q.tensor_add(
                        gl[:, :], p_f2[:, :], b2b[:, :]), waits=[tmm],
                        needs=("u0itb",), drain=True)
                    tv = ph.op("vector", lambda q: q.tensor_mul(
                        gl[:, :], gl[:, :], itb[:, :]), drain=True, inc=ev)
                    tsg = ph.op("scalar", lambda q: q.activation(
                        pr[:, :], gl[:, 3:16:4], AF.Sigmoid), waits=[tv],
                        inc=es_)
                    tv = ph.op("vector", lambda q: q.tensor_scalar(
                        fl_[:, :], pr[:, :], -0.09, 0.1, ALU.mult, ALU.add),
                        waits=[tsg], drain=True)
                    tv = ph.op("vector", lambda q: q.tensor_scalar(
                        fac[:, :], fl_[:, :], -1.0, 1.0, ALU.mult, ALU.add),
                        drain=True)
                    tv = ph.op("vector", lambda q: q.tensor_mul(
                        pv[:, :], fac[:, :], pr[:, :]), drain=True)
                    tv = ph.op("vector", lambda q: q.tensor_max(
                        m3[:, :], gl[:, 0:16:4], gl[:, 1:16:4]), drain=True)
                    tv = ph.op("vector", lambda q: q.tensor_max(
                        m3[:, :], m3[:, :], gl[:, 2:16:4]), drain=True)
                    for j in range(3):
                        tv = ph.op("vector", lambda q, j=j: q.tensor_sub(
                            gl[:, j:16:4], gl[:, j:16:4], m3[:, :]),
                            drain=True, inc=ev if j == 2 else None)
                    tex = None
                    for j in range(3):
                        tex = ph.op("scalar", lambda q, j=j: q.activation(
                            gl[:, j:16:4], gl[:, j:16:4], AF.Exp),
                            waits=[tv] if j == 0 else (), inc=es_)
                    tv = ph.op("vector", lambda q: q.tensor_add(
                        s3[:, :], gl[:, 0:16:4], gl[:, 1:16:4]), waits=[tex],
                        drain=True)
                    tv = ph.op("vector", lambda q: q.tensor_add(
                        s3[:, :], s3[:, :], gl[:, 2:16:4]), drain=True)
                    tv = ph.op("vector", lambda q: q.reciprocal(
                        s3[:, :], s3[:, :]), drain=True)
                    tv = ph.op("vector", lambda q: q.tensor_scalar(
                        fac[:, :], pv[:, :], -1.0, 1.0, ALU.mult, ALU.add),
                        drain=True)
                    tv = ph.op("vector", lambda q: q.tensor_mul(
                        fac[:, :], fac[:, :], s3[:, :]), drain=True)
                    for j in range(3):
                        tv = ph.op("vector", lambda q, j=j: q.tensor_mul(
                            gl[:, j:16:4], gl[:, j:16:4], fac[:, :]),
                            drain=True)
                    tv = ph.op("vector", lambda q: q.tensor_copy(
                        gl[:, 3:16:4], pv[:, :]), drain=True, inc=ev)
                    ttr = ph.op("tensor", lambda q: q.matmul(
                        p_tr[:, :], gl[:, :], C.ident[:, 0:128],
                        start=True, stop=True), waits=[tv], inc=et)
                    tgfree = ph.op("vector", lambda q, g=g, tt=tt: q.tensor_copy(
                        Grows[0:16, g * TG + tt * 128:g * TG + (tt + 1) * 128],
                        p_tr[0:16, :]), waits=[ttr], drain=True, inc=ev)
                    tf2[(g, tt)] = tgfree
        P.mark("u0gates", tgfree)

    with ExitStack() as st:
        E = st.enter_context
        wob = E(nc.sbuf_tensor("u0_wob", [128, NCH * 1024], BF16))
        gb = E(nc.sbuf_tensor("u0_gb", [128, 4 * TG], BF16))
        fst = E(nc.sbuf_tensor("u0_fst", [128, 2 * TG], BF16))
        flt = E(nc.sbuf_tensor("u0_flt", [128, 2 * TG], BF16))
        dlt = E(nc.sbuf_tensor("u0_dlt", [128, 2 * TG], BF16))
        vt = E(nc.sbuf_tensor("u0_vt", [128, 2 * TG], BF16))
        ocs = E(nc.sbuf_tensor("u0_ocs", [128, 2 * TG], F32))
        tmp = E(nc.sbuf_tensor("u0_tmp", [128, TG], F32))
        onb = E(nc.sbuf_tensor("u0_onb", [128, NCH * TG], BF16))
        sq2 = E(nc.sbuf_tensor("u0_sq2", [128, TG], F32))
        r2 = E(nc.sbuf_tensor("u0_r2", [1, TG], F32))
        p_bcg = [E(nc.psum_tensor(f"u0_pbc{i}", [128, TG], F32))
                 for i in range(2)]
        p_r = E(nc.psum_tensor("u0_pr2", [1, TG], F32))
        p_br = E(nc.psum_tensor("u0_pbr", [128, TG], F32))
        p_wo = [E(nc.psum_tensor(f"u0_pwo{i}", [128, TG], F32)) for i in range(2)]

        with Ph(P, "u0_wol") as ph:
            ph.dma("gpsimd", wob[:, :].rearrange("p (k m) -> p k m", k=NCH),
                   a_wo.rearrange("(k p) m -> p k m", p=128), ldg)
            P.mark("u0wo", ph.settle("gpsimd", ldg))

        tpb = {}
        tonb = None
        twolast = {}
        for g in range(NTG):
            a, b = g * TG, (g + 1) * TG
            for h in range(H):
                jg0 = (g * H + h) * 4
                with Ph(P, f"u0_oc{g}_{h}") as ph:
                    for j in range(4):
                        pbg = p_bcg[j % 2]
                        tb = ph.op("tensor", lambda q, pbg=pbg, h=h, j=j, a=a,
                                   b=b: q.matmul(
                            pbg[:, :],
                            selb[0:16, (4 * h + j) * 128:(4 * h + j + 1) * 128],
                            Grows[0:16, a:b],
                            start=True, stop=True),
                            waits=[tpb.get(jg0 + j - 2)],
                            needs=("u0gates", "u0ld") if j == 0 else (),
                            inc=et)
                        tpb[jg0 + j] = ph.op("vector", lambda q, pbg=pbg, j=j:
                                             q.tensor_copy(
                            gb[:, j * TG:(j + 1) * TG], pbg[:, :]),
                            waits=[tb], drain=True, inc=ev)
                    for (buf, dr, c0) in [(fst, fs_dram, a), (flt, fl_dram, a),
                                          (dlt, dl_dram, a),
                                          (vt, v_dram, 32 + a)]:
                        ph.dma("sync",
                               buf[:, :].rearrange("p (c t) -> p c t", c=2),
                               dr.ap()[h * 256:(h + 1) * 256, c0:c0 + TG]
                               .rearrange("(c p) t -> p c t", p=128), ld,
                               waits=[tonb])
                    tldt = ph.settle("sync", ld)
                    tsq = {}
                    tmr = None
                    for dj in range(2):
                        tv = ph.op("vector", lambda q, dj=dj: q.tensor_mul(
                            ocs[:, dj * TG:(dj + 1) * TG],
                            fst[:, dj * TG:(dj + 1) * TG], gb[:, 0:TG]),
                            waits=[tldt, tpb[jg0 + 3]], drain=True)
                        for (buf, j) in [(flt, 1), (dlt, 2), (vt, 3)]:
                            tv = ph.op("vector", lambda q, buf=buf, j=j, dj=dj:
                                       q.tensor_mul(
                                tmp[:, :], buf[:, dj * TG:(dj + 1) * TG],
                                gb[:, j * TG:(j + 1) * TG]), drain=True)
                            tv = ph.op("vector", lambda q, dj=dj: q.tensor_add(
                                ocs[:, dj * TG:(dj + 1) * TG],
                                ocs[:, dj * TG:(dj + 1) * TG], tmp[:, :]),
                                drain=True,
                                inc=ev if j == 3 else None)
                        tsq[dj] = ph.op("scalar", lambda q, dj=dj: q.activation(
                            sq2[:, :], ocs[:, dj * TG:(dj + 1) * TG],
                            AF.Square), waits=[tv, tmr], inc=es_)
                        tmr = ph.op("tensor", lambda q, dj=dj: q.matmul(
                            p_r[:, :], C.ones_col[:, :], sq2[:, :],
                            start=(dj == 0), stop=(dj == 1)),
                            waits=[tsq[dj]], needs=("consts",), inc=et)
                    tv = ph.op("vector", lambda q: q.tensor_scalar(
                        r2[:, :], p_r[0:1, :], 1.0 / 256, 1e-5, ALU.mult,
                        ALU.add), waits=[tmr], drain=True, inc=ev)
                    tv = ph.op("scalar", lambda q: q.activation(
                        r2[:, :], r2[:, :], AF.Ln), waits=[tv], drain=True)
                    tv = ph.op("scalar", lambda q: q.activation(
                        r2[:, :], r2[:, :], AF.Exp, scale=-0.5), drain=True,
                        inc=es_)
                    tbr = ph.op("tensor", lambda q: q.matmul(
                        p_br[:, :], C.ones_row[0:1, 0:128], r2[0:1, :],
                        start=True, stop=True), waits=[tv], inc=et)
                    for dj in range(2):
                        tv = ph.op("vector", lambda q, dj=dj: q.tensor_mul(
                            ocs[:, dj * TG:(dj + 1) * TG],
                            ocs[:, dj * TG:(dj + 1) * TG], p_br[:, :]),
                            waits=[tbr, twolast.get(g - 1)], drain=True)
                        tonb = ph.op("vector", lambda q, h=h, dj=dj:
                                     q.tensor_scalar(
                            onb[:, (h * 2 + dj) * TG:(h * 2 + dj + 1) * TG],
                            ocs[:, dj * TG:(dj + 1) * TG], nwb[:, dj:dj + 1],
                            None, ALU.mult), drain=True, inc=ev)
            with Ph(P, f"u0_wo{g}") as ph:
                for m in range(NCH):
                    pw = p_wo[m % 2]
                    for k in range(NCH):
                        tmm = ph.op("tensor", lambda q, pw=pw, k=k, m=m:
                                    q.matmul(
                            pw[:, :], wob[:, k * 1024 + m * 128:
                                          k * 1024 + (m + 1) * 128],
                            onb[:, k * TG:(k + 1) * TG],
                            start=(k == 0), stop=(k == NCH - 1)),
                            waits=[tonb, twolast.get((g, m - 2))]
                            if k == 0 else (),
                            needs=("u0wo",) if k == 0 else (),
                            inc=et if k == NCH - 1 else None)
                    twolast[(g, m)] = ph.op("vector", lambda q, pw=pw, m=m,
                                            a=a, b=b: q.tensor_add(
                        C.X[:, m * T + a:m * T + b],
                        C.X[:, m * T + a:m * T + b],
                        pw[:, :]), waits=[tmm], drain=True, inc=ev,
                        needs=("X",))
                twolast[g] = twolast[(g, NCH - 1)]
        P.mark("X", twolast[NTG - 1])

    ustack.__exit__(None, None, None)
    emit_ln(P, C, "u0ln")


def emit_unit(P, C, idx, W):
    fn = {0: emit_u_l0attn, 1: emit_u_l0ffn, 2: emit_u_l1attn, 3: emit_u_l1ffn, 4: emit_u_l2attn,
          5: emit_u_l2ffn, 6: emit_u_l3attn, 7: emit_u_l3ffn}.get(idx)
    if fn is None:
        raise NotImplementedError(f"unit {idx}")
    fn(P, C, W)


def shard_stream(x_full):
    outs = []
    for c in range(NCORES):
        bidx, half = c // 2, c % 2
        sl = x_full[bidx, half * T:(half + 1) * T, :]
        outs.append(np.ascontiguousarray(sl.T.astype(np.float32)))
    return outs


def unshard_stream(shards):
    x = np.zeros((4, SEQ, D), np.float32)
    for c in range(NCORES):
        bidx, half = c // 2, c % 2
        x[bidx, half * T:(half + 1) * T, :] = shards[c].T
    return x


def build_unit_prog(units, W, x_shards):
    P = Prog()
    C = make_ctx(P)
    a_x = P.in_("xin", x_shards)
    o_x = P.out_("xout", [D, T], F32)
    sload = P.gsem("dma_sync")
    sloadg = P.gsem("dma_gp")
    with Ph(P, "loadx") as ph:
        src = a_x.rearrange("(c p) t -> p c t", p=128)
        ph.dma("sync", C.X[:, :].rearrange("p (c t) -> p c t", c=NCH), src, sload)
        ph.dma("gpsimd", C.Xb[:, :].rearrange("p (c t) -> p c t", c=NCH), src, sloadg)
        P.mark("X", ph.settle("sync", sload))
        P.mark("Xb", ph.settle("gpsimd", sloadg))
    for u in units:
        emit_unit(P, C, u, W)
    sst = P.gsem("dma_st")
    with Ph(P, "storex") as ph:
        dst = o_x.rearrange("(c p) t -> p c t", p=128)
        ph.dma("sync", dst, C.X[:, :].rearrange("p (c t) -> p c t", c=NCH), sst,
               needs=("X",))
        ph.op("sync", lambda q: None, waits=[sst])
    return P


def run_units(units, W, x_full):
    x_shards = shard_stream(x_full)
    P = build_unit_prog(units, W, x_shards)
    res = P.run()
    out = unshard_stream([res.results[c]["xout"] for c in range(NCORES)])
    P.close()
    return out


def kernel(**inputs):
    x = np.asarray(inputs["L0_attn_hidden_states"], dtype=np.float32)
    return run_units([0, 1, 2, 3, 4, 5, 6, 7], inputs, x)
